# version 07
# baseline (speedup 1.0000x reference)
import numpy as np
import ml_dtypes
import concourse.bass as bass
import concourse.mybir as mybir
from concourse.bass_utils import run_bass_kernel_spmd

F32 = mybir.dt.float32
BF16 = mybir.dt.bfloat16
AF = mybir.ActivationFunctionType
ALU = mybir.AluOpType
AX = mybir.AxisListType

N_CORES = 8
DBG_STAGES = 5
T_CORE = 4096
BLK = 512
CH = 128
NBLK = T_CORE // BLK
NCH = BLK // CH
DM = 1024
FF = 4096
EPS = 1e-5
RM = float(np.float32(1.5 * 2.0 ** 23))
LAMBDA_INIT = 0.8 - 0.6 * float(np.exp(-0.3 * 1.0))
QSCALE = float(32 ** -0.5)


def _perms():
    pq = np.empty(1024, dtype=np.int64)
    for q in range(32):
        s, h = q // 16, q % 16
        for d in range(32):
            pq[q * 32 + d] = h * 64 + d * 2 + s
    pv = np.empty(1024, dtype=np.int64)
    for dd in range(64):
        for h in range(16):
            pv[dd * 16 + h] = h * 64 + dd
    pwo = np.empty(1024, dtype=np.int64)
    for dd in range(64):
        for a in range(16):
            pwo[dd * 16 + a] = a * 64 + dd
    return pq, pv, pwo


class Em:
    def __init__(self, nc):
        self.nc = nc
        self.ops = {k: [] for k in ("pe", "dve", "act", "pool", "sp")}
        self.sems = {}
        self.cnt = {}
        self.hwm = {k: {} for k in self.ops}

    def sem(self, name):
        self.sems[name] = self.nc.semaphore(name).__enter__()
        self.cnt[name] = 0

    def inc(self, name, by=1):
        self.cnt[name] += by
        return self.cnt[name]

    def val(self, name):
        return self.cnt[name]

    def wait(self, eng, name, v):
        if v <= 0 or self.hwm[eng].get(name, 0) >= v:
            return
        self.hwm[eng][name] = v
        s = self.sems[name]
        self.ops[eng].append(lambda e, s=s, v=v: e.wait_ge(s, v))

    def op(self, eng, fn, inc=None, by=1):
        if inc is None:
            self.ops[eng].append(fn)
            return None
        if eng in ("dve", "act", "pool"):
            self.wait(eng, inc, self.val(inc))
        v = self.inc(inc, by)
        s = self.sems[inc]
        self.ops[eng].append(lambda e, fn=fn, s=s, by=by: fn(e).then_inc(s, by))
        return v

    def pesync(self):
        self.wait("pe", "sPE", self.val("sPE"))


def build_program():
    nc = bass.Bass("TRN2", target_bir_lowering=False)

    xin = nc.dram_tensor("x", [T_CORE, DM], F32, kind="ExternalInput").ap()
    yout = nc.dram_tensor("y", [T_CORE, DM], F32, kind="ExternalOutput").ap()
    wqt = nc.dram_tensor("wqt", [DM, DM], F32, kind="ExternalInput").ap()
    wkt = nc.dram_tensor("wkt", [DM, DM], F32, kind="ExternalInput").ap()
    wvt = nc.dram_tensor("wvt", [DM, DM], F32, kind="ExternalInput").ap()
    wot = nc.dram_tensor("wot", [DM, DM], F32, kind="ExternalInput").ap()
    w1t = nc.dram_tensor("w1t", [DM, FF], F32, kind="ExternalInput").ap()
    w2t = nc.dram_tensor("w2t", [FF, DM], F32, kind="ExternalInput").ap()
    q1t = nc.dram_tensor("q1t", [DM, FF], F32, kind="ExternalInput").ap()
    q2t = nc.dram_tensor("q2t", [FF, DM], F32, kind="ExternalInput").ap()
    vec = nc.dram_tensor("vec", [16, DM], F32, kind="ExternalInput").ap()
    vb1 = nc.dram_tensor("vb1", [1, FF], F32, kind="ExternalInput").ap()
    vfb1 = nc.dram_tensor("vfb1", [1, FF], F32, kind="ExternalInput").ap()
    simg = [nc.dram_tensor(f"simg{i}", [128, 32768], BF16, kind="Internal").ap()
            for i in range(5)]

    em = Em(nc)
    for s in ("sDMA", "sPE", "sDVE", "sACT", "sPOOL", "sDMA2", "sWL0", "sWL1", "sXY0", "sXY1", "sXY2", "sXY3", "sGP", "sPF0", "sPF1", "sPF2", "sPF3"):
        em.sem(s)

    ctxs = []

    def sb(name, shape, dtype):
        t = nc.sbuf_tensor(name, shape, dtype)
        h = t.__enter__()
        ctxs.append(t)
        return h

    def psalloc(name, shape, dtype):
        t = nc.psum_tensor(name, shape, dtype)
        h = t.__enter__()
        ctxs.append(t)
        return h

    XS = sb("xs", [128, NCH * DM], F32)
    WB = [sb("wb0", [128, 32768], BF16), sb("wb1", [128, 32768], BF16)]
    UB = sb("ub", [128, DM], BF16)
    UB2 = sb("ub2", [128, DM], BF16)
    HT = sb("ht", [128, 1024], BF16)
    HTB = sb("htb", [128, 4096], BF16)
    T1Q = sb("t1q", [128, 1024], BF16)
    T1K = sb("t1k", [128, 1024], BF16)
    T1VP = sb("t1vp", [128, 2048], BF16)
    ATTN = sb("attn", [128, 1024], BF16)
    IN2 = sb("in2", [128, 1024], BF16)
    AT2T = sb("at2t", [128, 1024], BF16)
    SQ = sb("sq", [128, 2048], F32)
    XR = sb("xr", [128, 2048], BF16)
    T1N = sb("t1n", [128, 1024], BF16)
    HTOK = sb("htok", [128, 2048], BF16)
    QT = sb("qt", [128, 2048], F32)
    BROWS = sb("brows", [128, 4096], BF16)
    IDN = sb("idn", [128, 128], BF16)
    ONESB = sb("onesb", [128, 128], BF16)
    CAB = sb("cab", [128, 32], BF16)
    SM = sb("sm", [128, 12], F32)
    SM2 = sb("sm2", [128, 12], F32)
    SMX = sb("smx", [128, 32], F32)
    GSL = sb("gsl", [128, 24], F32)
    GSL2 = sb("gsl2", [128, 8], F32)
    GSLQ = sb("gslq", [128, 8], F32)
    GSB = sb("gsb", [128, 16], BF16)
    LAM = sb("lam", [128, 1], F32)
    SMALL = sb("small", [1, 8], F32)
    SMALLB = sb("smallb", [1, 2], BF16)
    CST = sb("cst", [128, 4], F32)

    EXPS = QT[:, 0:1024]
    TRASH = QT[:, 1024:2048]
    SSUB = SQ[:, 0:1024]
    T1O = HTOK
    VR = XS[:, 0:1024]
    VTV = HTB[:, 0:2048]
    VTQ = HTB[:, 2048:3072]
    VTK = HTB[:, 3072:4096]
    NEGMAX = SMX
    RINV = SMX
    FACT = SMX

    ROWMAP = {
        ("CQ", 0): (0, 0), ("CQ", 1): (0, 512),
        ("CK", 0): (32, 0), ("CK", 1): (32, 512),
        ("CV", 0): (64, 0), ("CV", 1): (64, 512),
        ("CO", 0): (96, 0), ("CO", 1): (96, 512),
    }
    for c in range(8):
        ROWMAP[("Q1R", c)] = (0, 1024 + 512 * c) if c < 4 else (64, 1024 + 512 * (c - 4))
        ROWMAP[("C1", c)] = (32, 1024 + 512 * c) if c < 4 else (96, 1024 + 512 * (c - 4))
    for c in range(2):
        ROWMAP[("C2", c)] = (0, 3072 + 512 * c)
        ROWMAP[("Q2R", c)] = (32, 3072 + 512 * c)

    def brow(name, c):
        p, col = ROWMAP[(name, c)]
        return BROWS[p:p + 1, col:col + 512], p

    PA = psalloc("pa", [128, 2048], F32)
    PB = psalloc("pb", [128, 1024], F32)
    PC = psalloc("pc", [128, 1024], F32)
    PTTA = PA[:, 0:512].bitcast(BF16)
    PTTB = PB[:, 0:512].bitcast(BF16)
    PTTC = PC[:, 0:512].bitcast(BF16)


    W = em.wait

    def dma(dst, src, inc="sDMA"):
        return em.op("sp", lambda e: e.dma_start(dst, src), inc=inc, by=16)

    for i, cv in enumerate([RM, -RM, 16.0, -0.5]):
        em.op("pool", lambda e, i=i, cv=cv: e.memset(CST[:, i:i + 1], float(cv)), inc="sPOOL")
        nc.const_aps.register(F32, float(cv), CST[:, i:i + 1])
    em.op("pool", lambda e: e.memset(ONESB[:], 1.0), inc="sPOOL")
    W("pool", "sPOOL", em.val("sPOOL"))
    em.op("pool", lambda e: e.affine_select(
        IDN[:], ONESB[:], [[-1, 128]], ALU.is_equal, 0.0,
        base=0, channel_multiplier=1), inc="sPOOL")
    em.op("pool", lambda e: e.memset(T1VP[:], 0.0), inc="sPOOL")
    em.op("pool", lambda e: e.memset(IN2[:], 0.0), inc="sPOOL")
    em.op("pool", lambda e: e.memset(XR[:], 0.0), inc="sPOOL")
    em.op("pool", lambda e: e.memset(CAB[:], 0.0), inc="sPOOL")
    em.op("pool", lambda e: e.memset(BROWS[:], 0.0), inc="sPOOL")
    W("pool", "sPOOL", em.val("sPOOL"))
    for c in range(4):
        em.op("pool", lambda e, c=c: e.affine_select(
            CAB[32 * c:32 * c + 32, 0:16], ONESB[32 * c:32 * c + 32, 0:16],
            [[-1, 16]], ALU.is_equal, 0.0, base=0, channel_multiplier=1), inc="sPOOL")
    v_consts = em.val("sPOOL")

    for j in range(8):
        dma(GSL[:, j:j + 1], vec[0:1, 128 * j:128 * j + 128].transpose([1, 0]))
    for j in range(8):
        dma(GSL[:, 8 + j:9 + j], vec[2:3, 128 * j:128 * j + 128].transpose([1, 0]))
    for j in range(8):
        dma(GSL[:, 16 + j:17 + j], vec[1:2, 128 * j:128 * j + 128].transpose([1, 0]))
    for j in range(8):
        dma(GSL2[:, j:j + 1], vec[3:4, 128 * j:128 * j + 128].transpose([1, 0]))
    v_gsl = em.val("sDMA")
    dma(VR[0:1, :], vec[4:5, :])
    dma(VR[32:33, :], vec[5:6, :])
    dma(VR[64:65, :], vec[6:7, :])
    dma(VR[96:97, 0:128], vec[7:8, 0:128])
    v_vr = em.val("sDMA")


    W("dve", "sDMA", v_vr)
    em.op("dve", lambda e: e.tensor_mul(VR[96:97, 256:288], VR[96:97, 0:32], VR[96:97, 64:96]), inc="sDVE")
    em.op("dve", lambda e: e.tensor_mul(VR[96:97, 288:320], VR[96:97, 32:64], VR[96:97, 96:128]), inc="sDVE")
    em.op("dve", lambda e: e.reduce_sum(VR[96:97, 320:321], VR[96:97, 256:288].rearrange("p (a b) -> p a b", a=1), axis=AX.X), inc="sDVE")
    em.op("dve", lambda e: e.reduce_sum(VR[96:97, 321:322], VR[96:97, 288:320].rearrange("p (a b) -> p a b", a=1), axis=AX.X), inc="sDVE")
    W("act", "sDVE", em.val("sDVE"))
    em.op("act", lambda e: e.activation(VR[96:97, 322:323], VR[96:97, 320:321], AF.Exp), inc="sACT")
    em.op("act", lambda e: e.activation(VR[96:97, 323:324], VR[96:97, 321:322], AF.Exp), inc="sACT")
    W("dve", "sACT", em.val("sACT"))
    em.op("dve", lambda e: e.tensor_sub(VR[96:97, 324:325], VR[96:97, 323:324], VR[96:97, 322:323]), inc="sDVE")
    em.op("dve", lambda e: e.tensor_scalar(SMALL[0:1, 0:1], VR[96:97, 324:325], float(LAMBDA_INIT), None, ALU.subtract), inc="sDVE")
    em.op("dve", lambda e: e.tensor_copy(SMALLB[0:1, 0:1], SMALL[0:1, 0:1]), inc="sDVE")
    W("pe", "sDVE", em.val("sDVE"))
    W("pe", "sPOOL", v_consts)
    em.op("pe", lambda e: e.matmul(PA[:, 0:1], ONESB[0:1, 0:128], SMALLB[0:1, 0:1],
                                   start=True, stop=True), inc="sPE")
    W("dve", "sPE", em.val("sPE"))
    em.op("dve", lambda e: e.tensor_copy(LAM[:, 0:1], PA[:, 0:1]), inc="sDVE")
    W("pool", "sDVE", em.val("sDVE"))
    W("pool", "sPOOL", v_consts)
    for c in range(4):
        em.op("pool", lambda e, c=c: e.affine_select(
            CAB[32 * c:32 * c + 32, 16:32],
            LAM[32 * c:32 * c + 32, 0:1].broadcast_to([32, 16]),
            [[-1, 16]], ALU.is_equal, 0.0, base=-16, channel_multiplier=1), inc="sPOOL")
    v_cab = em.val("sPOOL")

    def gdma(dst, src):
        return em.op("pool", lambda e: e.dma_start(dst, src), inc="sGP", by=16)

    for j in range(8):
        gdma(simg[0][:, 24576 + 1024 * j:24576 + 1024 * j + 1024], wot[128 * j:128 * j + 128, :])
    gp_img0 = em.val("sGP")
    gp_img = {1: 0, 2: 0}

    img_ready = {}
    rows_gate = {}
    HTBF = HTB[:].bitcast(F32)
    ring = [SQ[:, 0:2048], QT[:, 0:2048], HTBF[:, 0:1024], HTBF[:, 1024:2048]]
    ring_w = [2048, 2048, 1024, 1024]
    rs = {"i": 0, "done": {}}

    def _staged(src_ap, width, consume, row):
        while True:
            s = rs["i"] % 4
            rs["i"] += 1
            if ring_w[s] >= width:
                break
        psem = "sPF%d" % s
        buf = ring[s][0:1, 0:width] if row else ring[s][:, 0:width]
        if s in rs["done"]:
            for sem_n, v in rs["done"][s]:
                W("sp", sem_n, v)
        vd = dma(buf, src_ap, inc=psem)
        rs["done"][s] = consume(buf, vd, psem)

    def staged(src_ap, width, consume):
        _staged(src_ap, width, consume, False)

    def staged_row(src_ap, width, consume):
        _staged(src_ap, width, consume, True)

    def prep():
        W("dve", "sDMA", v_vr)
        W("act", "sDMA", v_vr)
        W("pe", "sDMA", v_vr)
        W("pe", "sPOOL", v_consts)
        em.op("dve", lambda e: e.tensor_scalar(GSLQ[:], GSL[:, 0:8], QSCALE, None, ALU.mult), inc="sDVE")
        v_gslq = em.val("sDVE")

        pa_busy = {"v": em.val("sDVE")}
        for mi, wt in enumerate([wqt, wkt, wvt]):
            for j in range(8):
                def consume(buf, vd, psem, mi=mi, j=j):
                    col = mi * 8192 + j * 1024
                    scap = GSLQ[:, j:j + 1] if mi == 0 else GSL[:, j:j + 1]
                    W("act", psem, vd)
                    if mi == 0:
                        W("act", "sDVE", v_gslq)
                    em.op("act", lambda e, col=col, scap=scap: e.activation(
                        WB[1][:, col:col + 1024], buf, AF.Copy, scale=scap), inc="sACT")
                    va = em.val("sACT")
                    W("pe", psem, vd)
                    if j == 0:
                        W("pe", "sDVE", pa_busy["v"])
                    for hh in range(2):
                        em.op("pe", lambda e, j=j, hh=hh: e.matmul(
                            PA[0:1, 512 * hh:512 * hh + 512], GSL[:, 16 + j:17 + j],
                            buf[:, 512 * hh:512 * hh + 512],
                            start=(j == 0), stop=(j == 7)), inc="sPE")
                    return [("sACT", va), ("sPE", em.val("sPE"))]
                staged(wt[128 * j:128 * j + 128, :], 1024, consume)
            W("dve", "sPE", em.val("sPE"))
            nm = ["CQ", "CK", "CV"][mi]
            sc = QSCALE if mi == 0 else 1.0
            for c in range(2):
                r, _ = brow(nm, c)
                em.op("dve", lambda e, r=r, c=c, sc=sc: e.tensor_scalar(
                    r, PA[0:1, 512 * c:512 * c + 512], float(sc), None, ALU.mult), inc="sDVE")
            pa_busy["v"] = em.val("sDVE")
        W("pool", "sDMA", v_vr)
        for c in range(2):
            r, _ = brow("CO", c)
            em.op("pool", lambda e, r=r, c=c: e.tensor_copy(r, VR[0:1, 512 * c:512 * c + 512]), inc="sPOOL")
            r2, _ = brow("C2", c)
            em.op("pool", lambda e, r=r2, c=c: e.tensor_copy(r, VR[32:33, 512 * c:512 * c + 512]), inc="sPOOL")
        v_vr_done = em.val("sPOOL")
        def qrow_consume(buf, vd, psem, names):
            W("pool", psem, vd)
            em.op("pool", lambda e: e.tensor_scalar(buf, buf, 64.0, RM, ALU.mult, ALU.add), inc="sPOOL")
            em.op("pool", lambda e: e.tensor_scalar(buf, buf, RM, -128.0, ALU.subtract, ALU.max), inc="sPOOL")
            for c, (nm, ch) in enumerate(names):
                r, _ = brow(nm, ch)
                em.op("pool", lambda e, r=r, c=c: e.tensor_scalar(
                    r, buf[0:1, 512 * c:512 * c + 512], 127.0, 1.0 / 64.0, ALU.min, ALU.mult), inc="sPOOL")
            return [("sPOOL", em.val("sPOOL"))]
        staged_row(vec[6:7, :], 1024, lambda buf, vd, psem: qrow_consume(buf, vd, psem, [("Q2R", 0), ("Q2R", 1)]))
        for hh in range(2):
            staged_row(vfb1[0:1, 2048 * hh:2048 * hh + 2048], 2048,
                       lambda buf, vd, psem, hh=hh: qrow_consume(buf, vd, psem, [("Q1R", hh * 4 + c) for c in range(4)]))
        rows_gate[0] = [("sDVE", pa_busy["v"]), ("sPOOL", v_vr_done)]
        rows_gate[3] = rows_gate[4] = [("sPOOL", em.val("sPOOL")), ("sPOOL", v_vr_done)]
        W("sp", "sACT", em.val("sACT"))
        W("sp", "sDMA2", em.val("sDMA2"))
        vstore = dma(simg[0][:, 0:24576], WB[1][:, 0:24576], inc="sDMA2")
        img_ready[0] = (vstore, gp_img0)

        for half in range(2):
            W("act", "sDMA2", em.val("sDMA2"))
            for j in range(8):
                def consume(buf, vd, psem, j=j, half=half):
                    col = j * 2048
                    W("act", psem, vd)
                    em.op("act", lambda e, col=col, j=j: e.activation(
                        WB[1][:, col:col + 2048], buf, AF.Copy, scale=GSL[:, 8 + j:9 + j]), inc="sACT")
                    va = em.val("sACT")
                    W("pe", psem, vd)
                    if j == 0:
                        W("pe", "sDVE", pa_busy["v"])
                    for qo in range(4):
                        em.op("pe", lambda e, j=j, qo=qo: e.matmul(
                            PA[0:1, 512 * qo:512 * qo + 512], GSL2[:, j:j + 1],
                            buf[:, 512 * qo:512 * qo + 512],
                            start=(j == 0), stop=(j == 7)), inc="sPE")
                    return [("sACT", va), ("sPE", em.val("sPE"))]
                staged(w1t[128 * j:128 * j + 128, 2048 * half:2048 * half + 2048], 2048, consume)
            for j in range(16):
                def consume_w2(buf, vd, psem, j=j):
                    W("act", psem, vd)
                    em.op("act", lambda e, j=j: e.copy(
                        WB[1][:, 16384 + 1024 * j:16384 + 1024 * j + 1024], buf), inc="sACT")
                    return [("sACT", em.val("sACT"))]
                staged(w2t[2048 * half + 128 * j:2048 * half + 128 * j + 128, :], 1024, consume_w2)
            def consume_b(buf, vd, psem, half=half):
                W("dve", psem, vd)
                W("dve", "sPE", em.val("sPE"))
                for qo in range(4):
                    r, _ = brow("C1", half * 4 + qo)
                    em.op("dve", lambda e, r=r, qo=qo, buf=buf: e.tensor_tensor(
                        r, buf[0:1, 512 * qo:512 * qo + 512],
                        PA[0:1, 512 * qo:512 * qo + 512], ALU.add), inc="sDVE")
                return [("sDVE", em.val("sDVE"))]
            staged_row(vb1[0:1, 2048 * half:2048 * half + 2048], 2048, consume_b)
            pa_busy["v"] = em.val("sDVE")
            W("sp", "sACT", em.val("sACT"))
            W("sp", "sDMA2", em.val("sDMA2"))
            vstore = dma(simg[1 + half][:], WB[1][:], inc="sDMA2")
            img_ready[1 + half] = (vstore, gp_img[1 + half])
        rows_gate[1] = rows_gate[2] = [("sDVE", pa_busy["v"])]

        qtile = {"i": 0}
        def quant_tile(src_ap, width, col, img):
            lane = "pool" if qtile["i"] % 3 == 2 else "dve"
            qtile["i"] += 1
            def consume(buf, vd, psem, lane=lane, col=col, width=width, img=img):
                W(lane, psem, vd)
                em.op(lane, lambda e: e.tensor_scalar(buf, buf, 64.0, RM, ALU.mult, ALU.add), inc="sDVE" if lane == "dve" else "sPOOL")
                em.op(lane, lambda e: e.tensor_scalar(buf, buf, RM, -128.0, ALU.subtract, ALU.max), inc="sDVE" if lane == "dve" else "sPOOL")
                em.op(lane, lambda e, col=col: e.tensor_scalar(
                    WB[1][:, col:col + width], buf, 127.0, 1.0 / 64.0, ALU.min, ALU.mult), inc="sDVE" if lane == "dve" else "sPOOL")
                return [(("sDVE" if lane == "dve" else "sPOOL"), em.val("sDVE" if lane == "dve" else "sPOOL"))]
            staged(src_ap, width, consume)
        for half in range(2):
            W("dve", "sDMA2", em.val("sDMA2"))
            W("pool", "sDMA2", em.val("sDMA2"))
            for j in range(8):
                quant_tile(q1t[128 * j:128 * j + 128, 2048 * half:2048 * half + 2048], 2048, j * 2048, 3 + half)
            for j in range(16):
                quant_tile(q2t[2048 * half + 128 * j:2048 * half + 128 * j + 128, :], 1024, 16384 + j * 1024, 3 + half)
            W("sp", "sDVE", em.val("sDVE"))
            W("sp", "sPOOL", em.val("sPOOL"))
            W("sp", "sDMA2", em.val("sDMA2"))
            vstore = dma(simg[3 + half][:], WB[1][:], inc="sDMA2")
            img_ready[3 + half] = (vstore, 0)

    prep()
    v_prep = (em.val("sDMA2"), em.val("sGP"))
    prep_bnd = {"pa_dve": em.val("sDVE"), "pa_act": 0, "pb_dve": 0, "pc_dve": 0, "pc_act": 0}
    v_prep_lanes = {s: em.val(s) for s in ("sDVE", "sPOOL", "sACT", "sPE")}

    for eng in ("pe", "dve", "act"):
        W(eng, "sPOOL", v_cab)

    def ln_stats(xap, compound):
        em.op("dve", lambda e: e.reduce_sum(SM[:, 0:1], xap.rearrange("p (a b) -> p a b", a=1), axis=AX.X), inc="sDVE")
        em.op("act", lambda e: e.activation(TRASH, xap, AF.Square, accum_out=SM2[:, 0:1]), inc="sACT")
        W("dve", "sACT", em.val("sACT"))
        em.op("dve", lambda e: e.tensor_scalar(SM[:, 0:1], SM[:, 0:1], 1.0 / 1024.0, None, ALU.mult), inc="sDVE")
        em.op("dve", lambda e: e.tensor_mul(SM2[:, 1:2], SM[:, 0:1], SM[:, 0:1]), inc="sDVE")
        em.op("dve", lambda e: e.tensor_scalar(SM2[:, 2:3], SM2[:, 0:1], 1.0 / 1024.0, None, ALU.mult), inc="sDVE")
        em.op("dve", lambda e: e.tensor_sub(SM2[:, 3:4], SM2[:, 2:3], SM2[:, 1:2]), inc="sDVE")
        em.op("dve", lambda e: e.tensor_scalar(SM2[:, 4:5], SM2[:, 3:4], EPS, None, ALU.add), inc="sDVE")
        W("act", "sDVE", em.val("sDVE"))
        em.op("act", lambda e: e.activation(SM2[:, 5:6], SM2[:, 4:5], AF.Ln), inc="sACT")
        em.op("act", lambda e: e.activation(SM[:, 1:2], SM2[:, 5:6], AF.Exp, scale=-0.5), inc="sACT")
        if compound:
            em.op("dve", lambda e: e.reciprocal(SM2[:, 6:7], SM2[:, 4:5]), inc="sDVE")
            em.op("dve", lambda e: e.tensor_mul(SM2[:, 7:8], SM2[:, 3:4], SM2[:, 6:7]), inc="sDVE")
            em.op("dve", lambda e: e.tensor_scalar(SM2[:, 8:9], SM2[:, 7:8], EPS, None, ALU.add), inc="sDVE")
            W("act", "sDVE", em.val("sDVE"))
            em.op("act", lambda e: e.activation(SM2[:, 9:10], SM2[:, 8:9], AF.Ln), inc="sACT")
            W("dve", "sACT", em.val("sACT"))
            em.op("dve", lambda e: e.tensor_add(SM2[:, 10:11], SM2[:, 5:6], SM2[:, 9:10]), inc="sDVE")
            W("act", "sDVE", em.val("sDVE"))
            em.op("act", lambda e: e.activation(SM[:, 2:3], SM2[:, 10:11], AF.Exp, scale=-0.5), inc="sACT")
        return em.val("sACT")

    def transposes(tiles, dst_fn, ntiles, ptt):
        em.pesync()
        for j in range(ntiles):
            em.op("pe", lambda e, j=j: e.transpose(
                ptt[:, 128 * (j % 8):128 * (j % 8) + 128], tiles(j), IDN[:]), inc="sPE")
            if j % 8 == 7:
                W("act", "sPE", em.val("sPE"))
                em.op("act", lambda e, g=j // 8: e.copy(dst_fn(g), ptt[:, 0:1024]), inc="sACT")
                if j + 1 < ntiles:
                    W("pe", "sACT", em.val("sACT"))
        return em.val("sACT")

    def bias_mm(ps_ap, name, chunk, start=False, stop=True):
        r, p = brow(name, chunk)
        em.op("pe", lambda e, r=r, p=p, ps_ap=ps_ap: e.matmul(
            ps_ap, ONESB[p:p + 1, 0:128], r, start=start, stop=stop,
            tile_position=(p, 0)), inc="sPE")

    def ubp(c):
        return UB if c % 2 == 0 else UB2

    AST = {}

    def attn_front(c, xl):
        st = AST.setdefault(c, {})
        xap = XS[:, DM * c:DM * c + DM]
        sx, vx = xl[c]
        W("dve", sx, vx)
        W("act", sx, vx)
        W("pool", sx, vx)
        if c - 2 in AST and "wo" in AST[c - 2]:
            W("pool", "sPE", AST[c - 2]["wo"])
        va = ln_stats(xap, compound=True)
        W("pool", "sACT", va)
        W("pool", "sDVE", em.val("sDVE"))
        em.op("pool", lambda e: e.tensor_scalar(ubp(c)[:], xap, SM[:, 0:1], SM[:, 2:3], ALU.subtract, ALU.mult), inc="sPOOL")
        p2 = 8 + 2 * (c % 2)
        em.op("pool", lambda e, p2=p2: e.tensor_copy(SM[:, p2:p2 + 2], SM[:, 0:2]), inc="sPOOL")
        st["ub"] = em.val("sPOOL")

    def attn_fq(c, wb, v_wload, prev, bnd=None):
        st = AST[c]
        if prev is not None:
            W("pe", "sDVE", AST[prev]["t1_dve"])
            W("pe", "sACT", max(AST[prev]["t1_act"], AST[prev]["exp"]))
        else:
            W("pe", "sDVE", max(bnd["pa_dve"], bnd["pb_dve"]))
            W("pe", "sACT", bnd["pa_act"])
        W("pe", "sPOOL", st["ub"])
        va = transposes(lambda j: ubp(c)[:, 128 * j:128 * j + 128], lambda g: HT[:, 0:1024], 8, PTTB)
        st["ht_pe"] = em.val("sPE")
        W("pe", "sACT", va)
        W("pe", v_wload[0], v_wload[1])
        em.pesync()
        for j in range(8):
            for mi in range(3):
                dstps, dcol = (PA, 1024 * mi) if mi < 2 else (PB, 0)
                for half in range(2):
                    em.op("pe", lambda e, j=j, mi=mi, half=half, dstps=dstps, dcol=dcol: e.matmul(
                        dstps[:, dcol + 512 * half:dcol + 512 * half + 512],
                        HT[:, 128 * j:128 * j + 128],
                        WB[wb][:, 8192 * mi + 1024 * j + 512 * half:8192 * mi + 1024 * j + 512 * half + 512],
                        start=(j == 0), stop=False), inc="sPE")
        for mi, nm in enumerate(["CQ", "CK", "CV"]):
            dstps, dcol = (PA, 1024 * mi) if mi < 2 else (PB, 0)
            for half in range(2):
                bias_mm(dstps[:, dcol + 512 * half:dcol + 512 * half + 512], nm, half)
        st["qkv"] = em.val("sPE")

    def attn_evac_vt(c, prevst=None):
        st = AST[c]
        W("dve", "sPE", st["qkv"])
        W("act", "sPE", st["qkv"])
        if prevst is not None:
            W("act", "sDVE", prevst["applied_d"])
            W("act", "sPOOL", prevst["applied_p"])
            W("dve", "sPE", prevst["o2"])
        em.op("dve", lambda e: e.tensor_copy(T1Q[:], PA[:, 0:1024]), inc="sDVE")
        em.op("act", lambda e: e.copy(T1K[:], PA[:, 1024:2048]), inc="sACT")
        st["t1_act"] = em.val("sACT")
        em.op("dve", lambda e: e.tensor_copy(
            T1VP[:].rearrange("p (d h) -> p d h", d=64)[:, :, 0:16], PB[:, 0:1024]), inc="sDVE")
        st["t1_dve"] = em.val("sDVE")
        W("dve", "sACT", st["t1_act"])
        em.op("dve", lambda e: e.transpose(VTQ[:], T1Q[:]), inc="sDVE")
        em.op("dve", lambda e: e.transpose(VTK[:], T1K[:]), inc="sDVE")
        em.op("dve", lambda e: e.transpose(VTV[:], T1VP[:]), inc="sDVE")
        st["vt"] = em.val("sDVE")

    def attn_scores(c):
        st = AST[c]
        W("pe", "sDVE", st["vt"])
        W("pe", "sACT", st["t1_act"])
        em.pesync()
        for i in range(128):
            r, tl = i % 4, i // 4
            em.op("pe", lambda e, r=r, tl=tl: e.matmul(
                PA[32 * r:32 * r + 32, 32 * tl:32 * tl + 32],
                VTQ[32 * r:32 * r + 32, :].rearrange("p (w cc) -> p w cc", cc=32)[:, :, tl],
                VTK[32 * r:32 * r + 32, :].rearrange("p (w cc) -> p w cc", cc=32)[:, :, tl],
                start=True, stop=True, tile_position=(32 * r, 32 * r)), inc="sPE")
        st["scores"] = em.val("sPE")

    def attn_softmax(c):
        st = AST[c]
        W("act", "sPE", st["scores"])
        em.op("act", lambda e: e.activation(EXPS, PA[:, 0:1024], AF.Exp), inc="sACT")
        st["exp"] = em.val("sACT")
        W("dve", "sACT", st["exp"])
        em.op("dve", lambda e: e.reduce_sum(
            RINV[:], EXPS.rearrange("p (t k) -> p t k", k=32), axis=AX.X), inc="sDVE")
        em.op("dve", lambda e: e.reciprocal(RINV[:], RINV[:]), inc="sDVE")
        st["rinv"] = em.val("sDVE")
        W("pool", "sDVE", st["rinv"])
        em.op("pool", lambda e: e.tensor_tensor(
            ATTN[:].rearrange("p (t k) -> p t k", k=32),
            EXPS.rearrange("p (t k) -> p t k", k=32),
            RINV[:].unsqueeze(2).broadcast_to([128, 32, 32]), ALU.mult), inc="sPOOL")
        st["attn"] = em.val("sPOOL")

    def attn_combine(c, prev, bnd=None):
        st = AST[c]
        W("pe", "sPOOL", st["attn"])
        W("pe", "sPOOL", v_cab)
        if prev is not None and "conn" in AST.get(prev, {}):
            W("pe", "sDVE", AST[prev]["conn"])
        elif bnd is not None:
            W("pe", "sDVE", bnd["pc_dve"])
            W("pe", "sACT", bnd["pc_act"])
        em.pesync()
        for r in range(4):
            for half in range(2):
                em.op("pe", lambda e, r=r, half=half: e.matmul(
                    PC[32 * r:32 * r + 16, 0:512],
                    CAB[32 * r:32 * r + 32, 16 * half:16 * half + 16],
                    ATTN[32 * r:32 * r + 32, :].rearrange("p (t k) -> p t k", k=32)[:, :, 16 * half:16 * half + 16],
                    start=(half == 0), stop=(half == 1), tile_position=(32 * r, 32 * r)), inc="sPE")
        st["comb"] = em.val("sPE")

    def attn_in2(c):
        st = AST[c]
        W("act", "sPE", st["comb"])
        for r in range(4):
            em.op("act", lambda e, r=r: e.copy(
                IN2[32 * r:32 * r + 16, :].rearrange("p (t k) -> p t k", k=32)[:, :, 0:16],
                PC[32 * r:32 * r + 16, 0:512]), inc="sACT")
        st["in2"] = em.val("sACT")
        W("dve", "sACT", st["in2"])
        em.op("dve", lambda e: e.transpose(AT2T[:], IN2[:]), inc="sDVE")
        st["at2t"] = em.val("sDVE")

    def attn_einsum2(c):
        st = AST[c]
        W("pe", "sDVE", st["at2t"])
        W("pe", "sACT", st["in2"])
        for h in range(2):
            em.pesync()
            for i in range(64):
                r, tj = i % 4, i // 4
                tl = 16 * h + tj
                em.op("pe", lambda e, r=r, tl=tl, tj=tj: e.matmul(
                    PC[32 * r:32 * r + 32, 64 * tj:64 * tj + 64],
                    AT2T[32 * r:32 * r + 32, 32 * tl:32 * tl + 32],
                    VTV[32 * r:32 * r + 32, :].rearrange("p (w cc) -> p w cc", cc=32)[:, :, tl],
                    start=True, stop=True, tile_position=(32 * r, 32 * r)), inc="sPE")
            W("act", "sPE", em.val("sPE"))
            dstb = T1Q if h == 0 else T1K
            em.op("act", lambda e, dstb=dstb: e.copy(dstb[:], PC[:]), inc="sACT")
            em.op("act", lambda e, h=h: e.activation(
                SQ[:, 1024 * h:1024 * h + 1024], PC[:], AF.Square), inc="sACT")
            st["o2e%d" % h] = em.val("sACT")
            if h == 0:
                W("pe", "sACT", st["o2e0"])
        st["o2"] = em.val("sPE")

    def attn_rms(c):
        st = AST[c]
        W("dve", "sACT", st["o2e1"])
        em.op("dve", lambda e: e.reduce_sum(
            FACT[:], SQ[:].rearrange("p (t d) -> p t d", d=64), axis=AX.X), inc="sDVE")
        em.op("dve", lambda e: e.tensor_scalar(FACT[:], FACT[:], 1.0 / 64.0, EPS, ALU.mult, ALU.add), inc="sDVE")
        W("act", "sDVE", em.val("sDVE"))
        em.op("act", lambda e: e.activation(FACT[:], FACT[:], AF.Ln), inc="sACT")
        em.op("act", lambda e: e.activation(FACT[:], FACT[:], AF.Exp, scale=-0.5), inc="sACT")
        W("dve", "sACT", em.val("sACT"))
        em.op("dve", lambda e: e.tensor_scalar(FACT[:], FACT[:], float(1.0 - LAMBDA_INIT), None, ALU.mult), inc="sDVE")
        vsm = em.val("sDVE")
        em.op("dve", lambda e: e.tensor_tensor(
            bass.AP(XR[:].tensor, XR[:].offset, [[2048, 128], [1, 16], [32, 64]]),
            T1Q[:].rearrange("p (t d) -> p t d", d=64),
            FACT[:, 0:16].unsqueeze(2).broadcast_to([128, 16, 64]), ALU.mult), inc="sDVE")
        st["applied_d"] = em.val("sDVE")
        W("pool", "sDVE", vsm)
        W("pool", "sACT", st["o2e1"])
        em.op("pool", lambda e: e.tensor_tensor(
            bass.AP(XR[:].tensor, XR[:].offset + 16, [[2048, 128], [1, 16], [32, 64]]),
            T1K[:].rearrange("p (t d) -> p t d", d=64),
            FACT[:, 16:32].unsqueeze(2).broadcast_to([128, 16, 64]), ALU.mult), inc="sPOOL")
        st["applied_p"] = em.val("sPOOL")
        W("dve", "sPOOL", st["applied_p"])
        em.op("dve", lambda e: e.transpose(T1O[:], XR[:]), inc="sDVE")
        st["t1o"] = em.val("sDVE")
        W("pool", "sDVE", st["t1o"])
        em.op("pool", lambda e: e.tensor_copy(
            T1N[:].rearrange("p (d a) -> p d a", a=16),
            T1O[:].rearrange("p (d a) -> p d a", a=32)[:, :, 0:16]), inc="sPOOL")
        st["t1n"] = em.val("sPOOL")

    def attn_fmwo(c, wb):
        st = AST[c]
        W("pe", "sPOOL", st["t1n"])
        if c - 1 in AST and "fm" in AST[c - 1]:
            W("pe", "sACT", AST[c - 1]["fm"])
        va = transposes(lambda j: T1N[:, 128 * j:128 * j + 128], lambda g: ubp(c)[:, 0:1024], 8, PTTC)
        st["fm"] = va
        W("pe", "sACT", va)
        em.pesync()
        for j in range(8):
            for half in range(2):
                em.op("pe", lambda e, j=j, half=half: e.matmul(
                    PC[:, 512 * half:512 * half + 512],
                    ubp(c)[:, 128 * j:128 * j + 128],
                    WB[wb][:, 24576 + 1024 * j + 512 * half:24576 + 1024 * j + 512 * half + 512],
                    start=(j == 0), stop=False), inc="sPE")
        for half in range(2):
            bias_mm(PC[:, 512 * half:512 * half + 512], "CO", half)
        st["wo"] = em.val("sPE")

    def attn_conn(c):
        st = AST[c]
        xap = XS[:, DM * c:DM * c + DM]
        p2 = 8 + 2 * (c % 2)
        em.op("pool", lambda e, p2=p2: e.tensor_scalar(
            SSUB[:], xap, SM[:, p2:p2 + 1], SM[:, p2 + 1:p2 + 2], ALU.subtract, ALU.mult), inc="sPOOL")
        vss = em.val("sPOOL")
        W("dve", "sPE", st["wo"])
        W("dve", "sPOOL", vss)
        em.op("dve", lambda e: e.tensor_tensor(xap, SSUB[:], PC[:], ALU.add), inc="sDVE")
        st["conn"] = em.val("sDVE")

    def stage_attn_pipelined(wb, v_wload, xl, bnd):
        AST.clear()
        attn_front(0, xl)
        attn_fq(0, wb, v_wload, None, bnd)
        attn_evac_vt(0)
        for cm in range(NCH):
            cf = cm + 1 if cm + 1 < NCH else None
            if cf is not None:
                attn_front(cf, xl)
            attn_scores(cm)
            attn_softmax(cm)
            if cf is not None:
                attn_fq(cf, wb, v_wload, cm)
            attn_combine(cm, cm - 1 if cm > 0 else None, bnd)
            attn_in2(cm)
            attn_einsum2(cm)
            attn_rms(cm)
            attn_fmwo(cm, wb)
            attn_conn(cm)
            if cf is not None:
                attn_evac_vt(cf, AST[cm])
        return {"pa_dve": AST[3]["t1_dve"], "pa_act": AST[3]["exp"],
                "pb_dve": AST[3]["t1_dve"], "pc_dve": AST[3]["conn"], "pc_act": 0}

    FST = {}

    def ffn_front(blk, c, half, quant):
        st = FST.setdefault(c, {})
        xap = XS[:, DM * c:DM * c + DM]
        if half == 1:
            return
        W("act", "sDVE", em.val("sDVE"))
        va = ln_stats(xap, compound=False)
        W("dve", "sACT", va)
        if not quant:
            em.op("dve", lambda e: e.tensor_scalar(
                ubp(c)[:], xap, SM[:, 0:1], SM[:, 1:2], ALU.subtract, ALU.mult), inc="sDVE")
        else:
            em.op("dve", lambda e: e.tensor_scalar(SM[:, 3:4], SM[:, 1:2], 16.0, None, ALU.mult), inc="sDVE")
            em.op("dve", lambda e: e.tensor_scalar(
                SSUB[:], xap, SM[:, 0:1], SM[:, 3:4], ALU.subtract, ALU.mult), inc="sDVE")
            em.op("dve", lambda e: e.tensor_scalar(
                SSUB[:], SSUB[:], RM, RM, ALU.add, ALU.subtract), inc="sDVE")
            em.op("dve", lambda e: e.tensor_scalar(
                SSUB[:], SSUB[:], -128.0, 127.0, ALU.max, ALU.min), inc="sDVE")
            em.op("dve", lambda e: e.tensor_scalar(
                ubp(c)[:], SSUB[:], 1.0 / 16.0, None, ALU.mult), inc="sDVE")
        st["ub"] = em.val("sDVE")

    def ffn_front_T(c, half, ptt_free=0):
        st = FST[c]
        if half == 1:
            return
        W("pe", "sDVE", st["ub"])
        if ptt_free:
            W("pe", "sACT", ptt_free)
        va = transposes(lambda j: ubp(c)[:, 128 * j:128 * j + 128],
                        lambda g, c=c: HTB[:, 1024 * c:1024 * c + 1024], 8, PTTC)
        st["stash"] = va
        st["stash_pe"] = em.val("sPE")

    def ffn_w1(c, wb, v_wload, half, quant, prev, bnd=None):
        st = FST[c]
        W("pe", "sACT", st["stash"])
        if prev is not None and "evac" in FST.get(prev, {}):
            W("pe", "sACT", FST[prev]["evac"])
            W("pe", "sDVE", FST[prev].get("evac_dve", 0))
        elif bnd is not None:
            W("pe", "sACT", bnd["pa_act"])
            W("pe", "sDVE", bnd["pa_dve"])
        W("pe", v_wload[0], v_wload[1])
        em.pesync()
        base0 = 1024 * c
        for j in range(8):
            for qo in range(4):
                em.op("pe", lambda e, j=j, qo=qo, base0=base0: e.matmul(
                    PA[:, 512 * qo:512 * qo + 512],
                    HTB[:, base0 + 128 * j:base0 + 128 * j + 128],
                    WB[wb][:, 2048 * j + 512 * qo:2048 * j + 512 * qo + 512],
                    start=(j == 0), stop=False), inc="sPE")
        nm1 = "Q1R" if quant else "C1"
        for qo in range(4):
            bias_mm(PA[:, 512 * qo:512 * qo + 512], nm1, half * 4 + qo)
        st["w1"] = em.val("sPE")

    def ffn_evac(c, half, quant):
        st = FST[c]
        W("act", "sPE", st["w1"])
        if not quant:
            em.op("act", lambda e: e.activation(HTOK[:, 0:1024], PA[:, 0:1024], AF.Relu), inc="sACT")
            em.op("act", lambda e: e.activation(HTOK[:, 1024:2048], PA[:, 1024:2048], AF.Relu), inc="sACT")
            st["evac"] = em.val("sACT")
            st["evac_dve"] = 0
            st["hid"] = ("act", st["evac"])
        else:
            W("act", "sDVE", em.val("sDVE"))
            em.op("act", lambda e: e.activation(QT[:, 0:1024], PA[:, 0:1024], AF.Identity, bias=RM, scale=16.0), inc="sACT")
            em.op("act", lambda e: e.activation(QT[:, 1024:2048], PA[:, 1024:2048], AF.Identity, bias=RM, scale=16.0), inc="sACT")
            em.op("act", lambda e: e.activation(SQ[:, 0:1024], QT[:, 0:1024], AF.Relu, bias=-RM, scale=1.0), inc="sACT")
            em.op("act", lambda e: e.activation(SQ[:, 1024:2048], QT[:, 1024:2048], AF.Relu, bias=-RM, scale=1.0), inc="sACT")
            st["evac"] = em.val("sACT")
            W("dve", "sACT", st["evac"])
            em.op("dve", lambda e: e.tensor_scalar(
                HTOK[:], SQ[:, 0:2048], 127.0, 1.0 / 16.0, ALU.min, ALU.mult), inc="sDVE")
            st["evac_dve"] = em.val("sDVE")
            st["hid"] = ("dve", st["evac_dve"])

    def ffn_hidT_w2(c, wb, half, quant, prev, ptt_free=0, pb_free=0):
        st = FST[c]
        eng, v = st["hid"]
        W("pe", "sACT" if eng == "act" else "sDVE", v)
        if ptt_free:
            W("pe", "sACT", ptt_free)
        va = transposes(lambda j: HTOK[:, 128 * j:128 * j + 128],
                        lambda g: (T1Q if g == 0 else T1K)[:, 0:1024], 16, PTTC)
        st["hidT_evac"] = va
        W("pe", "sACT", va)
        if prev is not None and "res" in FST.get(prev, {}):
            W("pe", "sDVE", FST[prev]["res"])
        elif pb_free:
            W("pe", "sDVE", pb_free)
        em.pesync()
        for j in range(16):
            srcb = T1Q if j < 8 else T1K
            jj = j % 8
            for hh in range(2):
                em.op("pe", lambda e, srcb=srcb, jj=jj, j=j, hh=hh: e.matmul(
                    PB[:, 512 * hh:512 * hh + 512],
                    srcb[:, 128 * jj:128 * jj + 128],
                    WB[wb][:, 16384 + 1024 * j + 512 * hh:16384 + 1024 * j + 512 * hh + 512],
                    start=(j == 0), stop=(half == 0 and j == 15)), inc="sPE")
        if half == 1:
            nm2 = "Q2R" if quant else "C2"
            for hh in range(2):
                bias_mm(PB[:, 512 * hh:512 * hh + 512], nm2, hh)
        st["w2"] = em.val("sPE")

    def ffn_residual(blk, c, half, quant, last_stage=False, xl_next=None):
        st = FST[c]
        xap = XS[:, DM * c:DM * c + DM]
        W("dve", "sPE", st["w2"])
        em.op("dve", lambda e: e.tensor_tensor(xap, xap, PB[:], ALU.add), inc="sDVE")
        if half == 1 and not quant:
            W("sp", "sDVE", em.val("sDVE"))
            xr = dma(SQ[:, 0:1024],
                     bass.AP(xin.tensor, (blk * BLK + c * CH) * DM, [[DM, 128], [1, DM]]))
            W("dve", "sDMA", xr)
            em.op("dve", lambda e: e.tensor_tensor(xap, xap, SQ[:, 0:1024], ALU.add), inc="sDVE")
        st["res"] = em.val("sDVE")
        if last_stage:
            sx = "sXY%d" % c
            W("sp", "sDVE", st["res"])
            dma(bass.AP(yout.tensor, (blk * BLK + c * CH) * DM, [[DM, 128], [1, DM]]),
                XS[:, DM * c:DM * c + DM], inc=sx)
            if blk + 1 < NBLK:
                W("sp", sx, em.val(sx))
                dma(XS[:, DM * c:DM * c + DM],
                    bass.AP(xin.tensor, ((blk + 1) * BLK + c * CH) * DM, [[DM, 128], [1, DM]]),
                    inc=sx)
                xl_next[c] = (sx, em.val(sx))

    def stage_ffn_pipelined(blk, wb, v_wload, half, quant, bnd, last_stage=False, xl_next=None):
        if half == 0:
            FST.clear()
        ffn_front(blk, 0, half, quant)
        W("pe", "sDVE", bnd["pc_dve"])
        ffn_front_T(0, half, ptt_free=bnd["pc_act"])
        for cm in range(NCH):
            cf = cm + 1 if cm + 1 < NCH else None
            if cf is not None:
                ffn_front(blk, cf, half, quant)
            ffn_w1(cm, wb, v_wload, half, quant, cm - 1 if cm > 0 else None, bnd)
            ffn_evac(cm, half, quant)
            prev_hevac = FST.get(cm - 1, {}).get("hidT_evac", 0)
            if cm == 0:
                prev_hevac = FST[0].get("stash", 0) if half == 0 else bnd["pc_act"]
            if cf is not None:
                ffn_front_T(cf, half, ptt_free=prev_hevac)
            if half == 0 and cf is not None:
                pfree = FST[cf].get("stash", 0)
            else:
                pfree = prev_hevac
            ffn_hidT_w2(cm, wb, half, quant, cm - 1 if cm > 0 else None, ptt_free=pfree,
                        pb_free=bnd["pb_dve"] if cm == 0 else 0)
            ffn_residual(blk, cm, half, quant, last_stage=last_stage, xl_next=xl_next)
        return {"pa_act": FST[3]["evac"], "pa_dve": FST[3].get("evac_dve", 0),
                "pb_dve": FST[3]["res"], "pc_act": FST[3]["hidT_evac"], "pc_dve": 0}

    prev_pe_done = [0, 0]
    bnd = prep_bnd
    gidx = 0
    xl = {}
    for blk in range(NBLK):
        if blk == 0:
            W("sp", "sDVE", em.val("sDVE"))
            W("sp", "sPOOL", v_prep_lanes["sPOOL"])
            for c in range(NCH):
                dma(XS[:, DM * c:DM * c + DM],
                    bass.AP(xin.tensor, (c * CH) * DM, [[DM, 128], [1, DM]]),
                    inc="sXY%d" % c)
                xl[c] = ("sXY%d" % c, em.val("sXY%d" % c))
        for sidx in range(DBG_STAGES):
            par = gidx % 2
            wsem = "sWL%d" % par
            if gidx in (1, 3):
                W("sp", "sDMA2", v_prep[0])
                W("sp", "sGP", v_prep[1])
            else:
                st_v, gp_v = img_ready[sidx]
                W("sp", "sDMA2", st_v)
                if gp_v:
                    W("sp", "sGP", gp_v)
            for sem_n, v in rows_gate[sidx]:
                W("pe", sem_n, v)
            if blk == 0:
                W("pe", "sPOOL", v_prep_lanes["sPOOL"])
            if prev_pe_done[par] > 0:
                W("sp", "sPE", prev_pe_done[par])
            wl = dma(WB[par][:], simg[sidx][:], inc=wsem)
            if sidx == 0:
                bnd = stage_attn_pipelined(par, (wsem, wl), xl, bnd)
            elif sidx in (1, 2):
                bnd = stage_ffn_pipelined(blk, par, (wsem, wl), sidx - 1, False, bnd)
            else:
                last = (sidx == DBG_STAGES - 1)
                xln = {}
                bnd = stage_ffn_pipelined(blk, par, (wsem, wl), sidx - 3, True, bnd,
                                          last_stage=last, xl_next=xln)
                if last:
                    xl = xln
            prev_pe_done[par] = em.val("sPE")
            gidx += 1

    finals = [("sXY%d" % c, em.val("sXY%d" % c)) for c in range(NCH)]
    for _s, _v in finals:
        em.ops["sp"].append(lambda e, _s=_s, _v=_v: e.wait_ge(em.sems[_s], _v))

    with nc.Block() as block:
        block.sync(lambda e: ([f(e) for f in em.ops["sp"]], None)[1])
        block.tensor(lambda e: ([f(e) for f in em.ops["pe"]], None)[1])
        block.vector(lambda e: ([f(e) for f in em.ops["dve"]], None)[1])
        block.scalar(lambda e: ([f(e) for f in em.ops["act"]], None)[1])
        block.gpsimd(lambda e: ([f(e) for f in em.ops["pool"]], None)[1])

    for c in reversed(ctxs):
        c.__exit__(None, None, None)
    return nc


def _prep_host(inputs):
    pq, pv, pwo = _perms()
    f32 = np.float32
    Wq = np.asarray(inputs["L0_attn_Wq"], f32)
    Wk = np.asarray(inputs["L0_attn_Wk"], f32)
    Wv = np.asarray(inputs["L0_attn_Wv"], f32)
    Wo = np.asarray(inputs["L0_attn_Wo"], f32)
    feed = {
        "wqt": np.ascontiguousarray(Wq[pq, :].T),
        "wkt": np.ascontiguousarray(Wk[pq, :].T),
        "wvt": np.ascontiguousarray(Wv[pv, :].T),
        "wot": np.ascontiguousarray(Wo[:, pwo].T),
        "w1t": np.ascontiguousarray(np.asarray(inputs["L0_attn_W1"], f32).T),
        "w2t": np.ascontiguousarray(np.asarray(inputs["L0_attn_W2"], f32).T),
        "q1t": np.ascontiguousarray(np.asarray(inputs["L0_ffn_w1"], f32).T),
        "q2t": np.ascontiguousarray(np.asarray(inputs["L0_ffn_w2"], f32).T),
        "vb1": np.ascontiguousarray(np.asarray(inputs["L0_attn_b1"], f32).reshape(1, FF)),
        "vfb1": np.ascontiguousarray(np.asarray(inputs["L0_ffn_b1"], f32).reshape(1, FF)),
    }
    vecm = np.zeros((16, DM), f32)
    vecm[0] = np.asarray(inputs["L0_attn_ln1_g"], f32)
    vecm[1] = np.asarray(inputs["L0_attn_ln1_b"], f32)
    vecm[2] = np.asarray(inputs["L0_attn_ln2_g"], f32)
    vecm[3] = np.asarray(inputs["L0_attn_ln2_b"], f32)
    vecm[4] = np.asarray(inputs["L0_attn_bo"], f32)
    vecm[5] = np.asarray(inputs["L0_attn_b2"], f32)
    vecm[6] = np.asarray(inputs["L0_ffn_b2"], f32)
    vecm[7, 0:32] = np.asarray(inputs["L0_attn_lq1"], f32)
    vecm[7, 32:64] = np.asarray(inputs["L0_attn_lq2"], f32)
    vecm[7, 64:96] = np.asarray(inputs["L0_attn_lk1"], f32)
    vecm[7, 96:128] = np.asarray(inputs["L0_attn_lk2"], f32)
    feed["vec"] = vecm
    return feed


def kernel(**inputs):
    x = np.asarray(inputs["L0_attn_x"], np.float32).reshape(-1, DM)
    feed = _prep_host(inputs)
    nc = build_program()
    in_maps = []
    for c in range(N_CORES):
        m = dict(feed)
        m["x"] = np.ascontiguousarray(x[c * T_CORE:(c + 1) * T_CORE])
        in_maps.append(m)
    res = run_bass_kernel_spmd(nc, in_maps, core_ids=list(range(N_CORES)))
    y = np.concatenate([np.asarray(res.results[c]["y"]) for c in range(N_CORES)], axis=0)
    return y.reshape(4, 8192, DM)


# version 08
# speedup vs baseline: 1.0931x; 1.0931x over previous
import numpy as np
import ml_dtypes
import concourse.bass as bass
import concourse.mybir as mybir
from concourse.bass_utils import run_bass_kernel_spmd

F32 = mybir.dt.float32
BF16 = mybir.dt.bfloat16
AF = mybir.ActivationFunctionType
ALU = mybir.AluOpType
AX = mybir.AxisListType

N_CORES = 8
DBG_STAGES = 5
T_CORE = 4096
BLK = 512
CH = 128
NBLK = T_CORE // BLK
NCH = BLK // CH
DM = 1024
FF = 4096
EPS = 1e-5
RM = float(np.float32(1.5 * 2.0 ** 23))
LAMBDA_INIT = 0.8 - 0.6 * float(np.exp(-0.3 * 1.0))
QSCALE = float(32 ** -0.5)


def _perms():
    pq = np.empty(1024, dtype=np.int64)
    for q in range(32):
        s, h = q // 16, q % 16
        for d in range(32):
            pq[q * 32 + d] = h * 64 + d * 2 + s
    pv = np.empty(1024, dtype=np.int64)
    for dd in range(64):
        for h in range(16):
            pv[dd * 16 + h] = h * 64 + dd
    pwo = np.empty(1024, dtype=np.int64)
    for dd in range(64):
        for a in range(16):
            pwo[dd * 16 + a] = a * 64 + dd
    return pq, pv, pwo


class Em:
    def __init__(self, nc):
        self.nc = nc
        self.ops = {k: [] for k in ("pe", "dve", "act", "pool", "sp")}
        self.sems = {}
        self.cnt = {}
        self.hwm = {k: {} for k in self.ops}

    def sem(self, name):
        self.sems[name] = self.nc.semaphore(name).__enter__()
        self.cnt[name] = 0

    def inc(self, name, by=1):
        self.cnt[name] += by
        return self.cnt[name]

    def val(self, name):
        return self.cnt[name]

    def wait(self, eng, name, v):
        if v <= 0 or self.hwm[eng].get(name, 0) >= v:
            return
        self.hwm[eng][name] = v
        s = self.sems[name]
        self.ops[eng].append(lambda e, s=s, v=v: e.wait_ge(s, v))

    def op(self, eng, fn, inc=None, by=1):
        if inc is None:
            self.ops[eng].append(fn)
            return None
        if eng in ("dve", "act", "pool"):
            self.wait(eng, inc, self.val(inc))
        v = self.inc(inc, by)
        s = self.sems[inc]
        self.ops[eng].append(lambda e, fn=fn, s=s, by=by: fn(e).then_inc(s, by))
        return v

    def pesync(self):
        self.wait("pe", "sPE", self.val("sPE"))


def build_program():
    nc = bass.Bass("TRN2", target_bir_lowering=False)

    xin = nc.dram_tensor("x", [T_CORE, DM], F32, kind="ExternalInput").ap()
    yout = nc.dram_tensor("y", [T_CORE, DM], F32, kind="ExternalOutput").ap()
    wqt = nc.dram_tensor("wqt", [DM, DM], F32, kind="ExternalInput").ap()
    wkt = nc.dram_tensor("wkt", [DM, DM], F32, kind="ExternalInput").ap()
    wvt = nc.dram_tensor("wvt", [DM, DM], F32, kind="ExternalInput").ap()
    wot = nc.dram_tensor("wot", [DM, DM], F32, kind="ExternalInput").ap()
    w1t = nc.dram_tensor("w1t", [DM, FF], F32, kind="ExternalInput").ap()
    w2t = nc.dram_tensor("w2t", [FF, DM], F32, kind="ExternalInput").ap()
    q1t = nc.dram_tensor("q1t", [DM, FF], F32, kind="ExternalInput").ap()
    q2t = nc.dram_tensor("q2t", [FF, DM], F32, kind="ExternalInput").ap()
    vec = nc.dram_tensor("vec", [16, DM], F32, kind="ExternalInput").ap()
    vb1 = nc.dram_tensor("vb1", [1, FF], F32, kind="ExternalInput").ap()
    vfb1 = nc.dram_tensor("vfb1", [1, FF], F32, kind="ExternalInput").ap()
    simg = [nc.dram_tensor(f"simg{i}", [128, 32768], BF16, kind="Internal").ap()
            for i in range(5)]

    em = Em(nc)
    for s in ("sDMA", "sPE", "sDVE", "sACT", "sPOOL", "sDMA2", "sWL0", "sWL1", "sXY0", "sXY1", "sXY2", "sXY3", "sGP", "sPF0", "sPF1", "sPF2", "sPF3"):
        em.sem(s)

    ctxs = []

    def sb(name, shape, dtype):
        t = nc.sbuf_tensor(name, shape, dtype)
        h = t.__enter__()
        ctxs.append(t)
        return h

    def psalloc(name, shape, dtype):
        t = nc.psum_tensor(name, shape, dtype)
        h = t.__enter__()
        ctxs.append(t)
        return h

    XS = sb("xs", [128, NCH * DM], F32)
    WB = [sb("wb0", [128, 32768], BF16), sb("wb1", [128, 32768], BF16)]
    UB = sb("ub", [128, DM], BF16)
    UB2 = sb("ub2", [128, DM], BF16)
    HT = sb("ht", [128, 1024], BF16)
    HTB = sb("htb", [128, 4096], BF16)
    T1Q = sb("t1q", [128, 1024], BF16)
    T1K = sb("t1k", [128, 1024], BF16)
    T1VP = sb("t1vp", [128, 2048], BF16)
    ATTN = sb("attn", [128, 1024], BF16)
    IN2 = sb("in2", [128, 1024], BF16)
    AT2T = sb("at2t", [128, 1024], BF16)
    SQ = sb("sq", [128, 2048], F32)
    XR = sb("xr", [128, 2048], BF16)
    T1N = sb("t1n", [128, 1024], BF16)
    HTOK = sb("htok", [128, 2048], BF16)
    QT = sb("qt", [128, 2048], F32)
    BROWS = sb("brows", [128, 4096], BF16)
    IDN = sb("idn", [128, 128], BF16)
    ONESB = sb("onesb", [128, 128], BF16)
    CAB = sb("cab", [128, 32], BF16)
    SM = sb("sm", [128, 16], F32)
    SM2 = sb("sm2", [128, 12], F32)
    SMX = sb("smx", [128, 32], F32)
    GSL = sb("gsl", [128, 24], F32)
    GSL2 = sb("gsl2", [128, 8], F32)
    GSLQ = sb("gslq", [128, 8], F32)
    GSB = sb("gsb", [128, 16], BF16)
    LAM = sb("lam", [128, 1], F32)
    SMALL = sb("small", [1, 8], F32)
    SMALLB = sb("smallb", [1, 2], BF16)
    CST = sb("cst", [128, 4], F32)

    EXPS = QT[:, 0:1024]
    TRASH = QT[:, 1024:2048]
    SSUB = SQ[:, 0:1024]
    T1O = HTOK
    VR = XS[:, 0:1024]
    VTV = HTB[:, 0:2048]
    VTQ = HTB[:, 2048:3072]
    VTK = HTB[:, 3072:4096]
    NEGMAX = SMX
    RINV = SMX
    FACT = SMX

    ROWMAP = {
        ("CQ", 0): (0, 0), ("CQ", 1): (0, 512),
        ("CK", 0): (32, 0), ("CK", 1): (32, 512),
        ("CV", 0): (64, 0), ("CV", 1): (64, 512),
        ("CO", 0): (96, 0), ("CO", 1): (96, 512),
    }
    for c in range(8):
        ROWMAP[("Q1R", c)] = (0, 1024 + 512 * c) if c < 4 else (64, 1024 + 512 * (c - 4))
        ROWMAP[("C1", c)] = (32, 1024 + 512 * c) if c < 4 else (96, 1024 + 512 * (c - 4))
    for c in range(2):
        ROWMAP[("C2", c)] = (0, 3072 + 512 * c)
        ROWMAP[("Q2R", c)] = (32, 3072 + 512 * c)

    def brow(name, c):
        p, col = ROWMAP[(name, c)]
        return BROWS[p:p + 1, col:col + 512], p

    PA = psalloc("pa", [128, 2048], F32)
    PB = psalloc("pb", [128, 1024], F32)
    PC = psalloc("pc", [128, 1024], F32)
    PTTA = PA[:, 0:512].bitcast(BF16)
    PTTB = PB[:, 0:512].bitcast(BF16)
    TRASHB = QT[:, 1024:1536].bitcast(BF16)
    PTTC = PC[:, 0:512].bitcast(BF16)


    W = em.wait

    def dma(dst, src, inc="sDMA"):
        return em.op("sp", lambda e: e.dma_start(dst, src), inc=inc, by=16)

    for i, cv in enumerate([RM, -RM, 16.0, -0.5]):
        em.op("pool", lambda e, i=i, cv=cv: e.memset(CST[:, i:i + 1], float(cv)), inc="sPOOL")
        nc.const_aps.register(F32, float(cv), CST[:, i:i + 1])
    em.op("pool", lambda e: e.memset(ONESB[:], 1.0), inc="sPOOL")
    W("pool", "sPOOL", em.val("sPOOL"))
    em.op("pool", lambda e: e.affine_select(
        IDN[:], ONESB[:], [[-1, 128]], ALU.is_equal, 0.0,
        base=0, channel_multiplier=1), inc="sPOOL")
    em.op("pool", lambda e: e.memset(T1VP[:], 0.0), inc="sPOOL")
    em.op("pool", lambda e: e.memset(IN2[:], 0.0), inc="sPOOL")
    em.op("pool", lambda e: e.memset(XR[:], 0.0), inc="sPOOL")
    em.op("pool", lambda e: e.memset(CAB[:], 0.0), inc="sPOOL")
    em.op("pool", lambda e: e.memset(BROWS[:], 0.0), inc="sPOOL")
    W("pool", "sPOOL", em.val("sPOOL"))
    for c in range(4):
        em.op("pool", lambda e, c=c: e.affine_select(
            CAB[32 * c:32 * c + 32, 0:16], ONESB[32 * c:32 * c + 32, 0:16],
            [[-1, 16]], ALU.is_equal, 0.0, base=0, channel_multiplier=1), inc="sPOOL")
    v_consts = em.val("sPOOL")

    for j in range(8):
        dma(GSL[:, j:j + 1], vec[0:1, 128 * j:128 * j + 128].transpose([1, 0]))
    for j in range(8):
        dma(GSL[:, 8 + j:9 + j], vec[2:3, 128 * j:128 * j + 128].transpose([1, 0]))
    for j in range(8):
        dma(GSL[:, 16 + j:17 + j], vec[1:2, 128 * j:128 * j + 128].transpose([1, 0]))
    for j in range(8):
        dma(GSL2[:, j:j + 1], vec[3:4, 128 * j:128 * j + 128].transpose([1, 0]))
    v_gsl = em.val("sDMA")
    dma(VR[0:1, :], vec[4:5, :])
    dma(VR[32:33, :], vec[5:6, :])
    dma(VR[64:65, :], vec[6:7, :])
    dma(VR[96:97, 0:128], vec[7:8, 0:128])
    v_vr = em.val("sDMA")


    W("dve", "sDMA", v_vr)
    em.op("dve", lambda e: e.tensor_mul(VR[96:97, 256:288], VR[96:97, 0:32], VR[96:97, 64:96]), inc="sDVE")
    em.op("dve", lambda e: e.tensor_mul(VR[96:97, 288:320], VR[96:97, 32:64], VR[96:97, 96:128]), inc="sDVE")
    em.op("dve", lambda e: e.reduce_sum(VR[96:97, 320:321], VR[96:97, 256:288].rearrange("p (a b) -> p a b", a=1), axis=AX.X), inc="sDVE")
    em.op("dve", lambda e: e.reduce_sum(VR[96:97, 321:322], VR[96:97, 288:320].rearrange("p (a b) -> p a b", a=1), axis=AX.X), inc="sDVE")
    W("act", "sDVE", em.val("sDVE"))
    em.op("act", lambda e: e.activation(VR[96:97, 322:323], VR[96:97, 320:321], AF.Exp), inc="sACT")
    em.op("act", lambda e: e.activation(VR[96:97, 323:324], VR[96:97, 321:322], AF.Exp), inc="sACT")
    W("dve", "sACT", em.val("sACT"))
    em.op("dve", lambda e: e.tensor_sub(VR[96:97, 324:325], VR[96:97, 323:324], VR[96:97, 322:323]), inc="sDVE")
    em.op("dve", lambda e: e.tensor_scalar(SMALL[0:1, 0:1], VR[96:97, 324:325], float(LAMBDA_INIT), None, ALU.subtract), inc="sDVE")
    em.op("dve", lambda e: e.tensor_copy(SMALLB[0:1, 0:1], SMALL[0:1, 0:1]), inc="sDVE")
    W("pe", "sDVE", em.val("sDVE"))
    W("pe", "sPOOL", v_consts)
    em.op("pe", lambda e: e.matmul(PA[:, 0:1], ONESB[0:1, 0:128], SMALLB[0:1, 0:1],
                                   start=True, stop=True), inc="sPE")
    W("dve", "sPE", em.val("sPE"))
    em.op("dve", lambda e: e.tensor_copy(LAM[:, 0:1], PA[:, 0:1]), inc="sDVE")
    W("pool", "sDVE", em.val("sDVE"))
    W("pool", "sPOOL", v_consts)
    for c in range(4):
        em.op("pool", lambda e, c=c: e.affine_select(
            CAB[32 * c:32 * c + 32, 16:32],
            LAM[32 * c:32 * c + 32, 0:1].broadcast_to([32, 16]),
            [[-1, 16]], ALU.is_equal, 0.0, base=-16, channel_multiplier=1), inc="sPOOL")
    v_cab = em.val("sPOOL")

    def gdma(dst, src):
        return em.op("pool", lambda e: e.dma_start(dst, src), inc="sGP", by=16)

    for j in range(8):
        gdma(simg[0][:, 24576 + 1024 * j:24576 + 1024 * j + 1024], wot[128 * j:128 * j + 128, :])
    gp_img0 = em.val("sGP")
    gp_img = {1: 0, 2: 0}

    img_ready = {}
    rows_gate = {}
    HTBF = HTB[:].bitcast(F32)
    ring = [SQ[:, 0:2048], QT[:, 0:2048], HTBF[:, 0:1024], HTBF[:, 1024:2048]]
    ring_w = [2048, 2048, 1024, 1024]
    rs = {"i": 0, "done": {}}

    def _staged(src_ap, width, consume, row):
        while True:
            s = rs["i"] % 4
            rs["i"] += 1
            if ring_w[s] >= width:
                break
        psem = "sPF%d" % s
        buf = ring[s][0:1, 0:width] if row else ring[s][:, 0:width]
        if s in rs["done"]:
            for sem_n, v in rs["done"][s]:
                W("sp", sem_n, v)
        vd = dma(buf, src_ap, inc=psem)
        rs["done"][s] = consume(buf, vd, psem)

    def staged(src_ap, width, consume):
        _staged(src_ap, width, consume, False)

    def staged_row(src_ap, width, consume):
        _staged(src_ap, width, consume, True)

    def prep():
        W("dve", "sDMA", v_vr)
        W("act", "sDMA", v_vr)
        W("pe", "sDMA", v_vr)
        W("pe", "sPOOL", v_consts)
        em.op("dve", lambda e: e.tensor_scalar(GSLQ[:], GSL[:, 0:8], QSCALE, None, ALU.mult), inc="sDVE")
        v_gslq = em.val("sDVE")

        pa_busy = {"v": em.val("sDVE")}
        for mi, wt in enumerate([wqt, wkt, wvt]):
            for j in range(8):
                def consume(buf, vd, psem, mi=mi, j=j):
                    col = mi * 8192 + j * 1024
                    scap = GSLQ[:, j:j + 1] if mi == 0 else GSL[:, j:j + 1]
                    W("act", psem, vd)
                    if mi == 0:
                        W("act", "sDVE", v_gslq)
                    em.op("act", lambda e, col=col, scap=scap: e.activation(
                        WB[1][:, col:col + 1024], buf, AF.Copy, scale=scap), inc="sACT")
                    va = em.val("sACT")
                    W("pe", psem, vd)
                    if j == 0:
                        W("pe", "sDVE", pa_busy["v"])
                    for hh in range(2):
                        em.op("pe", lambda e, j=j, hh=hh: e.matmul(
                            PA[0:1, 512 * hh:512 * hh + 512], GSL[:, 16 + j:17 + j],
                            buf[:, 512 * hh:512 * hh + 512],
                            start=(j == 0), stop=(j == 7)), inc="sPE")
                    return [("sACT", va), ("sPE", em.val("sPE"))]
                staged(wt[128 * j:128 * j + 128, :], 1024, consume)
            W("dve", "sPE", em.val("sPE"))
            nm = ["CQ", "CK", "CV"][mi]
            sc = QSCALE if mi == 0 else 1.0
            for c in range(2):
                r, _ = brow(nm, c)
                em.op("dve", lambda e, r=r, c=c, sc=sc: e.tensor_scalar(
                    r, PA[0:1, 512 * c:512 * c + 512], float(sc), None, ALU.mult), inc="sDVE")
            pa_busy["v"] = em.val("sDVE")
        W("pool", "sDMA", v_vr)
        for c in range(2):
            r, _ = brow("CO", c)
            em.op("pool", lambda e, r=r, c=c: e.tensor_copy(r, VR[0:1, 512 * c:512 * c + 512]), inc="sPOOL")
            r2, _ = brow("C2", c)
            em.op("pool", lambda e, r=r2, c=c: e.tensor_copy(r, VR[32:33, 512 * c:512 * c + 512]), inc="sPOOL")
        v_vr_done = em.val("sPOOL")
        def qrow_consume(buf, vd, psem, names):
            W("pool", psem, vd)
            em.op("pool", lambda e: e.tensor_scalar(buf, buf, 64.0, RM, ALU.mult, ALU.add), inc="sPOOL")
            em.op("pool", lambda e: e.tensor_scalar(buf, buf, RM, -128.0, ALU.subtract, ALU.max), inc="sPOOL")
            for c, (nm, ch) in enumerate(names):
                r, _ = brow(nm, ch)
                em.op("pool", lambda e, r=r, c=c: e.tensor_scalar(
                    r, buf[0:1, 512 * c:512 * c + 512], 127.0, 1.0 / 64.0, ALU.min, ALU.mult), inc="sPOOL")
            return [("sPOOL", em.val("sPOOL"))]
        staged_row(vec[6:7, :], 1024, lambda buf, vd, psem: qrow_consume(buf, vd, psem, [("Q2R", 0), ("Q2R", 1)]))
        for hh in range(2):
            staged_row(vfb1[0:1, 2048 * hh:2048 * hh + 2048], 2048,
                       lambda buf, vd, psem, hh=hh: qrow_consume(buf, vd, psem, [("Q1R", hh * 4 + c) for c in range(4)]))
        rows_gate[0] = [("sDVE", pa_busy["v"]), ("sPOOL", v_vr_done)]
        rows_gate[3] = rows_gate[4] = [("sPOOL", em.val("sPOOL")), ("sPOOL", v_vr_done)]
        W("sp", "sACT", em.val("sACT"))
        W("sp", "sDMA2", em.val("sDMA2"))
        vstore = dma(simg[0][:, 0:24576], WB[1][:, 0:24576], inc="sDMA2")
        img_ready[0] = (vstore, gp_img0)

        for half in range(2):
            W("act", "sDMA2", em.val("sDMA2"))
            for j in range(8):
                def consume(buf, vd, psem, j=j, half=half):
                    col = j * 2048
                    W("act", psem, vd)
                    em.op("act", lambda e, col=col, j=j: e.activation(
                        WB[1][:, col:col + 2048], buf, AF.Copy, scale=GSL[:, 8 + j:9 + j]), inc="sACT")
                    va = em.val("sACT")
                    W("pe", psem, vd)
                    if j == 0:
                        W("pe", "sDVE", pa_busy["v"])
                    for qo in range(4):
                        em.op("pe", lambda e, j=j, qo=qo: e.matmul(
                            PA[0:1, 512 * qo:512 * qo + 512], GSL2[:, j:j + 1],
                            buf[:, 512 * qo:512 * qo + 512],
                            start=(j == 0), stop=(j == 7)), inc="sPE")
                    return [("sACT", va), ("sPE", em.val("sPE"))]
                staged(w1t[128 * j:128 * j + 128, 2048 * half:2048 * half + 2048], 2048, consume)
            for j in range(16):
                def consume_w2(buf, vd, psem, j=j):
                    W("act", psem, vd)
                    em.op("act", lambda e, j=j: e.copy(
                        WB[1][:, 16384 + 1024 * j:16384 + 1024 * j + 1024], buf), inc="sACT")
                    return [("sACT", em.val("sACT"))]
                staged(w2t[2048 * half + 128 * j:2048 * half + 128 * j + 128, :], 1024, consume_w2)
            def consume_b(buf, vd, psem, half=half):
                W("dve", psem, vd)
                W("dve", "sPE", em.val("sPE"))
                for qo in range(4):
                    r, _ = brow("C1", half * 4 + qo)
                    em.op("dve", lambda e, r=r, qo=qo, buf=buf: e.tensor_tensor(
                        r, buf[0:1, 512 * qo:512 * qo + 512],
                        PA[0:1, 512 * qo:512 * qo + 512], ALU.add), inc="sDVE")
                return [("sDVE", em.val("sDVE"))]
            staged_row(vb1[0:1, 2048 * half:2048 * half + 2048], 2048, consume_b)
            pa_busy["v"] = em.val("sDVE")
            W("sp", "sACT", em.val("sACT"))
            W("sp", "sDMA2", em.val("sDMA2"))
            vstore = dma(simg[1 + half][:], WB[1][:], inc="sDMA2")
            img_ready[1 + half] = (vstore, gp_img[1 + half])
        rows_gate[1] = rows_gate[2] = [("sDVE", pa_busy["v"])]

        qtile = {"i": 0}
        def quant_tile(src_ap, width, col, img):
            lane = "pool" if qtile["i"] % 3 == 2 else "dve"
            qtile["i"] += 1
            def consume(buf, vd, psem, lane=lane, col=col, width=width, img=img):
                W(lane, psem, vd)
                em.op(lane, lambda e: e.tensor_scalar(buf, buf, 64.0, RM, ALU.mult, ALU.add), inc="sDVE" if lane == "dve" else "sPOOL")
                em.op(lane, lambda e: e.tensor_scalar(buf, buf, RM, -128.0, ALU.subtract, ALU.max), inc="sDVE" if lane == "dve" else "sPOOL")
                em.op(lane, lambda e, col=col: e.tensor_scalar(
                    WB[1][:, col:col + width], buf, 127.0, 1.0 / 64.0, ALU.min, ALU.mult), inc="sDVE" if lane == "dve" else "sPOOL")
                return [(("sDVE" if lane == "dve" else "sPOOL"), em.val("sDVE" if lane == "dve" else "sPOOL"))]
            staged(src_ap, width, consume)
        for half in range(2):
            W("dve", "sDMA2", em.val("sDMA2"))
            W("pool", "sDMA2", em.val("sDMA2"))
            for j in range(8):
                quant_tile(q1t[128 * j:128 * j + 128, 2048 * half:2048 * half + 2048], 2048, j * 2048, 3 + half)
            for j in range(16):
                quant_tile(q2t[2048 * half + 128 * j:2048 * half + 128 * j + 128, :], 1024, 16384 + j * 1024, 3 + half)
            W("sp", "sDVE", em.val("sDVE"))
            W("sp", "sPOOL", em.val("sPOOL"))
            W("sp", "sDMA2", em.val("sDMA2"))
            vstore = dma(simg[3 + half][:], WB[1][:], inc="sDMA2")
            img_ready[3 + half] = (vstore, 0)

    prep()
    v_prep = (em.val("sDMA2"), em.val("sGP"))
    prep_bnd = {"pa_dve": em.val("sDVE"), "pa_act": 0, "pb_dve": 0, "pc_dve": 0, "pc_act": 0}
    v_prep_lanes = {s: em.val(s) for s in ("sDVE", "sPOOL", "sACT", "sPE")}

    for eng in ("pe", "dve", "act"):
        W(eng, "sPOOL", v_cab)

    def ln_stats(xap, compound):
        em.op("dve", lambda e: e.reduce_sum(SM[:, 0:1], xap.rearrange("p (a b) -> p a b", a=1), axis=AX.X), inc="sDVE")
        em.op("act", lambda e: e.activation(TRASH, xap, AF.Square, accum_out=SM2[:, 0:1]), inc="sACT")
        W("dve", "sACT", em.val("sACT"))
        em.op("dve", lambda e: e.tensor_scalar(SM[:, 0:1], SM[:, 0:1], 1.0 / 1024.0, None, ALU.mult), inc="sDVE")
        em.op("dve", lambda e: e.tensor_mul(SM2[:, 1:2], SM[:, 0:1], SM[:, 0:1]), inc="sDVE")
        em.op("dve", lambda e: e.tensor_scalar(SM2[:, 2:3], SM2[:, 0:1], 1.0 / 1024.0, None, ALU.mult), inc="sDVE")
        em.op("dve", lambda e: e.tensor_sub(SM2[:, 3:4], SM2[:, 2:3], SM2[:, 1:2]), inc="sDVE")
        em.op("dve", lambda e: e.tensor_scalar(SM2[:, 4:5], SM2[:, 3:4], EPS, None, ALU.add), inc="sDVE")
        W("act", "sDVE", em.val("sDVE"))
        em.op("act", lambda e: e.activation(SM2[:, 5:6], SM2[:, 4:5], AF.Ln), inc="sACT")
        em.op("act", lambda e: e.activation(SM[:, 1:2], SM2[:, 5:6], AF.Exp, scale=-0.5), inc="sACT")
        if compound:
            em.op("dve", lambda e: e.reciprocal(SM2[:, 6:7], SM2[:, 4:5]), inc="sDVE")
            em.op("dve", lambda e: e.tensor_mul(SM2[:, 7:8], SM2[:, 3:4], SM2[:, 6:7]), inc="sDVE")
            em.op("dve", lambda e: e.tensor_scalar(SM2[:, 8:9], SM2[:, 7:8], EPS, None, ALU.add), inc="sDVE")
            W("act", "sDVE", em.val("sDVE"))
            em.op("act", lambda e: e.activation(SM2[:, 9:10], SM2[:, 8:9], AF.Ln), inc="sACT")
            W("dve", "sACT", em.val("sACT"))
            em.op("dve", lambda e: e.tensor_add(SM2[:, 10:11], SM2[:, 5:6], SM2[:, 9:10]), inc="sDVE")
            W("act", "sDVE", em.val("sDVE"))
            em.op("act", lambda e: e.activation(SM[:, 2:3], SM2[:, 10:11], AF.Exp, scale=-0.5), inc="sACT")
        return em.val("sACT")

    def transposes(tiles, dst_fn, ntiles, ptt):
        em.pesync()
        for j in range(ntiles):
            em.op("pe", lambda e, j=j: e.transpose(
                ptt[:, 128 * (j % 8):128 * (j % 8) + 128], tiles(j), IDN[:]), inc="sPE")
            if j % 8 == 7:
                W("act", "sPE", em.val("sPE"))
                em.op("act", lambda e, g=j // 8: e.copy(dst_fn(g), ptt[:, 0:1024]), inc="sACT")
                if j + 1 < ntiles:
                    W("pe", "sACT", em.val("sACT"))
        return em.val("sACT")

    def bias_mm(ps_ap, name, chunk, start=False, stop=True):
        r, p = brow(name, chunk)
        em.op("pe", lambda e, r=r, p=p, ps_ap=ps_ap: e.matmul(
            ps_ap, ONESB[p:p + 1, 0:128], r, start=start, stop=stop,
            tile_position=(p, 0)), inc="sPE")

    def ubp(c):
        return UB if c % 2 == 0 else UB2

    AST = {}

    def attn_front(c, xl):
        st = AST.setdefault(c, {})
        xap = XS[:, DM * c:DM * c + DM]
        sx, vx = xl[c]
        W("dve", sx, vx)
        W("act", sx, vx)
        W("pool", sx, vx)
        if c - 3 in AST and "applied_p" in AST[c - 3]:
            W("act", "sPOOL", AST[c - 3]["applied_p"])
        if c - 2 in AST and "ht_pe" in AST[c - 2]:
            W("pool", "sPE", AST[c - 2]["ht_pe"])
        if c - 1 in AST and "ub" in AST[c - 1]:
            W("dve", "sPOOL", AST[c - 1]["ub"])
            W("act", "sPOOL", AST[c - 1]["ub"])
        va = ln_stats(xap, compound=True)
        W("pool", "sACT", va)
        W("pool", "sDVE", em.val("sDVE"))
        em.op("pool", lambda e: e.tensor_scalar(ubp(c)[:], xap, SM[:, 0:1], SM[:, 2:3], ALU.subtract, ALU.mult), inc="sPOOL")
        p2 = 8 + 2 * c
        em.op("pool", lambda e, p2=p2: e.tensor_copy(SM[:, p2:p2 + 2], SM[:, 0:2]), inc="sPOOL")
        st["ub"] = em.val("sPOOL")

    def attn_fq(c, wb, v_wload, prev, bnd=None):
        st = AST[c]
        if prev is not None:
            W("pe", "sDVE", AST[prev]["t1_dve"])
            W("pe", "sACT", max(AST[prev]["t1_act"], AST[prev]["exp"]))
        else:
            W("pe", "sDVE", max(bnd["pa_dve"], bnd["pb_dve"]))
            W("pe", "sACT", bnd["pa_act"])
        W("pe", "sPOOL", st["ub"])
        va = transposes(lambda j: ubp(c)[:, 128 * j:128 * j + 128], lambda g: HT[:, 0:1024], 8, PTTB)
        st["ht_pe"] = em.val("sPE")
        W("pe", "sACT", va)
        W("pe", v_wload[0], v_wload[1])
        em.pesync()
        for j in range(8):
            for mi in range(3):
                dstps, dcol = (PA, 1024 * mi) if mi < 2 else (PB, 0)
                for half in range(2):
                    em.op("pe", lambda e, j=j, mi=mi, half=half, dstps=dstps, dcol=dcol: e.matmul(
                        dstps[:, dcol + 512 * half:dcol + 512 * half + 512],
                        HT[:, 128 * j:128 * j + 128],
                        WB[wb][:, 8192 * mi + 1024 * j + 512 * half:8192 * mi + 1024 * j + 512 * half + 512],
                        start=(j == 0), stop=False), inc="sPE")
        for mi, nm in enumerate(["CQ", "CK", "CV"]):
            dstps, dcol = (PA, 1024 * mi) if mi < 2 else (PB, 0)
            for half in range(2):
                bias_mm(dstps[:, dcol + 512 * half:dcol + 512 * half + 512], nm, half)
        st["qkv"] = em.val("sPE")

    def attn_evacs(c, prevst=None):
        st = AST[c]
        W("dve", "sPE", st["qkv"])
        W("act", "sPE", st["qkv"])
        if prevst is not None:
            W("act", "sDVE", prevst["vt"])
        em.op("dve", lambda e: e.tensor_copy(T1Q[:], PA[:, 0:1024]), inc="sDVE")
        em.op("act", lambda e: e.copy(T1K[:], PA[:, 1024:2048]), inc="sACT")
        st["t1_act"] = em.val("sACT")
        em.op("dve", lambda e: e.tensor_copy(
            T1VP[:].rearrange("p (d h) -> p d h", d=64)[:, :, 0:16], PB[:, 0:1024]), inc="sDVE")
        st["t1_dve"] = em.val("sDVE")

    def attn_vtrans(c, prevst=None):
        st = AST[c]
        if prevst is not None:
            W("dve", "sPE", prevst["o2"])
        W("dve", "sACT", st["t1_act"])
        em.op("dve", lambda e: e.transpose(VTQ[:], T1Q[:]), inc="sDVE")
        em.op("dve", lambda e: e.transpose(VTK[:], T1K[:]), inc="sDVE")
        em.op("dve", lambda e: e.transpose(VTV[:], T1VP[:]), inc="sDVE")
        st["vt"] = em.val("sDVE")

    def attn_scores(c):
        st = AST[c]
        W("pe", "sDVE", st["vt"])
        W("pe", "sACT", st["t1_act"])
        em.pesync()
        for i in range(128):
            r, tl = i % 4, i // 4
            em.op("pe", lambda e, r=r, tl=tl: e.matmul(
                PA[32 * r:32 * r + 32, 32 * tl:32 * tl + 32],
                VTQ[32 * r:32 * r + 32, :].rearrange("p (w cc) -> p w cc", cc=32)[:, :, tl],
                VTK[32 * r:32 * r + 32, :].rearrange("p (w cc) -> p w cc", cc=32)[:, :, tl],
                start=True, stop=True, tile_position=(32 * r, 32 * r)), inc="sPE")
        st["scores"] = em.val("sPE")

    def attn_softmax(c):
        st = AST[c]
        W("act", "sPE", st["scores"])
        em.op("act", lambda e: e.activation(EXPS, PA[:, 0:1024], AF.Exp), inc="sACT")
        st["exp"] = em.val("sACT")
        W("dve", "sACT", st["exp"])
        em.op("dve", lambda e: e.reduce_sum(
            RINV[:], EXPS.rearrange("p (t k) -> p t k", k=32), axis=AX.X), inc="sDVE")
        em.op("dve", lambda e: e.reciprocal(RINV[:], RINV[:]), inc="sDVE")
        st["rinv"] = em.val("sDVE")
        W("pool", "sDVE", st["rinv"])
        em.op("pool", lambda e: e.tensor_tensor(
            ATTN[:].rearrange("p (t k) -> p t k", k=32),
            EXPS.rearrange("p (t k) -> p t k", k=32),
            RINV[:].unsqueeze(2).broadcast_to([128, 32, 32]), ALU.mult), inc="sPOOL")
        st["attn"] = em.val("sPOOL")

    def attn_combine(c, prev, bnd=None):
        st = AST[c]
        W("pe", "sPOOL", st["attn"])
        W("pe", "sPOOL", v_cab)
        if prev is not None and "conn" in AST.get(prev, {}):
            W("pe", "sDVE", AST[prev]["conn"])
        elif bnd is not None:
            W("pe", "sDVE", bnd["pc_dve"])
            W("pe", "sACT", bnd["pc_act"])
        em.pesync()
        for r in range(4):
            for half in range(2):
                em.op("pe", lambda e, r=r, half=half: e.matmul(
                    PC[32 * r:32 * r + 16, 0:512],
                    CAB[32 * r:32 * r + 32, 16 * half:16 * half + 16],
                    ATTN[32 * r:32 * r + 32, :].rearrange("p (t k) -> p t k", k=32)[:, :, 16 * half:16 * half + 16],
                    start=(half == 0), stop=(half == 1), tile_position=(32 * r, 32 * r)), inc="sPE")
        st["comb"] = em.val("sPE")

    def attn_in2(c):
        st = AST[c]
        W("act", "sPE", st["comb"])
        for r in range(4):
            em.op("act", lambda e, r=r: e.copy(
                IN2[32 * r:32 * r + 16, :].rearrange("p (t k) -> p t k", k=32)[:, :, 0:16],
                PC[32 * r:32 * r + 16, 0:512]), inc="sACT")
        st["in2"] = em.val("sACT")
        W("dve", "sACT", st["in2"])
        em.op("dve", lambda e: e.transpose(AT2T[:], IN2[:]), inc="sDVE")
        st["at2t"] = em.val("sDVE")

    def attn_einsum2(c):
        st = AST[c]
        W("pe", "sDVE", st["at2t"])
        W("pe", "sACT", st["in2"])
        for h in range(2):
            em.pesync()
            for i in range(64):
                r, tj = i % 4, i // 4
                tl = 16 * h + tj
                em.op("pe", lambda e, r=r, tl=tl, tj=tj: e.matmul(
                    PC[32 * r:32 * r + 32, 64 * tj:64 * tj + 64],
                    AT2T[32 * r:32 * r + 32, 32 * tl:32 * tl + 32],
                    VTV[32 * r:32 * r + 32, :].rearrange("p (w cc) -> p w cc", cc=32)[:, :, tl],
                    start=True, stop=True, tile_position=(32 * r, 32 * r)), inc="sPE")
            W("act", "sPE", em.val("sPE"))
            dstb = ATTN[:] if h == 0 else TRASHB
            em.op("act", lambda e, dstb=dstb: e.copy(dstb, PC[:]), inc="sACT")
            em.op("act", lambda e, h=h: e.activation(
                SQ[:, 1024 * h:1024 * h + 1024], PC[:], AF.Square), inc="sACT")
            st["o2e%d" % h] = em.val("sACT")
            if h == 0:
                W("pe", "sACT", st["o2e0"])
        st["o2"] = em.val("sPE")

    def attn_rms(c):
        st = AST[c]
        W("dve", "sACT", st["o2e1"])
        em.op("dve", lambda e: e.reduce_sum(
            FACT[:], SQ[:].rearrange("p (t d) -> p t d", d=64), axis=AX.X), inc="sDVE")
        em.op("dve", lambda e: e.tensor_scalar(FACT[:], FACT[:], 1.0 / 64.0, EPS, ALU.mult, ALU.add), inc="sDVE")
        W("act", "sDVE", em.val("sDVE"))
        em.op("act", lambda e: e.activation(FACT[:], FACT[:], AF.Ln), inc="sACT")
        em.op("act", lambda e: e.activation(FACT[:], FACT[:], AF.Exp, scale=-0.5), inc="sACT")
        W("dve", "sACT", em.val("sACT"))
        em.op("dve", lambda e: e.tensor_scalar(FACT[:], FACT[:], float(1.0 - LAMBDA_INIT), None, ALU.mult), inc="sDVE")
        vsm = em.val("sDVE")
        em.op("dve", lambda e: e.tensor_tensor(
            bass.AP(XR[:].tensor, XR[:].offset, [[2048, 128], [1, 16], [32, 64]]),
            ATTN[:].rearrange("p (t d) -> p t d", d=64),
            FACT[:, 0:16].unsqueeze(2).broadcast_to([128, 16, 64]), ALU.mult), inc="sDVE")
        st["applied_d"] = em.val("sDVE")
        W("pool", "sDVE", vsm)
        W("pool", "sACT", st["o2e1"])
        em.op("pool", lambda e: e.tensor_tensor(
            bass.AP(XR[:].tensor, XR[:].offset + 16, [[2048, 128], [1, 16], [32, 64]]),
            TRASHB.rearrange("p (t d) -> p t d", d=64),
            FACT[:, 16:32].unsqueeze(2).broadcast_to([128, 16, 64]), ALU.mult), inc="sPOOL")
        st["applied_p"] = em.val("sPOOL")
        W("dve", "sPOOL", st["applied_p"])
        em.op("dve", lambda e: e.transpose(T1O[:], XR[:]), inc="sDVE")
        st["t1o"] = em.val("sDVE")
        W("pool", "sDVE", st["t1o"])
        em.op("pool", lambda e: e.tensor_copy(
            T1N[:].rearrange("p (d a) -> p d a", a=16),
            T1O[:].rearrange("p (d a) -> p d a", a=32)[:, :, 0:16]), inc="sPOOL")
        st["t1n"] = em.val("sPOOL")

    def attn_fmwo(c, wb):
        st = AST[c]
        W("pe", "sPOOL", st["t1n"])
        if c - 1 in AST and "fm" in AST[c - 1]:
            W("pe", "sACT", AST[c - 1]["fm"])
        va = transposes(lambda j: T1N[:, 128 * j:128 * j + 128], lambda g: HT[:, 0:1024], 8, PTTC)
        st["fm"] = va
        W("pe", "sACT", va)
        em.pesync()
        for j in range(8):
            for half in range(2):
                em.op("pe", lambda e, j=j, half=half: e.matmul(
                    PC[:, 512 * half:512 * half + 512],
                    HT[:, 128 * j:128 * j + 128],
                    WB[wb][:, 24576 + 1024 * j + 512 * half:24576 + 1024 * j + 512 * half + 512],
                    start=(j == 0), stop=False), inc="sPE")
        for half in range(2):
            bias_mm(PC[:, 512 * half:512 * half + 512], "CO", half)
        st["wo"] = em.val("sPE")

    def attn_conn(c):
        st = AST[c]
        xap = XS[:, DM * c:DM * c + DM]
        p2 = 8 + 2 * c
        em.op("pool", lambda e, p2=p2: e.tensor_scalar(
            SSUB[:], xap, SM[:, p2:p2 + 1], SM[:, p2 + 1:p2 + 2], ALU.subtract, ALU.mult), inc="sPOOL")
        vss = em.val("sPOOL")
        W("dve", "sPE", st["wo"])
        W("dve", "sPOOL", vss)
        em.op("dve", lambda e: e.tensor_tensor(xap, SSUB[:], PC[:], ALU.add), inc="sDVE")
        st["conn"] = em.val("sDVE")

    def stage_attn_pipelined(wb, v_wload, xl, bnd):
        AST.clear()
        attn_front(0, xl)
        attn_fq(0, wb, v_wload, None, bnd)
        attn_front(1, xl)
        attn_evacs(0)
        attn_vtrans(0)
        attn_scores(0)
        attn_softmax(0)
        attn_fq(1, wb, v_wload, 0)
        for k in range(NCH):
            if k + 2 < NCH:
                attn_front(k + 2, xl)
            attn_combine(k, k - 1 if k > 0 else None, bnd)
            attn_in2(k)
            if k + 1 < NCH:
                attn_evacs(k + 1, AST[k])
            attn_einsum2(k)
            if k + 1 < NCH:
                attn_vtrans(k + 1, AST[k])
                attn_scores(k + 1)
            attn_rms(k)
            if k + 1 < NCH:
                attn_softmax(k + 1)
            attn_fmwo(k, wb)
            attn_conn(k)
            if k + 2 < NCH:
                attn_fq(k + 2, wb, v_wload, k + 1)
        return {"pa_dve": AST[3]["t1_dve"], "pa_act": AST[3]["exp"],
                "pb_dve": AST[3]["t1_dve"], "pc_dve": AST[3]["conn"], "pc_act": 0}

    FST = {}

    def ffn_front(blk, c, half, quant):
        st = FST.setdefault(c, {})
        xap = XS[:, DM * c:DM * c + DM]
        if half == 1:
            return
        W("act", "sDVE", em.val("sDVE"))
        va = ln_stats(xap, compound=False)
        W("dve", "sACT", va)
        if not quant:
            em.op("dve", lambda e: e.tensor_scalar(
                ubp(c)[:], xap, SM[:, 0:1], SM[:, 1:2], ALU.subtract, ALU.mult), inc="sDVE")
        else:
            em.op("dve", lambda e: e.tensor_scalar(SM[:, 3:4], SM[:, 1:2], 16.0, None, ALU.mult), inc="sDVE")
            em.op("dve", lambda e: e.tensor_scalar(
                SSUB[:], xap, SM[:, 0:1], SM[:, 3:4], ALU.subtract, ALU.mult), inc="sDVE")
            em.op("dve", lambda e: e.tensor_scalar(
                SSUB[:], SSUB[:], RM, RM, ALU.add, ALU.subtract), inc="sDVE")
            em.op("dve", lambda e: e.tensor_scalar(
                SSUB[:], SSUB[:], -128.0, 127.0, ALU.max, ALU.min), inc="sDVE")
            em.op("dve", lambda e: e.tensor_scalar(
                ubp(c)[:], SSUB[:], 1.0 / 16.0, None, ALU.mult), inc="sDVE")
        st["ub"] = em.val("sDVE")

    def ffn_front_T(c, half, ptt_free=0):
        st = FST[c]
        if half == 1:
            return
        W("pe", "sDVE", st["ub"])
        if ptt_free:
            W("pe", "sACT", ptt_free)
        va = transposes(lambda j: ubp(c)[:, 128 * j:128 * j + 128],
                        lambda g, c=c: HTB[:, 1024 * c:1024 * c + 1024], 8, PTTC)
        st["stash"] = va
        st["stash_pe"] = em.val("sPE")

    def ffn_w1(c, wb, v_wload, half, quant, prev, bnd=None):
        st = FST[c]
        W("pe", "sACT", st["stash"])
        if prev is not None and "evac" in FST.get(prev, {}):
            W("pe", "sACT", FST[prev]["evac"])
            W("pe", "sDVE", FST[prev].get("evac_dve", 0))
        elif bnd is not None:
            W("pe", "sACT", bnd["pa_act"])
            W("pe", "sDVE", bnd["pa_dve"])
        W("pe", v_wload[0], v_wload[1])
        em.pesync()
        base0 = 1024 * c
        for j in range(8):
            for qo in range(4):
                em.op("pe", lambda e, j=j, qo=qo, base0=base0: e.matmul(
                    PA[:, 512 * qo:512 * qo + 512],
                    HTB[:, base0 + 128 * j:base0 + 128 * j + 128],
                    WB[wb][:, 2048 * j + 512 * qo:2048 * j + 512 * qo + 512],
                    start=(j == 0), stop=False), inc="sPE")
        nm1 = "Q1R" if quant else "C1"
        for qo in range(4):
            bias_mm(PA[:, 512 * qo:512 * qo + 512], nm1, half * 4 + qo)
        st["w1"] = em.val("sPE")

    def ffn_evac(c, half, quant):
        st = FST[c]
        W("act", "sPE", st["w1"])
        if not quant:
            em.op("act", lambda e: e.activation(HTOK[:, 0:1024], PA[:, 0:1024], AF.Relu), inc="sACT")
            em.op("act", lambda e: e.activation(HTOK[:, 1024:2048], PA[:, 1024:2048], AF.Relu), inc="sACT")
            st["evac"] = em.val("sACT")
            st["evac_dve"] = 0
            st["hid"] = ("act", st["evac"])
        else:
            W("act", "sDVE", em.val("sDVE"))
            em.op("act", lambda e: e.activation(QT[:, 0:1024], PA[:, 0:1024], AF.Identity, bias=RM, scale=16.0), inc="sACT")
            em.op("act", lambda e: e.activation(QT[:, 1024:2048], PA[:, 1024:2048], AF.Identity, bias=RM, scale=16.0), inc="sACT")
            em.op("act", lambda e: e.activation(SQ[:, 0:1024], QT[:, 0:1024], AF.Relu, bias=-RM, scale=1.0), inc="sACT")
            em.op("act", lambda e: e.activation(SQ[:, 1024:2048], QT[:, 1024:2048], AF.Relu, bias=-RM, scale=1.0), inc="sACT")
            st["evac"] = em.val("sACT")
            W("dve", "sACT", st["evac"])
            em.op("dve", lambda e: e.tensor_scalar(
                HTOK[:], SQ[:, 0:2048], 127.0, 1.0 / 16.0, ALU.min, ALU.mult), inc="sDVE")
            st["evac_dve"] = em.val("sDVE")
            st["hid"] = ("dve", st["evac_dve"])

    def ffn_hidT_w2(c, wb, half, quant, prev, ptt_free=0, pb_free=0):
        st = FST[c]
        eng, v = st["hid"]
        W("pe", "sACT" if eng == "act" else "sDVE", v)
        if ptt_free:
            W("pe", "sACT", ptt_free)
        va = transposes(lambda j: HTOK[:, 128 * j:128 * j + 128],
                        lambda g: (T1Q if g == 0 else T1K)[:, 0:1024], 16, PTTC)
        st["hidT_evac"] = va
        W("pe", "sACT", va)
        if prev is not None and "res" in FST.get(prev, {}):
            W("pe", "sDVE", FST[prev]["res"])
        elif pb_free:
            W("pe", "sDVE", pb_free)
        em.pesync()
        for j in range(16):
            srcb = T1Q if j < 8 else T1K
            jj = j % 8
            for hh in range(2):
                em.op("pe", lambda e, srcb=srcb, jj=jj, j=j, hh=hh: e.matmul(
                    PB[:, 512 * hh:512 * hh + 512],
                    srcb[:, 128 * jj:128 * jj + 128],
                    WB[wb][:, 16384 + 1024 * j + 512 * hh:16384 + 1024 * j + 512 * hh + 512],
                    start=(j == 0), stop=(half == 0 and j == 15)), inc="sPE")
        if half == 1:
            nm2 = "Q2R" if quant else "C2"
            for hh in range(2):
                bias_mm(PB[:, 512 * hh:512 * hh + 512], nm2, hh)
        st["w2"] = em.val("sPE")

    def ffn_residual(blk, c, half, quant, last_stage=False, xl_next=None):
        st = FST[c]
        xap = XS[:, DM * c:DM * c + DM]
        W("dve", "sPE", st["w2"])
        em.op("dve", lambda e: e.tensor_tensor(xap, xap, PB[:], ALU.add), inc="sDVE")
        if half == 1 and not quant:
            W("sp", "sDVE", em.val("sDVE"))
            xr = dma(SQ[:, 0:1024],
                     bass.AP(xin.tensor, (blk * BLK + c * CH) * DM, [[DM, 128], [1, DM]]))
            W("dve", "sDMA", xr)
            em.op("dve", lambda e: e.tensor_tensor(xap, xap, SQ[:, 0:1024], ALU.add), inc="sDVE")
        st["res"] = em.val("sDVE")
        if last_stage:
            sx = "sXY%d" % c
            W("sp", "sDVE", st["res"])
            dma(bass.AP(yout.tensor, (blk * BLK + c * CH) * DM, [[DM, 128], [1, DM]]),
                XS[:, DM * c:DM * c + DM], inc=sx)
            if blk + 1 < NBLK:
                W("sp", sx, em.val(sx))
                dma(XS[:, DM * c:DM * c + DM],
                    bass.AP(xin.tensor, ((blk + 1) * BLK + c * CH) * DM, [[DM, 128], [1, DM]]),
                    inc=sx)
                xl_next[c] = (sx, em.val(sx))

    def stage_ffn_pipelined(blk, wb, v_wload, half, quant, bnd, last_stage=False, xl_next=None):
        if half == 0:
            FST.clear()
        ffn_front(blk, 0, half, quant)
        W("pe", "sDVE", bnd["pc_dve"])
        ffn_front_T(0, half, ptt_free=bnd["pc_act"])
        for cm in range(NCH):
            cf = cm + 1 if cm + 1 < NCH else None
            if cf is not None:
                ffn_front(blk, cf, half, quant)
            ffn_w1(cm, wb, v_wload, half, quant, cm - 1 if cm > 0 else None, bnd)
            ffn_evac(cm, half, quant)
            prev_hevac = FST.get(cm - 1, {}).get("hidT_evac", 0)
            if cm == 0:
                prev_hevac = FST[0].get("stash", 0) if half == 0 else bnd["pc_act"]
            if cf is not None:
                ffn_front_T(cf, half, ptt_free=prev_hevac)
            if half == 0 and cf is not None:
                pfree = FST[cf].get("stash", 0)
            else:
                pfree = prev_hevac
            ffn_hidT_w2(cm, wb, half, quant, cm - 1 if cm > 0 else None, ptt_free=pfree,
                        pb_free=bnd["pb_dve"] if cm == 0 else 0)
            ffn_residual(blk, cm, half, quant, last_stage=last_stage, xl_next=xl_next)
        return {"pa_act": FST[3]["evac"], "pa_dve": FST[3].get("evac_dve", 0),
                "pb_dve": FST[3]["res"], "pc_act": FST[3]["hidT_evac"], "pc_dve": 0}

    prev_pe_done = [0, 0]
    bnd = prep_bnd
    gidx = 0
    xl = {}
    for blk in range(NBLK):
        if blk == 0:
            W("sp", "sDVE", em.val("sDVE"))
            W("sp", "sPOOL", v_prep_lanes["sPOOL"])
            for c in range(NCH):
                dma(XS[:, DM * c:DM * c + DM],
                    bass.AP(xin.tensor, (c * CH) * DM, [[DM, 128], [1, DM]]),
                    inc="sXY%d" % c)
                xl[c] = ("sXY%d" % c, em.val("sXY%d" % c))
        for sidx in range(DBG_STAGES):
            par = gidx % 2
            wsem = "sWL%d" % par
            if gidx in (1, 3):
                W("sp", "sDMA2", v_prep[0])
                W("sp", "sGP", v_prep[1])
            else:
                st_v, gp_v = img_ready[sidx]
                W("sp", "sDMA2", st_v)
                if gp_v:
                    W("sp", "sGP", gp_v)
            for sem_n, v in rows_gate[sidx]:
                W("pe", sem_n, v)
            if blk == 0:
                W("pe", "sPOOL", v_prep_lanes["sPOOL"])
            if prev_pe_done[par] > 0:
                W("sp", "sPE", prev_pe_done[par])
            wl = dma(WB[par][:], simg[sidx][:], inc=wsem)
            if sidx == 0:
                bnd = stage_attn_pipelined(par, (wsem, wl), xl, bnd)
            elif sidx in (1, 2):
                bnd = stage_ffn_pipelined(blk, par, (wsem, wl), sidx - 1, False, bnd)
            else:
                last = (sidx == DBG_STAGES - 1)
                xln = {}
                bnd = stage_ffn_pipelined(blk, par, (wsem, wl), sidx - 3, True, bnd,
                                          last_stage=last, xl_next=xln)
                if last:
                    xl = xln
            prev_pe_done[par] = em.val("sPE")
            gidx += 1

    finals = [("sXY%d" % c, em.val("sXY%d" % c)) for c in range(NCH)]
    for _s, _v in finals:
        em.ops["sp"].append(lambda e, _s=_s, _v=_v: e.wait_ge(em.sems[_s], _v))

    with nc.Block() as block:
        block.sync(lambda e: ([f(e) for f in em.ops["sp"]], None)[1])
        block.tensor(lambda e: ([f(e) for f in em.ops["pe"]], None)[1])
        block.vector(lambda e: ([f(e) for f in em.ops["dve"]], None)[1])
        block.scalar(lambda e: ([f(e) for f in em.ops["act"]], None)[1])
        block.gpsimd(lambda e: ([f(e) for f in em.ops["pool"]], None)[1])

    for c in reversed(ctxs):
        c.__exit__(None, None, None)
    return nc


def _prep_host(inputs):
    pq, pv, pwo = _perms()
    f32 = np.float32
    Wq = np.asarray(inputs["L0_attn_Wq"], f32)
    Wk = np.asarray(inputs["L0_attn_Wk"], f32)
    Wv = np.asarray(inputs["L0_attn_Wv"], f32)
    Wo = np.asarray(inputs["L0_attn_Wo"], f32)
    feed = {
        "wqt": np.ascontiguousarray(Wq[pq, :].T),
        "wkt": np.ascontiguousarray(Wk[pq, :].T),
        "wvt": np.ascontiguousarray(Wv[pv, :].T),
        "wot": np.ascontiguousarray(Wo[:, pwo].T),
        "w1t": np.ascontiguousarray(np.asarray(inputs["L0_attn_W1"], f32).T),
        "w2t": np.ascontiguousarray(np.asarray(inputs["L0_attn_W2"], f32).T),
        "q1t": np.ascontiguousarray(np.asarray(inputs["L0_ffn_w1"], f32).T),
        "q2t": np.ascontiguousarray(np.asarray(inputs["L0_ffn_w2"], f32).T),
        "vb1": np.ascontiguousarray(np.asarray(inputs["L0_attn_b1"], f32).reshape(1, FF)),
        "vfb1": np.ascontiguousarray(np.asarray(inputs["L0_ffn_b1"], f32).reshape(1, FF)),
    }
    vecm = np.zeros((16, DM), f32)
    vecm[0] = np.asarray(inputs["L0_attn_ln1_g"], f32)
    vecm[1] = np.asarray(inputs["L0_attn_ln1_b"], f32)
    vecm[2] = np.asarray(inputs["L0_attn_ln2_g"], f32)
    vecm[3] = np.asarray(inputs["L0_attn_ln2_b"], f32)
    vecm[4] = np.asarray(inputs["L0_attn_bo"], f32)
    vecm[5] = np.asarray(inputs["L0_attn_b2"], f32)
    vecm[6] = np.asarray(inputs["L0_ffn_b2"], f32)
    vecm[7, 0:32] = np.asarray(inputs["L0_attn_lq1"], f32)
    vecm[7, 32:64] = np.asarray(inputs["L0_attn_lq2"], f32)
    vecm[7, 64:96] = np.asarray(inputs["L0_attn_lk1"], f32)
    vecm[7, 96:128] = np.asarray(inputs["L0_attn_lk2"], f32)
    feed["vec"] = vecm
    return feed


def kernel(**inputs):
    x = np.asarray(inputs["L0_attn_x"], np.float32).reshape(-1, DM)
    feed = _prep_host(inputs)
    nc = build_program()
    in_maps = []
    for c in range(N_CORES):
        m = dict(feed)
        m["x"] = np.ascontiguousarray(x[c * T_CORE:(c + 1) * T_CORE])
        in_maps.append(m)
    res = run_bass_kernel_spmd(nc, in_maps, core_ids=list(range(N_CORES)))
    y = np.concatenate([np.asarray(res.results[c]["y"]) for c in range(N_CORES)], axis=0)
    return y.reshape(4, 8192, DM)


# version 09
# speedup vs baseline: 1.1622x; 1.0633x over previous
import numpy as np
import ml_dtypes
import concourse.bass as bass
import concourse.mybir as mybir
from concourse.bass_utils import run_bass_kernel_spmd

F32 = mybir.dt.float32
BF16 = mybir.dt.bfloat16
AF = mybir.ActivationFunctionType
ALU = mybir.AluOpType
AX = mybir.AxisListType

N_CORES = 8
DBG_STAGES = 5
T_CORE = 4096
BLK = 512
CH = 128
NBLK = T_CORE // BLK
NCH = BLK // CH
DM = 1024
FF = 4096
EPS = 1e-5
RM = float(np.float32(1.5 * 2.0 ** 23))
LAMBDA_INIT = 0.8 - 0.6 * float(np.exp(-0.3 * 1.0))
QSCALE = float(32 ** -0.5)


def _perms():
    pq = np.empty(1024, dtype=np.int64)
    for q in range(32):
        s, h = q // 16, q % 16
        for d in range(32):
            pq[q * 32 + d] = h * 64 + d * 2 + s
    pv = np.empty(1024, dtype=np.int64)
    for dd in range(64):
        for h in range(16):
            pv[dd * 16 + h] = h * 64 + dd
    pwo = np.empty(1024, dtype=np.int64)
    for dd in range(64):
        for a in range(16):
            pwo[dd * 16 + a] = a * 64 + dd
    return pq, pv, pwo


class Em:
    def __init__(self, nc):
        self.nc = nc
        self.ops = {k: [] for k in ("pe", "dve", "act", "pool", "sp")}
        self.sems = {}
        self.cnt = {}
        self.hwm = {k: {} for k in self.ops}

    def sem(self, name):
        self.sems[name] = self.nc.semaphore(name).__enter__()
        self.cnt[name] = 0

    def inc(self, name, by=1):
        self.cnt[name] += by
        return self.cnt[name]

    def val(self, name):
        return self.cnt[name]

    def wait(self, eng, name, v):
        if v <= 0 or self.hwm[eng].get(name, 0) >= v:
            return
        self.hwm[eng][name] = v
        s = self.sems[name]
        self.ops[eng].append(lambda e, s=s, v=v: e.wait_ge(s, v))

    def op(self, eng, fn, inc=None, by=1):
        if inc is None:
            self.ops[eng].append(fn)
            return None
        if eng in ("dve", "act", "pool"):
            self.wait(eng, inc, self.val(inc))
        v = self.inc(inc, by)
        s = self.sems[inc]
        self.ops[eng].append(lambda e, fn=fn, s=s, by=by: fn(e).then_inc(s, by))
        return v

    def pesync(self):
        self.wait("pe", "sPE", self.val("sPE"))


def build_program():
    nc = bass.Bass("TRN2", target_bir_lowering=False)

    xin = nc.dram_tensor("x", [T_CORE, DM], F32, kind="ExternalInput").ap()
    yout = nc.dram_tensor("y", [T_CORE, DM], F32, kind="ExternalOutput").ap()
    wqt = nc.dram_tensor("wqt", [DM, DM], F32, kind="ExternalInput").ap()
    wkt = nc.dram_tensor("wkt", [DM, DM], F32, kind="ExternalInput").ap()
    wvt = nc.dram_tensor("wvt", [DM, DM], F32, kind="ExternalInput").ap()
    wot = nc.dram_tensor("wot", [DM, DM], F32, kind="ExternalInput").ap()
    w1t = nc.dram_tensor("w1t", [DM, FF], F32, kind="ExternalInput").ap()
    w2t = nc.dram_tensor("w2t", [FF, DM], F32, kind="ExternalInput").ap()
    q1t = nc.dram_tensor("q1t", [DM, FF], F32, kind="ExternalInput").ap()
    q2t = nc.dram_tensor("q2t", [FF, DM], F32, kind="ExternalInput").ap()
    vec = nc.dram_tensor("vec", [16, DM], F32, kind="ExternalInput").ap()
    vb1 = nc.dram_tensor("vb1", [1, FF], F32, kind="ExternalInput").ap()
    vfb1 = nc.dram_tensor("vfb1", [1, FF], F32, kind="ExternalInput").ap()
    simg = [nc.dram_tensor(f"simg{i}", [128, 32768], BF16, kind="Internal").ap()
            for i in range(5)]

    em = Em(nc)
    for s in ("sDMA", "sPE", "sDVE", "sACT", "sPOOL", "sDMA2", "sWL0", "sWL1", "sXY0", "sXY1", "sXY2", "sXY3", "sGP", "sPF0", "sPF1", "sPF2", "sPF3"):
        em.sem(s)

    ctxs = []

    def sb(name, shape, dtype):
        t = nc.sbuf_tensor(name, shape, dtype)
        h = t.__enter__()
        ctxs.append(t)
        return h

    def psalloc(name, shape, dtype):
        t = nc.psum_tensor(name, shape, dtype)
        h = t.__enter__()
        ctxs.append(t)
        return h

    XS = sb("xs", [128, NCH * DM], F32)
    WB = [sb("wb0", [128, 32768], BF16), sb("wb1", [128, 32768], BF16)]
    UB = sb("ub", [128, DM], BF16)
    UB2 = sb("ub2", [128, DM], BF16)
    HT = sb("ht", [128, 1024], BF16)
    HTB = sb("htb", [128, 4096], BF16)
    T1Q = sb("t1q", [128, 1024], BF16)
    T1K = sb("t1k", [128, 1024], BF16)
    T1VP = sb("t1vp", [128, 2048], BF16)
    ATTN = sb("attn", [128, 1024], BF16)
    IN2 = sb("in2", [128, 1024], BF16)
    AT2T = sb("at2t", [128, 1024], BF16)
    SQ = sb("sq", [128, 2048], F32)
    XR = sb("xr", [128, 2048], BF16)
    T1N = sb("t1n", [128, 1024], BF16)
    HTOK = sb("htok", [128, 2048], BF16)
    QT = sb("qt", [128, 2048], F32)
    BROWS = sb("brows", [128, 4096], BF16)
    IDN = sb("idn", [128, 128], BF16)
    ONESB = sb("onesb", [128, 128], BF16)
    CAB = sb("cab", [128, 32], BF16)
    SM = sb("sm", [128, 16], F32)
    SM2 = sb("sm2", [128, 12], F32)
    SMX = sb("smx", [128, 32], F32)
    GSL = sb("gsl", [128, 24], F32)
    GSL2 = sb("gsl2", [128, 8], F32)
    GSLQ = sb("gslq", [128, 8], F32)
    GSB = sb("gsb", [128, 16], BF16)
    LAM = sb("lam", [128, 1], F32)
    SMALL = sb("small", [1, 8], F32)
    SMALLB = sb("smallb", [1, 2], BF16)
    CST = sb("cst", [128, 4], F32)

    EXPS = QT[:, 0:1024]
    TRASH = QT[:, 1024:2048]
    SSUB = SQ[:, 0:1024]
    T1O = HTOK
    VR = XS[:, 0:1024]
    VTV = HTB[:, 0:2048]
    VTQ = HTB[:, 2048:3072]
    VTK = HTB[:, 3072:4096]
    NEGMAX = SMX
    RINV = SMX
    FACT = SMX

    ROWMAP = {
        ("CQ", 0): (0, 0), ("CQ", 1): (0, 512),
        ("CK", 0): (32, 0), ("CK", 1): (32, 512),
        ("CV", 0): (64, 0), ("CV", 1): (64, 512),
        ("CO", 0): (96, 0), ("CO", 1): (96, 512),
    }
    for c in range(8):
        ROWMAP[("Q1R", c)] = (0, 1024 + 512 * c) if c < 4 else (64, 1024 + 512 * (c - 4))
        ROWMAP[("C1", c)] = (32, 1024 + 512 * c) if c < 4 else (96, 1024 + 512 * (c - 4))
    for c in range(2):
        ROWMAP[("C2", c)] = (0, 3072 + 512 * c)
        ROWMAP[("Q2R", c)] = (32, 3072 + 512 * c)

    def brow(name, c):
        p, col = ROWMAP[(name, c)]
        return BROWS[p:p + 1, col:col + 512], p

    PA = psalloc("pa", [128, 2048], F32)
    PB = psalloc("pb", [128, 1024], F32)
    PC = psalloc("pc", [128, 1024], F32)
    PTTA = PA[:, 0:512].bitcast(BF16)
    PTTB = PB[:, 0:512].bitcast(BF16)
    TRASHB = QT[:, 1024:1536].bitcast(BF16)
    PTTC = PC[:, 0:512].bitcast(BF16)


    W = em.wait

    def dma(dst, src, inc="sDMA"):
        return em.op("sp", lambda e: e.dma_start(dst, src), inc=inc, by=16)

    for i, cv in enumerate([RM, -RM, 16.0, -0.5]):
        em.op("pool", lambda e, i=i, cv=cv: e.memset(CST[:, i:i + 1], float(cv)), inc="sPOOL")
        nc.const_aps.register(F32, float(cv), CST[:, i:i + 1])
    em.op("pool", lambda e: e.memset(ONESB[:], 1.0), inc="sPOOL")
    W("pool", "sPOOL", em.val("sPOOL"))
    em.op("pool", lambda e: e.affine_select(
        IDN[:], ONESB[:], [[-1, 128]], ALU.is_equal, 0.0,
        base=0, channel_multiplier=1), inc="sPOOL")
    em.op("pool", lambda e: e.memset(T1VP[:], 0.0), inc="sPOOL")
    em.op("pool", lambda e: e.memset(IN2[:], 0.0), inc="sPOOL")
    em.op("pool", lambda e: e.memset(XR[:], 0.0), inc="sPOOL")
    em.op("pool", lambda e: e.memset(CAB[:], 0.0), inc="sPOOL")
    em.op("pool", lambda e: e.memset(BROWS[:], 0.0), inc="sPOOL")
    W("pool", "sPOOL", em.val("sPOOL"))
    for c in range(4):
        em.op("pool", lambda e, c=c: e.affine_select(
            CAB[32 * c:32 * c + 32, 0:16], ONESB[32 * c:32 * c + 32, 0:16],
            [[-1, 16]], ALU.is_equal, 0.0, base=0, channel_multiplier=1), inc="sPOOL")
    v_consts = em.val("sPOOL")

    for j in range(8):
        dma(GSL[:, j:j + 1], vec[0:1, 128 * j:128 * j + 128].transpose([1, 0]))
    for j in range(8):
        dma(GSL[:, 8 + j:9 + j], vec[2:3, 128 * j:128 * j + 128].transpose([1, 0]))
    for j in range(8):
        dma(GSL[:, 16 + j:17 + j], vec[1:2, 128 * j:128 * j + 128].transpose([1, 0]))
    for j in range(8):
        dma(GSL2[:, j:j + 1], vec[3:4, 128 * j:128 * j + 128].transpose([1, 0]))
    v_gsl = em.val("sDMA")
    dma(VR[0:1, :], vec[4:5, :])
    dma(VR[32:33, :], vec[5:6, :])
    dma(VR[64:65, :], vec[6:7, :])
    dma(VR[96:97, 0:128], vec[7:8, 0:128])
    v_vr = em.val("sDMA")


    W("dve", "sDMA", v_vr)
    em.op("dve", lambda e: e.tensor_mul(VR[96:97, 256:288], VR[96:97, 0:32], VR[96:97, 64:96]), inc="sDVE")
    em.op("dve", lambda e: e.tensor_mul(VR[96:97, 288:320], VR[96:97, 32:64], VR[96:97, 96:128]), inc="sDVE")
    em.op("dve", lambda e: e.reduce_sum(VR[96:97, 320:321], VR[96:97, 256:288].rearrange("p (a b) -> p a b", a=1), axis=AX.X), inc="sDVE")
    em.op("dve", lambda e: e.reduce_sum(VR[96:97, 321:322], VR[96:97, 288:320].rearrange("p (a b) -> p a b", a=1), axis=AX.X), inc="sDVE")
    W("act", "sDVE", em.val("sDVE"))
    em.op("act", lambda e: e.activation(VR[96:97, 322:323], VR[96:97, 320:321], AF.Exp), inc="sACT")
    em.op("act", lambda e: e.activation(VR[96:97, 323:324], VR[96:97, 321:322], AF.Exp), inc="sACT")
    W("dve", "sACT", em.val("sACT"))
    em.op("dve", lambda e: e.tensor_sub(VR[96:97, 324:325], VR[96:97, 323:324], VR[96:97, 322:323]), inc="sDVE")
    em.op("dve", lambda e: e.tensor_scalar(SMALL[0:1, 0:1], VR[96:97, 324:325], float(LAMBDA_INIT), None, ALU.subtract), inc="sDVE")
    em.op("dve", lambda e: e.tensor_copy(SMALLB[0:1, 0:1], SMALL[0:1, 0:1]), inc="sDVE")
    W("pe", "sDVE", em.val("sDVE"))
    W("pe", "sPOOL", v_consts)
    em.op("pe", lambda e: e.matmul(PA[:, 0:1], ONESB[0:1, 0:128], SMALLB[0:1, 0:1],
                                   start=True, stop=True), inc="sPE")
    W("dve", "sPE", em.val("sPE"))
    em.op("dve", lambda e: e.tensor_copy(LAM[:, 0:1], PA[:, 0:1]), inc="sDVE")
    W("pool", "sDVE", em.val("sDVE"))
    W("pool", "sPOOL", v_consts)
    for c in range(4):
        em.op("pool", lambda e, c=c: e.affine_select(
            CAB[32 * c:32 * c + 32, 16:32],
            LAM[32 * c:32 * c + 32, 0:1].broadcast_to([32, 16]),
            [[-1, 16]], ALU.is_equal, 0.0, base=-16, channel_multiplier=1), inc="sPOOL")
    v_cab = em.val("sPOOL")

    def gdma(dst, src):
        return em.op("pool", lambda e: e.dma_start(dst, src), inc="sGP", by=16)

    for j in range(8):
        gdma(simg[0][:, 24576 + 1024 * j:24576 + 1024 * j + 1024], wot[128 * j:128 * j + 128, :])
    gp_img0 = em.val("sGP")
    gp_img = {1: 0, 2: 0}

    img_ready = {}
    rows_gate = {}
    HTBF = HTB[:].bitcast(F32)
    ring = [SQ[:, 0:2048], QT[:, 0:2048], HTBF[:, 0:1024], HTBF[:, 1024:2048]]
    ring_w = [2048, 2048, 1024, 1024]
    rs = {"i": 0, "done": {}}

    def _staged(src_ap, width, consume, row):
        while True:
            s = rs["i"] % 4
            rs["i"] += 1
            if ring_w[s] >= width:
                break
        psem = "sPF%d" % s
        buf = ring[s][0:1, 0:width] if row else ring[s][:, 0:width]
        if s in rs["done"]:
            for sem_n, v in rs["done"][s]:
                W("sp", sem_n, v)
        vd = dma(buf, src_ap, inc=psem)
        rs["done"][s] = consume(buf, vd, psem)

    def staged(src_ap, width, consume):
        _staged(src_ap, width, consume, False)

    def staged_row(src_ap, width, consume):
        _staged(src_ap, width, consume, True)

    def staged_q(src_ap, consume):
        while True:
            s = rs["i"] % 4
            rs["i"] += 1
            if ring_w[s] >= 2048:
                break
        psem = "sPF%d" % s
        slot = ring[s]
        if s in rs["done"]:
            for sem_n, v in rs["done"][s]:
                W("sp", sem_n, v)
        vd = dma(slot[:, 0:1024], src_ap, inc=psem)
        rs["done"][s] = consume(slot, vd, psem)

    def prep():
        W("dve", "sDMA", v_vr)
        W("act", "sDMA", v_vr)
        W("pe", "sDMA", v_vr)
        W("pe", "sPOOL", v_consts)
        em.op("dve", lambda e: e.tensor_scalar(GSLQ[:], GSL[:, 0:8], QSCALE, None, ALU.mult), inc="sDVE")
        v_gslq = em.val("sDVE")

        pa_busy = {"v": em.val("sDVE")}
        for mi, wt in enumerate([wqt, wkt, wvt]):
            for j in range(8):
                def consume(buf, vd, psem, mi=mi, j=j):
                    col = mi * 8192 + j * 1024
                    scap = GSLQ[:, j:j + 1] if mi == 0 else GSL[:, j:j + 1]
                    W("act", psem, vd)
                    if mi == 0:
                        W("act", "sDVE", v_gslq)
                    em.op("act", lambda e, col=col, scap=scap: e.activation(
                        WB[1][:, col:col + 1024], buf, AF.Copy, scale=scap), inc="sACT")
                    va = em.val("sACT")
                    W("pe", psem, vd)
                    if j == 0:
                        W("pe", "sDVE", pa_busy["v"])
                    for hh in range(2):
                        em.op("pe", lambda e, j=j, hh=hh: e.matmul(
                            PA[0:1, 512 * hh:512 * hh + 512], GSL[:, 16 + j:17 + j],
                            buf[:, 512 * hh:512 * hh + 512],
                            start=(j == 0), stop=(j == 7)), inc="sPE")
                    return [("sACT", va), ("sPE", em.val("sPE"))]
                staged(wt[128 * j:128 * j + 128, :], 1024, consume)
            W("dve", "sPE", em.val("sPE"))
            nm = ["CQ", "CK", "CV"][mi]
            sc = QSCALE if mi == 0 else 1.0
            for c in range(2):
                r, _ = brow(nm, c)
                em.op("dve", lambda e, r=r, c=c, sc=sc: e.tensor_scalar(
                    r, PA[0:1, 512 * c:512 * c + 512], float(sc), None, ALU.mult), inc="sDVE")
            pa_busy["v"] = em.val("sDVE")
        W("pool", "sDMA", v_vr)
        for c in range(2):
            r, _ = brow("CO", c)
            em.op("pool", lambda e, r=r, c=c: e.tensor_copy(r, VR[0:1, 512 * c:512 * c + 512]), inc="sPOOL")
            r2, _ = brow("C2", c)
            em.op("pool", lambda e, r=r2, c=c: e.tensor_copy(r, VR[32:33, 512 * c:512 * c + 512]), inc="sPOOL")
        v_vr_done = em.val("sPOOL")
        def qrow_consume(buf, vd, psem, names):
            W("pool", psem, vd)
            em.op("pool", lambda e: e.tensor_scalar(buf, buf, 64.0, RM, ALU.mult, ALU.add), inc="sPOOL")
            em.op("pool", lambda e: e.tensor_scalar(buf, buf, RM, -128.0, ALU.subtract, ALU.max), inc="sPOOL")
            for c, (nm, ch) in enumerate(names):
                r, _ = brow(nm, ch)
                em.op("pool", lambda e, r=r, c=c: e.tensor_scalar(
                    r, buf[0:1, 512 * c:512 * c + 512], 127.0, 1.0 / 64.0, ALU.min, ALU.mult), inc="sPOOL")
            return [("sPOOL", em.val("sPOOL"))]
        staged_row(vec[6:7, :], 1024, lambda buf, vd, psem: qrow_consume(buf, vd, psem, [("Q2R", 0), ("Q2R", 1)]))
        for hh in range(2):
            staged_row(vfb1[0:1, 2048 * hh:2048 * hh + 2048], 2048,
                       lambda buf, vd, psem, hh=hh: qrow_consume(buf, vd, psem, [("Q1R", hh * 4 + c) for c in range(4)]))
        rows_gate[0] = [("sDVE", pa_busy["v"]), ("sPOOL", v_vr_done)]
        rows_gate[3] = rows_gate[4] = [("sPOOL", em.val("sPOOL")), ("sPOOL", v_vr_done)]
        W("sp", "sACT", em.val("sACT"))
        W("sp", "sDMA2", em.val("sDMA2"))
        vstore = dma(simg[0][:, 0:24576], WB[1][:, 0:24576], inc="sDMA2")
        img_ready[0] = (vstore, gp_img0)

        for half in range(2):
            W("act", "sDMA2", em.val("sDMA2"))
            for j in range(8):
                def consume(buf, vd, psem, j=j, half=half):
                    col = j * 2048
                    W("act", psem, vd)
                    em.op("act", lambda e, col=col, j=j: e.activation(
                        WB[1][:, col:col + 2048], buf, AF.Copy, scale=GSL[:, 8 + j:9 + j]), inc="sACT")
                    va = em.val("sACT")
                    W("pe", psem, vd)
                    if j == 0:
                        W("pe", "sDVE", pa_busy["v"])
                    for qo in range(4):
                        em.op("pe", lambda e, j=j, qo=qo: e.matmul(
                            PA[0:1, 512 * qo:512 * qo + 512], GSL2[:, j:j + 1],
                            buf[:, 512 * qo:512 * qo + 512],
                            start=(j == 0), stop=(j == 7)), inc="sPE")
                    return [("sACT", va), ("sPE", em.val("sPE"))]
                staged(w1t[128 * j:128 * j + 128, 2048 * half:2048 * half + 2048], 2048, consume)
            for j in range(16):
                lane = ("act", "dve", "pool")[j % 3]
                def consume_w2(buf, vd, psem, j=j, lane=lane):
                    lsem = {"act": "sACT", "dve": "sDVE", "pool": "sPOOL"}[lane]
                    W(lane, psem, vd)
                    dst = WB[1][:, 16384 + 1024 * j:16384 + 1024 * j + 1024]
                    if lane == "act":
                        em.op("act", lambda e: e.copy(dst, buf), inc="sACT")
                    else:
                        em.op(lane, lambda e: e.tensor_copy(dst, buf), inc=lsem)
                    return [(lsem, em.val(lsem))]
                staged(w2t[2048 * half + 128 * j:2048 * half + 128 * j + 128, :], 1024, consume_w2)
            def consume_b(buf, vd, psem, half=half):
                W("dve", psem, vd)
                W("dve", "sPE", em.val("sPE"))
                for qo in range(4):
                    r, _ = brow("C1", half * 4 + qo)
                    em.op("dve", lambda e, r=r, qo=qo, buf=buf: e.tensor_tensor(
                        r, buf[0:1, 512 * qo:512 * qo + 512],
                        PA[0:1, 512 * qo:512 * qo + 512], ALU.add), inc="sDVE")
                return [("sDVE", em.val("sDVE"))]
            staged_row(vb1[0:1, 2048 * half:2048 * half + 2048], 2048, consume_b)
            pa_busy["v"] = em.val("sDVE")
            W("sp", "sACT", em.val("sACT"))
            W("sp", "sDVE", em.val("sDVE"))
            W("sp", "sPOOL", em.val("sPOOL"))
            W("sp", "sDMA2", em.val("sDMA2"))
            vstore = dma(simg[1 + half][:], WB[1][:], inc="sDMA2")
            img_ready[1 + half] = (vstore, gp_img[1 + half])
        rows_gate[1] = rows_gate[2] = [("sDVE", pa_busy["v"])]

        qtile = {"i": 0}
        def quant_tile(src_ap, col):
            lane = "pool" if qtile["i"] % 3 == 2 else "dve"
            lsem = "sPOOL" if lane == "pool" else "sDVE"
            qtile["i"] += 1
            def consume(slot, vd, psem, lane=lane, lsem=lsem, col=col):
                a, b = slot[:, 0:1024], slot[:, 1024:2048]
                W("act", psem, vd)
                em.op("act", lambda e, a=a, b=b: e.activation(
                    b, a, AF.Identity, bias=RM, scale=64.0), inc="sACT")
                va = em.val("sACT")
                W(lane, "sACT", va)
                em.op(lane, lambda e, b=b: e.tensor_scalar(b, b, RM, -128.0, ALU.subtract, ALU.max), inc=lsem)
                em.op(lane, lambda e, b=b, col=col: e.tensor_scalar(
                    WB[1][:, col:col + 1024], b, 127.0, 1.0 / 64.0, ALU.min, ALU.mult), inc=lsem)
                return [("sACT", va), (lsem, em.val(lsem))]
            staged_q(src_ap, consume)
        for half in range(2):
            W("dve", "sDMA2", em.val("sDMA2"))
            W("pool", "sDMA2", em.val("sDMA2"))
            W("act", "sDMA2", em.val("sDMA2"))
            for j in range(8):
                for k in range(2):
                    quant_tile(q1t[128 * j:128 * j + 128, 2048 * half + 1024 * k:2048 * half + 1024 * k + 1024],
                               j * 2048 + 1024 * k)
            for j in range(16):
                quant_tile(q2t[2048 * half + 128 * j:2048 * half + 128 * j + 128, :], 16384 + j * 1024)
            W("sp", "sDVE", em.val("sDVE"))
            W("sp", "sPOOL", em.val("sPOOL"))
            W("sp", "sACT", em.val("sACT"))
            W("sp", "sDMA2", em.val("sDMA2"))
            vstore = dma(simg[3 + half][:], WB[1][:], inc="sDMA2")
            img_ready[3 + half] = (vstore, 0)

    prep()
    v_prep = (em.val("sDMA2"), em.val("sGP"))
    prep_bnd = {"pa_dve": em.val("sDVE"), "pa_act": 0, "pb_dve": 0, "pc_dve": 0, "pc_act": 0}
    v_prep_lanes = {s: em.val(s) for s in ("sDVE", "sPOOL", "sACT", "sPE")}

    for eng in ("pe", "dve", "act"):
        W(eng, "sPOOL", v_cab)

    def ln_stats(xap, compound):
        em.op("dve", lambda e: e.reduce_sum(SM[:, 0:1], xap.rearrange("p (a b) -> p a b", a=1), axis=AX.X), inc="sDVE")
        em.op("act", lambda e: e.activation(TRASH, xap, AF.Square, accum_out=SM2[:, 0:1]), inc="sACT")
        W("dve", "sACT", em.val("sACT"))
        em.op("dve", lambda e: e.tensor_scalar(SM[:, 0:1], SM[:, 0:1], 1.0 / 1024.0, None, ALU.mult), inc="sDVE")
        em.op("dve", lambda e: e.tensor_mul(SM2[:, 1:2], SM[:, 0:1], SM[:, 0:1]), inc="sDVE")
        em.op("dve", lambda e: e.tensor_scalar(SM2[:, 2:3], SM2[:, 0:1], 1.0 / 1024.0, None, ALU.mult), inc="sDVE")
        em.op("dve", lambda e: e.tensor_sub(SM2[:, 3:4], SM2[:, 2:3], SM2[:, 1:2]), inc="sDVE")
        em.op("dve", lambda e: e.tensor_scalar(SM2[:, 4:5], SM2[:, 3:4], EPS, None, ALU.add), inc="sDVE")
        W("act", "sDVE", em.val("sDVE"))
        em.op("act", lambda e: e.activation(SM2[:, 5:6], SM2[:, 4:5], AF.Ln), inc="sACT")
        em.op("act", lambda e: e.activation(SM[:, 1:2], SM2[:, 5:6], AF.Exp, scale=-0.5), inc="sACT")
        if compound:
            em.op("dve", lambda e: e.reciprocal(SM2[:, 6:7], SM2[:, 4:5]), inc="sDVE")
            em.op("dve", lambda e: e.tensor_mul(SM2[:, 7:8], SM2[:, 3:4], SM2[:, 6:7]), inc="sDVE")
            em.op("dve", lambda e: e.tensor_scalar(SM2[:, 8:9], SM2[:, 7:8], EPS, None, ALU.add), inc="sDVE")
            W("act", "sDVE", em.val("sDVE"))
            em.op("act", lambda e: e.activation(SM2[:, 9:10], SM2[:, 8:9], AF.Ln), inc="sACT")
            W("dve", "sACT", em.val("sACT"))
            em.op("dve", lambda e: e.tensor_add(SM2[:, 10:11], SM2[:, 5:6], SM2[:, 9:10]), inc="sDVE")
            W("act", "sDVE", em.val("sDVE"))
            em.op("act", lambda e: e.activation(SM[:, 2:3], SM2[:, 10:11], AF.Exp, scale=-0.5), inc="sACT")
        return em.val("sACT")

    def transposes(tiles, dst_fn, ntiles, ptt):
        em.pesync()
        for j in range(ntiles):
            em.op("pe", lambda e, j=j: e.transpose(
                ptt[:, 128 * (j % 8):128 * (j % 8) + 128], tiles(j), IDN[:]), inc="sPE")
            if j % 8 == 7:
                W("act", "sPE", em.val("sPE"))
                em.op("act", lambda e, g=j // 8: e.copy(dst_fn(g), ptt[:, 0:1024]), inc="sACT")
                if j + 1 < ntiles:
                    W("pe", "sACT", em.val("sACT"))
        return em.val("sACT")

    def bias_mm(ps_ap, name, chunk, start=False, stop=True):
        r, p = brow(name, chunk)
        em.op("pe", lambda e, r=r, p=p, ps_ap=ps_ap: e.matmul(
            ps_ap, ONESB[p:p + 1, 0:128], r, start=start, stop=stop,
            tile_position=(p, 0)), inc="sPE")

    def ubp(c):
        return UB if c % 2 == 0 else UB2

    AST = {}

    def attn_front(c, xl):
        st = AST.setdefault(c, {})
        xap = XS[:, DM * c:DM * c + DM]
        sx, vx = xl[c]
        W("dve", sx, vx)
        W("act", sx, vx)
        W("pool", sx, vx)
        if c - 3 in AST and "applied_p" in AST[c - 3]:
            W("act", "sPOOL", AST[c - 3]["applied_p"])
        if c - 2 in AST and "ht_pe" in AST[c - 2]:
            W("pool", "sPE", AST[c - 2]["ht_pe"])
        if c - 1 in AST and "ub" in AST[c - 1]:
            W("dve", "sPOOL", AST[c - 1]["ub"])
            W("act", "sPOOL", AST[c - 1]["ub"])
        va = ln_stats(xap, compound=True)
        W("pool", "sACT", va)
        W("pool", "sDVE", em.val("sDVE"))
        em.op("pool", lambda e: e.tensor_scalar(ubp(c)[:], xap, SM[:, 0:1], SM[:, 2:3], ALU.subtract, ALU.mult), inc="sPOOL")
        p2 = 8 + 2 * c
        em.op("pool", lambda e, p2=p2: e.tensor_copy(SM[:, p2:p2 + 2], SM[:, 0:2]), inc="sPOOL")
        st["ub"] = em.val("sPOOL")

    def attn_fq(c, wb, v_wload, prev, bnd=None):
        st = AST[c]
        if prev is not None:
            W("pe", "sDVE", AST[prev]["t1_dve"])
            W("pe", "sACT", max(AST[prev]["t1_act"], AST[prev]["exp"]))
        else:
            W("pe", "sDVE", max(bnd["pa_dve"], bnd["pb_dve"]))
            W("pe", "sACT", bnd["pa_act"])
        W("pe", "sPOOL", st["ub"])
        va = transposes(lambda j: ubp(c)[:, 128 * j:128 * j + 128], lambda g: HT[:, 0:1024], 8, PTTB)
        st["ht_pe"] = em.val("sPE")
        W("pe", "sACT", va)
        W("pe", v_wload[0], v_wload[1])
        em.pesync()
        for j in range(8):
            for mi in range(3):
                dstps, dcol = (PA, 1024 * mi) if mi < 2 else (PB, 0)
                for half in range(2):
                    em.op("pe", lambda e, j=j, mi=mi, half=half, dstps=dstps, dcol=dcol: e.matmul(
                        dstps[:, dcol + 512 * half:dcol + 512 * half + 512],
                        HT[:, 128 * j:128 * j + 128],
                        WB[wb][:, 8192 * mi + 1024 * j + 512 * half:8192 * mi + 1024 * j + 512 * half + 512],
                        start=(j == 0), stop=False), inc="sPE")
        for mi, nm in enumerate(["CQ", "CK", "CV"]):
            dstps, dcol = (PA, 1024 * mi) if mi < 2 else (PB, 0)
            for half in range(2):
                bias_mm(dstps[:, dcol + 512 * half:dcol + 512 * half + 512], nm, half)
        st["qkv"] = em.val("sPE")

    def attn_evacs(c, prevst=None):
        st = AST[c]
        W("dve", "sPE", st["qkv"])
        W("act", "sPE", st["qkv"])
        if prevst is not None:
            W("act", "sDVE", prevst["vt"])
        em.op("dve", lambda e: e.tensor_copy(T1Q[:], PA[:, 0:1024]), inc="sDVE")
        em.op("act", lambda e: e.copy(T1K[:], PA[:, 1024:2048]), inc="sACT")
        st["t1_act"] = em.val("sACT")
        em.op("dve", lambda e: e.tensor_copy(
            T1VP[:].rearrange("p (d h) -> p d h", d=64)[:, :, 0:16], PB[:, 0:1024]), inc="sDVE")
        st["t1_dve"] = em.val("sDVE")

    def attn_vtrans(c, prevst=None):
        st = AST[c]
        if prevst is not None:
            W("dve", "sPE", prevst["o2"])
        W("dve", "sACT", st["t1_act"])
        em.op("dve", lambda e: e.transpose(VTQ[:], T1Q[:]), inc="sDVE")
        em.op("dve", lambda e: e.transpose(VTK[:], T1K[:]), inc="sDVE")
        em.op("dve", lambda e: e.transpose(VTV[:], T1VP[:]), inc="sDVE")
        st["vt"] = em.val("sDVE")

    def attn_scores(c):
        st = AST[c]
        W("pe", "sDVE", st["vt"])
        W("pe", "sACT", st["t1_act"])
        em.pesync()
        for i in range(128):
            r, tl = i % 4, i // 4
            em.op("pe", lambda e, r=r, tl=tl: e.matmul(
                PA[32 * r:32 * r + 32, 32 * tl:32 * tl + 32],
                VTQ[32 * r:32 * r + 32, :].rearrange("p (w cc) -> p w cc", cc=32)[:, :, tl],
                VTK[32 * r:32 * r + 32, :].rearrange("p (w cc) -> p w cc", cc=32)[:, :, tl],
                start=True, stop=True, tile_position=(32 * r, 32 * r)), inc="sPE")
        st["scores"] = em.val("sPE")

    def attn_softmax(c):
        st = AST[c]
        W("act", "sPE", st["scores"])
        em.op("act", lambda e: e.activation(EXPS, PA[:, 0:1024], AF.Exp), inc="sACT")
        st["exp"] = em.val("sACT")
        W("dve", "sACT", st["exp"])
        em.op("dve", lambda e: e.reduce_sum(
            RINV[:], EXPS.rearrange("p (t k) -> p t k", k=32), axis=AX.X), inc="sDVE")
        em.op("dve", lambda e: e.reciprocal(RINV[:], RINV[:]), inc="sDVE")
        st["rinv"] = em.val("sDVE")
        W("pool", "sDVE", st["rinv"])
        em.op("pool", lambda e: e.tensor_tensor(
            ATTN[:].rearrange("p (t k) -> p t k", k=32),
            EXPS.rearrange("p (t k) -> p t k", k=32),
            RINV[:].unsqueeze(2).broadcast_to([128, 32, 32]), ALU.mult), inc="sPOOL")
        st["attn"] = em.val("sPOOL")

    def attn_combine(c, prev, bnd=None):
        st = AST[c]
        W("pe", "sPOOL", st["attn"])
        W("pe", "sPOOL", v_cab)
        if prev is not None and "conn" in AST.get(prev, {}):
            W("pe", "sDVE", AST[prev]["conn"])
        elif bnd is not None:
            W("pe", "sDVE", bnd["pc_dve"])
            W("pe", "sACT", bnd["pc_act"])
        em.pesync()
        for r in range(4):
            for half in range(2):
                em.op("pe", lambda e, r=r, half=half: e.matmul(
                    PC[32 * r:32 * r + 16, 0:512],
                    CAB[32 * r:32 * r + 32, 16 * half:16 * half + 16],
                    ATTN[32 * r:32 * r + 32, :].rearrange("p (t k) -> p t k", k=32)[:, :, 16 * half:16 * half + 16],
                    start=(half == 0), stop=(half == 1), tile_position=(32 * r, 32 * r)), inc="sPE")
        st["comb"] = em.val("sPE")

    def attn_in2(c):
        st = AST[c]
        W("act", "sPE", st["comb"])
        for r in range(4):
            em.op("act", lambda e, r=r: e.copy(
                IN2[32 * r:32 * r + 16, :].rearrange("p (t k) -> p t k", k=32)[:, :, 0:16],
                PC[32 * r:32 * r + 16, 0:512]), inc="sACT")
        st["in2"] = em.val("sACT")
        W("dve", "sACT", st["in2"])
        em.op("dve", lambda e: e.transpose(AT2T[:], IN2[:]), inc="sDVE")
        st["at2t"] = em.val("sDVE")

    def attn_einsum2(c):
        st = AST[c]
        W("pe", "sDVE", st["at2t"])
        W("pe", "sACT", st["in2"])
        for h in range(2):
            em.pesync()
            for i in range(64):
                r, tj = i % 4, i // 4
                tl = 16 * h + tj
                em.op("pe", lambda e, r=r, tl=tl, tj=tj: e.matmul(
                    PC[32 * r:32 * r + 32, 64 * tj:64 * tj + 64],
                    AT2T[32 * r:32 * r + 32, 32 * tl:32 * tl + 32],
                    VTV[32 * r:32 * r + 32, :].rearrange("p (w cc) -> p w cc", cc=32)[:, :, tl],
                    start=True, stop=True, tile_position=(32 * r, 32 * r)), inc="sPE")
            W("act", "sPE", em.val("sPE"))
            dstb = ATTN[:] if h == 0 else TRASHB
            em.op("act", lambda e, dstb=dstb: e.copy(dstb, PC[:]), inc="sACT")
            em.op("act", lambda e, h=h: e.activation(
                SQ[:, 1024 * h:1024 * h + 1024], PC[:], AF.Square), inc="sACT")
            st["o2e%d" % h] = em.val("sACT")
            if h == 0:
                W("pe", "sACT", st["o2e0"])
        st["o2"] = em.val("sPE")

    def attn_rms(c):
        st = AST[c]
        W("dve", "sACT", st["o2e1"])
        em.op("dve", lambda e: e.reduce_sum(
            FACT[:], SQ[:].rearrange("p (t d) -> p t d", d=64), axis=AX.X), inc="sDVE")
        em.op("dve", lambda e: e.tensor_scalar(FACT[:], FACT[:], 1.0 / 64.0, EPS, ALU.mult, ALU.add), inc="sDVE")
        W("act", "sDVE", em.val("sDVE"))
        em.op("act", lambda e: e.activation(FACT[:], FACT[:], AF.Ln), inc="sACT")
        em.op("act", lambda e: e.activation(FACT[:], FACT[:], AF.Exp, scale=-0.5), inc="sACT")
        W("dve", "sACT", em.val("sACT"))
        em.op("dve", lambda e: e.tensor_scalar(FACT[:], FACT[:], float(1.0 - LAMBDA_INIT), None, ALU.mult), inc="sDVE")
        vsm = em.val("sDVE")
        em.op("dve", lambda e: e.tensor_tensor(
            bass.AP(XR[:].tensor, XR[:].offset, [[2048, 128], [1, 16], [32, 64]]),
            ATTN[:].rearrange("p (t d) -> p t d", d=64),
            FACT[:, 0:16].unsqueeze(2).broadcast_to([128, 16, 64]), ALU.mult), inc="sDVE")
        st["applied_d"] = em.val("sDVE")
        W("pool", "sDVE", vsm)
        W("pool", "sACT", st["o2e1"])
        em.op("pool", lambda e: e.tensor_tensor(
            bass.AP(XR[:].tensor, XR[:].offset + 16, [[2048, 128], [1, 16], [32, 64]]),
            TRASHB.rearrange("p (t d) -> p t d", d=64),
            FACT[:, 16:32].unsqueeze(2).broadcast_to([128, 16, 64]), ALU.mult), inc="sPOOL")
        st["applied_p"] = em.val("sPOOL")
        W("dve", "sPOOL", st["applied_p"])
        em.op("dve", lambda e: e.transpose(T1O[:], XR[:]), inc="sDVE")
        st["t1o"] = em.val("sDVE")
        W("pool", "sDVE", st["t1o"])
        em.op("pool", lambda e: e.tensor_copy(
            T1N[:].rearrange("p (d a) -> p d a", a=16),
            T1O[:].rearrange("p (d a) -> p d a", a=32)[:, :, 0:16]), inc="sPOOL")
        st["t1n"] = em.val("sPOOL")

    def attn_fmwo(c, wb):
        st = AST[c]
        W("pe", "sPOOL", st["t1n"])
        if c - 1 in AST and "fm" in AST[c - 1]:
            W("pe", "sACT", AST[c - 1]["fm"])
        va = transposes(lambda j: T1N[:, 128 * j:128 * j + 128], lambda g: HT[:, 0:1024], 8, PTTC)
        st["fm"] = va
        W("pe", "sACT", va)
        em.pesync()
        for j in range(8):
            for half in range(2):
                em.op("pe", lambda e, j=j, half=half: e.matmul(
                    PC[:, 512 * half:512 * half + 512],
                    HT[:, 128 * j:128 * j + 128],
                    WB[wb][:, 24576 + 1024 * j + 512 * half:24576 + 1024 * j + 512 * half + 512],
                    start=(j == 0), stop=False), inc="sPE")
        for half in range(2):
            bias_mm(PC[:, 512 * half:512 * half + 512], "CO", half)
        st["wo"] = em.val("sPE")

    def attn_conn(c):
        st = AST[c]
        xap = XS[:, DM * c:DM * c + DM]
        p2 = 8 + 2 * c
        em.op("pool", lambda e, p2=p2: e.tensor_scalar(
            SSUB[:], xap, SM[:, p2:p2 + 1], SM[:, p2 + 1:p2 + 2], ALU.subtract, ALU.mult), inc="sPOOL")
        vss = em.val("sPOOL")
        W("dve", "sPE", st["wo"])
        W("dve", "sPOOL", vss)
        em.op("dve", lambda e: e.tensor_tensor(xap, SSUB[:], PC[:], ALU.add), inc="sDVE")
        st["conn"] = em.val("sDVE")

    def stage_attn_pipelined(wb, v_wload, xl, bnd):
        AST.clear()
        attn_front(0, xl)
        attn_fq(0, wb, v_wload, None, bnd)
        attn_front(1, xl)
        attn_evacs(0)
        attn_vtrans(0)
        attn_scores(0)
        attn_softmax(0)
        attn_fq(1, wb, v_wload, 0)
        for k in range(NCH):
            if k + 2 < NCH:
                attn_front(k + 2, xl)
            attn_combine(k, k - 1 if k > 0 else None, bnd)
            attn_in2(k)
            if k + 1 < NCH:
                attn_evacs(k + 1, AST[k])
            attn_einsum2(k)
            if k + 1 < NCH:
                attn_vtrans(k + 1, AST[k])
                attn_scores(k + 1)
            attn_rms(k)
            if k + 1 < NCH:
                attn_softmax(k + 1)
            attn_fmwo(k, wb)
            attn_conn(k)
            if k + 2 < NCH:
                attn_fq(k + 2, wb, v_wload, k + 1)
        return {"pa_dve": AST[3]["t1_dve"], "pa_act": AST[3]["exp"],
                "pb_dve": AST[3]["t1_dve"], "pc_dve": AST[3]["conn"], "pc_act": 0}

    FST = {}

    def ffn_front(blk, c, half, quant):
        st = FST.setdefault(c, {})
        xap = XS[:, DM * c:DM * c + DM]
        if half == 1:
            return
        W("act", "sDVE", em.val("sDVE"))
        va = ln_stats(xap, compound=False)
        W("dve", "sACT", va)
        if not quant:
            em.op("dve", lambda e: e.tensor_scalar(
                ubp(c)[:], xap, SM[:, 0:1], SM[:, 1:2], ALU.subtract, ALU.mult), inc="sDVE")
        else:
            em.op("dve", lambda e: e.tensor_scalar(SM[:, 3:4], SM[:, 1:2], 16.0, None, ALU.mult), inc="sDVE")
            em.op("dve", lambda e: e.tensor_scalar(
                SSUB[:], xap, SM[:, 0:1], SM[:, 3:4], ALU.subtract, ALU.mult), inc="sDVE")
            em.op("dve", lambda e: e.tensor_scalar(
                SSUB[:], SSUB[:], RM, RM, ALU.add, ALU.subtract), inc="sDVE")
            em.op("dve", lambda e: e.tensor_scalar(
                SSUB[:], SSUB[:], -128.0, 127.0, ALU.max, ALU.min), inc="sDVE")
            em.op("dve", lambda e: e.tensor_scalar(
                ubp(c)[:], SSUB[:], 1.0 / 16.0, None, ALU.mult), inc="sDVE")
        st["ub"] = em.val("sDVE")

    def ffn_front_T(c, half, ptt_free=0):
        st = FST[c]
        if half == 1:
            return
        W("pe", "sDVE", st["ub"])
        if ptt_free:
            W("pe", "sACT", ptt_free)
        va = transposes(lambda j: ubp(c)[:, 128 * j:128 * j + 128],
                        lambda g, c=c: HTB[:, 1024 * c:1024 * c + 1024], 8, PTTC)
        st["stash"] = va
        st["stash_pe"] = em.val("sPE")

    def ffn_w1h(c, wb, v_wload, half, quant, hh, prev, bnd=None):
        st = FST[c]
        if hh == 0:
            W("pe", "sACT", st["stash"])
            W("pe", v_wload[0], v_wload[1])
        if prev is not None and ("evA%d" % hh) in FST.get(prev, {}):
            a, d = FST[prev]["evA%d" % hh]
            W("pe", "sACT", a)
            if d:
                W("pe", "sDVE", d)
        elif bnd is not None:
            W("pe", "sACT", bnd["pa_act"])
            W("pe", "sDVE", bnd["pa_dve"])
        em.pesync()
        base0 = 1024 * c
        for j in range(8):
            for qo in (2 * hh, 2 * hh + 1):
                em.op("pe", lambda e, j=j, qo=qo, base0=base0: e.matmul(
                    PA[:, 512 * qo:512 * qo + 512],
                    HTB[:, base0 + 128 * j:base0 + 128 * j + 128],
                    WB[wb][:, 2048 * j + 512 * qo:2048 * j + 512 * qo + 512],
                    start=(j == 0), stop=False), inc="sPE")
        nm1 = "Q1R" if quant else "C1"
        for qo in (2 * hh, 2 * hh + 1):
            bias_mm(PA[:, 512 * qo:512 * qo + 512], nm1, half * 4 + qo)
        st["w1h%d" % hh] = em.val("sPE")

    def ffn_evach(c, half, quant, hh):
        st = FST[c]
        W("act", "sPE", st["w1h%d" % hh])
        lo = 1024 * hh
        if not quant:
            em.op("act", lambda e, lo=lo: e.activation(
                HTOK[:, lo:lo + 1024], PA[:, lo:lo + 1024], AF.Relu), inc="sACT")
            st["evA%d" % hh] = (em.val("sACT"), 0)
            st["hid%d" % hh] = ("sACT", em.val("sACT"))
        else:
            if hh == 0:
                W("act", "sDVE", em.val("sDVE"))
            em.op("act", lambda e, lo=lo: e.activation(
                QT[:, lo:lo + 1024], PA[:, lo:lo + 1024], AF.Identity, bias=RM, scale=16.0), inc="sACT")
            em.op("act", lambda e, lo=lo: e.activation(
                SQ[:, lo:lo + 1024], QT[:, lo:lo + 1024], AF.Relu, bias=-RM, scale=1.0), inc="sACT")
            va = em.val("sACT")
            W("dve", "sACT", va)
            em.op("dve", lambda e, lo=lo: e.tensor_scalar(
                HTOK[:, lo:lo + 1024], SQ[:, lo:lo + 1024], 127.0, 1.0 / 16.0, ALU.min, ALU.mult), inc="sDVE")
            st["evA%d" % hh] = (va, em.val("sDVE"))
            st["hid%d" % hh] = ("sDVE", em.val("sDVE"))

    def ffn_hidTh(c, hh, ptt_free=0):
        st = FST[c]
        sem_n, v = st["hid%d" % hh]
        W("pe", sem_n, v)
        if ptt_free:
            W("pe", "sACT", ptt_free)
        va = transposes(lambda j, hh=hh: HTOK[:, 1024 * hh + 128 * j:1024 * hh + 128 * j + 128],
                        lambda g, hh=hh: (T1Q if hh == 0 else T1K)[:, 0:1024], 8, PTTC)
        st["hidT%d" % hh] = va

    def ffn_w2h(c, wb, half, quant, hh, prev, pb_free=0):
        st = FST[c]
        W("pe", "sACT", st["hidT%d" % hh])
        if hh == 0:
            if prev is not None and "res" in FST.get(prev, {}):
                W("pe", "sDVE", FST[prev]["res"])
            elif pb_free:
                W("pe", "sDVE", pb_free)
        em.pesync()
        srcb = T1Q if hh == 0 else T1K
        for j in range(8):
            for cc in range(2):
                em.op("pe", lambda e, j=j, cc=cc, srcb=srcb, hh=hh: e.matmul(
                    PB[:, 512 * cc:512 * cc + 512],
                    srcb[:, 128 * j:128 * j + 128],
                    WB[wb][:, 16384 + 1024 * (8 * hh + j) + 512 * cc:16384 + 1024 * (8 * hh + j) + 512 * cc + 512],
                    start=(hh == 0 and j == 0), stop=(half == 0 and hh == 1 and j == 7)), inc="sPE")
        if hh == 1:
            if half == 1:
                nm2 = "Q2R" if quant else "C2"
                for cc in range(2):
                    bias_mm(PB[:, 512 * cc:512 * cc + 512], nm2, cc)
            st["hidT_evac"] = st["hidT1"]
            st["w2"] = em.val("sPE")

    def ffn_residual(blk, c, half, quant, last_stage=False, xl_next=None):
        st = FST[c]
        xap = XS[:, DM * c:DM * c + DM]
        W("dve", "sPE", st["w2"])
        em.op("dve", lambda e: e.tensor_tensor(xap, xap, PB[:], ALU.add), inc="sDVE")
        if half == 1 and not quant:
            W("sp", "sDVE", em.val("sDVE"))
            xr = dma(SQ[:, 0:1024],
                     bass.AP(xin.tensor, (blk * BLK + c * CH) * DM, [[DM, 128], [1, DM]]))
            W("dve", "sDMA", xr)
            em.op("dve", lambda e: e.tensor_tensor(xap, xap, SQ[:, 0:1024], ALU.add), inc="sDVE")
        st["res"] = em.val("sDVE")
        if last_stage:
            sx = "sXY%d" % c
            W("sp", "sDVE", st["res"])
            dma(bass.AP(yout.tensor, (blk * BLK + c * CH) * DM, [[DM, 128], [1, DM]]),
                XS[:, DM * c:DM * c + DM], inc=sx)
            if blk + 1 < NBLK:
                W("sp", sx, em.val(sx))
                dma(XS[:, DM * c:DM * c + DM],
                    bass.AP(xin.tensor, ((blk + 1) * BLK + c * CH) * DM, [[DM, 128], [1, DM]]),
                    inc=sx)
                xl_next[c] = (sx, em.val(sx))

    def stage_ffn_pipelined(blk, wb, v_wload, half, quant, bnd, last_stage=False, xl_next=None):
        if half == 0:
            FST.clear()
        ffn_front(blk, 0, half, quant)
        W("pe", "sDVE", bnd["pc_dve"])
        ffn_front_T(0, half, ptt_free=bnd["pc_act"])
        for cm in range(NCH):
            cf = cm + 1 if cm + 1 < NCH else None
            if cf is not None:
                ffn_front(blk, cf, half, quant)
            prev = cm - 1 if cm > 0 else None
            ffn_w1h(cm, wb, v_wload, half, quant, 0, prev, bnd)
            ffn_w1h(cm, wb, v_wload, half, quant, 1, prev, bnd)
            ffn_evach(cm, half, quant, 0)
            prev_hevac = FST.get(cm - 1, {}).get("hidT_evac", 0)
            if cm == 0:
                prev_hevac = FST[0].get("stash", 0) if half == 0 else bnd["pc_act"]
            if cf is not None:
                ffn_front_T(cf, half, ptt_free=prev_hevac)
            if half == 0 and cf is not None:
                pfree = FST[cf].get("stash", 0)
            else:
                pfree = prev_hevac
            ffn_hidTh(cm, 0, ptt_free=pfree)
            ffn_evach(cm, half, quant, 1)
            ffn_w2h(cm, wb, half, quant, 0, prev, pb_free=bnd["pb_dve"] if cm == 0 else 0)
            ffn_hidTh(cm, 1, ptt_free=FST[cm]["hidT0"])
            ffn_w2h(cm, wb, half, quant, 1, prev)
            ffn_residual(blk, cm, half, quant, last_stage=last_stage, xl_next=xl_next)
        return {"pa_act": max(FST[3]["evA0"][0], FST[3]["evA1"][0]),
                "pa_dve": max(FST[3]["evA0"][1], FST[3]["evA1"][1]),
                "pb_dve": FST[3]["res"], "pc_act": FST[3]["hidT_evac"], "pc_dve": 0}

    prev_pe_done = [0, 0]
    bnd = prep_bnd
    gidx = 0
    xl = {}
    for blk in range(NBLK):
        if blk == 0:
            W("sp", "sDVE", em.val("sDVE"))
            W("sp", "sPOOL", v_prep_lanes["sPOOL"])
            for c in range(NCH):
                dma(XS[:, DM * c:DM * c + DM],
                    bass.AP(xin.tensor, (c * CH) * DM, [[DM, 128], [1, DM]]),
                    inc="sXY%d" % c)
                xl[c] = ("sXY%d" % c, em.val("sXY%d" % c))
        for sidx in range(DBG_STAGES):
            par = gidx % 2
            wsem = "sWL%d" % par
            if gidx in (1, 3):
                W("sp", "sDMA2", v_prep[0])
                W("sp", "sGP", v_prep[1])
            else:
                st_v, gp_v = img_ready[sidx]
                W("sp", "sDMA2", st_v)
                if gp_v:
                    W("sp", "sGP", gp_v)
            for sem_n, v in rows_gate[sidx]:
                W("pe", sem_n, v)
            if blk == 0:
                W("pe", "sPOOL", v_prep_lanes["sPOOL"])
            if prev_pe_done[par] > 0:
                W("sp", "sPE", prev_pe_done[par])
            wl = dma(WB[par][:], simg[sidx][:], inc=wsem)
            if sidx == 0:
                bnd = stage_attn_pipelined(par, (wsem, wl), xl, bnd)
            elif sidx in (1, 2):
                bnd = stage_ffn_pipelined(blk, par, (wsem, wl), sidx - 1, False, bnd)
            else:
                last = (sidx == DBG_STAGES - 1)
                xln = {}
                bnd = stage_ffn_pipelined(blk, par, (wsem, wl), sidx - 3, True, bnd,
                                          last_stage=last, xl_next=xln)
                if last:
                    xl = xln
            prev_pe_done[par] = em.val("sPE")
            gidx += 1

    finals = [("sXY%d" % c, em.val("sXY%d" % c)) for c in range(NCH)]
    for _s, _v in finals:
        em.ops["sp"].append(lambda e, _s=_s, _v=_v: e.wait_ge(em.sems[_s], _v))

    with nc.Block() as block:
        block.sync(lambda e: ([f(e) for f in em.ops["sp"]], None)[1])
        block.tensor(lambda e: ([f(e) for f in em.ops["pe"]], None)[1])
        block.vector(lambda e: ([f(e) for f in em.ops["dve"]], None)[1])
        block.scalar(lambda e: ([f(e) for f in em.ops["act"]], None)[1])
        block.gpsimd(lambda e: ([f(e) for f in em.ops["pool"]], None)[1])

    for c in reversed(ctxs):
        c.__exit__(None, None, None)
    return nc


def _prep_host(inputs):
    pq, pv, pwo = _perms()
    f32 = np.float32
    Wq = np.asarray(inputs["L0_attn_Wq"], f32)
    Wk = np.asarray(inputs["L0_attn_Wk"], f32)
    Wv = np.asarray(inputs["L0_attn_Wv"], f32)
    Wo = np.asarray(inputs["L0_attn_Wo"], f32)
    feed = {
        "wqt": np.ascontiguousarray(Wq[pq, :].T),
        "wkt": np.ascontiguousarray(Wk[pq, :].T),
        "wvt": np.ascontiguousarray(Wv[pv, :].T),
        "wot": np.ascontiguousarray(Wo[:, pwo].T),
        "w1t": np.ascontiguousarray(np.asarray(inputs["L0_attn_W1"], f32).T),
        "w2t": np.ascontiguousarray(np.asarray(inputs["L0_attn_W2"], f32).T),
        "q1t": np.ascontiguousarray(np.asarray(inputs["L0_ffn_w1"], f32).T),
        "q2t": np.ascontiguousarray(np.asarray(inputs["L0_ffn_w2"], f32).T),
        "vb1": np.ascontiguousarray(np.asarray(inputs["L0_attn_b1"], f32).reshape(1, FF)),
        "vfb1": np.ascontiguousarray(np.asarray(inputs["L0_ffn_b1"], f32).reshape(1, FF)),
    }
    vecm = np.zeros((16, DM), f32)
    vecm[0] = np.asarray(inputs["L0_attn_ln1_g"], f32)
    vecm[1] = np.asarray(inputs["L0_attn_ln1_b"], f32)
    vecm[2] = np.asarray(inputs["L0_attn_ln2_g"], f32)
    vecm[3] = np.asarray(inputs["L0_attn_ln2_b"], f32)
    vecm[4] = np.asarray(inputs["L0_attn_bo"], f32)
    vecm[5] = np.asarray(inputs["L0_attn_b2"], f32)
    vecm[6] = np.asarray(inputs["L0_ffn_b2"], f32)
    vecm[7, 0:32] = np.asarray(inputs["L0_attn_lq1"], f32)
    vecm[7, 32:64] = np.asarray(inputs["L0_attn_lq2"], f32)
    vecm[7, 64:96] = np.asarray(inputs["L0_attn_lk1"], f32)
    vecm[7, 96:128] = np.asarray(inputs["L0_attn_lk2"], f32)
    feed["vec"] = vecm
    return feed


def kernel(**inputs):
    x = np.asarray(inputs["L0_attn_x"], np.float32).reshape(-1, DM)
    feed = _prep_host(inputs)
    nc = build_program()
    in_maps = []
    for c in range(N_CORES):
        m = dict(feed)
        m["x"] = np.ascontiguousarray(x[c * T_CORE:(c + 1) * T_CORE])
        in_maps.append(m)
    res = run_bass_kernel_spmd(nc, in_maps, core_ids=list(range(N_CORES)))
    y = np.concatenate([np.asarray(res.results[c]["y"]) for c in range(N_CORES)], axis=0)
    return y.reshape(4, 8192, DM)


# version 10
# speedup vs baseline: 1.2536x; 1.0787x over previous
import numpy as np
import ml_dtypes
import concourse.bass as bass
import concourse.mybir as mybir
from concourse.bass_utils import run_bass_kernel_spmd

F32 = mybir.dt.float32
BF16 = mybir.dt.bfloat16
AF = mybir.ActivationFunctionType
ALU = mybir.AluOpType
AX = mybir.AxisListType

N_CORES = 8
DBG_STAGES = 5
T_CORE = 4096
BLK = 512
CH = 128
NBLK = T_CORE // BLK
NCH = BLK // CH
DM = 1024
FF = 4096
EPS = 1e-5
RM = float(np.float32(1.5 * 2.0 ** 23))
LAMBDA_INIT = 0.8 - 0.6 * float(np.exp(-0.3 * 1.0))
QSCALE = float(32 ** -0.5)


def _perms():
    pq = np.empty(1024, dtype=np.int64)
    for q in range(32):
        s, h = q // 16, q % 16
        for d in range(32):
            pq[q * 32 + d] = h * 64 + d * 2 + s
    pv = np.empty(1024, dtype=np.int64)
    for dd in range(64):
        for h in range(16):
            pv[dd * 16 + h] = h * 64 + dd
    pwo = np.empty(1024, dtype=np.int64)
    for dd in range(64):
        for a in range(16):
            pwo[dd * 16 + a] = a * 64 + dd
    return pq, pv, pwo


class Em:
    def __init__(self, nc):
        self.nc = nc
        self.ops = {k: [] for k in ("pe", "dve", "act", "pool", "sp")}
        self.sems = {}
        self.cnt = {}
        self.hwm = {k: {} for k in self.ops}

    def sem(self, name):
        self.sems[name] = self.nc.semaphore(name).__enter__()
        self.cnt[name] = 0

    def inc(self, name, by=1):
        self.cnt[name] += by
        return self.cnt[name]

    def val(self, name):
        return self.cnt[name]

    def wait(self, eng, name, v):
        if v <= 0 or self.hwm[eng].get(name, 0) >= v:
            return
        self.hwm[eng][name] = v
        s = self.sems[name]
        self.ops[eng].append(lambda e, s=s, v=v: e.wait_ge(s, v))

    def op(self, eng, fn, inc=None, by=1):
        if inc is None:
            self.ops[eng].append(fn)
            return None
        if eng in ("dve", "act", "pool"):
            self.wait(eng, inc, self.val(inc))
        v = self.inc(inc, by)
        s = self.sems[inc]
        self.ops[eng].append(lambda e, fn=fn, s=s, by=by: fn(e).then_inc(s, by))
        return v

    def pesync(self):
        self.wait("pe", "sPE", self.val("sPE"))


def build_program():
    nc = bass.Bass("TRN2", target_bir_lowering=False)

    xin = nc.dram_tensor("x", [T_CORE, DM], F32, kind="ExternalInput").ap()
    yout = nc.dram_tensor("y", [T_CORE, DM], F32, kind="ExternalOutput").ap()
    wqt = nc.dram_tensor("wqt", [DM, DM], F32, kind="ExternalInput").ap()
    wkt = nc.dram_tensor("wkt", [DM, DM], F32, kind="ExternalInput").ap()
    wvt = nc.dram_tensor("wvt", [DM, DM], F32, kind="ExternalInput").ap()
    wot = nc.dram_tensor("wot", [DM, DM], F32, kind="ExternalInput").ap()
    w1t = nc.dram_tensor("w1t", [DM, FF], F32, kind="ExternalInput").ap()
    w2t = nc.dram_tensor("w2t", [FF, DM], F32, kind="ExternalInput").ap()
    q1t = nc.dram_tensor("q1t", [DM, FF], F32, kind="ExternalInput").ap()
    q2t = nc.dram_tensor("q2t", [FF, DM], F32, kind="ExternalInput").ap()
    vec = nc.dram_tensor("vec", [16, DM], F32, kind="ExternalInput").ap()
    vb1 = nc.dram_tensor("vb1", [1, FF], F32, kind="ExternalInput").ap()
    vfb1 = nc.dram_tensor("vfb1", [1, FF], F32, kind="ExternalInput").ap()
    simg = [nc.dram_tensor(f"simg{i}", [128, 32768], BF16, kind="Internal").ap()
            for i in range(5)]

    em = Em(nc)
    for s in ("sDMA", "sPE", "sDVE", "sACT", "sPOOL", "sDMA2", "sWL0", "sWL1", "sXY0", "sXY1", "sXY2", "sXY3", "sGP", "sPF0", "sPF1", "sPF2", "sPF3"):
        em.sem(s)

    ctxs = []

    def sb(name, shape, dtype):
        t = nc.sbuf_tensor(name, shape, dtype)
        h = t.__enter__()
        ctxs.append(t)
        return h

    def psalloc(name, shape, dtype):
        t = nc.psum_tensor(name, shape, dtype)
        h = t.__enter__()
        ctxs.append(t)
        return h

    XS = sb("xs", [128, NCH * DM], F32)
    WB = [sb("wb0", [128, 32768], BF16), sb("wb1", [128, 32768], BF16)]
    UB = sb("ub", [128, DM], BF16)
    UB2 = sb("ub2", [128, DM], BF16)
    HT = sb("ht", [128, 1024], BF16)
    HTB = sb("htb", [128, 4096], BF16)
    T1Q = sb("t1q", [128, 1024], BF16)
    T1K = sb("t1k", [128, 1024], BF16)
    T1VP = sb("t1vp", [128, 2048], BF16)
    ATTN = sb("attn", [128, 1024], BF16)
    IN2 = sb("in2", [128, 1024], BF16)
    AT2T = sb("at2t", [128, 1024], BF16)
    SQ = sb("sq", [128, 2048], F32)
    XR = sb("xr", [128, 2048], BF16)
    T1N = sb("t1n", [128, 1024], BF16)
    HTOK = sb("htok", [128, 2048], BF16)
    QT = sb("qt", [128, 2048], F32)
    BROWS = sb("brows", [128, 4096], BF16)
    IDN = sb("idn", [128, 128], BF16)
    ONESB = sb("onesb", [128, 128], BF16)
    CAB = sb("cab", [128, 32], BF16)
    SM = sb("sm", [128, 16], F32)
    SM2 = sb("sm2", [128, 12], F32)
    SMX = sb("smx", [128, 32], F32)
    GSL = sb("gsl", [128, 24], F32)
    GSL2 = sb("gsl2", [128, 8], F32)
    GSLQ = sb("gslq", [128, 8], F32)
    GSB = sb("gsb", [128, 16], BF16)
    LAM = sb("lam", [128, 1], F32)
    SMALL = sb("small", [1, 8], F32)
    SMALLB = sb("smallb", [1, 2], BF16)
    CST = sb("cst", [128, 8], F32)

    EXPS = QT[:, 0:1024]
    TRASH = QT[:, 1024:2048]
    SSUB = SQ[:, 0:1024]
    T1O = HTOK
    VR = XS[:, 0:1024]
    VTV = HTB[:, 0:2048]
    VTQ = HTB[:, 2048:3072]
    VTK = HTB[:, 3072:4096]
    NEGMAX = SMX
    RINV = SMX
    FACT = SMX

    ROWMAP = {
        ("CQ", 0): (0, 0), ("CQ", 1): (0, 512),
        ("CK", 0): (32, 0), ("CK", 1): (32, 512),
        ("CV", 0): (64, 0), ("CV", 1): (64, 512),
        ("CO", 0): (96, 0), ("CO", 1): (96, 512),
    }
    for c in range(8):
        ROWMAP[("Q1R", c)] = (0, 1024 + 512 * c) if c < 4 else (64, 1024 + 512 * (c - 4))
        ROWMAP[("C1", c)] = (32, 1024 + 512 * c) if c < 4 else (96, 1024 + 512 * (c - 4))
    for c in range(2):
        ROWMAP[("C2", c)] = (0, 3072 + 512 * c)
        ROWMAP[("Q2R", c)] = (32, 3072 + 512 * c)

    def brow(name, c):
        p, col = ROWMAP[(name, c)]
        return BROWS[p:p + 1, col:col + 512], p

    PA = psalloc("pa", [128, 2048], F32)
    PB = psalloc("pb", [128, 1024], F32)
    PC = psalloc("pc", [128, 1024], F32)
    PTTA = PA[:, 0:512].bitcast(BF16)
    PTTB = PB[:, 0:512].bitcast(BF16)
    PTTB2 = PB[:, 512:1024].bitcast(BF16)
    PTTD = PC[:, 512:1024].bitcast(BF16)
    TRASHB = QT[:, 1024:1536].bitcast(BF16)
    PTTC = PC[:, 0:512].bitcast(BF16)


    W = em.wait

    def dma(dst, src, inc="sDMA"):
        return em.op("sp", lambda e: e.dma_start(dst, src), inc=inc, by=16)

    for i, cv in enumerate([RM, -RM, 16.0, -0.5, 64.0, 1.0 / 64.0, -RM / 64.0]):
        em.op("pool", lambda e, i=i, cv=cv: e.memset(CST[:, i:i + 1], float(cv)), inc="sPOOL")
        nc.const_aps.register(F32, float(cv), CST[:, i:i + 1])
    em.op("pool", lambda e: e.memset(ONESB[:], 1.0), inc="sPOOL")
    W("pool", "sPOOL", em.val("sPOOL"))
    em.op("pool", lambda e: e.affine_select(
        IDN[:], ONESB[:], [[-1, 128]], ALU.is_equal, 0.0,
        base=0, channel_multiplier=1), inc="sPOOL")
    em.op("pool", lambda e: e.memset(T1VP[:], 0.0), inc="sPOOL")
    em.op("pool", lambda e: e.memset(IN2[:], 0.0), inc="sPOOL")
    em.op("pool", lambda e: e.memset(XR[:], 0.0), inc="sPOOL")
    em.op("pool", lambda e: e.memset(CAB[:], 0.0), inc="sPOOL")
    em.op("pool", lambda e: e.memset(BROWS[:], 0.0), inc="sPOOL")
    W("pool", "sPOOL", em.val("sPOOL"))
    for c in range(4):
        em.op("pool", lambda e, c=c: e.affine_select(
            CAB[32 * c:32 * c + 32, 0:16], ONESB[32 * c:32 * c + 32, 0:16],
            [[-1, 16]], ALU.is_equal, 0.0, base=0, channel_multiplier=1), inc="sPOOL")
    v_consts = em.val("sPOOL")

    for j in range(8):
        dma(GSL[:, j:j + 1], vec[0:1, 128 * j:128 * j + 128].transpose([1, 0]))
    for j in range(8):
        dma(GSL[:, 8 + j:9 + j], vec[2:3, 128 * j:128 * j + 128].transpose([1, 0]))
    for j in range(8):
        dma(GSL[:, 16 + j:17 + j], vec[1:2, 128 * j:128 * j + 128].transpose([1, 0]))
    for j in range(8):
        dma(GSL2[:, j:j + 1], vec[3:4, 128 * j:128 * j + 128].transpose([1, 0]))
    v_gsl = em.val("sDMA")
    dma(VR[0:1, :], vec[4:5, :])
    dma(VR[32:33, :], vec[5:6, :])
    dma(VR[64:65, :], vec[6:7, :])
    dma(VR[96:97, 0:128], vec[7:8, 0:128])
    v_vr = em.val("sDMA")


    W("dve", "sDMA", v_vr)
    em.op("dve", lambda e: e.tensor_mul(VR[96:97, 256:288], VR[96:97, 0:32], VR[96:97, 64:96]), inc="sDVE")
    em.op("dve", lambda e: e.tensor_mul(VR[96:97, 288:320], VR[96:97, 32:64], VR[96:97, 96:128]), inc="sDVE")
    em.op("dve", lambda e: e.reduce_sum(VR[96:97, 320:321], VR[96:97, 256:288].rearrange("p (a b) -> p a b", a=1), axis=AX.X), inc="sDVE")
    em.op("dve", lambda e: e.reduce_sum(VR[96:97, 321:322], VR[96:97, 288:320].rearrange("p (a b) -> p a b", a=1), axis=AX.X), inc="sDVE")
    W("act", "sDVE", em.val("sDVE"))
    em.op("act", lambda e: e.activation(VR[96:97, 322:323], VR[96:97, 320:321], AF.Exp), inc="sACT")
    em.op("act", lambda e: e.activation(VR[96:97, 323:324], VR[96:97, 321:322], AF.Exp), inc="sACT")
    W("dve", "sACT", em.val("sACT"))
    em.op("dve", lambda e: e.tensor_sub(VR[96:97, 324:325], VR[96:97, 323:324], VR[96:97, 322:323]), inc="sDVE")
    em.op("dve", lambda e: e.tensor_scalar(SMALL[0:1, 0:1], VR[96:97, 324:325], float(LAMBDA_INIT), None, ALU.subtract), inc="sDVE")
    em.op("dve", lambda e: e.tensor_copy(SMALLB[0:1, 0:1], SMALL[0:1, 0:1]), inc="sDVE")
    W("pe", "sDVE", em.val("sDVE"))
    W("pe", "sPOOL", v_consts)
    em.op("pe", lambda e: e.matmul(PA[:, 0:1], ONESB[0:1, 0:128], SMALLB[0:1, 0:1],
                                   start=True, stop=True), inc="sPE")
    W("dve", "sPE", em.val("sPE"))
    em.op("dve", lambda e: e.tensor_copy(LAM[:, 0:1], PA[:, 0:1]), inc="sDVE")
    W("pool", "sDVE", em.val("sDVE"))
    W("pool", "sPOOL", v_consts)
    for c in range(4):
        em.op("pool", lambda e, c=c: e.affine_select(
            CAB[32 * c:32 * c + 32, 16:32],
            LAM[32 * c:32 * c + 32, 0:1].broadcast_to([32, 16]),
            [[-1, 16]], ALU.is_equal, 0.0, base=-16, channel_multiplier=1), inc="sPOOL")
    v_cab = em.val("sPOOL")

    def gdma(dst, src):
        return em.op("pool", lambda e: e.dma_start(dst, src), inc="sGP", by=16)

    for j in range(8):
        gdma(simg[0][:, 24576 + 1024 * j:24576 + 1024 * j + 1024], wot[128 * j:128 * j + 128, :])
    gp_img0 = em.val("sGP")
    gp_img = {1: 0, 2: 0}

    img_ready = {}
    rows_gate = {}
    HTBF = HTB[:].bitcast(F32)
    ring = [SQ[:, 0:2048], QT[:, 0:2048], HTBF[:, 0:1024], HTBF[:, 1024:2048]]
    ring_w = [2048, 2048, 1024, 1024]
    rs = {"i": 0, "done": {}}

    def _staged(src_ap, width, consume, row):
        while True:
            s = rs["i"] % 4
            rs["i"] += 1
            if ring_w[s] >= width:
                break
        psem = "sPF%d" % s
        buf = ring[s][0:1, 0:width] if row else ring[s][:, 0:width]
        if s in rs["done"]:
            for sem_n, v in rs["done"][s]:
                W("sp", sem_n, v)
        vd = dma(buf, src_ap, inc=psem)
        rs["done"][s] = consume(buf, vd, psem)

    def staged(src_ap, width, consume):
        _staged(src_ap, width, consume, False)

    def staged_row(src_ap, width, consume):
        _staged(src_ap, width, consume, True)

    def staged_q(src_ap, consume):
        while True:
            s = rs["i"] % 4
            rs["i"] += 1
            if ring_w[s] >= 2048:
                break
        psem = "sPF%d" % s
        slot = ring[s]
        if s in rs["done"]:
            for sem_n, v in rs["done"][s]:
                W("sp", sem_n, v)
        vd = dma(slot[:, 0:1024], src_ap, inc=psem)
        rs["done"][s] = consume(slot, vd, psem)

    def prep():
        W("dve", "sDMA", v_vr)
        W("act", "sDMA", v_vr)
        W("pe", "sDMA", v_vr)
        W("pe", "sPOOL", v_consts)
        em.op("dve", lambda e: e.tensor_scalar(GSLQ[:], GSL[:, 0:8], QSCALE, None, ALU.mult), inc="sDVE")
        v_gslq = em.val("sDVE")

        pa_busy = {"v": em.val("sDVE")}
        for mi, wt in enumerate([wqt, wkt, wvt]):
            for j in range(8):
                def consume(buf, vd, psem, mi=mi, j=j):
                    col = mi * 8192 + j * 1024
                    scap = GSLQ[:, j:j + 1] if mi == 0 else GSL[:, j:j + 1]
                    W("act", psem, vd)
                    if mi == 0:
                        W("act", "sDVE", v_gslq)
                    em.op("act", lambda e, col=col, scap=scap: e.activation(
                        WB[1][:, col:col + 1024], buf, AF.Copy, scale=scap), inc="sACT")
                    va = em.val("sACT")
                    W("pe", psem, vd)
                    if j == 0:
                        W("pe", "sDVE", pa_busy["v"])
                    for hh in range(2):
                        em.op("pe", lambda e, j=j, hh=hh: e.matmul(
                            PA[0:1, 512 * hh:512 * hh + 512], GSL[:, 16 + j:17 + j],
                            buf[:, 512 * hh:512 * hh + 512],
                            start=(j == 0), stop=(j == 7)), inc="sPE")
                    return [("sACT", va), ("sPE", em.val("sPE"))]
                staged(wt[128 * j:128 * j + 128, :], 1024, consume)
            W("dve", "sPE", em.val("sPE"))
            nm = ["CQ", "CK", "CV"][mi]
            sc = QSCALE if mi == 0 else 1.0
            for c in range(2):
                r, _ = brow(nm, c)
                em.op("dve", lambda e, r=r, c=c, sc=sc: e.tensor_scalar(
                    r, PA[0:1, 512 * c:512 * c + 512], float(sc), None, ALU.mult), inc="sDVE")
            pa_busy["v"] = em.val("sDVE")
        W("pool", "sDMA", v_vr)
        for c in range(2):
            r, _ = brow("CO", c)
            em.op("pool", lambda e, r=r, c=c: e.tensor_copy(r, VR[0:1, 512 * c:512 * c + 512]), inc="sPOOL")
            r2, _ = brow("C2", c)
            em.op("pool", lambda e, r=r2, c=c: e.tensor_copy(r, VR[32:33, 512 * c:512 * c + 512]), inc="sPOOL")
        v_vr_done = em.val("sPOOL")
        def qrow_consume(buf, vd, psem, names):
            W("pool", psem, vd)
            em.op("pool", lambda e: e.tensor_scalar(buf, buf, 64.0, RM, ALU.mult, ALU.add), inc="sPOOL")
            em.op("pool", lambda e: e.tensor_scalar(buf, buf, RM, -128.0, ALU.subtract, ALU.max), inc="sPOOL")
            for c, (nm, ch) in enumerate(names):
                r, _ = brow(nm, ch)
                em.op("pool", lambda e, r=r, c=c: e.tensor_scalar(
                    r, buf[0:1, 512 * c:512 * c + 512], 127.0, 1.0 / 64.0, ALU.min, ALU.mult), inc="sPOOL")
            return [("sPOOL", em.val("sPOOL"))]
        staged_row(vec[6:7, :], 1024, lambda buf, vd, psem: qrow_consume(buf, vd, psem, [("Q2R", 0), ("Q2R", 1)]))
        for hh in range(2):
            staged_row(vfb1[0:1, 2048 * hh:2048 * hh + 2048], 2048,
                       lambda buf, vd, psem, hh=hh: qrow_consume(buf, vd, psem, [("Q1R", hh * 4 + c) for c in range(4)]))
        rows_gate[0] = [("sDVE", pa_busy["v"]), ("sPOOL", v_vr_done)]
        rows_gate[3] = rows_gate[4] = [("sPOOL", em.val("sPOOL")), ("sPOOL", v_vr_done)]
        W("sp", "sACT", em.val("sACT"))
        W("sp", "sDMA2", em.val("sDMA2"))
        vstore = dma(simg[0][:, 0:24576], WB[1][:, 0:24576], inc="sDMA2")
        img_ready[0] = (vstore, gp_img0)

        for half in range(2):
            W("act", "sDMA2", em.val("sDMA2"))
            for j in range(8):
                def consume(buf, vd, psem, j=j, half=half):
                    col = j * 2048
                    W("act", psem, vd)
                    em.op("act", lambda e, col=col, j=j: e.activation(
                        WB[1][:, col:col + 2048], buf, AF.Copy, scale=GSL[:, 8 + j:9 + j]), inc="sACT")
                    va = em.val("sACT")
                    W("pe", psem, vd)
                    if j == 0:
                        W("pe", "sDVE", pa_busy["v"])
                    for qo in range(4):
                        em.op("pe", lambda e, j=j, qo=qo: e.matmul(
                            PA[0:1, 512 * qo:512 * qo + 512], GSL2[:, j:j + 1],
                            buf[:, 512 * qo:512 * qo + 512],
                            start=(j == 0), stop=(j == 7)), inc="sPE")
                    return [("sACT", va), ("sPE", em.val("sPE"))]
                staged(w1t[128 * j:128 * j + 128, 2048 * half:2048 * half + 2048], 2048, consume)
            for j in range(16):
                lane = ("act", "dve", "pool")[j % 3]
                def consume_w2(buf, vd, psem, j=j, lane=lane):
                    lsem = {"act": "sACT", "dve": "sDVE", "pool": "sPOOL"}[lane]
                    W(lane, psem, vd)
                    dst = WB[1][:, 16384 + 1024 * j:16384 + 1024 * j + 1024]
                    if lane == "act":
                        em.op("act", lambda e: e.copy(dst, buf), inc="sACT")
                    else:
                        em.op(lane, lambda e: e.tensor_copy(dst, buf), inc=lsem)
                    return [(lsem, em.val(lsem))]
                staged(w2t[2048 * half + 128 * j:2048 * half + 128 * j + 128, :], 1024, consume_w2)
            def consume_b(buf, vd, psem, half=half):
                W("dve", psem, vd)
                W("dve", "sPE", em.val("sPE"))
                for qo in range(4):
                    r, _ = brow("C1", half * 4 + qo)
                    em.op("dve", lambda e, r=r, qo=qo, buf=buf: e.tensor_tensor(
                        r, buf[0:1, 512 * qo:512 * qo + 512],
                        PA[0:1, 512 * qo:512 * qo + 512], ALU.add), inc="sDVE")
                return [("sDVE", em.val("sDVE"))]
            staged_row(vb1[0:1, 2048 * half:2048 * half + 2048], 2048, consume_b)
            pa_busy["v"] = em.val("sDVE")
            W("sp", "sACT", em.val("sACT"))
            W("sp", "sDVE", em.val("sDVE"))
            W("sp", "sPOOL", em.val("sPOOL"))
            W("sp", "sDMA2", em.val("sDMA2"))
            vstore = dma(simg[1 + half][:], WB[1][:], inc="sDMA2")
            img_ready[1 + half] = (vstore, gp_img[1 + half])
        rows_gate[1] = rows_gate[2] = [("sDVE", pa_busy["v"])]

        def quant_tile(src_ap, col):
            def consume(slot, vd, psem, col=col):
                a, b = slot[:, 0:1024], slot[:, 1024:2048]
                W("act", psem, vd)
                em.op("act", lambda e, a=a, b=b: e.activation(
                    b, a, AF.Identity, bias=RM, scale=64.0), inc="sACT")
                va = em.val("sACT")
                W("dve", "sACT", va)
                em.op("dve", lambda e, b=b: e.tensor_scalar(
                    b, b, RM - 128.0, RM + 127.0, ALU.max, ALU.min), inc="sDVE")
                vd2 = em.val("sDVE")
                W("act", "sDVE", vd2)
                em.op("act", lambda e, b=b, col=col: e.activation(
                    WB[1][:, col:col + 1024], b, AF.Identity, bias=-RM / 64.0, scale=1.0 / 64.0), inc="sACT")
                return [("sACT", em.val("sACT")), ("sDVE", vd2)]
            staged_q(src_ap, consume)
        for half in range(2):
            W("dve", "sDMA2", em.val("sDMA2"))
            W("pool", "sDMA2", em.val("sDMA2"))
            W("act", "sDMA2", em.val("sDMA2"))
            for j in range(8):
                for k in range(2):
                    quant_tile(q1t[128 * j:128 * j + 128, 2048 * half + 1024 * k:2048 * half + 1024 * k + 1024],
                               j * 2048 + 1024 * k)
            for j in range(16):
                quant_tile(q2t[2048 * half + 128 * j:2048 * half + 128 * j + 128, :], 16384 + j * 1024)
            W("sp", "sDVE", em.val("sDVE"))
            W("sp", "sPOOL", em.val("sPOOL"))
            W("sp", "sACT", em.val("sACT"))
            W("sp", "sDMA2", em.val("sDMA2"))
            vstore = dma(simg[3 + half][:], WB[1][:], inc="sDMA2")
            img_ready[3 + half] = (vstore, 0)

    prep()
    v_prep = (em.val("sDMA2"), em.val("sGP"))
    prep_bnd = {"pa_dve": em.val("sDVE"), "pa_act": 0, "pb_dve": 0, "pc_dve": 0, "pc_act": 0}
    v_prep_lanes = {s: em.val(s) for s in ("sDVE", "sPOOL", "sACT", "sPE")}

    for eng in ("pe", "dve", "act"):
        W(eng, "sPOOL", v_cab)

    def ln_stats(xap, compound):
        em.op("dve", lambda e: e.reduce_sum(SM[:, 0:1], xap.rearrange("p (a b) -> p a b", a=1), axis=AX.X), inc="sDVE")
        em.op("act", lambda e: e.activation(TRASH, xap, AF.Square, accum_out=SM2[:, 0:1]), inc="sACT")
        W("dve", "sACT", em.val("sACT"))
        em.op("dve", lambda e: e.tensor_scalar(SM[:, 0:1], SM[:, 0:1], 1.0 / 1024.0, None, ALU.mult), inc="sDVE")
        em.op("dve", lambda e: e.tensor_mul(SM2[:, 1:2], SM[:, 0:1], SM[:, 0:1]), inc="sDVE")
        em.op("dve", lambda e: e.tensor_scalar(SM2[:, 2:3], SM2[:, 0:1], 1.0 / 1024.0, None, ALU.mult), inc="sDVE")
        em.op("dve", lambda e: e.tensor_sub(SM2[:, 3:4], SM2[:, 2:3], SM2[:, 1:2]), inc="sDVE")
        em.op("dve", lambda e: e.tensor_scalar(SM2[:, 4:5], SM2[:, 3:4], EPS, None, ALU.add), inc="sDVE")
        W("act", "sDVE", em.val("sDVE"))
        em.op("act", lambda e: e.activation(SM2[:, 5:6], SM2[:, 4:5], AF.Ln), inc="sACT")
        em.op("act", lambda e: e.activation(SM[:, 1:2], SM2[:, 5:6], AF.Exp, scale=-0.5), inc="sACT")
        if compound:
            em.op("dve", lambda e: e.reciprocal(SM2[:, 6:7], SM2[:, 4:5]), inc="sDVE")
            em.op("dve", lambda e: e.tensor_mul(SM2[:, 7:8], SM2[:, 3:4], SM2[:, 6:7]), inc="sDVE")
            em.op("dve", lambda e: e.tensor_scalar(SM2[:, 8:9], SM2[:, 7:8], EPS, None, ALU.add), inc="sDVE")
            W("act", "sDVE", em.val("sDVE"))
            em.op("act", lambda e: e.activation(SM2[:, 9:10], SM2[:, 8:9], AF.Ln), inc="sACT")
            W("dve", "sACT", em.val("sACT"))
            em.op("dve", lambda e: e.tensor_add(SM2[:, 10:11], SM2[:, 5:6], SM2[:, 9:10]), inc="sDVE")
            W("act", "sDVE", em.val("sDVE"))
            em.op("act", lambda e: e.activation(SM[:, 2:3], SM2[:, 10:11], AF.Exp, scale=-0.5), inc="sACT")
        return em.val("sACT")

    def transposes(tiles, dst_fn, ntiles, ptt, ptt2=None):
        em.pesync()
        evac_hwm = {0: 0, 1: 0}
        for j in range(ntiles):
            g = j // 4
            par = g % 2 if ptt2 is not None else 0
            buf = ptt2 if par == 1 else ptt
            if j % 4 == 0 and evac_hwm[par]:
                W("pe", "sACT", evac_hwm[par])
            em.op("pe", lambda e, j=j, buf=buf: e.transpose(
                buf[:, 128 * (j % 4):128 * (j % 4) + 128], tiles(j), IDN[:]), inc="sPE")
            if j % 4 == 3:
                W("act", "sPE", em.val("sPE"))
                em.op("act", lambda e, g=g, buf=buf: e.copy(dst_fn(g), buf[:, 0:512]), inc="sACT")
                evac_hwm[par] = em.val("sACT")
        return em.val("sACT")

    def bias_mm(ps_ap, name, chunk, start=False, stop=True):
        r, p = brow(name, chunk)
        em.op("pe", lambda e, r=r, p=p, ps_ap=ps_ap: e.matmul(
            ps_ap, ONESB[p:p + 1, 0:128], r, start=start, stop=stop,
            tile_position=(p, 0)), inc="sPE")

    def ubp(c):
        return UB if c % 2 == 0 else UB2

    AST = {}

    def attn_front(c, xl):
        st = AST.setdefault(c, {})
        xap = XS[:, DM * c:DM * c + DM]
        sx, vx = xl[c]
        W("dve", sx, vx)
        W("act", sx, vx)
        W("pool", sx, vx)
        if c - 3 in AST and "applied_p" in AST[c - 3]:
            W("act", "sPOOL", AST[c - 3]["applied_p"])
        if c - 2 in AST and "ht_pe" in AST[c - 2]:
            W("pool", "sPE", AST[c - 2]["ht_pe"])
        if c - 1 in AST and "ub" in AST[c - 1]:
            W("dve", "sPOOL", AST[c - 1]["ub"])
            W("act", "sPOOL", AST[c - 1]["ub"])
        va = ln_stats(xap, compound=True)
        W("pool", "sACT", va)
        W("pool", "sDVE", em.val("sDVE"))
        em.op("pool", lambda e: e.tensor_scalar(ubp(c)[:], xap, SM[:, 0:1], SM[:, 2:3], ALU.subtract, ALU.mult), inc="sPOOL")
        p2 = 8 + 2 * c
        em.op("pool", lambda e, p2=p2: e.tensor_copy(SM[:, p2:p2 + 2], SM[:, 0:2]), inc="sPOOL")
        st["ub"] = em.val("sPOOL")

    def attn_fq(c, wb, v_wload, prev, bnd=None):
        st = AST[c]
        if prev is not None:
            W("pe", "sDVE", AST[prev]["t1_dve"])
            W("pe", "sACT", max(AST[prev]["t1_act"], AST[prev]["exp"]))
        else:
            W("pe", "sDVE", max(bnd["pa_dve"], bnd["pb_dve"]))
            W("pe", "sACT", bnd["pa_act"])
        W("pe", "sPOOL", st["ub"])
        va = transposes(lambda j: ubp(c)[:, 128 * j:128 * j + 128],
                        lambda g: HT[:, 512 * g:512 * g + 512], 8, PTTB, PTTB2)
        st["ht_pe"] = em.val("sPE")
        W("pe", "sACT", va)
        W("pe", v_wload[0], v_wload[1])
        em.pesync()
        for j in range(8):
            for mi in range(3):
                dstps, dcol = (PA, 1024 * mi) if mi < 2 else (PB, 0)
                for half in range(2):
                    em.op("pe", lambda e, j=j, mi=mi, half=half, dstps=dstps, dcol=dcol: e.matmul(
                        dstps[:, dcol + 512 * half:dcol + 512 * half + 512],
                        HT[:, 128 * j:128 * j + 128],
                        WB[wb][:, 8192 * mi + 1024 * j + 512 * half:8192 * mi + 1024 * j + 512 * half + 512],
                        start=(j == 0), stop=False), inc="sPE")
        for mi, nm in enumerate(["CQ", "CK", "CV"]):
            dstps, dcol = (PA, 1024 * mi) if mi < 2 else (PB, 0)
            for half in range(2):
                bias_mm(dstps[:, dcol + 512 * half:dcol + 512 * half + 512], nm, half)
        st["qkv"] = em.val("sPE")

    def attn_evacs(c, prevst=None):
        st = AST[c]
        W("dve", "sPE", st["qkv"])
        W("act", "sPE", st["qkv"])
        if prevst is not None:
            W("act", "sDVE", prevst["vt"])
        em.op("dve", lambda e: e.tensor_copy(T1Q[:], PA[:, 0:1024]), inc="sDVE")
        em.op("act", lambda e: e.copy(T1K[:], PA[:, 1024:2048]), inc="sACT")
        st["t1_act"] = em.val("sACT")
        em.op("dve", lambda e: e.tensor_copy(
            T1VP[:].rearrange("p (d h) -> p d h", d=64)[:, :, 0:16], PB[:, 0:1024]), inc="sDVE")
        st["t1_dve"] = em.val("sDVE")

    def attn_vtrans(c, prevst=None):
        st = AST[c]
        if prevst is not None:
            W("dve", "sPE", prevst["o2"])
        W("dve", "sACT", st["t1_act"])
        em.op("dve", lambda e: e.transpose(VTQ[:], T1Q[:]), inc="sDVE")
        em.op("dve", lambda e: e.transpose(VTK[:], T1K[:]), inc="sDVE")
        em.op("dve", lambda e: e.transpose(VTV[:], T1VP[:]), inc="sDVE")
        st["vt"] = em.val("sDVE")

    def attn_scores(c):
        st = AST[c]
        W("pe", "sDVE", st["vt"])
        W("pe", "sACT", st["t1_act"])
        em.pesync()
        for i in range(128):
            r, tl = i % 4, i // 4
            em.op("pe", lambda e, r=r, tl=tl: e.matmul(
                PA[32 * r:32 * r + 32, 32 * tl:32 * tl + 32],
                VTQ[32 * r:32 * r + 32, :].rearrange("p (w cc) -> p w cc", cc=32)[:, :, tl],
                VTK[32 * r:32 * r + 32, :].rearrange("p (w cc) -> p w cc", cc=32)[:, :, tl],
                start=True, stop=True, tile_position=(32 * r, 32 * r)), inc="sPE")
        st["scores"] = em.val("sPE")

    def attn_softmax(c):
        st = AST[c]
        W("act", "sPE", st["scores"])
        em.op("act", lambda e: e.activation(EXPS, PA[:, 0:1024], AF.Exp), inc="sACT")
        st["exp"] = em.val("sACT")
        W("dve", "sACT", st["exp"])
        em.op("dve", lambda e: e.reduce_sum(
            RINV[:], EXPS.rearrange("p (t k) -> p t k", k=32), axis=AX.X), inc="sDVE")
        em.op("dve", lambda e: e.reciprocal(RINV[:], RINV[:]), inc="sDVE")
        st["rinv"] = em.val("sDVE")
        W("pool", "sDVE", st["rinv"])
        em.op("pool", lambda e: e.tensor_tensor(
            ATTN[:].rearrange("p (t k) -> p t k", k=32),
            EXPS.rearrange("p (t k) -> p t k", k=32),
            RINV[:].unsqueeze(2).broadcast_to([128, 32, 32]), ALU.mult), inc="sPOOL")
        st["attn"] = em.val("sPOOL")

    def attn_combine(c, prev, bnd=None):
        st = AST[c]
        W("pe", "sPOOL", st["attn"])
        W("pe", "sPOOL", v_cab)
        if prev is not None and "conn" in AST.get(prev, {}):
            W("pe", "sDVE", AST[prev]["conn"])
        elif bnd is not None:
            W("pe", "sDVE", bnd["pc_dve"])
            W("pe", "sACT", bnd["pc_act"])
        em.pesync()
        for r in range(4):
            for half in range(2):
                em.op("pe", lambda e, r=r, half=half: e.matmul(
                    PC[32 * r:32 * r + 16, 0:512],
                    CAB[32 * r:32 * r + 32, 16 * half:16 * half + 16],
                    ATTN[32 * r:32 * r + 32, :].rearrange("p (t k) -> p t k", k=32)[:, :, 16 * half:16 * half + 16],
                    start=(half == 0), stop=(half == 1), tile_position=(32 * r, 32 * r)), inc="sPE")
        st["comb"] = em.val("sPE")

    def attn_in2(c):
        st = AST[c]
        W("act", "sPE", st["comb"])
        for r in range(4):
            em.op("act", lambda e, r=r: e.copy(
                IN2[32 * r:32 * r + 16, :].rearrange("p (t k) -> p t k", k=32)[:, :, 0:16],
                PC[32 * r:32 * r + 16, 0:512]), inc="sACT")
        st["in2"] = em.val("sACT")
        W("dve", "sACT", st["in2"])
        em.op("dve", lambda e: e.transpose(AT2T[:], IN2[:]), inc="sDVE")
        st["at2t"] = em.val("sDVE")

    def attn_einsum2(c):
        st = AST[c]
        W("pe", "sDVE", st["at2t"])
        W("pe", "sACT", st["in2"])
        for h in range(2):
            em.pesync()
            for i in range(64):
                r, tj = i % 4, i // 4
                tl = 16 * h + tj
                em.op("pe", lambda e, r=r, tl=tl, tj=tj: e.matmul(
                    PC[32 * r:32 * r + 32, 64 * tj:64 * tj + 64],
                    AT2T[32 * r:32 * r + 32, 32 * tl:32 * tl + 32],
                    VTV[32 * r:32 * r + 32, :].rearrange("p (w cc) -> p w cc", cc=32)[:, :, tl],
                    start=True, stop=True, tile_position=(32 * r, 32 * r)), inc="sPE")
            W("act", "sPE", em.val("sPE"))
            dstb = ATTN[:] if h == 0 else TRASHB
            em.op("act", lambda e, dstb=dstb: e.copy(dstb, PC[:]), inc="sACT")
            em.op("act", lambda e, h=h: e.activation(
                SQ[:, 1024 * h:1024 * h + 1024], PC[:], AF.Square), inc="sACT")
            st["o2e%d" % h] = em.val("sACT")
            if h == 0:
                W("pe", "sACT", st["o2e0"])
        st["o2"] = em.val("sPE")

    def attn_rms(c):
        st = AST[c]
        W("dve", "sACT", st["o2e1"])
        em.op("dve", lambda e: e.reduce_sum(
            FACT[:], SQ[:].rearrange("p (t d) -> p t d", d=64), axis=AX.X), inc="sDVE")
        em.op("dve", lambda e: e.tensor_scalar(FACT[:], FACT[:], 1.0 / 64.0, EPS, ALU.mult, ALU.add), inc="sDVE")
        W("act", "sDVE", em.val("sDVE"))
        em.op("act", lambda e: e.activation(FACT[:], FACT[:], AF.Ln), inc="sACT")
        em.op("act", lambda e: e.activation(FACT[:], FACT[:], AF.Exp, scale=-0.5), inc="sACT")
        W("dve", "sACT", em.val("sACT"))
        em.op("dve", lambda e: e.tensor_scalar(FACT[:], FACT[:], float(1.0 - LAMBDA_INIT), None, ALU.mult), inc="sDVE")
        vsm = em.val("sDVE")
        em.op("dve", lambda e: e.tensor_tensor(
            bass.AP(XR[:].tensor, XR[:].offset, [[2048, 128], [1, 16], [32, 64]]),
            ATTN[:].rearrange("p (t d) -> p t d", d=64),
            FACT[:, 0:16].unsqueeze(2).broadcast_to([128, 16, 64]), ALU.mult), inc="sDVE")
        st["applied_d"] = em.val("sDVE")
        W("pool", "sDVE", vsm)
        W("pool", "sACT", st["o2e1"])
        em.op("pool", lambda e: e.tensor_tensor(
            bass.AP(XR[:].tensor, XR[:].offset + 16, [[2048, 128], [1, 16], [32, 64]]),
            TRASHB.rearrange("p (t d) -> p t d", d=64),
            FACT[:, 16:32].unsqueeze(2).broadcast_to([128, 16, 64]), ALU.mult), inc="sPOOL")
        st["applied_p"] = em.val("sPOOL")
        W("dve", "sPOOL", st["applied_p"])
        em.op("dve", lambda e: e.transpose(T1O[:], XR[:]), inc="sDVE")
        st["t1o"] = em.val("sDVE")
        W("pool", "sDVE", st["t1o"])
        em.op("pool", lambda e: e.tensor_copy(
            T1N[:].rearrange("p (d a) -> p d a", a=16),
            T1O[:].rearrange("p (d a) -> p d a", a=32)[:, :, 0:16]), inc="sPOOL")
        st["t1n"] = em.val("sPOOL")

    def attn_fmwo(c, wb):
        st = AST[c]
        W("pe", "sPOOL", st["t1n"])
        if c - 1 in AST and "fm" in AST[c - 1]:
            W("pe", "sACT", AST[c - 1]["fm"])
        va = transposes(lambda j: T1N[:, 128 * j:128 * j + 128],
                        lambda g: HT[:, 512 * g:512 * g + 512], 8, PTTC, PTTD)
        st["fm"] = va
        W("pe", "sACT", va)
        em.pesync()
        for j in range(8):
            for half in range(2):
                em.op("pe", lambda e, j=j, half=half: e.matmul(
                    PC[:, 512 * half:512 * half + 512],
                    HT[:, 128 * j:128 * j + 128],
                    WB[wb][:, 24576 + 1024 * j + 512 * half:24576 + 1024 * j + 512 * half + 512],
                    start=(j == 0), stop=False), inc="sPE")
        for half in range(2):
            bias_mm(PC[:, 512 * half:512 * half + 512], "CO", half)
        st["wo"] = em.val("sPE")

    def attn_conn(c):
        st = AST[c]
        xap = XS[:, DM * c:DM * c + DM]
        p2 = 8 + 2 * c
        em.op("pool", lambda e, p2=p2: e.tensor_scalar(
            SSUB[:], xap, SM[:, p2:p2 + 1], SM[:, p2 + 1:p2 + 2], ALU.subtract, ALU.mult), inc="sPOOL")
        vss = em.val("sPOOL")
        W("dve", "sPE", st["wo"])
        W("dve", "sPOOL", vss)
        em.op("dve", lambda e: e.tensor_tensor(xap, SSUB[:], PC[:], ALU.add), inc="sDVE")
        st["conn"] = em.val("sDVE")

    def stage_attn_pipelined(wb, v_wload, xl, bnd):
        AST.clear()
        attn_front(0, xl)
        attn_fq(0, wb, v_wload, None, bnd)
        attn_front(1, xl)
        attn_evacs(0)
        attn_vtrans(0)
        attn_scores(0)
        attn_softmax(0)
        attn_fq(1, wb, v_wload, 0)
        for k in range(NCH):
            if k + 2 < NCH:
                attn_front(k + 2, xl)
            attn_combine(k, k - 1 if k > 0 else None, bnd)
            attn_in2(k)
            if k + 1 < NCH:
                attn_evacs(k + 1, AST[k])
            attn_einsum2(k)
            if k + 1 < NCH:
                attn_vtrans(k + 1, AST[k])
                attn_scores(k + 1)
            attn_rms(k)
            if k + 1 < NCH:
                attn_softmax(k + 1)
            attn_fmwo(k, wb)
            attn_conn(k)
            if k + 2 < NCH:
                attn_fq(k + 2, wb, v_wload, k + 1)
        return {"pa_dve": AST[3]["t1_dve"], "pa_act": AST[3]["exp"],
                "pb_dve": AST[3]["t1_dve"], "pc_dve": AST[3]["conn"], "pc_act": 0}

    FST = {}

    def ffn_front(blk, c, half, quant):
        st = FST.setdefault(c, {})
        xap = XS[:, DM * c:DM * c + DM]
        if half == 1:
            return
        W("act", "sDVE", em.val("sDVE"))
        va = ln_stats(xap, compound=False)
        W("dve", "sACT", va)
        if not quant:
            em.op("dve", lambda e: e.tensor_scalar(
                ubp(c)[:], xap, SM[:, 0:1], SM[:, 1:2], ALU.subtract, ALU.mult), inc="sDVE")
        else:
            em.op("dve", lambda e: e.tensor_scalar(SM[:, 3:4], SM[:, 1:2], 16.0, None, ALU.mult), inc="sDVE")
            em.op("dve", lambda e: e.tensor_scalar(
                SSUB[:], xap, SM[:, 0:1], SM[:, 3:4], ALU.subtract, ALU.mult), inc="sDVE")
            em.op("dve", lambda e: e.tensor_scalar(
                SSUB[:], SSUB[:], RM, RM, ALU.add, ALU.subtract), inc="sDVE")
            em.op("dve", lambda e: e.tensor_scalar(
                SSUB[:], SSUB[:], -128.0, 127.0, ALU.max, ALU.min), inc="sDVE")
            em.op("dve", lambda e: e.tensor_scalar(
                ubp(c)[:], SSUB[:], 1.0 / 16.0, None, ALU.mult), inc="sDVE")
        st["ub"] = em.val("sDVE")

    def ffn_front_T(c, half, ptt_free=0):
        st = FST[c]
        if half == 1:
            return
        W("pe", "sDVE", st["ub"])
        if ptt_free:
            W("pe", "sACT", ptt_free)
        va = transposes(lambda j: ubp(c)[:, 128 * j:128 * j + 128],
                        lambda g, c=c: HTB[:, 1024 * c + 512 * g:1024 * c + 512 * g + 512], 8, PTTC, PTTD)
        st["stash"] = va
        st["stash_pe"] = em.val("sPE")

    def ffn_w1h(c, wb, v_wload, half, quant, hh, prev, bnd=None):
        st = FST[c]
        if hh == 0:
            W("pe", "sACT", st["stash"])
            W("pe", v_wload[0], v_wload[1])
        if prev is not None and ("evA%d" % hh) in FST.get(prev, {}):
            a, d = FST[prev]["evA%d" % hh]
            W("pe", "sACT", a)
            if d:
                W("pe", "sDVE", d)
        elif bnd is not None:
            W("pe", "sACT", bnd["pa_act"])
            W("pe", "sDVE", bnd["pa_dve"])
        em.pesync()
        base0 = 1024 * c
        for j in range(8):
            for qo in (2 * hh, 2 * hh + 1):
                em.op("pe", lambda e, j=j, qo=qo, base0=base0: e.matmul(
                    PA[:, 512 * qo:512 * qo + 512],
                    HTB[:, base0 + 128 * j:base0 + 128 * j + 128],
                    WB[wb][:, 2048 * j + 512 * qo:2048 * j + 512 * qo + 512],
                    start=(j == 0), stop=False), inc="sPE")
        nm1 = "Q1R" if quant else "C1"
        for qo in (2 * hh, 2 * hh + 1):
            bias_mm(PA[:, 512 * qo:512 * qo + 512], nm1, half * 4 + qo)
        st["w1h%d" % hh] = em.val("sPE")

    def ffn_evach(c, half, quant, hh):
        st = FST[c]
        W("act", "sPE", st["w1h%d" % hh])
        lo = 1024 * hh
        if not quant:
            em.op("act", lambda e, lo=lo: e.activation(
                HTOK[:, lo:lo + 1024], PA[:, lo:lo + 1024], AF.Relu), inc="sACT")
            st["evA%d" % hh] = (em.val("sACT"), 0)
            st["hid%d" % hh] = ("sACT", em.val("sACT"))
        else:
            if hh == 0:
                W("act", "sDVE", em.val("sDVE"))
            em.op("act", lambda e, lo=lo: e.activation(
                QT[:, lo:lo + 1024], PA[:, lo:lo + 1024], AF.Identity, bias=RM, scale=16.0), inc="sACT")
            em.op("act", lambda e, lo=lo: e.activation(
                SQ[:, lo:lo + 1024], QT[:, lo:lo + 1024], AF.Relu, bias=-RM, scale=1.0), inc="sACT")
            va = em.val("sACT")
            W("dve", "sACT", va)
            em.op("dve", lambda e, lo=lo: e.tensor_scalar(
                HTOK[:, lo:lo + 1024], SQ[:, lo:lo + 1024], 127.0, 1.0 / 16.0, ALU.min, ALU.mult), inc="sDVE")
            st["evA%d" % hh] = (va, em.val("sDVE"))
            st["hid%d" % hh] = ("sDVE", em.val("sDVE"))

    def ffn_hidTh(c, hh, ptt_free=0):
        st = FST[c]
        sem_n, v = st["hid%d" % hh]
        W("pe", sem_n, v)
        if ptt_free:
            W("pe", "sACT", ptt_free)
        va = transposes(lambda j, hh=hh: HTOK[:, 1024 * hh + 128 * j:1024 * hh + 128 * j + 128],
                        lambda g, hh=hh: (T1Q if hh == 0 else T1K)[:, 512 * g:512 * g + 512], 8, PTTC, PTTD)
        st["hidT%d" % hh] = va

    def ffn_w2h(c, wb, half, quant, hh, prev, pb_free=0):
        st = FST[c]
        W("pe", "sACT", st["hidT%d" % hh])
        if hh == 0:
            if prev is not None and "res" in FST.get(prev, {}):
                W("pe", "sDVE", FST[prev]["res"])
            elif pb_free:
                W("pe", "sDVE", pb_free)
        em.pesync()
        srcb = T1Q if hh == 0 else T1K
        for j in range(8):
            for cc in range(2):
                em.op("pe", lambda e, j=j, cc=cc, srcb=srcb, hh=hh: e.matmul(
                    PB[:, 512 * cc:512 * cc + 512],
                    srcb[:, 128 * j:128 * j + 128],
                    WB[wb][:, 16384 + 1024 * (8 * hh + j) + 512 * cc:16384 + 1024 * (8 * hh + j) + 512 * cc + 512],
                    start=(hh == 0 and j == 0), stop=(half == 0 and hh == 1 and j == 7)), inc="sPE")
        if hh == 1:
            if half == 1:
                nm2 = "Q2R" if quant else "C2"
                for cc in range(2):
                    bias_mm(PB[:, 512 * cc:512 * cc + 512], nm2, cc)
            st["hidT_evac"] = st["hidT1"]
            st["w2"] = em.val("sPE")

    def ffn_residual(blk, c, half, quant, last_stage=False, xl_next=None):
        st = FST[c]
        xap = XS[:, DM * c:DM * c + DM]
        W("dve", "sPE", st["w2"])
        em.op("dve", lambda e: e.tensor_tensor(xap, xap, PB[:], ALU.add), inc="sDVE")
        if half == 1 and not quant:
            W("sp", "sDVE", em.val("sDVE"))
            xr = dma(SQ[:, 0:1024],
                     bass.AP(xin.tensor, (blk * BLK + c * CH) * DM, [[DM, 128], [1, DM]]))
            W("dve", "sDMA", xr)
            em.op("dve", lambda e: e.tensor_tensor(xap, xap, SQ[:, 0:1024], ALU.add), inc="sDVE")
        st["res"] = em.val("sDVE")
        if last_stage:
            sx = "sXY%d" % c
            W("sp", "sDVE", st["res"])
            dma(bass.AP(yout.tensor, (blk * BLK + c * CH) * DM, [[DM, 128], [1, DM]]),
                XS[:, DM * c:DM * c + DM], inc=sx)
            if blk + 1 < NBLK:
                W("sp", sx, em.val(sx))
                dma(XS[:, DM * c:DM * c + DM],
                    bass.AP(xin.tensor, ((blk + 1) * BLK + c * CH) * DM, [[DM, 128], [1, DM]]),
                    inc=sx)
                xl_next[c] = (sx, em.val(sx))

    def stage_ffn_pipelined(blk, wb, v_wload, half, quant, bnd, last_stage=False, xl_next=None):
        if half == 0:
            FST.clear()
        ffn_front(blk, 0, half, quant)
        W("pe", "sDVE", bnd["pc_dve"])
        ffn_front_T(0, half, ptt_free=bnd["pc_act"])
        for cm in range(NCH):
            cf = cm + 1 if cm + 1 < NCH else None
            if cf is not None:
                ffn_front(blk, cf, half, quant)
            prev = cm - 1 if cm > 0 else None
            ffn_w1h(cm, wb, v_wload, half, quant, 0, prev, bnd)
            ffn_w1h(cm, wb, v_wload, half, quant, 1, prev, bnd)
            ffn_evach(cm, half, quant, 0)
            prev_hevac = FST.get(cm - 1, {}).get("hidT_evac", 0)
            if cm == 0:
                prev_hevac = FST[0].get("stash", 0) if half == 0 else bnd["pc_act"]
            if cf is not None:
                ffn_front_T(cf, half, ptt_free=prev_hevac)
            if half == 0 and cf is not None:
                pfree = FST[cf].get("stash", 0)
            else:
                pfree = prev_hevac
            ffn_hidTh(cm, 0, ptt_free=pfree)
            ffn_evach(cm, half, quant, 1)
            ffn_w2h(cm, wb, half, quant, 0, prev, pb_free=bnd["pb_dve"] if cm == 0 else 0)
            ffn_hidTh(cm, 1, ptt_free=FST[cm]["hidT0"])
            ffn_w2h(cm, wb, half, quant, 1, prev)
            ffn_residual(blk, cm, half, quant, last_stage=last_stage, xl_next=xl_next)
        return {"pa_act": max(FST[3]["evA0"][0], FST[3]["evA1"][0]),
                "pa_dve": max(FST[3]["evA0"][1], FST[3]["evA1"][1]),
                "pb_dve": FST[3]["res"], "pc_act": FST[3]["hidT_evac"], "pc_dve": 0}

    prev_pe_done = [0, 0]
    bnd = prep_bnd
    gidx = 0
    xl = {}
    for blk in range(NBLK):
        if blk == 0:
            W("sp", "sDVE", em.val("sDVE"))
            W("sp", "sPOOL", v_prep_lanes["sPOOL"])
            for c in range(NCH):
                dma(XS[:, DM * c:DM * c + DM],
                    bass.AP(xin.tensor, (c * CH) * DM, [[DM, 128], [1, DM]]),
                    inc="sXY%d" % c)
                xl[c] = ("sXY%d" % c, em.val("sXY%d" % c))
        for sidx in range(DBG_STAGES):
            par = gidx % 2
            wsem = "sWL%d" % par
            if gidx in (1, 3):
                W("sp", "sDMA2", v_prep[0])
                W("sp", "sGP", v_prep[1])
            else:
                st_v, gp_v = img_ready[sidx]
                W("sp", "sDMA2", st_v)
                if gp_v:
                    W("sp", "sGP", gp_v)
            for sem_n, v in rows_gate[sidx]:
                W("pe", sem_n, v)
            if blk == 0:
                W("pe", "sPOOL", v_prep_lanes["sPOOL"])
            if prev_pe_done[par] > 0:
                W("sp", "sPE", prev_pe_done[par])
            wl = dma(WB[par][:], simg[sidx][:], inc=wsem)
            if sidx == 0:
                bnd = stage_attn_pipelined(par, (wsem, wl), xl, bnd)
            elif sidx in (1, 2):
                bnd = stage_ffn_pipelined(blk, par, (wsem, wl), sidx - 1, False, bnd)
            else:
                last = (sidx == DBG_STAGES - 1)
                xln = {}
                bnd = stage_ffn_pipelined(blk, par, (wsem, wl), sidx - 3, True, bnd,
                                          last_stage=last, xl_next=xln)
                if last:
                    xl = xln
            prev_pe_done[par] = em.val("sPE")
            gidx += 1

    finals = [("sXY%d" % c, em.val("sXY%d" % c)) for c in range(NCH)]
    for _s, _v in finals:
        em.ops["sp"].append(lambda e, _s=_s, _v=_v: e.wait_ge(em.sems[_s], _v))

    with nc.Block() as block:
        block.sync(lambda e: ([f(e) for f in em.ops["sp"]], None)[1])
        block.tensor(lambda e: ([f(e) for f in em.ops["pe"]], None)[1])
        block.vector(lambda e: ([f(e) for f in em.ops["dve"]], None)[1])
        block.scalar(lambda e: ([f(e) for f in em.ops["act"]], None)[1])
        block.gpsimd(lambda e: ([f(e) for f in em.ops["pool"]], None)[1])

    for c in reversed(ctxs):
        c.__exit__(None, None, None)
    return nc


def _prep_host(inputs):
    pq, pv, pwo = _perms()
    f32 = np.float32
    Wq = np.asarray(inputs["L0_attn_Wq"], f32)
    Wk = np.asarray(inputs["L0_attn_Wk"], f32)
    Wv = np.asarray(inputs["L0_attn_Wv"], f32)
    Wo = np.asarray(inputs["L0_attn_Wo"], f32)
    feed = {
        "wqt": np.ascontiguousarray(Wq[pq, :].T),
        "wkt": np.ascontiguousarray(Wk[pq, :].T),
        "wvt": np.ascontiguousarray(Wv[pv, :].T),
        "wot": np.ascontiguousarray(Wo[:, pwo].T),
        "w1t": np.ascontiguousarray(np.asarray(inputs["L0_attn_W1"], f32).T),
        "w2t": np.ascontiguousarray(np.asarray(inputs["L0_attn_W2"], f32).T),
        "q1t": np.ascontiguousarray(np.asarray(inputs["L0_ffn_w1"], f32).T),
        "q2t": np.ascontiguousarray(np.asarray(inputs["L0_ffn_w2"], f32).T),
        "vb1": np.ascontiguousarray(np.asarray(inputs["L0_attn_b1"], f32).reshape(1, FF)),
        "vfb1": np.ascontiguousarray(np.asarray(inputs["L0_ffn_b1"], f32).reshape(1, FF)),
    }
    vecm = np.zeros((16, DM), f32)
    vecm[0] = np.asarray(inputs["L0_attn_ln1_g"], f32)
    vecm[1] = np.asarray(inputs["L0_attn_ln1_b"], f32)
    vecm[2] = np.asarray(inputs["L0_attn_ln2_g"], f32)
    vecm[3] = np.asarray(inputs["L0_attn_ln2_b"], f32)
    vecm[4] = np.asarray(inputs["L0_attn_bo"], f32)
    vecm[5] = np.asarray(inputs["L0_attn_b2"], f32)
    vecm[6] = np.asarray(inputs["L0_ffn_b2"], f32)
    vecm[7, 0:32] = np.asarray(inputs["L0_attn_lq1"], f32)
    vecm[7, 32:64] = np.asarray(inputs["L0_attn_lq2"], f32)
    vecm[7, 64:96] = np.asarray(inputs["L0_attn_lk1"], f32)
    vecm[7, 96:128] = np.asarray(inputs["L0_attn_lk2"], f32)
    feed["vec"] = vecm
    return feed


def kernel(**inputs):
    x = np.asarray(inputs["L0_attn_x"], np.float32).reshape(-1, DM)
    feed = _prep_host(inputs)
    nc = build_program()
    in_maps = []
    for c in range(N_CORES):
        m = dict(feed)
        m["x"] = np.ascontiguousarray(x[c * T_CORE:(c + 1) * T_CORE])
        in_maps.append(m)
    res = run_bass_kernel_spmd(nc, in_maps, core_ids=list(range(N_CORES)))
    y = np.concatenate([np.asarray(res.results[c]["y"]) for c in range(N_CORES)], axis=0)
    return y.reshape(4, 8192, DM)


# version 11
# speedup vs baseline: 1.3106x; 1.0454x over previous
import numpy as np
import ml_dtypes
import concourse.bass as bass
import concourse.mybir as mybir
from concourse.bass_utils import run_bass_kernel_spmd

F32 = mybir.dt.float32
BF16 = mybir.dt.bfloat16
AF = mybir.ActivationFunctionType
ALU = mybir.AluOpType
AX = mybir.AxisListType

N_CORES = 8
DBG_STAGES = 5
T_CORE = 4096
BLK = 512
CH = 128
NBLK = T_CORE // BLK
NCH = BLK // CH
DM = 1024
FF = 4096
EPS = 1e-5
RM = float(np.float32(1.5 * 2.0 ** 23))
LAMBDA_INIT = 0.8 - 0.6 * float(np.exp(-0.3 * 1.0))
QSCALE = float(32 ** -0.5)


def _perms():
    pq = np.empty(1024, dtype=np.int64)
    for q in range(32):
        s, h = q // 16, q % 16
        for d in range(32):
            pq[q * 32 + d] = h * 64 + d * 2 + s
    pv = np.empty(1024, dtype=np.int64)
    for dd in range(64):
        for h in range(16):
            pv[dd * 16 + h] = h * 64 + dd
    pwo = np.empty(1024, dtype=np.int64)
    for dd in range(64):
        for a in range(16):
            pwo[dd * 16 + a] = a * 64 + dd
    return pq, pv, pwo


class Em:
    def __init__(self, nc):
        self.nc = nc
        self.ops = {k: [] for k in ("pe", "dve", "act", "pool", "sp")}
        self.sems = {}
        self.cnt = {}
        self.hwm = {k: {} for k in self.ops}

    def sem(self, name):
        self.sems[name] = self.nc.semaphore(name).__enter__()
        self.cnt[name] = 0

    def inc(self, name, by=1):
        self.cnt[name] += by
        return self.cnt[name]

    def val(self, name):
        return self.cnt[name]

    def wait(self, eng, name, v):
        if v <= 0 or self.hwm[eng].get(name, 0) >= v:
            return
        self.hwm[eng][name] = v
        s = self.sems[name]
        self.ops[eng].append(lambda e, s=s, v=v: e.wait_ge(s, v))

    def op(self, eng, fn, inc=None, by=1):
        if inc is None:
            self.ops[eng].append(fn)
            return None
        if eng in ("dve", "act", "pool"):
            self.wait(eng, inc, self.val(inc))
        v = self.inc(inc, by)
        s = self.sems[inc]
        self.ops[eng].append(lambda e, fn=fn, s=s, by=by: fn(e).then_inc(s, by))
        return v

    def pesync(self):
        self.wait("pe", "sPE", self.val("sPE"))


def build_program():
    nc = bass.Bass("TRN2", target_bir_lowering=False)

    xin = nc.dram_tensor("x", [T_CORE, DM], F32, kind="ExternalInput").ap()
    yout = nc.dram_tensor("y", [T_CORE, DM], F32, kind="ExternalOutput").ap()
    wqt = nc.dram_tensor("wqt", [DM, DM], F32, kind="ExternalInput").ap()
    wkt = nc.dram_tensor("wkt", [DM, DM], F32, kind="ExternalInput").ap()
    wvt = nc.dram_tensor("wvt", [DM, DM], F32, kind="ExternalInput").ap()
    wot = nc.dram_tensor("wot", [DM, DM], F32, kind="ExternalInput").ap()
    w1t = nc.dram_tensor("w1t", [DM, FF], F32, kind="ExternalInput").ap()
    w2t = nc.dram_tensor("w2t", [FF, DM], F32, kind="ExternalInput").ap()
    q1t = nc.dram_tensor("q1t", [DM, FF], F32, kind="ExternalInput").ap()
    q2t = nc.dram_tensor("q2t", [FF, DM], F32, kind="ExternalInput").ap()
    vec = nc.dram_tensor("vec", [16, DM], F32, kind="ExternalInput").ap()
    vb1 = nc.dram_tensor("vb1", [1, FF], F32, kind="ExternalInput").ap()
    vfb1 = nc.dram_tensor("vfb1", [1, FF], F32, kind="ExternalInput").ap()
    simg = [nc.dram_tensor(f"simg{i}", [128, 32768], BF16, kind="Internal").ap()
            for i in range(5)]

    em = Em(nc)
    for s in ("sDMA", "sPE", "sDVE", "sACT", "sPOOL", "sDMA2", "sWL0", "sWL1", "sXY0", "sXY1", "sXY2", "sXY3", "sGP", "sPF0", "sPF1", "sPF2", "sPF3"):
        em.sem(s)

    ctxs = []

    def sb(name, shape, dtype):
        t = nc.sbuf_tensor(name, shape, dtype)
        h = t.__enter__()
        ctxs.append(t)
        return h

    def psalloc(name, shape, dtype):
        t = nc.psum_tensor(name, shape, dtype)
        h = t.__enter__()
        ctxs.append(t)
        return h

    XS = sb("xs", [128, NCH * DM], F32)
    WB = [sb("wb0", [128, 32768], BF16), sb("wb1", [128, 32768], BF16)]
    UB = sb("ub", [128, DM], BF16)
    UB2 = sb("ub2", [128, DM], BF16)
    HT = sb("ht", [128, 1024], BF16)
    HTB = sb("htb", [128, 4096], BF16)
    T1Q = sb("t1q", [128, 1024], BF16)
    T1K = sb("t1k", [128, 1024], BF16)
    T1VP = sb("t1vp", [128, 2048], BF16)
    ATTN = sb("attn", [128, 1024], BF16)
    IN2 = sb("in2", [128, 1024], BF16)
    AT2T = sb("at2t", [128, 1024], BF16)
    SQ = sb("sq", [128, 2048], F32)
    XR = sb("xr", [128, 2048], BF16)
    T1N = sb("t1n", [128, 1024], BF16)
    HTOK = sb("htok", [128, 2048], BF16)
    QT = sb("qt", [128, 2048], F32)
    BROWS = sb("brows", [128, 4096], BF16)
    IDN = sb("idn", [128, 128], BF16)
    ONESB = sb("onesb", [128, 128], BF16)
    CAB = sb("cab", [128, 32], BF16)
    SM = sb("sm", [128, 16], F32)
    SM2 = sb("sm2", [128, 12], F32)
    SMX = sb("smx", [128, 32], F32)
    FACTB = sb("factb", [128, 32], F32)
    GSL = sb("gsl", [128, 24], F32)
    GSL2 = sb("gsl2", [128, 8], F32)
    GSLQ = sb("gslq", [128, 8], F32)
    GSB = sb("gsb", [128, 16], BF16)
    LAM = sb("lam", [128, 1], F32)
    SMALL = sb("small", [1, 8], F32)
    SMALLB = sb("smallb", [1, 2], BF16)
    CST = sb("cst", [128, 8], F32)

    EXPS = QT[:, 0:1024]
    TRASH = QT[:, 1024:2048]
    SSUB = SQ[:, 0:1024]
    T1O = HTOK
    VR = XS[:, 0:1024]
    VTV = HTB[:, 0:2048]
    VTQ = HTB[:, 2048:3072]
    VTK = HTB[:, 3072:4096]
    NEGMAX = SMX
    RINV = SMX
    FACT = FACTB

    ROWMAP = {
        ("CQ", 0): (0, 0), ("CQ", 1): (0, 512),
        ("CK", 0): (32, 0), ("CK", 1): (32, 512),
        ("CV", 0): (64, 0), ("CV", 1): (64, 512),
        ("CO", 0): (96, 0), ("CO", 1): (96, 512),
    }
    for c in range(8):
        ROWMAP[("Q1R", c)] = (0, 1024 + 512 * c) if c < 4 else (64, 1024 + 512 * (c - 4))
        ROWMAP[("C1", c)] = (32, 1024 + 512 * c) if c < 4 else (96, 1024 + 512 * (c - 4))
    for c in range(2):
        ROWMAP[("C2", c)] = (0, 3072 + 512 * c)
        ROWMAP[("Q2R", c)] = (32, 3072 + 512 * c)

    def brow(name, c):
        p, col = ROWMAP[(name, c)]
        return BROWS[p:p + 1, col:col + 512], p

    PA = psalloc("pa", [128, 2048], F32)
    PB = psalloc("pb", [128, 1024], F32)
    PC = psalloc("pc", [128, 1024], F32)
    PTTA = PA[:, 0:512].bitcast(BF16)
    PTTB = PB[:, 0:512].bitcast(BF16)
    PTTB2 = PB[:, 512:1024].bitcast(BF16)
    PTTD = PC[:, 512:1024].bitcast(BF16)
    TRASHB = QT[:, 1024:1536].bitcast(BF16)
    TRASHB2 = QT[:, 1536:2048].bitcast(BF16)
    PTTC = PC[:, 0:512].bitcast(BF16)


    W = em.wait

    def dma(dst, src, inc="sDMA"):
        return em.op("sp", lambda e: e.dma_start(dst, src), inc=inc, by=16)

    for i, cv in enumerate([RM, -RM, 16.0, -0.5, 64.0, 1.0 / 64.0, -RM / 64.0]):
        em.op("pool", lambda e, i=i, cv=cv: e.memset(CST[:, i:i + 1], float(cv)), inc="sPOOL")
        nc.const_aps.register(F32, float(cv), CST[:, i:i + 1])
    em.op("pool", lambda e: e.memset(ONESB[:], 1.0), inc="sPOOL")
    W("pool", "sPOOL", em.val("sPOOL"))
    em.op("pool", lambda e: e.affine_select(
        IDN[:], ONESB[:], [[-1, 128]], ALU.is_equal, 0.0,
        base=0, channel_multiplier=1), inc="sPOOL")
    em.op("pool", lambda e: e.memset(T1VP[:], 0.0), inc="sPOOL")
    em.op("pool", lambda e: e.memset(IN2[:], 0.0), inc="sPOOL")
    em.op("pool", lambda e: e.memset(XR[:], 0.0), inc="sPOOL")
    em.op("pool", lambda e: e.memset(CAB[:], 0.0), inc="sPOOL")
    em.op("pool", lambda e: e.memset(BROWS[:], 0.0), inc="sPOOL")
    W("pool", "sPOOL", em.val("sPOOL"))
    for c in range(4):
        em.op("pool", lambda e, c=c: e.affine_select(
            CAB[32 * c:32 * c + 32, 0:16], ONESB[32 * c:32 * c + 32, 0:16],
            [[-1, 16]], ALU.is_equal, 0.0, base=0, channel_multiplier=1), inc="sPOOL")
    v_consts = em.val("sPOOL")

    for j in range(8):
        dma(GSL[:, j:j + 1], vec[0:1, 128 * j:128 * j + 128].transpose([1, 0]))
    for j in range(8):
        dma(GSL[:, 8 + j:9 + j], vec[2:3, 128 * j:128 * j + 128].transpose([1, 0]))
    for j in range(8):
        dma(GSL[:, 16 + j:17 + j], vec[1:2, 128 * j:128 * j + 128].transpose([1, 0]))
    for j in range(8):
        dma(GSL2[:, j:j + 1], vec[3:4, 128 * j:128 * j + 128].transpose([1, 0]))
    v_gsl = em.val("sDMA")
    dma(VR[0:1, :], vec[4:5, :])
    dma(VR[32:33, :], vec[5:6, :])
    dma(VR[64:65, :], vec[6:7, :])
    dma(VR[96:97, 0:128], vec[7:8, 0:128])
    v_vr = em.val("sDMA")


    W("dve", "sDMA", v_vr)
    em.op("dve", lambda e: e.tensor_mul(VR[96:97, 256:288], VR[96:97, 0:32], VR[96:97, 64:96]), inc="sDVE")
    em.op("dve", lambda e: e.tensor_mul(VR[96:97, 288:320], VR[96:97, 32:64], VR[96:97, 96:128]), inc="sDVE")
    em.op("dve", lambda e: e.reduce_sum(VR[96:97, 320:321], VR[96:97, 256:288].rearrange("p (a b) -> p a b", a=1), axis=AX.X), inc="sDVE")
    em.op("dve", lambda e: e.reduce_sum(VR[96:97, 321:322], VR[96:97, 288:320].rearrange("p (a b) -> p a b", a=1), axis=AX.X), inc="sDVE")
    W("act", "sDVE", em.val("sDVE"))
    em.op("act", lambda e: e.activation(VR[96:97, 322:323], VR[96:97, 320:321], AF.Exp), inc="sACT")
    em.op("act", lambda e: e.activation(VR[96:97, 323:324], VR[96:97, 321:322], AF.Exp), inc="sACT")
    W("dve", "sACT", em.val("sACT"))
    em.op("dve", lambda e: e.tensor_sub(VR[96:97, 324:325], VR[96:97, 323:324], VR[96:97, 322:323]), inc="sDVE")
    em.op("dve", lambda e: e.tensor_scalar(SMALL[0:1, 0:1], VR[96:97, 324:325], float(LAMBDA_INIT), None, ALU.subtract), inc="sDVE")
    em.op("dve", lambda e: e.tensor_copy(SMALLB[0:1, 0:1], SMALL[0:1, 0:1]), inc="sDVE")
    W("pe", "sDVE", em.val("sDVE"))
    W("pe", "sPOOL", v_consts)
    em.op("pe", lambda e: e.matmul(PA[:, 0:1], ONESB[0:1, 0:128], SMALLB[0:1, 0:1],
                                   start=True, stop=True), inc="sPE")
    W("dve", "sPE", em.val("sPE"))
    em.op("dve", lambda e: e.tensor_copy(LAM[:, 0:1], PA[:, 0:1]), inc="sDVE")
    W("pool", "sDVE", em.val("sDVE"))
    W("pool", "sPOOL", v_consts)
    for c in range(4):
        em.op("pool", lambda e, c=c: e.affine_select(
            CAB[32 * c:32 * c + 32, 16:32],
            LAM[32 * c:32 * c + 32, 0:1].broadcast_to([32, 16]),
            [[-1, 16]], ALU.is_equal, 0.0, base=-16, channel_multiplier=1), inc="sPOOL")
    v_cab = em.val("sPOOL")

    def gdma(dst, src):
        return em.op("pool", lambda e: e.dma_start(dst, src), inc="sGP", by=16)

    for j in range(8):
        gdma(simg[0][:, 24576 + 1024 * j:24576 + 1024 * j + 1024], wot[128 * j:128 * j + 128, :])
    gp_img0 = em.val("sGP")
    gp_img = {1: 0, 2: 0}

    img_ready = {}
    rows_gate = {}
    HTBF = HTB[:].bitcast(F32)
    ring = [SQ[:, 0:2048], QT[:, 0:2048], HTBF[:, 0:1024], HTBF[:, 1024:2048]]
    ring_w = [2048, 2048, 1024, 1024]
    rs = {"i": 0, "done": {}}

    def _staged(src_ap, width, consume, row):
        while True:
            s = rs["i"] % 4
            rs["i"] += 1
            if ring_w[s] >= width:
                break
        psem = "sPF%d" % s
        buf = ring[s][0:1, 0:width] if row else ring[s][:, 0:width]
        if s in rs["done"]:
            for sem_n, v in rs["done"][s]:
                W("sp", sem_n, v)
        vd = dma(buf, src_ap, inc=psem)
        rs["done"][s] = consume(buf, vd, psem)

    def staged(src_ap, width, consume):
        _staged(src_ap, width, consume, False)

    def staged_row(src_ap, width, consume):
        _staged(src_ap, width, consume, True)

    def staged_q(src_ap, consume):
        while True:
            s = rs["i"] % 4
            rs["i"] += 1
            if ring_w[s] >= 2048:
                break
        psem = "sPF%d" % s
        slot = ring[s]
        if s in rs["done"]:
            for sem_n, v in rs["done"][s]:
                W("sp", sem_n, v)
        vd = dma(slot[:, 0:1024], src_ap, inc=psem)
        rs["done"][s] = consume(slot, vd, psem)

    def prep():
        W("dve", "sDMA", v_vr)
        W("act", "sDMA", v_vr)
        W("pe", "sDMA", v_vr)
        W("pe", "sPOOL", v_consts)
        em.op("dve", lambda e: e.tensor_scalar(GSLQ[:], GSL[:, 0:8], QSCALE, None, ALU.mult), inc="sDVE")
        v_gslq = em.val("sDVE")

        pa_busy = {"v": em.val("sDVE")}
        for mi, wt in enumerate([wqt, wkt, wvt]):
            for j in range(8):
                def consume(buf, vd, psem, mi=mi, j=j):
                    col = mi * 8192 + j * 1024
                    scap = GSLQ[:, j:j + 1] if mi == 0 else GSL[:, j:j + 1]
                    W("act", psem, vd)
                    if mi == 0:
                        W("act", "sDVE", v_gslq)
                    em.op("act", lambda e, col=col, scap=scap: e.activation(
                        WB[1][:, col:col + 1024], buf, AF.Copy, scale=scap), inc="sACT")
                    va = em.val("sACT")
                    W("pe", psem, vd)
                    if j == 0:
                        W("pe", "sDVE", pa_busy["v"])
                    for hh in range(2):
                        em.op("pe", lambda e, j=j, hh=hh: e.matmul(
                            PA[0:1, 512 * hh:512 * hh + 512], GSL[:, 16 + j:17 + j],
                            buf[:, 512 * hh:512 * hh + 512],
                            start=(j == 0), stop=(j == 7)), inc="sPE")
                    return [("sACT", va), ("sPE", em.val("sPE"))]
                staged(wt[128 * j:128 * j + 128, :], 1024, consume)
            W("dve", "sPE", em.val("sPE"))
            nm = ["CQ", "CK", "CV"][mi]
            sc = QSCALE if mi == 0 else 1.0
            for c in range(2):
                r, _ = brow(nm, c)
                em.op("dve", lambda e, r=r, c=c, sc=sc: e.tensor_scalar(
                    r, PA[0:1, 512 * c:512 * c + 512], float(sc), None, ALU.mult), inc="sDVE")
            pa_busy["v"] = em.val("sDVE")
        W("pool", "sDMA", v_vr)
        for c in range(2):
            r, _ = brow("CO", c)
            em.op("pool", lambda e, r=r, c=c: e.tensor_copy(r, VR[0:1, 512 * c:512 * c + 512]), inc="sPOOL")
            r2, _ = brow("C2", c)
            em.op("pool", lambda e, r=r2, c=c: e.tensor_copy(r, VR[32:33, 512 * c:512 * c + 512]), inc="sPOOL")
        v_vr_done = em.val("sPOOL")
        def qrow_consume(buf, vd, psem, names):
            W("pool", psem, vd)
            em.op("pool", lambda e: e.tensor_scalar(buf, buf, 64.0, RM, ALU.mult, ALU.add), inc="sPOOL")
            em.op("pool", lambda e: e.tensor_scalar(buf, buf, RM, -128.0, ALU.subtract, ALU.max), inc="sPOOL")
            for c, (nm, ch) in enumerate(names):
                r, _ = brow(nm, ch)
                em.op("pool", lambda e, r=r, c=c: e.tensor_scalar(
                    r, buf[0:1, 512 * c:512 * c + 512], 127.0, 1.0 / 64.0, ALU.min, ALU.mult), inc="sPOOL")
            return [("sPOOL", em.val("sPOOL"))]
        staged_row(vec[6:7, :], 1024, lambda buf, vd, psem: qrow_consume(buf, vd, psem, [("Q2R", 0), ("Q2R", 1)]))
        for hh in range(2):
            staged_row(vfb1[0:1, 2048 * hh:2048 * hh + 2048], 2048,
                       lambda buf, vd, psem, hh=hh: qrow_consume(buf, vd, psem, [("Q1R", hh * 4 + c) for c in range(4)]))
        rows_gate[0] = [("sDVE", pa_busy["v"]), ("sPOOL", v_vr_done)]
        rows_gate[3] = rows_gate[4] = [("sPOOL", em.val("sPOOL")), ("sPOOL", v_vr_done)]
        W("sp", "sACT", em.val("sACT"))
        W("sp", "sDMA2", em.val("sDMA2"))
        vstore = dma(simg[0][:, 0:24576], WB[1][:, 0:24576], inc="sDMA2")
        img_ready[0] = (vstore, gp_img0)

        for half in range(2):
            W("act", "sDMA2", em.val("sDMA2"))
            for j in range(8):
                def consume(buf, vd, psem, j=j, half=half):
                    col = j * 2048
                    W("act", psem, vd)
                    em.op("act", lambda e, col=col, j=j: e.activation(
                        WB[1][:, col:col + 2048], buf, AF.Copy, scale=GSL[:, 8 + j:9 + j]), inc="sACT")
                    va = em.val("sACT")
                    W("pe", psem, vd)
                    if j == 0:
                        W("pe", "sDVE", pa_busy["v"])
                    for qo in range(4):
                        em.op("pe", lambda e, j=j, qo=qo: e.matmul(
                            PA[0:1, 512 * qo:512 * qo + 512], GSL2[:, j:j + 1],
                            buf[:, 512 * qo:512 * qo + 512],
                            start=(j == 0), stop=(j == 7)), inc="sPE")
                    return [("sACT", va), ("sPE", em.val("sPE"))]
                staged(w1t[128 * j:128 * j + 128, 2048 * half:2048 * half + 2048], 2048, consume)
            for j in range(16):
                lane = ("act", "dve", "pool")[j % 3]
                def consume_w2(buf, vd, psem, j=j, lane=lane):
                    lsem = {"act": "sACT", "dve": "sDVE", "pool": "sPOOL"}[lane]
                    W(lane, psem, vd)
                    dst = WB[1][:, 16384 + 1024 * j:16384 + 1024 * j + 1024]
                    if lane == "act":
                        em.op("act", lambda e: e.copy(dst, buf), inc="sACT")
                    else:
                        em.op(lane, lambda e: e.tensor_copy(dst, buf), inc=lsem)
                    return [(lsem, em.val(lsem))]
                staged(w2t[2048 * half + 128 * j:2048 * half + 128 * j + 128, :], 1024, consume_w2)
            def consume_b(buf, vd, psem, half=half):
                W("dve", psem, vd)
                W("dve", "sPE", em.val("sPE"))
                for qo in range(4):
                    r, _ = brow("C1", half * 4 + qo)
                    em.op("dve", lambda e, r=r, qo=qo, buf=buf: e.tensor_tensor(
                        r, buf[0:1, 512 * qo:512 * qo + 512],
                        PA[0:1, 512 * qo:512 * qo + 512], ALU.add), inc="sDVE")
                return [("sDVE", em.val("sDVE"))]
            staged_row(vb1[0:1, 2048 * half:2048 * half + 2048], 2048, consume_b)
            pa_busy["v"] = em.val("sDVE")
            W("sp", "sACT", em.val("sACT"))
            W("sp", "sDVE", em.val("sDVE"))
            W("sp", "sPOOL", em.val("sPOOL"))
            W("sp", "sDMA2", em.val("sDMA2"))
            vstore = dma(simg[1 + half][:], WB[1][:], inc="sDMA2")
            img_ready[1 + half] = (vstore, gp_img[1 + half])
        rows_gate[1] = rows_gate[2] = [("sDVE", pa_busy["v"])]

        def quant_tile(src_ap, col):
            def consume(slot, vd, psem, col=col):
                a, b = slot[:, 0:1024], slot[:, 1024:2048]
                W("act", psem, vd)
                em.op("act", lambda e, a=a, b=b: e.activation(
                    b, a, AF.Identity, bias=RM, scale=64.0), inc="sACT")
                va = em.val("sACT")
                W("dve", "sACT", va)
                em.op("dve", lambda e, b=b: e.tensor_scalar(
                    b, b, RM - 128.0, RM + 127.0, ALU.max, ALU.min), inc="sDVE")
                vd2 = em.val("sDVE")
                W("act", "sDVE", vd2)
                em.op("act", lambda e, b=b, col=col: e.activation(
                    WB[1][:, col:col + 1024], b, AF.Identity, bias=-RM / 64.0, scale=1.0 / 64.0), inc="sACT")
                return [("sACT", em.val("sACT")), ("sDVE", vd2)]
            staged_q(src_ap, consume)
        for half in range(2):
            W("dve", "sDMA2", em.val("sDMA2"))
            W("pool", "sDMA2", em.val("sDMA2"))
            W("act", "sDMA2", em.val("sDMA2"))
            for j in range(8):
                for k in range(2):
                    quant_tile(q1t[128 * j:128 * j + 128, 2048 * half + 1024 * k:2048 * half + 1024 * k + 1024],
                               j * 2048 + 1024 * k)
            for j in range(16):
                quant_tile(q2t[2048 * half + 128 * j:2048 * half + 128 * j + 128, :], 16384 + j * 1024)
            W("sp", "sDVE", em.val("sDVE"))
            W("sp", "sPOOL", em.val("sPOOL"))
            W("sp", "sACT", em.val("sACT"))
            W("sp", "sDMA2", em.val("sDMA2"))
            vstore = dma(simg[3 + half][:], WB[1][:], inc="sDMA2")
            img_ready[3 + half] = (vstore, 0)

    prep()
    v_prep = (em.val("sDMA2"), em.val("sGP"))
    prep_bnd = {"pa_dve": em.val("sDVE"), "pa_act": 0, "pb_dve": 0, "pc_dve": 0, "pc_act": 0}
    v_prep_lanes = {s: em.val(s) for s in ("sDVE", "sPOOL", "sACT", "sPE")}

    for eng in ("pe", "dve", "act"):
        W(eng, "sPOOL", v_cab)

    def ln_stats(xap, compound):
        em.op("dve", lambda e: e.reduce_sum(SM[:, 0:1], xap.rearrange("p (a b) -> p a b", a=1), axis=AX.X), inc="sDVE")
        em.op("act", lambda e: e.activation(TRASH, xap, AF.Square, accum_out=SM2[:, 0:1]), inc="sACT")
        W("dve", "sACT", em.val("sACT"))
        em.op("dve", lambda e: e.tensor_scalar(SM[:, 0:1], SM[:, 0:1], 1.0 / 1024.0, None, ALU.mult), inc="sDVE")
        em.op("dve", lambda e: e.tensor_mul(SM2[:, 1:2], SM[:, 0:1], SM[:, 0:1]), inc="sDVE")
        em.op("dve", lambda e: e.tensor_scalar(SM2[:, 2:3], SM2[:, 0:1], 1.0 / 1024.0, None, ALU.mult), inc="sDVE")
        em.op("dve", lambda e: e.tensor_sub(SM2[:, 3:4], SM2[:, 2:3], SM2[:, 1:2]), inc="sDVE")
        em.op("dve", lambda e: e.tensor_scalar(SM2[:, 4:5], SM2[:, 3:4], EPS, None, ALU.add), inc="sDVE")
        W("act", "sDVE", em.val("sDVE"))
        em.op("act", lambda e: e.activation(SM2[:, 5:6], SM2[:, 4:5], AF.Ln), inc="sACT")
        em.op("act", lambda e: e.activation(SM[:, 1:2], SM2[:, 5:6], AF.Exp, scale=-0.5), inc="sACT")
        if compound:
            em.op("dve", lambda e: e.reciprocal(SM2[:, 6:7], SM2[:, 4:5]), inc="sDVE")
            em.op("dve", lambda e: e.tensor_mul(SM2[:, 7:8], SM2[:, 3:4], SM2[:, 6:7]), inc="sDVE")
            em.op("dve", lambda e: e.tensor_scalar(SM2[:, 8:9], SM2[:, 7:8], EPS, None, ALU.add), inc="sDVE")
            W("act", "sDVE", em.val("sDVE"))
            em.op("act", lambda e: e.activation(SM2[:, 9:10], SM2[:, 8:9], AF.Ln), inc="sACT")
            W("dve", "sACT", em.val("sACT"))
            em.op("dve", lambda e: e.tensor_add(SM2[:, 10:11], SM2[:, 5:6], SM2[:, 9:10]), inc="sDVE")
            W("act", "sDVE", em.val("sDVE"))
            em.op("act", lambda e: e.activation(SM[:, 2:3], SM2[:, 10:11], AF.Exp, scale=-0.5), inc="sACT")
        return em.val("sACT")

    def transposes(tiles, dst_fn, ntiles, ptt, ptt2=None):
        em.pesync()
        evac_hwm = {0: 0, 1: 0}
        gvals = []
        for j in range(ntiles):
            g = j // 4
            par = g % 2 if ptt2 is not None else 0
            buf = ptt2 if par == 1 else ptt
            if j % 4 == 0 and evac_hwm[par]:
                W("pe", "sACT", evac_hwm[par])
            em.op("pe", lambda e, j=j, buf=buf: e.transpose(
                buf[:, 128 * (j % 4):128 * (j % 4) + 128], tiles(j), IDN[:]), inc="sPE")
            if j % 4 == 3:
                W("act", "sPE", em.val("sPE"))
                em.op("act", lambda e, g=g, buf=buf: e.copy(dst_fn(g), buf[:, 0:512]), inc="sACT")
                evac_hwm[par] = em.val("sACT")
                gvals.append(em.val("sACT"))
        return em.val("sACT"), gvals

    def bias_mm(ps_ap, name, chunk, start=False, stop=True):
        r, p = brow(name, chunk)
        em.op("pe", lambda e, r=r, p=p, ps_ap=ps_ap: e.matmul(
            ps_ap, ONESB[p:p + 1, 0:128], r, start=start, stop=stop,
            tile_position=(p, 0)), inc="sPE")

    def ubp(c):
        return UB if c % 2 == 0 else UB2

    AST = {}

    def attn_front(c, xl):
        st = AST.setdefault(c, {})
        xap = XS[:, DM * c:DM * c + DM]
        sx, vx = xl[c]
        W("dve", sx, vx)
        W("act", sx, vx)
        W("pool", sx, vx)
        if c - 3 in AST and "applied_p" in AST[c - 3]:
            W("act", "sPOOL", AST[c - 3]["applied_p"])
            W("act", "sDVE", AST[c - 3]["applied_d"])
        if c - 2 in AST and "ht_pe" in AST[c - 2]:
            W("pool", "sPE", AST[c - 2]["ht_pe"])
        if c - 1 in AST and "ub" in AST[c - 1]:
            W("dve", "sPOOL", AST[c - 1]["ub"])
            W("act", "sPOOL", AST[c - 1]["ub"])
        va = ln_stats(xap, compound=True)
        W("pool", "sACT", va)
        W("pool", "sDVE", em.val("sDVE"))
        em.op("pool", lambda e: e.tensor_scalar(ubp(c)[:], xap, SM[:, 0:1], SM[:, 2:3], ALU.subtract, ALU.mult), inc="sPOOL")
        p2 = 8 + 2 * c
        em.op("pool", lambda e, p2=p2: e.tensor_copy(SM[:, p2:p2 + 2], SM[:, 0:2]), inc="sPOOL")
        st["ub"] = em.val("sPOOL")

    def attn_fq(c, wb, v_wload, prev, bnd=None):
        st = AST[c]
        if prev is not None:
            W("pe", "sDVE", AST[prev]["t1_dve"])
            W("pe", "sACT", max(AST[prev]["t1_act"], AST[prev]["exp"]))
        else:
            W("pe", "sDVE", max(bnd["pa_dve"], bnd["pb_dve"]))
            W("pe", "sACT", bnd["pa_act"])
        W("pe", "sPOOL", st["ub"])
        va, htg = transposes(lambda j: ubp(c)[:, 128 * j:128 * j + 128],
                             lambda g: HT[:, 512 * g:512 * g + 512], 8, PTTB, PTTB2)
        st["ht_pe"] = em.val("sPE")
        W("pe", v_wload[0], v_wload[1])
        em.pesync()
        for j in range(8):
            if j % 4 == 0:
                W("pe", "sACT", htg[j // 4])
            for mi in range(2):
                for half in range(2):
                    em.op("pe", lambda e, j=j, mi=mi, half=half: e.matmul(
                        PA[:, 1024 * mi + 512 * half:1024 * mi + 512 * half + 512],
                        HT[:, 128 * j:128 * j + 128],
                        WB[wb][:, 8192 * mi + 1024 * j + 512 * half:8192 * mi + 1024 * j + 512 * half + 512],
                        start=(j == 0), stop=False), inc="sPE")
        for j in range(8):
            for half in range(2):
                em.op("pe", lambda e, j=j, half=half: e.matmul(
                    PB[:, 512 * half:512 * half + 512],
                    HT[:, 128 * j:128 * j + 128],
                    WB[wb][:, 16384 + 1024 * j + 512 * half:16384 + 1024 * j + 512 * half + 512],
                    start=(j == 0), stop=False), inc="sPE")
        for mi, nm in enumerate(["CQ", "CK", "CV"]):
            dstps, dcol = (PA, 1024 * mi) if mi < 2 else (PB, 0)
            for half in range(2):
                bias_mm(dstps[:, dcol + 512 * half:dcol + 512 * half + 512], nm, half)
        st["qkv"] = em.val("sPE")

    def attn_evacs(c, prevst=None):
        st = AST[c]
        W("dve", "sPE", st["qkv"])
        W("act", "sPE", st["qkv"])
        if prevst is not None:
            W("act", "sDVE", prevst["vt"])
        em.op("dve", lambda e: e.tensor_copy(T1Q[:], PA[:, 0:1024]), inc="sDVE")
        em.op("act", lambda e: e.copy(T1K[:], PA[:, 1024:2048]), inc="sACT")
        st["t1_act"] = em.val("sACT")
        em.op("dve", lambda e: e.tensor_copy(
            T1VP[:].rearrange("p (d h) -> p d h", d=64)[:, :, 0:16], PB[:, 0:1024]), inc="sDVE")
        st["t1_dve"] = em.val("sDVE")

    def attn_vtrans_qk(c, prevst=None):
        st = AST[c]
        if prevst is not None:
            W("dve", "sPE", prevst["scores"])
        W("dve", "sACT", st["t1_act"])
        em.op("dve", lambda e: e.transpose(VTQ[:], T1Q[:]), inc="sDVE")
        em.op("dve", lambda e: e.transpose(VTK[:], T1K[:]), inc="sDVE")
        st["vt"] = em.val("sDVE")

    def attn_vtrans_v(c, prevst=None):
        st = AST[c]
        if prevst is not None:
            W("dve", "sPE", prevst["o2"])
        em.op("dve", lambda e: e.transpose(VTV[:], T1VP[:]), inc="sDVE")
        st["vtv"] = em.val("sDVE")

    def attn_scores(c):
        st = AST[c]
        W("pe", "sDVE", st["vt"])
        W("pe", "sACT", st["t1_act"])
        em.pesync()
        for i in range(128):
            r, tl = i % 4, i // 4
            em.op("pe", lambda e, r=r, tl=tl: e.matmul(
                PA[32 * r:32 * r + 32, 32 * tl:32 * tl + 32],
                VTQ[32 * r:32 * r + 32, :].rearrange("p (w cc) -> p w cc", cc=32)[:, :, tl],
                VTK[32 * r:32 * r + 32, :].rearrange("p (w cc) -> p w cc", cc=32)[:, :, tl],
                start=True, stop=True, tile_position=(32 * r, 32 * r)), inc="sPE")
        st["scores"] = em.val("sPE")

    def attn_softmax(c):
        st = AST[c]
        W("act", "sPE", st["scores"])
        em.op("act", lambda e: e.activation(EXPS, PA[:, 0:1024], AF.Exp), inc="sACT")
        st["exp"] = em.val("sACT")
        W("dve", "sACT", st["exp"])
        em.op("dve", lambda e: e.reduce_sum(
            RINV[:], EXPS.rearrange("p (t k) -> p t k", k=32), axis=AX.X), inc="sDVE")
        em.op("dve", lambda e: e.reciprocal(RINV[:], RINV[:]), inc="sDVE")
        st["rinv"] = em.val("sDVE")
        W("pool", "sDVE", st["rinv"])
        em.op("pool", lambda e: e.tensor_tensor(
            ATTN[:].rearrange("p (t k) -> p t k", k=32),
            EXPS.rearrange("p (t k) -> p t k", k=32),
            RINV[:].unsqueeze(2).broadcast_to([128, 32, 32]), ALU.mult), inc="sPOOL")
        st["attn"] = em.val("sPOOL")

    def attn_combine(c, prev, bnd=None):
        st = AST[c]
        W("pe", "sPOOL", st["attn"])
        W("pe", "sPOOL", v_cab)
        if prev is not None and "conn" in AST.get(prev, {}):
            W("pe", "sDVE", AST[prev]["conn"])
        elif bnd is not None:
            W("pe", "sDVE", bnd["pc_dve"])
            W("pe", "sACT", bnd["pc_act"])
        em.pesync()
        for r in range(4):
            for half in range(2):
                em.op("pe", lambda e, r=r, half=half: e.matmul(
                    PC[32 * r:32 * r + 16, 0:512],
                    CAB[32 * r:32 * r + 32, 16 * half:16 * half + 16],
                    ATTN[32 * r:32 * r + 32, :].rearrange("p (t k) -> p t k", k=32)[:, :, 16 * half:16 * half + 16],
                    start=(half == 0), stop=(half == 1), tile_position=(32 * r, 32 * r)), inc="sPE")
        st["comb"] = em.val("sPE")

    def attn_in2(c):
        st = AST[c]
        W("act", "sPE", st["comb"])
        for r in range(4):
            em.op("act", lambda e, r=r: e.copy(
                IN2[32 * r:32 * r + 16, :].rearrange("p (t k) -> p t k", k=32)[:, :, 0:16],
                PC[32 * r:32 * r + 16, 0:512]), inc="sACT")
        st["in2"] = em.val("sACT")
        W("dve", "sACT", st["in2"])
        em.op("dve", lambda e: e.transpose(AT2T[:], IN2[:]), inc="sDVE")
        st["at2t"] = em.val("sDVE")

    def attn_einsum2(c):
        st = AST[c]
        W("pe", "sDVE", max(st["at2t"], st["vtv"]))
        W("pe", "sACT", st["in2"])
        for h in range(2):
            em.pesync()
            for i in range(64):
                r, tj = i % 4, i // 4
                tl = 16 * h + tj
                em.op("pe", lambda e, r=r, tl=tl, tj=tj: e.matmul(
                    PC[32 * r:32 * r + 32, 64 * tj:64 * tj + 64],
                    AT2T[32 * r:32 * r + 32, 32 * tl:32 * tl + 32],
                    VTV[32 * r:32 * r + 32, :].rearrange("p (w cc) -> p w cc", cc=32)[:, :, tl],
                    start=True, stop=True, tile_position=(32 * r, 32 * r)), inc="sPE")
            W("act", "sPE", em.val("sPE"))
            dstb = TRASHB if h == 0 else TRASHB2
            em.op("act", lambda e, dstb=dstb: e.copy(dstb, PC[:]), inc="sACT")
            em.op("act", lambda e, h=h: e.activation(
                SQ[:, 1024 * h:1024 * h + 1024], PC[:], AF.Square), inc="sACT")
            st["o2e%d" % h] = em.val("sACT")
            if h == 0:
                W("pe", "sACT", st["o2e0"])
        st["o2"] = em.val("sPE")

    def attn_rms(c):
        st = AST[c]
        W("dve", "sACT", st["o2e0"])
        em.op("dve", lambda e: e.reduce_sum(
            FACT[:, 0:16], SQ[:, 0:1024].rearrange("p (t d) -> p t d", d=64), axis=AX.X), inc="sDVE")
        W("dve", "sACT", st["o2e1"])
        em.op("dve", lambda e: e.reduce_sum(
            FACT[:, 16:32], SQ[:, 1024:2048].rearrange("p (t d) -> p t d", d=64), axis=AX.X), inc="sDVE")
        em.op("dve", lambda e: e.tensor_scalar(FACT[:], FACT[:], 1.0 / 64.0, EPS, ALU.mult, ALU.add), inc="sDVE")
        W("act", "sDVE", em.val("sDVE"))
        em.op("act", lambda e: e.activation(FACT[:], FACT[:], AF.Ln), inc="sACT")
        em.op("act", lambda e: e.activation(FACT[:], FACT[:], AF.Exp, scale=-0.5), inc="sACT")
        W("dve", "sACT", em.val("sACT"))
        em.op("dve", lambda e: e.tensor_scalar(FACT[:], FACT[:], float(1.0 - LAMBDA_INIT), None, ALU.mult), inc="sDVE")
        vsm = em.val("sDVE")
        em.op("dve", lambda e: e.tensor_tensor(
            bass.AP(XR[:].tensor, XR[:].offset, [[2048, 128], [1, 16], [32, 64]]),
            TRASHB.rearrange("p (t d) -> p t d", d=64),
            FACT[:, 0:16].unsqueeze(2).broadcast_to([128, 16, 64]), ALU.mult), inc="sDVE")
        st["applied_d"] = em.val("sDVE")
        W("pool", "sDVE", vsm)
        W("pool", "sACT", st["o2e1"])
        em.op("pool", lambda e: e.tensor_tensor(
            bass.AP(XR[:].tensor, XR[:].offset + 16, [[2048, 128], [1, 16], [32, 64]]),
            TRASHB2.rearrange("p (t d) -> p t d", d=64),
            FACT[:, 16:32].unsqueeze(2).broadcast_to([128, 16, 64]), ALU.mult), inc="sPOOL")
        st["applied_p"] = em.val("sPOOL")
        W("dve", "sPOOL", st["applied_p"])
        em.op("dve", lambda e: e.transpose(T1O[:], XR[:]), inc="sDVE")
        em.op("dve", lambda e: e.tensor_copy(
            T1N[:].rearrange("p (d a) -> p d a", a=16),
            T1O[:].rearrange("p (d a) -> p d a", a=32)[:, :, 0:16]), inc="sDVE")
        st["t1n"] = em.val("sDVE")

    def attn_fmwo(c, wb):
        st = AST[c]
        W("pe", "sDVE", st["t1n"])
        if c - 1 in AST and "fm" in AST[c - 1]:
            W("pe", "sACT", AST[c - 1]["fm"])
        va, fmg = transposes(lambda j: T1N[:, 128 * j:128 * j + 128],
                             lambda g: HT[:, 512 * g:512 * g + 512], 8, PTTC, PTTD)
        st["fm"] = va
        W("pe", "sACT", va)
        em.pesync()
        for j in range(8):
            for half in range(2):
                em.op("pe", lambda e, j=j, half=half: e.matmul(
                    PC[:, 512 * half:512 * half + 512],
                    HT[:, 128 * j:128 * j + 128],
                    WB[wb][:, 24576 + 1024 * j + 512 * half:24576 + 1024 * j + 512 * half + 512],
                    start=(j == 0), stop=False), inc="sPE")
        for half in range(2):
            bias_mm(PC[:, 512 * half:512 * half + 512], "CO", half)
        st["wo"] = em.val("sPE")

    def attn_conn(c):
        st = AST[c]
        xap = XS[:, DM * c:DM * c + DM]
        p2 = 8 + 2 * c
        em.op("pool", lambda e, p2=p2: e.tensor_scalar(
            SSUB[:], xap, SM[:, p2:p2 + 1], SM[:, p2 + 1:p2 + 2], ALU.subtract, ALU.mult), inc="sPOOL")
        vss = em.val("sPOOL")
        W("dve", "sPE", st["wo"])
        W("dve", "sPOOL", vss)
        em.op("dve", lambda e: e.tensor_tensor(xap, SSUB[:], PC[:], ALU.add), inc="sDVE")
        st["conn"] = em.val("sDVE")

    def stage_attn_pipelined(wb, v_wload, xl, bnd):
        AST.clear()
        attn_front(0, xl)
        attn_fq(0, wb, v_wload, None, bnd)
        attn_front(1, xl)
        attn_evacs(0)
        attn_vtrans_qk(0)
        attn_vtrans_v(0)
        attn_scores(0)
        attn_softmax(0)
        attn_fq(1, wb, v_wload, 0)
        for k in range(NCH):
            if k + 2 < NCH:
                attn_front(k + 2, xl)
            attn_combine(k, k - 1 if k > 0 else None, bnd)
            attn_in2(k)
            if k + 1 < NCH:
                attn_evacs(k + 1, AST[k])
                attn_vtrans_qk(k + 1, AST[k])
                attn_scores(k + 1)
                attn_softmax(k + 1)
            attn_einsum2(k)
            if k + 1 < NCH:
                attn_vtrans_v(k + 1, AST[k])
            attn_rms(k)
            if k + 2 < NCH:
                attn_fq(k + 2, wb, v_wload, k + 1)
            attn_fmwo(k, wb)
            attn_conn(k)
        return {"pa_dve": AST[3]["t1_dve"], "pa_act": AST[3]["exp"],
                "pb_dve": AST[3]["t1_dve"], "pc_dve": AST[3]["conn"], "pc_act": 0}

    FST = {}

    def ffn_front(blk, c, half, quant):
        st = FST.setdefault(c, {})
        xap = XS[:, DM * c:DM * c + DM]
        if half == 1:
            return
        W("act", "sDVE", em.val("sDVE"))
        va = ln_stats(xap, compound=False)
        W("dve", "sACT", va)
        if not quant:
            em.op("dve", lambda e: e.tensor_scalar(
                ubp(c)[:], xap, SM[:, 0:1], SM[:, 1:2], ALU.subtract, ALU.mult), inc="sDVE")
        else:
            em.op("dve", lambda e: e.tensor_scalar(SM[:, 3:4], SM[:, 1:2], 16.0, None, ALU.mult), inc="sDVE")
            em.op("dve", lambda e: e.tensor_scalar(
                SSUB[:], xap, SM[:, 0:1], SM[:, 3:4], ALU.subtract, ALU.mult), inc="sDVE")
            em.op("dve", lambda e: e.tensor_scalar(
                SSUB[:], SSUB[:], RM, RM, ALU.add, ALU.subtract), inc="sDVE")
            em.op("dve", lambda e: e.tensor_scalar(
                SSUB[:], SSUB[:], -128.0, 127.0, ALU.max, ALU.min), inc="sDVE")
            em.op("dve", lambda e: e.tensor_scalar(
                ubp(c)[:], SSUB[:], 1.0 / 16.0, None, ALU.mult), inc="sDVE")
        st["ub"] = em.val("sDVE")

    def ffn_front_T(c, half, ptt_free=0):
        st = FST[c]
        if half == 1:
            return
        W("pe", "sDVE", st["ub"])
        if ptt_free:
            W("pe", "sACT", ptt_free)
        va, gg = transposes(lambda j: ubp(c)[:, 128 * j:128 * j + 128],
                            lambda g, c=c: HTB[:, 1024 * c + 512 * g:1024 * c + 512 * g + 512], 8, PTTC, PTTD)
        st["stash"] = va
        st["stash_g"] = gg
        st["stash_pe"] = em.val("sPE")

    def ffn_w1h(c, wb, v_wload, half, quant, hh, prev, bnd=None):
        st = FST[c]
        if hh == 0:
            W("pe", v_wload[0], v_wload[1])
        if prev is not None and ("evA%d" % hh) in FST.get(prev, {}):
            a, d = FST[prev]["evA%d" % hh]
            W("pe", "sACT", a)
            if d:
                W("pe", "sDVE", d)
        elif bnd is not None:
            W("pe", "sACT", bnd["pa_act"])
            W("pe", "sDVE", bnd["pa_dve"])
        em.pesync()
        base0 = 1024 * c
        sg = st.get("stash_g")
        for j in range(8):
            if sg is not None and j % 4 == 0:
                W("pe", "sACT", sg[j // 4])
            for qo in (2 * hh, 2 * hh + 1):
                em.op("pe", lambda e, j=j, qo=qo, base0=base0: e.matmul(
                    PA[:, 512 * qo:512 * qo + 512],
                    HTB[:, base0 + 128 * j:base0 + 128 * j + 128],
                    WB[wb][:, 2048 * j + 512 * qo:2048 * j + 512 * qo + 512],
                    start=(j == 0), stop=False), inc="sPE")
        nm1 = "Q1R" if quant else "C1"
        for qo in (2 * hh, 2 * hh + 1):
            bias_mm(PA[:, 512 * qo:512 * qo + 512], nm1, half * 4 + qo)
        st["w1h%d" % hh] = em.val("sPE")

    def ffn_evach(c, half, quant, hh):
        st = FST[c]
        W("act", "sPE", st["w1h%d" % hh])
        lo = 1024 * hh
        if not quant:
            em.op("act", lambda e, lo=lo: e.activation(
                HTOK[:, lo:lo + 1024], PA[:, lo:lo + 1024], AF.Relu), inc="sACT")
            st["evA%d" % hh] = (em.val("sACT"), 0)
            st["hid%d" % hh] = ("sACT", em.val("sACT"))
        else:
            if hh == 0:
                W("act", "sDVE", em.val("sDVE"))
            em.op("act", lambda e, lo=lo: e.activation(
                QT[:, lo:lo + 1024], PA[:, lo:lo + 1024], AF.Identity, bias=RM, scale=16.0), inc="sACT")
            em.op("act", lambda e, lo=lo: e.activation(
                SQ[:, lo:lo + 1024], QT[:, lo:lo + 1024], AF.Relu, bias=-RM, scale=1.0), inc="sACT")
            va = em.val("sACT")
            W("dve", "sACT", va)
            em.op("dve", lambda e, lo=lo: e.tensor_scalar(
                HTOK[:, lo:lo + 1024], SQ[:, lo:lo + 1024], 127.0, 1.0 / 16.0, ALU.min, ALU.mult), inc="sDVE")
            st["evA%d" % hh] = (va, em.val("sDVE"))
            st["hid%d" % hh] = ("sDVE", em.val("sDVE"))

    def ffn_hidTh(c, hh, ptt_free=0):
        st = FST[c]
        sem_n, v = st["hid%d" % hh]
        W("pe", sem_n, v)
        if ptt_free:
            W("pe", "sACT", ptt_free)
        va, gg = transposes(lambda j, hh=hh: HTOK[:, 1024 * hh + 128 * j:1024 * hh + 128 * j + 128],
                            lambda g, hh=hh: (T1Q if hh == 0 else T1K)[:, 512 * g:512 * g + 512], 8, PTTC, PTTD)
        st["hidT%d" % hh] = va
        st["hidTg%d" % hh] = gg

    def ffn_w2h(c, wb, half, quant, hh, prev, pb_free=0):
        st = FST[c]
        if hh == 0:
            if prev is not None and "res" in FST.get(prev, {}):
                W("pe", "sDVE", FST[prev]["res"])
            elif pb_free:
                W("pe", "sDVE", pb_free)
        em.pesync()
        srcb = T1Q if hh == 0 else T1K
        hg = st["hidTg%d" % hh]
        for j in range(8):
            if j % 4 == 0:
                W("pe", "sACT", hg[j // 4])
            for cc in range(2):
                em.op("pe", lambda e, j=j, cc=cc, srcb=srcb, hh=hh: e.matmul(
                    PB[:, 512 * cc:512 * cc + 512],
                    srcb[:, 128 * j:128 * j + 128],
                    WB[wb][:, 16384 + 1024 * (8 * hh + j) + 512 * cc:16384 + 1024 * (8 * hh + j) + 512 * cc + 512],
                    start=(hh == 0 and j == 0), stop=(half == 0 and hh == 1 and j == 7)), inc="sPE")
        if hh == 1:
            if half == 1:
                nm2 = "Q2R" if quant else "C2"
                for cc in range(2):
                    bias_mm(PB[:, 512 * cc:512 * cc + 512], nm2, cc)
            st["hidT_evac"] = st["hidT1"]
            st["w2"] = em.val("sPE")

    def ffn_residual(blk, c, half, quant, last_stage=False, xl_next=None):
        st = FST[c]
        xap = XS[:, DM * c:DM * c + DM]
        W("dve", "sPE", st["w2"])
        em.op("dve", lambda e: e.tensor_tensor(xap, xap, PB[:], ALU.add), inc="sDVE")
        if half == 1 and not quant:
            W("sp", "sDVE", em.val("sDVE"))
            xr = dma(SQ[:, 0:1024],
                     bass.AP(xin.tensor, (blk * BLK + c * CH) * DM, [[DM, 128], [1, DM]]))
            W("dve", "sDMA", xr)
            em.op("dve", lambda e: e.tensor_tensor(xap, xap, SQ[:, 0:1024], ALU.add), inc="sDVE")
        st["res"] = em.val("sDVE")
        if last_stage:
            sx = "sXY%d" % c
            W("sp", "sDVE", st["res"])
            dma(bass.AP(yout.tensor, (blk * BLK + c * CH) * DM, [[DM, 128], [1, DM]]),
                XS[:, DM * c:DM * c + DM], inc=sx)
            if blk + 1 < NBLK:
                W("sp", sx, em.val(sx))
                dma(XS[:, DM * c:DM * c + DM],
                    bass.AP(xin.tensor, ((blk + 1) * BLK + c * CH) * DM, [[DM, 128], [1, DM]]),
                    inc=sx)
                xl_next[c] = (sx, em.val(sx))

    def stage_ffn_pipelined(blk, wb, v_wload, half, quant, bnd, last_stage=False, xl_next=None):
        if half == 0:
            FST.clear()
        ffn_front(blk, 0, half, quant)
        W("pe", "sDVE", bnd["pc_dve"])
        ffn_front_T(0, half, ptt_free=bnd["pc_act"])
        for cm in range(NCH):
            cf = cm + 1 if cm + 1 < NCH else None
            if cf is not None:
                ffn_front(blk, cf, half, quant)
            prev = cm - 1 if cm > 0 else None
            ffn_w1h(cm, wb, v_wload, half, quant, 0, prev, bnd)
            ffn_w1h(cm, wb, v_wload, half, quant, 1, prev, bnd)
            ffn_evach(cm, half, quant, 0)
            prev_hevac = FST.get(cm - 1, {}).get("hidT_evac", 0)
            if cm == 0:
                prev_hevac = FST[0].get("stash", 0) if half == 0 else bnd["pc_act"]
            if cf is not None:
                ffn_front_T(cf, half, ptt_free=prev_hevac)
            if half == 0 and cf is not None:
                pfree = FST[cf].get("stash", 0)
            else:
                pfree = prev_hevac
            ffn_hidTh(cm, 0, ptt_free=pfree)
            ffn_evach(cm, half, quant, 1)
            ffn_w2h(cm, wb, half, quant, 0, prev, pb_free=bnd["pb_dve"] if cm == 0 else 0)
            ffn_hidTh(cm, 1, ptt_free=FST[cm]["hidT0"])
            ffn_w2h(cm, wb, half, quant, 1, prev)
            ffn_residual(blk, cm, half, quant, last_stage=last_stage, xl_next=xl_next)
        return {"pa_act": max(FST[3]["evA0"][0], FST[3]["evA1"][0]),
                "pa_dve": max(FST[3]["evA0"][1], FST[3]["evA1"][1]),
                "pb_dve": FST[3]["res"], "pc_act": FST[3]["hidT_evac"], "pc_dve": 0}

    prev_pe_done = [0, 0]
    bnd = prep_bnd
    gidx = 0
    xl = {}
    for blk in range(NBLK):
        if blk == 0:
            W("sp", "sDVE", em.val("sDVE"))
            W("sp", "sPOOL", v_prep_lanes["sPOOL"])
            for c in range(NCH):
                dma(XS[:, DM * c:DM * c + DM],
                    bass.AP(xin.tensor, (c * CH) * DM, [[DM, 128], [1, DM]]),
                    inc="sXY%d" % c)
                xl[c] = ("sXY%d" % c, em.val("sXY%d" % c))
        for sidx in range(DBG_STAGES):
            par = gidx % 2
            wsem = "sWL%d" % par
            if gidx in (1, 3):
                W("sp", "sDMA2", v_prep[0])
                W("sp", "sGP", v_prep[1])
            else:
                st_v, gp_v = img_ready[sidx]
                W("sp", "sDMA2", st_v)
                if gp_v:
                    W("sp", "sGP", gp_v)
            for sem_n, v in rows_gate[sidx]:
                W("pe", sem_n, v)
            if blk == 0:
                W("pe", "sPOOL", v_prep_lanes["sPOOL"])
            if prev_pe_done[par] > 0:
                W("sp", "sPE", prev_pe_done[par])
            wl = dma(WB[par][:], simg[sidx][:], inc=wsem)
            if sidx == 0:
                bnd = stage_attn_pipelined(par, (wsem, wl), xl, bnd)
            elif sidx in (1, 2):
                bnd = stage_ffn_pipelined(blk, par, (wsem, wl), sidx - 1, False, bnd)
            else:
                last = (sidx == DBG_STAGES - 1)
                xln = {}
                bnd = stage_ffn_pipelined(blk, par, (wsem, wl), sidx - 3, True, bnd,
                                          last_stage=last, xl_next=xln)
                if last:
                    xl = xln
            prev_pe_done[par] = em.val("sPE")
            gidx += 1

    finals = [("sXY%d" % c, em.val("sXY%d" % c)) for c in range(NCH)]
    for _s, _v in finals:
        em.ops["sp"].append(lambda e, _s=_s, _v=_v: e.wait_ge(em.sems[_s], _v))

    with nc.Block() as block:
        block.sync(lambda e: ([f(e) for f in em.ops["sp"]], None)[1])
        block.tensor(lambda e: ([f(e) for f in em.ops["pe"]], None)[1])
        block.vector(lambda e: ([f(e) for f in em.ops["dve"]], None)[1])
        block.scalar(lambda e: ([f(e) for f in em.ops["act"]], None)[1])
        block.gpsimd(lambda e: ([f(e) for f in em.ops["pool"]], None)[1])

    for c in reversed(ctxs):
        c.__exit__(None, None, None)
    return nc


def _prep_host(inputs):
    pq, pv, pwo = _perms()
    f32 = np.float32
    Wq = np.asarray(inputs["L0_attn_Wq"], f32)
    Wk = np.asarray(inputs["L0_attn_Wk"], f32)
    Wv = np.asarray(inputs["L0_attn_Wv"], f32)
    Wo = np.asarray(inputs["L0_attn_Wo"], f32)
    feed = {
        "wqt": np.ascontiguousarray(Wq[pq, :].T),
        "wkt": np.ascontiguousarray(Wk[pq, :].T),
        "wvt": np.ascontiguousarray(Wv[pv, :].T),
        "wot": np.ascontiguousarray(Wo[:, pwo].T),
        "w1t": np.ascontiguousarray(np.asarray(inputs["L0_attn_W1"], f32).T),
        "w2t": np.ascontiguousarray(np.asarray(inputs["L0_attn_W2"], f32).T),
        "q1t": np.ascontiguousarray(np.asarray(inputs["L0_ffn_w1"], f32).T),
        "q2t": np.ascontiguousarray(np.asarray(inputs["L0_ffn_w2"], f32).T),
        "vb1": np.ascontiguousarray(np.asarray(inputs["L0_attn_b1"], f32).reshape(1, FF)),
        "vfb1": np.ascontiguousarray(np.asarray(inputs["L0_ffn_b1"], f32).reshape(1, FF)),
    }
    vecm = np.zeros((16, DM), f32)
    vecm[0] = np.asarray(inputs["L0_attn_ln1_g"], f32)
    vecm[1] = np.asarray(inputs["L0_attn_ln1_b"], f32)
    vecm[2] = np.asarray(inputs["L0_attn_ln2_g"], f32)
    vecm[3] = np.asarray(inputs["L0_attn_ln2_b"], f32)
    vecm[4] = np.asarray(inputs["L0_attn_bo"], f32)
    vecm[5] = np.asarray(inputs["L0_attn_b2"], f32)
    vecm[6] = np.asarray(inputs["L0_ffn_b2"], f32)
    vecm[7, 0:32] = np.asarray(inputs["L0_attn_lq1"], f32)
    vecm[7, 32:64] = np.asarray(inputs["L0_attn_lq2"], f32)
    vecm[7, 64:96] = np.asarray(inputs["L0_attn_lk1"], f32)
    vecm[7, 96:128] = np.asarray(inputs["L0_attn_lk2"], f32)
    feed["vec"] = vecm
    return feed


def kernel(**inputs):
    x = np.asarray(inputs["L0_attn_x"], np.float32).reshape(-1, DM)
    feed = _prep_host(inputs)
    nc = build_program()
    in_maps = []
    for c in range(N_CORES):
        m = dict(feed)
        m["x"] = np.ascontiguousarray(x[c * T_CORE:(c + 1) * T_CORE])
        in_maps.append(m)
    res = run_bass_kernel_spmd(nc, in_maps, core_ids=list(range(N_CORES)))
    y = np.concatenate([np.asarray(res.results[c]["y"]) for c in range(N_CORES)], axis=0)
    return y.reshape(4, 8192, DM)


# version 12
# speedup vs baseline: 1.3370x; 1.0201x over previous
import numpy as np
import ml_dtypes
import concourse.bass as bass
import concourse.mybir as mybir
from concourse.bass_utils import run_bass_kernel_spmd

F32 = mybir.dt.float32
BF16 = mybir.dt.bfloat16
AF = mybir.ActivationFunctionType
ALU = mybir.AluOpType
AX = mybir.AxisListType

N_CORES = 8
DBG_STAGES = 5
T_CORE = 4096
BLK = 512
CH = 128
NBLK = T_CORE // BLK
NCH = BLK // CH
DM = 1024
FF = 4096
EPS = 1e-5
RM = float(np.float32(1.5 * 2.0 ** 23))
LAMBDA_INIT = 0.8 - 0.6 * float(np.exp(-0.3 * 1.0))
QSCALE = float(32 ** -0.5)


def _perms():
    pq = np.empty(1024, dtype=np.int64)
    for q in range(32):
        s, h = q // 16, q % 16
        for d in range(32):
            pq[q * 32 + d] = h * 64 + d * 2 + s
    pv = np.empty(1024, dtype=np.int64)
    for dd in range(64):
        for h in range(16):
            pv[dd * 16 + h] = h * 64 + dd
    pwo = np.empty(1024, dtype=np.int64)
    for dd in range(64):
        for a in range(16):
            pwo[dd * 16 + a] = a * 64 + dd
    return pq, pv, pwo


class Em:
    def __init__(self, nc):
        self.nc = nc
        self.ops = {k: [] for k in ("pe", "dve", "act", "pool", "sp")}
        self.sems = {}
        self.cnt = {}
        self.hwm = {k: {} for k in self.ops}

    def sem(self, name):
        self.sems[name] = self.nc.semaphore(name).__enter__()
        self.cnt[name] = 0

    def inc(self, name, by=1):
        self.cnt[name] += by
        return self.cnt[name]

    def val(self, name):
        return self.cnt[name]

    def wait(self, eng, name, v):
        if v <= 0 or self.hwm[eng].get(name, 0) >= v:
            return
        self.hwm[eng][name] = v
        s = self.sems[name]
        self.ops[eng].append(lambda e, s=s, v=v: e.wait_ge(s, v))

    def op(self, eng, fn, inc=None, by=1):
        if inc is None:
            self.ops[eng].append(fn)
            return None
        if eng in ("dve", "act", "pool"):
            self.wait(eng, inc, self.val(inc))
        v = self.inc(inc, by)
        s = self.sems[inc]
        self.ops[eng].append(lambda e, fn=fn, s=s, by=by: fn(e).then_inc(s, by))
        return v

    def pesync(self):
        self.wait("pe", "sPE", self.val("sPE"))


def build_program():
    nc = bass.Bass("TRN2", target_bir_lowering=False)

    xin = nc.dram_tensor("x", [T_CORE, DM], F32, kind="ExternalInput").ap()
    yout = nc.dram_tensor("y", [T_CORE, DM], F32, kind="ExternalOutput").ap()
    wqt = nc.dram_tensor("wqt", [DM, DM], F32, kind="ExternalInput").ap()
    wkt = nc.dram_tensor("wkt", [DM, DM], F32, kind="ExternalInput").ap()
    wvt = nc.dram_tensor("wvt", [DM, DM], F32, kind="ExternalInput").ap()
    wot = nc.dram_tensor("wot", [DM, DM], F32, kind="ExternalInput").ap()
    w1t = nc.dram_tensor("w1t", [DM, FF], F32, kind="ExternalInput").ap()
    w2t = nc.dram_tensor("w2t", [FF, DM], F32, kind="ExternalInput").ap()
    q1t = nc.dram_tensor("q1t", [DM, FF], F32, kind="ExternalInput").ap()
    q2t = nc.dram_tensor("q2t", [FF, DM], F32, kind="ExternalInput").ap()
    vec = nc.dram_tensor("vec", [16, DM], F32, kind="ExternalInput").ap()
    vb1 = nc.dram_tensor("vb1", [1, FF], F32, kind="ExternalInput").ap()
    vfb1 = nc.dram_tensor("vfb1", [1, FF], F32, kind="ExternalInput").ap()
    simg = [nc.dram_tensor(f"simg{i}", [128, 32768], BF16, kind="Internal").ap()
            for i in range(5)]

    em = Em(nc)
    for s in ("sDMA", "sPE", "sDVE", "sACT", "sPOOL", "sDMA2", "sWL0", "sWL1", "sXY0", "sXY1", "sXY2", "sXY3", "sGP", "sPF0", "sPF1", "sPF2", "sPF3"):
        em.sem(s)

    ctxs = []

    def sb(name, shape, dtype):
        t = nc.sbuf_tensor(name, shape, dtype)
        h = t.__enter__()
        ctxs.append(t)
        return h

    def psalloc(name, shape, dtype):
        t = nc.psum_tensor(name, shape, dtype)
        h = t.__enter__()
        ctxs.append(t)
        return h

    XS = sb("xs", [128, NCH * DM], F32)
    WB = [sb("wb0", [128, 32768], BF16), sb("wb1", [128, 32768], BF16)]
    UB = sb("ub", [128, DM], BF16)
    UB2 = sb("ub2", [128, DM], BF16)
    HT = sb("ht", [128, 1024], BF16)
    HTB = sb("htb", [128, 4096], BF16)
    T1Q = sb("t1q", [128, 1024], BF16)
    T1K = sb("t1k", [128, 1024], BF16)
    T1VP = sb("t1vp", [128, 2048], BF16)
    ATTN = sb("attn", [128, 1024], BF16)
    IN2 = sb("in2", [128, 1024], BF16)
    AT2T = sb("at2t", [128, 1024], BF16)
    SQ = sb("sq", [128, 2048], F32)
    XR = sb("xr", [128, 2048], BF16)
    T1N = sb("t1n", [128, 1024], BF16)
    HTOK = sb("htok", [128, 2048], BF16)
    QT = sb("qt", [128, 2048], F32)
    BROWS = sb("brows", [128, 4096], BF16)
    IDN = sb("idn", [128, 128], BF16)
    ONESB = sb("onesb", [128, 128], BF16)
    CAB = sb("cab", [128, 32], BF16)
    SM = sb("sm", [128, 16], F32)
    SM2 = sb("sm2", [128, 12], F32)
    SMX = sb("smx", [128, 32], F32)
    FACTB = sb("factb", [128, 32], F32)
    GSL = sb("gsl", [128, 24], F32)
    GSL2 = sb("gsl2", [128, 8], F32)
    GSLQ = sb("gslq", [128, 8], F32)
    GSB = sb("gsb", [128, 16], BF16)
    LAM = sb("lam", [128, 1], F32)
    SMALL = sb("small", [1, 8], F32)
    SMALLB = sb("smallb", [1, 2], BF16)
    CST = sb("cst", [128, 8], F32)

    EXPS = QT[:, 0:1024]
    TRASH = QT[:, 1024:2048]
    SSUB = SQ[:, 0:1024]
    T1O = HTOK
    VR = XS[:, 0:1024]
    VTV = HTB[:, 0:2048]
    VTQ = HTB[:, 2048:3072]
    VTK = HTB[:, 3072:4096]
    NEGMAX = SMX
    RINV = SMX
    FACT = FACTB

    ROWMAP = {
        ("CQ", 0): (0, 0), ("CQ", 1): (0, 512),
        ("CK", 0): (32, 0), ("CK", 1): (32, 512),
        ("CV", 0): (64, 0), ("CV", 1): (64, 512),
        ("CO", 0): (96, 0), ("CO", 1): (96, 512),
    }
    for c in range(8):
        ROWMAP[("Q1R", c)] = (0, 1024 + 512 * c) if c < 4 else (64, 1024 + 512 * (c - 4))
        ROWMAP[("C1", c)] = (32, 1024 + 512 * c) if c < 4 else (96, 1024 + 512 * (c - 4))
    for c in range(2):
        ROWMAP[("C2", c)] = (0, 3072 + 512 * c)
        ROWMAP[("Q2R", c)] = (32, 3072 + 512 * c)

    def brow(name, c):
        p, col = ROWMAP[(name, c)]
        return BROWS[p:p + 1, col:col + 512], p

    PA = psalloc("pa", [128, 2048], F32)
    PB = psalloc("pb", [128, 1024], F32)
    PC = psalloc("pc", [128, 1024], F32)
    PTTA = PA[:, 0:512].bitcast(BF16)
    PTTB = PB[:, 0:512].bitcast(BF16)
    PTTB2 = PB[:, 512:1024].bitcast(BF16)
    PTTD = PC[:, 512:1024].bitcast(BF16)
    TRASHB = QT[:, 1024:1536].bitcast(BF16)
    TRASHB2 = QT[:, 1536:2048].bitcast(BF16)
    PTTC = PC[:, 0:512].bitcast(BF16)


    W = em.wait

    def dma(dst, src, inc="sDMA"):
        return em.op("sp", lambda e: e.dma_start(dst, src), inc=inc, by=16)

    for i, cv in enumerate([RM, -RM, 16.0, -0.5, 64.0, 1.0 / 64.0, -RM / 64.0]):
        em.op("pool", lambda e, i=i, cv=cv: e.memset(CST[:, i:i + 1], float(cv)), inc="sPOOL")
        nc.const_aps.register(F32, float(cv), CST[:, i:i + 1])
    em.op("pool", lambda e: e.memset(ONESB[:], 1.0), inc="sPOOL")
    W("pool", "sPOOL", em.val("sPOOL"))
    em.op("pool", lambda e: e.affine_select(
        IDN[:], ONESB[:], [[-1, 128]], ALU.is_equal, 0.0,
        base=0, channel_multiplier=1), inc="sPOOL")
    em.op("pool", lambda e: e.memset(T1VP[:], 0.0), inc="sPOOL")
    em.op("pool", lambda e: e.memset(IN2[:], 0.0), inc="sPOOL")
    em.op("pool", lambda e: e.memset(XR[:], 0.0), inc="sPOOL")
    em.op("pool", lambda e: e.memset(CAB[:], 0.0), inc="sPOOL")
    em.op("pool", lambda e: e.memset(BROWS[:], 0.0), inc="sPOOL")
    W("pool", "sPOOL", em.val("sPOOL"))
    for c in range(4):
        em.op("pool", lambda e, c=c: e.affine_select(
            CAB[32 * c:32 * c + 32, 0:16], ONESB[32 * c:32 * c + 32, 0:16],
            [[-1, 16]], ALU.is_equal, 0.0, base=0, channel_multiplier=1), inc="sPOOL")
    v_consts = em.val("sPOOL")

    for j in range(8):
        dma(GSL[:, j:j + 1], vec[0:1, 128 * j:128 * j + 128].transpose([1, 0]))
    for j in range(8):
        dma(GSL[:, 8 + j:9 + j], vec[2:3, 128 * j:128 * j + 128].transpose([1, 0]))
    for j in range(8):
        dma(GSL[:, 16 + j:17 + j], vec[1:2, 128 * j:128 * j + 128].transpose([1, 0]))
    for j in range(8):
        dma(GSL2[:, j:j + 1], vec[3:4, 128 * j:128 * j + 128].transpose([1, 0]))
    v_gsl = em.val("sDMA")
    dma(VR[0:1, :], vec[4:5, :])
    dma(VR[32:33, :], vec[5:6, :])
    dma(VR[64:65, :], vec[6:7, :])
    dma(VR[96:97, 0:128], vec[7:8, 0:128])
    v_vr = em.val("sDMA")


    W("dve", "sDMA", v_vr)
    em.op("dve", lambda e: e.tensor_mul(VR[96:97, 256:288], VR[96:97, 0:32], VR[96:97, 64:96]), inc="sDVE")
    em.op("dve", lambda e: e.tensor_mul(VR[96:97, 288:320], VR[96:97, 32:64], VR[96:97, 96:128]), inc="sDVE")
    em.op("dve", lambda e: e.reduce_sum(VR[96:97, 320:321], VR[96:97, 256:288].rearrange("p (a b) -> p a b", a=1), axis=AX.X), inc="sDVE")
    em.op("dve", lambda e: e.reduce_sum(VR[96:97, 321:322], VR[96:97, 288:320].rearrange("p (a b) -> p a b", a=1), axis=AX.X), inc="sDVE")
    W("act", "sDVE", em.val("sDVE"))
    em.op("act", lambda e: e.activation(VR[96:97, 322:323], VR[96:97, 320:321], AF.Exp), inc="sACT")
    em.op("act", lambda e: e.activation(VR[96:97, 323:324], VR[96:97, 321:322], AF.Exp), inc="sACT")
    W("dve", "sACT", em.val("sACT"))
    em.op("dve", lambda e: e.tensor_sub(VR[96:97, 324:325], VR[96:97, 323:324], VR[96:97, 322:323]), inc="sDVE")
    em.op("dve", lambda e: e.tensor_scalar(SMALL[0:1, 0:1], VR[96:97, 324:325], float(LAMBDA_INIT), None, ALU.subtract), inc="sDVE")
    em.op("dve", lambda e: e.tensor_copy(SMALLB[0:1, 0:1], SMALL[0:1, 0:1]), inc="sDVE")
    W("pe", "sDVE", em.val("sDVE"))
    W("pe", "sPOOL", v_consts)
    em.op("pe", lambda e: e.matmul(PA[:, 0:1], ONESB[0:1, 0:128], SMALLB[0:1, 0:1],
                                   start=True, stop=True), inc="sPE")
    W("dve", "sPE", em.val("sPE"))
    em.op("dve", lambda e: e.tensor_copy(LAM[:, 0:1], PA[:, 0:1]), inc="sDVE")
    W("pool", "sDVE", em.val("sDVE"))
    W("pool", "sPOOL", v_consts)
    for c in range(4):
        em.op("pool", lambda e, c=c: e.affine_select(
            CAB[32 * c:32 * c + 32, 16:32],
            LAM[32 * c:32 * c + 32, 0:1].broadcast_to([32, 16]),
            [[-1, 16]], ALU.is_equal, 0.0, base=-16, channel_multiplier=1), inc="sPOOL")
    v_cab = em.val("sPOOL")

    def gdma(dst, src):
        return em.op("pool", lambda e: e.dma_start(dst, src), inc="sGP", by=16)

    for j in range(8):
        gdma(simg[0][:, 24576 + 1024 * j:24576 + 1024 * j + 1024], wot[128 * j:128 * j + 128, :])
    gp_img0 = em.val("sGP")
    gp_img = {1: 0, 2: 0}

    img_ready = {}
    rows_gate = {}
    HTBF = HTB[:].bitcast(F32)
    ring = [SQ[:, 0:2048], QT[:, 0:2048], HTBF[:, 0:1024], HTBF[:, 1024:2048]]
    ring_w = [2048, 2048, 1024, 1024]
    rs = {"i": 0, "done": {}}

    def _staged(src_ap, width, consume, row):
        while True:
            s = rs["i"] % 4
            rs["i"] += 1
            if ring_w[s] >= width:
                break
        psem = "sPF%d" % s
        buf = ring[s][0:1, 0:width] if row else ring[s][:, 0:width]
        if s in rs["done"]:
            for sem_n, v in rs["done"][s]:
                W("sp", sem_n, v)
        vd = dma(buf, src_ap, inc=psem)
        rs["done"][s] = consume(buf, vd, psem)

    def staged(src_ap, width, consume):
        _staged(src_ap, width, consume, False)

    def staged_row(src_ap, width, consume):
        _staged(src_ap, width, consume, True)

    def staged_q(src_ap, consume):
        while True:
            s = rs["i"] % 4
            rs["i"] += 1
            if ring_w[s] >= 2048:
                break
        psem = "sPF%d" % s
        slot = ring[s]
        if s in rs["done"]:
            for sem_n, v in rs["done"][s]:
                W("sp", sem_n, v)
        vd = dma(slot[:, 0:1024], src_ap, inc=psem)
        rs["done"][s] = consume(slot, vd, psem)

    def prep():
        W("dve", "sDMA", v_vr)
        W("act", "sDMA", v_vr)
        W("pe", "sDMA", v_vr)
        W("pe", "sPOOL", v_consts)
        em.op("dve", lambda e: e.tensor_scalar(GSLQ[:], GSL[:, 0:8], QSCALE, None, ALU.mult), inc="sDVE")
        v_gslq = em.val("sDVE")

        pa_busy = {"v": em.val("sDVE")}
        for mi, wt in enumerate([wqt, wkt, wvt]):
            for j in range(8):
                def consume(buf, vd, psem, mi=mi, j=j):
                    col = mi * 8192 + j * 1024
                    scap = GSLQ[:, j:j + 1] if mi == 0 else GSL[:, j:j + 1]
                    W("act", psem, vd)
                    if mi == 0:
                        W("act", "sDVE", v_gslq)
                    em.op("act", lambda e, col=col, scap=scap: e.activation(
                        WB[1][:, col:col + 1024], buf, AF.Copy, scale=scap), inc="sACT")
                    va = em.val("sACT")
                    W("pe", psem, vd)
                    if j == 0:
                        W("pe", "sDVE", pa_busy["v"])
                    for hh in range(2):
                        em.op("pe", lambda e, j=j, hh=hh: e.matmul(
                            PA[0:1, 512 * hh:512 * hh + 512], GSL[:, 16 + j:17 + j],
                            buf[:, 512 * hh:512 * hh + 512],
                            start=(j == 0), stop=(j == 7)), inc="sPE")
                    return [("sACT", va), ("sPE", em.val("sPE"))]
                staged(wt[128 * j:128 * j + 128, :], 1024, consume)
            W("dve", "sPE", em.val("sPE"))
            nm = ["CQ", "CK", "CV"][mi]
            sc = QSCALE if mi == 0 else 1.0
            for c in range(2):
                r, _ = brow(nm, c)
                em.op("dve", lambda e, r=r, c=c, sc=sc: e.tensor_scalar(
                    r, PA[0:1, 512 * c:512 * c + 512], float(sc), None, ALU.mult), inc="sDVE")
            pa_busy["v"] = em.val("sDVE")
        W("pool", "sDMA", v_vr)
        for c in range(2):
            r, _ = brow("CO", c)
            em.op("pool", lambda e, r=r, c=c: e.tensor_copy(r, VR[0:1, 512 * c:512 * c + 512]), inc="sPOOL")
            r2, _ = brow("C2", c)
            em.op("pool", lambda e, r=r2, c=c: e.tensor_copy(r, VR[32:33, 512 * c:512 * c + 512]), inc="sPOOL")
        v_vr_done = em.val("sPOOL")
        def qrow_consume(buf, vd, psem, names):
            W("pool", psem, vd)
            em.op("pool", lambda e: e.tensor_scalar(buf, buf, 64.0, RM, ALU.mult, ALU.add), inc="sPOOL")
            em.op("pool", lambda e: e.tensor_scalar(buf, buf, RM, -128.0, ALU.subtract, ALU.max), inc="sPOOL")
            for c, (nm, ch) in enumerate(names):
                r, _ = brow(nm, ch)
                em.op("pool", lambda e, r=r, c=c: e.tensor_scalar(
                    r, buf[0:1, 512 * c:512 * c + 512], 127.0, 1.0 / 64.0, ALU.min, ALU.mult), inc="sPOOL")
            return [("sPOOL", em.val("sPOOL"))]
        staged_row(vec[6:7, :], 1024, lambda buf, vd, psem: qrow_consume(buf, vd, psem, [("Q2R", 0), ("Q2R", 1)]))
        for hh in range(2):
            staged_row(vfb1[0:1, 2048 * hh:2048 * hh + 2048], 2048,
                       lambda buf, vd, psem, hh=hh: qrow_consume(buf, vd, psem, [("Q1R", hh * 4 + c) for c in range(4)]))
        rows_gate[0] = [("sDVE", pa_busy["v"]), ("sPOOL", v_vr_done)]
        rows_gate[3] = rows_gate[4] = [("sPOOL", em.val("sPOOL")), ("sPOOL", v_vr_done)]
        W("sp", "sACT", em.val("sACT"))
        W("sp", "sDMA2", em.val("sDMA2"))
        vstore = dma(simg[0][:, 0:24576], WB[1][:, 0:24576], inc="sDMA2")
        img_ready[0] = (vstore, gp_img0)

        for half in range(2):
            W("act", "sDMA2", em.val("sDMA2"))
            for j in range(8):
                def consume(buf, vd, psem, j=j, half=half):
                    col = j * 2048
                    W("act", psem, vd)
                    em.op("act", lambda e, col=col, j=j: e.activation(
                        WB[1][:, col:col + 2048], buf, AF.Copy, scale=GSL[:, 8 + j:9 + j]), inc="sACT")
                    va = em.val("sACT")
                    W("pe", psem, vd)
                    if j == 0:
                        W("pe", "sDVE", pa_busy["v"])
                    for qo in range(4):
                        em.op("pe", lambda e, j=j, qo=qo: e.matmul(
                            PA[0:1, 512 * qo:512 * qo + 512], GSL2[:, j:j + 1],
                            buf[:, 512 * qo:512 * qo + 512],
                            start=(j == 0), stop=(j == 7)), inc="sPE")
                    return [("sACT", va), ("sPE", em.val("sPE"))]
                staged(w1t[128 * j:128 * j + 128, 2048 * half:2048 * half + 2048], 2048, consume)
            for j in range(16):
                lane = ("act", "dve", "pool")[j % 3]
                def consume_w2(buf, vd, psem, j=j, lane=lane):
                    lsem = {"act": "sACT", "dve": "sDVE", "pool": "sPOOL"}[lane]
                    W(lane, psem, vd)
                    dst = WB[1][:, 16384 + 1024 * j:16384 + 1024 * j + 1024]
                    if lane == "act":
                        em.op("act", lambda e: e.copy(dst, buf), inc="sACT")
                    else:
                        em.op(lane, lambda e: e.tensor_copy(dst, buf), inc=lsem)
                    return [(lsem, em.val(lsem))]
                staged(w2t[2048 * half + 128 * j:2048 * half + 128 * j + 128, :], 1024, consume_w2)
            def consume_b(buf, vd, psem, half=half):
                W("dve", psem, vd)
                W("dve", "sPE", em.val("sPE"))
                for qo in range(4):
                    r, _ = brow("C1", half * 4 + qo)
                    em.op("dve", lambda e, r=r, qo=qo, buf=buf: e.tensor_tensor(
                        r, buf[0:1, 512 * qo:512 * qo + 512],
                        PA[0:1, 512 * qo:512 * qo + 512], ALU.add), inc="sDVE")
                return [("sDVE", em.val("sDVE"))]
            staged_row(vb1[0:1, 2048 * half:2048 * half + 2048], 2048, consume_b)
            pa_busy["v"] = em.val("sDVE")
            W("sp", "sACT", em.val("sACT"))
            W("sp", "sDVE", em.val("sDVE"))
            W("sp", "sPOOL", em.val("sPOOL"))
            W("sp", "sDMA2", em.val("sDMA2"))
            vstore = dma(simg[1 + half][:], WB[1][:], inc="sDMA2")
            img_ready[1 + half] = (vstore, gp_img[1 + half])
        rows_gate[1] = rows_gate[2] = [("sDVE", pa_busy["v"])]

        def quant_tile(src_ap, col):
            def consume(slot, vd, psem, col=col):
                a, b = slot[:, 0:1024], slot[:, 1024:2048]
                W("act", psem, vd)
                em.op("act", lambda e, a=a, b=b: e.activation(
                    b, a, AF.Identity, bias=RM, scale=64.0), inc="sACT")
                va = em.val("sACT")
                W("dve", "sACT", va)
                em.op("dve", lambda e, b=b: e.tensor_scalar(
                    b, b, RM - 128.0, RM + 127.0, ALU.max, ALU.min), inc="sDVE")
                vd2 = em.val("sDVE")
                W("act", "sDVE", vd2)
                em.op("act", lambda e, b=b, col=col: e.activation(
                    WB[1][:, col:col + 1024], b, AF.Identity, bias=-RM / 64.0, scale=1.0 / 64.0), inc="sACT")
                return [("sACT", em.val("sACT")), ("sDVE", vd2)]
            staged_q(src_ap, consume)
        for half in range(2):
            W("dve", "sDMA2", em.val("sDMA2"))
            W("pool", "sDMA2", em.val("sDMA2"))
            W("act", "sDMA2", em.val("sDMA2"))
            for j in range(8):
                for k in range(2):
                    quant_tile(q1t[128 * j:128 * j + 128, 2048 * half + 1024 * k:2048 * half + 1024 * k + 1024],
                               j * 2048 + 1024 * k)
            for j in range(16):
                quant_tile(q2t[2048 * half + 128 * j:2048 * half + 128 * j + 128, :], 16384 + j * 1024)
            W("sp", "sDVE", em.val("sDVE"))
            W("sp", "sPOOL", em.val("sPOOL"))
            W("sp", "sACT", em.val("sACT"))
            W("sp", "sDMA2", em.val("sDMA2"))
            vstore = dma(simg[3 + half][:], WB[1][:], inc="sDMA2")
            img_ready[3 + half] = (vstore, 0)

    prep()
    v_prep = (em.val("sDMA2"), em.val("sGP"))
    prep_bnd = {"pa_dve": em.val("sDVE"), "pa_act": 0, "pb_dve": 0, "pc_dve": 0, "pc_act": 0}
    v_prep_lanes = {s: em.val(s) for s in ("sDVE", "sPOOL", "sACT", "sPE")}

    for eng in ("pe", "dve", "act"):
        W(eng, "sPOOL", v_cab)

    def ln_stats(xap, compound):
        em.op("dve", lambda e: e.reduce_sum(SM[:, 0:1], xap.rearrange("p (a b) -> p a b", a=1), axis=AX.X), inc="sDVE")
        em.op("act", lambda e: e.activation(TRASH, xap, AF.Square, accum_out=SM2[:, 0:1]), inc="sACT")
        W("dve", "sACT", em.val("sACT"))
        em.op("dve", lambda e: e.tensor_scalar(SM[:, 0:1], SM[:, 0:1], 1.0 / 1024.0, None, ALU.mult), inc="sDVE")
        em.op("dve", lambda e: e.tensor_mul(SM2[:, 1:2], SM[:, 0:1], SM[:, 0:1]), inc="sDVE")
        em.op("dve", lambda e: e.tensor_scalar(SM2[:, 2:3], SM2[:, 0:1], 1.0 / 1024.0, None, ALU.mult), inc="sDVE")
        em.op("dve", lambda e: e.tensor_sub(SM2[:, 3:4], SM2[:, 2:3], SM2[:, 1:2]), inc="sDVE")
        em.op("dve", lambda e: e.tensor_scalar(SM2[:, 4:5], SM2[:, 3:4], EPS, None, ALU.add), inc="sDVE")
        W("act", "sDVE", em.val("sDVE"))
        em.op("act", lambda e: e.activation(SM2[:, 5:6], SM2[:, 4:5], AF.Ln), inc="sACT")
        em.op("act", lambda e: e.activation(SM[:, 1:2], SM2[:, 5:6], AF.Exp, scale=-0.5), inc="sACT")
        if compound:
            em.op("dve", lambda e: e.reciprocal(SM2[:, 6:7], SM2[:, 4:5]), inc="sDVE")
            em.op("dve", lambda e: e.tensor_mul(SM2[:, 7:8], SM2[:, 3:4], SM2[:, 6:7]), inc="sDVE")
            em.op("dve", lambda e: e.tensor_scalar(SM2[:, 8:9], SM2[:, 7:8], EPS, None, ALU.add), inc="sDVE")
            W("act", "sDVE", em.val("sDVE"))
            em.op("act", lambda e: e.activation(SM2[:, 9:10], SM2[:, 8:9], AF.Ln), inc="sACT")
            W("dve", "sACT", em.val("sACT"))
            em.op("dve", lambda e: e.tensor_add(SM2[:, 10:11], SM2[:, 5:6], SM2[:, 9:10]), inc="sDVE")
            W("act", "sDVE", em.val("sDVE"))
            em.op("act", lambda e: e.activation(SM[:, 2:3], SM2[:, 10:11], AF.Exp, scale=-0.5), inc="sACT")
        return em.val("sACT")

    def transposes(tiles, dst_fn, ntiles, ptt, ptt2=None):
        em.pesync()
        evac_hwm = {0: 0, 1: 0}
        gvals = []
        for j in range(ntiles):
            g = j // 4
            par = g % 2 if ptt2 is not None else 0
            buf = ptt2 if par == 1 else ptt
            if j % 4 == 0 and evac_hwm[par]:
                W("pe", "sACT", evac_hwm[par])
            em.op("pe", lambda e, j=j, buf=buf: e.transpose(
                buf[:, 128 * (j % 4):128 * (j % 4) + 128], tiles(j), IDN[:]), inc="sPE")
            if j % 4 == 3:
                W("act", "sPE", em.val("sPE"))
                em.op("act", lambda e, g=g, buf=buf: e.copy(dst_fn(g), buf[:, 0:512]), inc="sACT")
                evac_hwm[par] = em.val("sACT")
                gvals.append(em.val("sACT"))
        return em.val("sACT"), gvals

    def bias_mm(ps_ap, name, chunk, start=False, stop=True):
        r, p = brow(name, chunk)
        em.op("pe", lambda e, r=r, p=p, ps_ap=ps_ap: e.matmul(
            ps_ap, ONESB[p:p + 1, 0:128], r, start=start, stop=stop,
            tile_position=(p, 0)), inc="sPE")

    def ubp(c):
        return UB if c % 2 == 0 else UB2

    AST = {}

    def attn_front(c, xl):
        st = AST.setdefault(c, {})
        xap = XS[:, DM * c:DM * c + DM]
        sx, vx = xl[c]
        W("dve", sx, vx)
        W("act", sx, vx)
        W("pool", sx, vx)
        if c - 3 in AST and "applied_p" in AST[c - 3]:
            W("act", "sPOOL", AST[c - 3]["applied_p"])
            W("act", "sDVE", AST[c - 3]["applied_d"])
        if c - 2 in AST and "ht_pe" in AST[c - 2]:
            W("pool", "sPE", AST[c - 2]["ht_pe"])
        if c - 1 in AST and "ub" in AST[c - 1]:
            W("dve", "sPOOL", AST[c - 1]["ub"])
            W("act", "sPOOL", AST[c - 1]["ub"])
        va = ln_stats(xap, compound=True)
        W("pool", "sACT", va)
        W("pool", "sDVE", em.val("sDVE"))
        em.op("pool", lambda e: e.tensor_scalar(ubp(c)[:], xap, SM[:, 0:1], SM[:, 2:3], ALU.subtract, ALU.mult), inc="sPOOL")
        p2 = 8 + 2 * c
        em.op("pool", lambda e, p2=p2: e.tensor_copy(SM[:, p2:p2 + 2], SM[:, 0:2]), inc="sPOOL")
        st["ub"] = em.val("sPOOL")

    def attn_fq(c, wb, v_wload, prev, bnd=None):
        st = AST[c]
        if prev is not None:
            W("pe", "sDVE", AST[prev]["t1_dve"])
            W("pe", "sACT", max(AST[prev]["t1_act"], AST[prev]["exp"]))
        else:
            W("pe", "sDVE", max(bnd["pa_dve"], bnd["pb_dve"]))
            W("pe", "sACT", bnd["pa_act"])
        W("pe", "sPOOL", st["ub"])
        va, htg = transposes(lambda j: ubp(c)[:, 128 * j:128 * j + 128],
                             lambda g: HT[:, 512 * g:512 * g + 512], 8, PTTB, PTTB2)
        st["ht_pe"] = em.val("sPE")
        W("pe", v_wload[0], v_wload[1])
        em.pesync()
        for j in range(8):
            if j % 4 == 0:
                W("pe", "sACT", htg[j // 4])
            for mi in range(2):
                for half in range(2):
                    em.op("pe", lambda e, j=j, mi=mi, half=half: e.matmul(
                        PA[:, 1024 * mi + 512 * half:1024 * mi + 512 * half + 512],
                        HT[:, 128 * j:128 * j + 128],
                        WB[wb][:, 8192 * mi + 1024 * j + 512 * half:8192 * mi + 1024 * j + 512 * half + 512],
                        start=(j == 0), stop=False), inc="sPE")
        for j in range(8):
            for half in range(2):
                em.op("pe", lambda e, j=j, half=half: e.matmul(
                    PB[:, 512 * half:512 * half + 512],
                    HT[:, 128 * j:128 * j + 128],
                    WB[wb][:, 16384 + 1024 * j + 512 * half:16384 + 1024 * j + 512 * half + 512],
                    start=(j == 0), stop=False), inc="sPE")
        for mi, nm in enumerate(["CQ", "CK", "CV"]):
            dstps, dcol = (PA, 1024 * mi) if mi < 2 else (PB, 0)
            for half in range(2):
                bias_mm(dstps[:, dcol + 512 * half:dcol + 512 * half + 512], nm, half)
        st["qkv"] = em.val("sPE")

    def attn_evacs(c, prevst=None):
        st = AST[c]
        W("dve", "sPE", st["qkv"])
        W("act", "sPE", st["qkv"])
        if prevst is not None:
            W("act", "sDVE", prevst["vt"])
        em.op("dve", lambda e: e.tensor_copy(T1Q[:], PA[:, 0:1024]), inc="sDVE")
        em.op("act", lambda e: e.copy(T1K[:], PA[:, 1024:2048]), inc="sACT")
        st["t1_act"] = em.val("sACT")
        em.op("dve", lambda e: e.tensor_copy(
            T1VP[:].rearrange("p (d h) -> p d h", d=64)[:, :, 0:16], PB[:, 0:1024]), inc="sDVE")
        st["t1_dve"] = em.val("sDVE")

    def attn_vtrans_qk(c, prevst=None):
        st = AST[c]
        if prevst is not None:
            W("dve", "sPE", prevst["scores"])
        W("dve", "sACT", st["t1_act"])
        em.op("dve", lambda e: e.transpose(VTQ[:], T1Q[:]), inc="sDVE")
        em.op("dve", lambda e: e.transpose(VTK[:], T1K[:]), inc="sDVE")
        st["vt"] = em.val("sDVE")

    def attn_vtrans_v(c, prevst=None):
        st = AST[c]
        if prevst is not None:
            W("dve", "sPE", prevst["o2"])
        em.op("dve", lambda e: e.transpose(VTV[:], T1VP[:]), inc="sDVE")
        st["vtv"] = em.val("sDVE")

    def attn_scores(c):
        st = AST[c]
        W("pe", "sDVE", st["vt"])
        W("pe", "sACT", st["t1_act"])
        em.pesync()
        for i in range(128):
            r, tl = i % 4, i // 4
            em.op("pe", lambda e, r=r, tl=tl: e.matmul(
                PA[32 * r:32 * r + 32, 32 * tl:32 * tl + 32],
                VTQ[32 * r:32 * r + 32, :].rearrange("p (w cc) -> p w cc", cc=32)[:, :, tl],
                VTK[32 * r:32 * r + 32, :].rearrange("p (w cc) -> p w cc", cc=32)[:, :, tl],
                start=True, stop=True, tile_position=(32 * r, 32 * r)), inc="sPE")
        st["scores"] = em.val("sPE")

    def attn_softmax(c):
        st = AST[c]
        W("act", "sPE", st["scores"])
        em.op("act", lambda e: e.activation(EXPS, PA[:, 0:1024], AF.Exp), inc="sACT")
        st["exp"] = em.val("sACT")
        W("dve", "sACT", st["exp"])
        em.op("dve", lambda e: e.reduce_sum(
            RINV[:], EXPS.rearrange("p (t k) -> p t k", k=32), axis=AX.X), inc="sDVE")
        em.op("dve", lambda e: e.reciprocal(RINV[:], RINV[:]), inc="sDVE")
        st["rinv"] = em.val("sDVE")
        W("pool", "sDVE", st["rinv"])
        em.op("pool", lambda e: e.tensor_tensor(
            ATTN[:].rearrange("p (t k) -> p t k", k=32),
            EXPS.rearrange("p (t k) -> p t k", k=32),
            RINV[:].unsqueeze(2).broadcast_to([128, 32, 32]), ALU.mult), inc="sPOOL")
        st["attn"] = em.val("sPOOL")

    def attn_combine(c, prev, bnd=None):
        st = AST[c]
        W("pe", "sPOOL", st["attn"])
        W("pe", "sPOOL", v_cab)
        if prev is not None and "conn" in AST.get(prev, {}):
            W("pe", "sDVE", AST[prev]["conn"])
        elif bnd is not None:
            W("pe", "sDVE", bnd["pc_dve"])
            W("pe", "sACT", bnd["pc_act"])
        em.pesync()
        for r in range(4):
            for half in range(2):
                em.op("pe", lambda e, r=r, half=half: e.matmul(
                    PC[32 * r:32 * r + 16, 0:512],
                    CAB[32 * r:32 * r + 32, 16 * half:16 * half + 16],
                    ATTN[32 * r:32 * r + 32, :].rearrange("p (t k) -> p t k", k=32)[:, :, 16 * half:16 * half + 16],
                    start=(half == 0), stop=(half == 1), tile_position=(32 * r, 32 * r)), inc="sPE")
        st["comb"] = em.val("sPE")

    def attn_in2(c):
        st = AST[c]
        W("act", "sPE", st["comb"])
        for r in range(4):
            em.op("act", lambda e, r=r: e.copy(
                IN2[32 * r:32 * r + 16, :].rearrange("p (t k) -> p t k", k=32)[:, :, 0:16],
                PC[32 * r:32 * r + 16, 0:512]), inc="sACT")
        st["in2"] = em.val("sACT")
        W("dve", "sACT", st["in2"])
        em.op("dve", lambda e: e.transpose(AT2T[:], IN2[:]), inc="sDVE")
        st["at2t"] = em.val("sDVE")

    def attn_einsum2(c):
        st = AST[c]
        W("pe", "sDVE", max(st["at2t"], st["vtv"]))
        W("pe", "sACT", st["in2"])
        for h in range(2):
            em.pesync()
            for i in range(64):
                r, tj = i % 4, i // 4
                tl = 16 * h + tj
                em.op("pe", lambda e, r=r, tl=tl, tj=tj: e.matmul(
                    PC[32 * r:32 * r + 32, 64 * tj:64 * tj + 64],
                    AT2T[32 * r:32 * r + 32, 32 * tl:32 * tl + 32],
                    VTV[32 * r:32 * r + 32, :].rearrange("p (w cc) -> p w cc", cc=32)[:, :, tl],
                    start=True, stop=True, tile_position=(32 * r, 32 * r)), inc="sPE")
            W("act", "sPE", em.val("sPE"))
            dstb = TRASHB if h == 0 else TRASHB2
            em.op("act", lambda e, dstb=dstb: e.copy(dstb, PC[:]), inc="sACT")
            em.op("act", lambda e, h=h: e.activation(
                SQ[:, 1024 * h:1024 * h + 1024], PC[:], AF.Square), inc="sACT")
            st["o2e%d" % h] = em.val("sACT")
            if h == 0:
                W("pe", "sACT", st["o2e0"])
        st["o2"] = em.val("sPE")

    def attn_rms(c):
        st = AST[c]
        W("dve", "sACT", st["o2e0"])
        em.op("dve", lambda e: e.reduce_sum(
            FACT[:, 0:16], SQ[:, 0:1024].rearrange("p (t d) -> p t d", d=64), axis=AX.X), inc="sDVE")
        W("dve", "sACT", st["o2e1"])
        em.op("dve", lambda e: e.reduce_sum(
            FACT[:, 16:32], SQ[:, 1024:2048].rearrange("p (t d) -> p t d", d=64), axis=AX.X), inc="sDVE")
        em.op("dve", lambda e: e.tensor_scalar(FACT[:], FACT[:], 1.0 / 64.0, EPS, ALU.mult, ALU.add), inc="sDVE")
        W("act", "sDVE", em.val("sDVE"))
        em.op("act", lambda e: e.activation(FACT[:], FACT[:], AF.Ln), inc="sACT")
        em.op("act", lambda e: e.activation(FACT[:], FACT[:], AF.Exp, scale=-0.5), inc="sACT")
        W("dve", "sACT", em.val("sACT"))
        em.op("dve", lambda e: e.tensor_scalar(FACT[:], FACT[:], float(1.0 - LAMBDA_INIT), None, ALU.mult), inc="sDVE")
        vsm = em.val("sDVE")
        em.op("dve", lambda e: e.tensor_tensor(
            bass.AP(XR[:].tensor, XR[:].offset, [[2048, 128], [1, 16], [32, 64]]),
            TRASHB.rearrange("p (t d) -> p t d", d=64),
            FACT[:, 0:16].unsqueeze(2).broadcast_to([128, 16, 64]), ALU.mult), inc="sDVE")
        st["applied_d"] = em.val("sDVE")
        W("pool", "sDVE", vsm)
        W("pool", "sACT", st["o2e1"])
        em.op("pool", lambda e: e.tensor_tensor(
            bass.AP(XR[:].tensor, XR[:].offset + 16, [[2048, 128], [1, 16], [32, 64]]),
            TRASHB2.rearrange("p (t d) -> p t d", d=64),
            FACT[:, 16:32].unsqueeze(2).broadcast_to([128, 16, 64]), ALU.mult), inc="sPOOL")
        st["applied_p"] = em.val("sPOOL")
        W("dve", "sPOOL", st["applied_p"])
        em.op("dve", lambda e: e.transpose(T1O[:], XR[:]), inc="sDVE")
        em.op("dve", lambda e: e.tensor_copy(
            T1N[:].rearrange("p (d a) -> p d a", a=16),
            T1O[:].rearrange("p (d a) -> p d a", a=32)[:, :, 0:16]), inc="sDVE")
        st["t1n"] = em.val("sDVE")

    def attn_fmwo(c, wb):
        st = AST[c]
        W("pe", "sDVE", st["t1n"])
        if c - 1 in AST and "fm" in AST[c - 1]:
            W("pe", "sACT", AST[c - 1]["fm"])
        va, fmg = transposes(lambda j: T1N[:, 128 * j:128 * j + 128],
                             lambda g: HT[:, 512 * g:512 * g + 512], 8, PTTC, PTTD)
        st["fm"] = va
        W("pe", "sACT", va)
        em.pesync()
        for j in range(8):
            for half in range(2):
                em.op("pe", lambda e, j=j, half=half: e.matmul(
                    PC[:, 512 * half:512 * half + 512],
                    HT[:, 128 * j:128 * j + 128],
                    WB[wb][:, 24576 + 1024 * j + 512 * half:24576 + 1024 * j + 512 * half + 512],
                    start=(j == 0), stop=False), inc="sPE")
        for half in range(2):
            bias_mm(PC[:, 512 * half:512 * half + 512], "CO", half)
        st["wo"] = em.val("sPE")

    def attn_conn(c):
        st = AST[c]
        xap = XS[:, DM * c:DM * c + DM]
        p2 = 8 + 2 * c
        em.op("pool", lambda e, p2=p2: e.tensor_scalar(
            SSUB[:], xap, SM[:, p2:p2 + 1], SM[:, p2 + 1:p2 + 2], ALU.subtract, ALU.mult), inc="sPOOL")
        vss = em.val("sPOOL")
        W("dve", "sPE", st["wo"])
        W("dve", "sPOOL", vss)
        em.op("dve", lambda e: e.tensor_tensor(xap, SSUB[:], PC[:], ALU.add), inc="sDVE")
        st["conn"] = em.val("sDVE")

    def stage_attn_pipelined(wb, v_wload, xl, bnd):
        AST.clear()
        attn_front(0, xl)
        attn_fq(0, wb, v_wload, None, bnd)
        attn_front(1, xl)
        attn_evacs(0)
        attn_vtrans_qk(0)
        attn_vtrans_v(0)
        attn_scores(0)
        attn_softmax(0)
        attn_fq(1, wb, v_wload, 0)
        for k in range(NCH):
            if k + 1 < NCH:
                attn_evacs(k + 1, AST[k])
                attn_vtrans_qk(k + 1, AST[k])
            if k + 2 < NCH:
                attn_front(k + 2, xl)
            attn_combine(k, k - 1 if k > 0 else None, bnd)
            attn_in2(k)
            if k + 1 < NCH:
                attn_scores(k + 1)
            attn_einsum2(k)
            if k + 1 < NCH:
                attn_vtrans_v(k + 1, AST[k])
            attn_rms(k)
            if k + 1 < NCH:
                attn_softmax(k + 1)
            if k + 2 < NCH:
                attn_fq(k + 2, wb, v_wload, k + 1)
            attn_fmwo(k, wb)
            attn_conn(k)
        return {"pa_dve": AST[3]["t1_dve"], "pa_act": AST[3]["exp"],
                "pb_dve": AST[3]["t1_dve"], "pc_dve": AST[3]["conn"], "pc_act": 0}

    FST = {}

    def ffn_front(blk, c, half, quant):
        st = FST.setdefault(c, {})
        xap = XS[:, DM * c:DM * c + DM]
        if half == 1:
            return
        W("act", "sDVE", em.val("sDVE"))
        va = ln_stats(xap, compound=False)
        W("dve", "sACT", va)
        if not quant:
            em.op("dve", lambda e: e.tensor_scalar(
                ubp(c)[:], xap, SM[:, 0:1], SM[:, 1:2], ALU.subtract, ALU.mult), inc="sDVE")
        else:
            em.op("dve", lambda e: e.tensor_scalar(SM[:, 3:4], SM[:, 1:2], 16.0, None, ALU.mult), inc="sDVE")
            em.op("dve", lambda e: e.tensor_scalar(
                SSUB[:], xap, SM[:, 0:1], SM[:, 3:4], ALU.subtract, ALU.mult), inc="sDVE")
            em.op("dve", lambda e: e.tensor_scalar(
                SSUB[:], SSUB[:], RM, RM, ALU.add, ALU.subtract), inc="sDVE")
            em.op("dve", lambda e: e.tensor_scalar(
                SSUB[:], SSUB[:], -128.0, 127.0, ALU.max, ALU.min), inc="sDVE")
            em.op("dve", lambda e: e.tensor_scalar(
                ubp(c)[:], SSUB[:], 1.0 / 16.0, None, ALU.mult), inc="sDVE")
        st["ub"] = em.val("sDVE")

    def ffn_front_T(c, half, ptt_free=0):
        st = FST[c]
        if half == 1:
            return
        W("pe", "sDVE", st["ub"])
        if ptt_free:
            W("pe", "sACT", ptt_free)
        va, gg = transposes(lambda j: ubp(c)[:, 128 * j:128 * j + 128],
                            lambda g, c=c: HTB[:, 1024 * c + 512 * g:1024 * c + 512 * g + 512], 8, PTTC, PTTD)
        st["stash"] = va
        st["stash_g"] = gg
        st["stash_pe"] = em.val("sPE")

    def ffn_w1h(c, wb, v_wload, half, quant, hh, prev, bnd=None):
        st = FST[c]
        if hh == 0:
            W("pe", v_wload[0], v_wload[1])
        if prev is not None and ("evA%d" % hh) in FST.get(prev, {}):
            a, d = FST[prev]["evA%d" % hh]
            W("pe", "sACT", a)
            if d:
                W("pe", "sDVE", d)
        elif bnd is not None:
            W("pe", "sACT", bnd["pa_act"])
            W("pe", "sDVE", bnd["pa_dve"])
        em.pesync()
        base0 = 1024 * c
        sg = st.get("stash_g")
        for j in range(8):
            if sg is not None and j % 4 == 0:
                W("pe", "sACT", sg[j // 4])
            for qo in (2 * hh, 2 * hh + 1):
                em.op("pe", lambda e, j=j, qo=qo, base0=base0: e.matmul(
                    PA[:, 512 * qo:512 * qo + 512],
                    HTB[:, base0 + 128 * j:base0 + 128 * j + 128],
                    WB[wb][:, 2048 * j + 512 * qo:2048 * j + 512 * qo + 512],
                    start=(j == 0), stop=False), inc="sPE")
        nm1 = "Q1R" if quant else "C1"
        for qo in (2 * hh, 2 * hh + 1):
            bias_mm(PA[:, 512 * qo:512 * qo + 512], nm1, half * 4 + qo)
        st["w1h%d" % hh] = em.val("sPE")

    def ffn_evach(c, half, quant, hh):
        st = FST[c]
        W("act", "sPE", st["w1h%d" % hh])
        lo = 1024 * hh
        if not quant:
            em.op("act", lambda e, lo=lo: e.activation(
                HTOK[:, lo:lo + 1024], PA[:, lo:lo + 1024], AF.Relu), inc="sACT")
            st["evA%d" % hh] = (em.val("sACT"), 0)
            st["hid%d" % hh] = ("sACT", em.val("sACT"))
        else:
            if hh == 0:
                W("act", "sDVE", em.val("sDVE"))
            em.op("act", lambda e, lo=lo: e.activation(
                QT[:, lo:lo + 1024], PA[:, lo:lo + 1024], AF.Identity, bias=RM, scale=16.0), inc="sACT")
            em.op("act", lambda e, lo=lo: e.activation(
                SQ[:, lo:lo + 1024], QT[:, lo:lo + 1024], AF.Relu, bias=-RM, scale=1.0), inc="sACT")
            va = em.val("sACT")
            W("dve", "sACT", va)
            em.op("dve", lambda e, lo=lo: e.tensor_scalar(
                HTOK[:, lo:lo + 1024], SQ[:, lo:lo + 1024], 127.0, 1.0 / 16.0, ALU.min, ALU.mult), inc="sDVE")
            st["evA%d" % hh] = (va, em.val("sDVE"))
            st["hid%d" % hh] = ("sDVE", em.val("sDVE"))

    def ffn_hidTh(c, hh, ptt_free=0):
        st = FST[c]
        sem_n, v = st["hid%d" % hh]
        W("pe", sem_n, v)
        if ptt_free:
            W("pe", "sACT", ptt_free)
        va, gg = transposes(lambda j, hh=hh: HTOK[:, 1024 * hh + 128 * j:1024 * hh + 128 * j + 128],
                            lambda g, hh=hh: (T1Q if hh == 0 else T1K)[:, 512 * g:512 * g + 512], 8, PTTC, PTTD)
        st["hidT%d" % hh] = va
        st["hidTg%d" % hh] = gg

    def ffn_w2h(c, wb, half, quant, hh, prev, pb_free=0):
        st = FST[c]
        if hh == 0:
            if prev is not None and "res" in FST.get(prev, {}):
                W("pe", "sDVE", FST[prev]["res"])
            elif pb_free:
                W("pe", "sDVE", pb_free)
        em.pesync()
        srcb = T1Q if hh == 0 else T1K
        hg = st["hidTg%d" % hh]
        for j in range(8):
            if j % 4 == 0:
                W("pe", "sACT", hg[j // 4])
            for cc in range(2):
                em.op("pe", lambda e, j=j, cc=cc, srcb=srcb, hh=hh: e.matmul(
                    PB[:, 512 * cc:512 * cc + 512],
                    srcb[:, 128 * j:128 * j + 128],
                    WB[wb][:, 16384 + 1024 * (8 * hh + j) + 512 * cc:16384 + 1024 * (8 * hh + j) + 512 * cc + 512],
                    start=(hh == 0 and j == 0), stop=(half == 0 and hh == 1 and j == 7)), inc="sPE")
        if hh == 1:
            if half == 1:
                nm2 = "Q2R" if quant else "C2"
                for cc in range(2):
                    bias_mm(PB[:, 512 * cc:512 * cc + 512], nm2, cc)
            st["hidT_evac"] = st["hidT1"]
            st["w2"] = em.val("sPE")

    def ffn_residual(blk, c, half, quant, last_stage=False, xl_next=None):
        st = FST[c]
        xap = XS[:, DM * c:DM * c + DM]
        W("dve", "sPE", st["w2"])
        em.op("dve", lambda e: e.tensor_tensor(xap, xap, PB[:], ALU.add), inc="sDVE")
        if half == 1 and not quant:
            W("sp", "sDVE", em.val("sDVE"))
            xr = dma(SQ[:, 0:1024],
                     bass.AP(xin.tensor, (blk * BLK + c * CH) * DM, [[DM, 128], [1, DM]]))
            W("dve", "sDMA", xr)
            em.op("dve", lambda e: e.tensor_tensor(xap, xap, SQ[:, 0:1024], ALU.add), inc="sDVE")
        st["res"] = em.val("sDVE")
        if last_stage:
            sx = "sXY%d" % c
            W("sp", "sDVE", st["res"])
            dma(bass.AP(yout.tensor, (blk * BLK + c * CH) * DM, [[DM, 128], [1, DM]]),
                XS[:, DM * c:DM * c + DM], inc=sx)
            if blk + 1 < NBLK:
                W("sp", sx, em.val(sx))
                dma(XS[:, DM * c:DM * c + DM],
                    bass.AP(xin.tensor, ((blk + 1) * BLK + c * CH) * DM, [[DM, 128], [1, DM]]),
                    inc=sx)
                xl_next[c] = (sx, em.val(sx))

    def stage_ffn_pipelined(blk, wb, v_wload, half, quant, bnd, last_stage=False, xl_next=None):
        if half == 0:
            FST.clear()
        ffn_front(blk, 0, half, quant)
        W("pe", "sDVE", bnd["pc_dve"])
        ffn_front_T(0, half, ptt_free=bnd["pc_act"])
        for cm in range(NCH):
            cf = cm + 1 if cm + 1 < NCH else None
            if cf is not None:
                ffn_front(blk, cf, half, quant)
            prev = cm - 1 if cm > 0 else None
            ffn_w1h(cm, wb, v_wload, half, quant, 0, prev, bnd)
            ffn_w1h(cm, wb, v_wload, half, quant, 1, prev, bnd)
            ffn_evach(cm, half, quant, 0)
            prev_hevac = FST.get(cm - 1, {}).get("hidT_evac", 0)
            if cm == 0:
                prev_hevac = FST[0].get("stash", 0) if half == 0 else bnd["pc_act"]
            early_ft = (cf is not None) and cm > 0
            if early_ft:
                ffn_front_T(cf, half, ptt_free=prev_hevac)
            if half == 0 and early_ft:
                pfree = FST[cf].get("stash", 0)
            else:
                pfree = prev_hevac
            ffn_hidTh(cm, 0, ptt_free=pfree)
            ffn_evach(cm, half, quant, 1)
            ffn_w2h(cm, wb, half, quant, 0, prev, pb_free=bnd["pb_dve"] if cm == 0 else 0)
            if cf is not None and not early_ft:
                ffn_front_T(cf, half, ptt_free=FST[cm]["hidT0"])
                pf1 = FST[cf].get("stash", 0) if half == 0 else FST[cm]["hidT0"]
            else:
                pf1 = FST[cm]["hidT0"]
            ffn_hidTh(cm, 1, ptt_free=pf1)
            ffn_w2h(cm, wb, half, quant, 1, prev)
            ffn_residual(blk, cm, half, quant, last_stage=last_stage, xl_next=xl_next)
        return {"pa_act": max(FST[3]["evA0"][0], FST[3]["evA1"][0]),
                "pa_dve": max(FST[3]["evA0"][1], FST[3]["evA1"][1]),
                "pb_dve": FST[3]["res"], "pc_act": FST[3]["hidT_evac"], "pc_dve": 0}

    prev_pe_done = [0, 0]
    bnd = prep_bnd
    gidx = 0
    xl = {}
    for blk in range(NBLK):
        if blk == 0:
            W("sp", "sDVE", em.val("sDVE"))
            W("sp", "sPOOL", v_prep_lanes["sPOOL"])
            for c in range(NCH):
                dma(XS[:, DM * c:DM * c + DM],
                    bass.AP(xin.tensor, (c * CH) * DM, [[DM, 128], [1, DM]]),
                    inc="sXY%d" % c)
                xl[c] = ("sXY%d" % c, em.val("sXY%d" % c))
        for sidx in range(DBG_STAGES):
            par = gidx % 2
            wsem = "sWL%d" % par
            if gidx in (1, 3):
                W("sp", "sDMA2", v_prep[0])
                W("sp", "sGP", v_prep[1])
            else:
                st_v, gp_v = img_ready[sidx]
                W("sp", "sDMA2", st_v)
                if gp_v:
                    W("sp", "sGP", gp_v)
            for sem_n, v in rows_gate[sidx]:
                W("pe", sem_n, v)
            if blk == 0:
                W("pe", "sPOOL", v_prep_lanes["sPOOL"])
            if prev_pe_done[par] > 0:
                W("sp", "sPE", prev_pe_done[par])
            wl = dma(WB[par][:], simg[sidx][:], inc=wsem)
            if sidx == 0:
                bnd = stage_attn_pipelined(par, (wsem, wl), xl, bnd)
            elif sidx in (1, 2):
                bnd = stage_ffn_pipelined(blk, par, (wsem, wl), sidx - 1, False, bnd)
            else:
                last = (sidx == DBG_STAGES - 1)
                xln = {}
                bnd = stage_ffn_pipelined(blk, par, (wsem, wl), sidx - 3, True, bnd,
                                          last_stage=last, xl_next=xln)
                if last:
                    xl = xln
            prev_pe_done[par] = em.val("sPE")
            gidx += 1

    finals = [("sXY%d" % c, em.val("sXY%d" % c)) for c in range(NCH)]
    for _s, _v in finals:
        em.ops["sp"].append(lambda e, _s=_s, _v=_v: e.wait_ge(em.sems[_s], _v))

    with nc.Block() as block:
        block.sync(lambda e: ([f(e) for f in em.ops["sp"]], None)[1])
        block.tensor(lambda e: ([f(e) for f in em.ops["pe"]], None)[1])
        block.vector(lambda e: ([f(e) for f in em.ops["dve"]], None)[1])
        block.scalar(lambda e: ([f(e) for f in em.ops["act"]], None)[1])
        block.gpsimd(lambda e: ([f(e) for f in em.ops["pool"]], None)[1])

    for c in reversed(ctxs):
        c.__exit__(None, None, None)
    return nc


def _prep_host(inputs):
    pq, pv, pwo = _perms()
    f32 = np.float32
    Wq = np.asarray(inputs["L0_attn_Wq"], f32)
    Wk = np.asarray(inputs["L0_attn_Wk"], f32)
    Wv = np.asarray(inputs["L0_attn_Wv"], f32)
    Wo = np.asarray(inputs["L0_attn_Wo"], f32)
    feed = {
        "wqt": np.ascontiguousarray(Wq[pq, :].T),
        "wkt": np.ascontiguousarray(Wk[pq, :].T),
        "wvt": np.ascontiguousarray(Wv[pv, :].T),
        "wot": np.ascontiguousarray(Wo[:, pwo].T),
        "w1t": np.ascontiguousarray(np.asarray(inputs["L0_attn_W1"], f32).T),
        "w2t": np.ascontiguousarray(np.asarray(inputs["L0_attn_W2"], f32).T),
        "q1t": np.ascontiguousarray(np.asarray(inputs["L0_ffn_w1"], f32).T),
        "q2t": np.ascontiguousarray(np.asarray(inputs["L0_ffn_w2"], f32).T),
        "vb1": np.ascontiguousarray(np.asarray(inputs["L0_attn_b1"], f32).reshape(1, FF)),
        "vfb1": np.ascontiguousarray(np.asarray(inputs["L0_ffn_b1"], f32).reshape(1, FF)),
    }
    vecm = np.zeros((16, DM), f32)
    vecm[0] = np.asarray(inputs["L0_attn_ln1_g"], f32)
    vecm[1] = np.asarray(inputs["L0_attn_ln1_b"], f32)
    vecm[2] = np.asarray(inputs["L0_attn_ln2_g"], f32)
    vecm[3] = np.asarray(inputs["L0_attn_ln2_b"], f32)
    vecm[4] = np.asarray(inputs["L0_attn_bo"], f32)
    vecm[5] = np.asarray(inputs["L0_attn_b2"], f32)
    vecm[6] = np.asarray(inputs["L0_ffn_b2"], f32)
    vecm[7, 0:32] = np.asarray(inputs["L0_attn_lq1"], f32)
    vecm[7, 32:64] = np.asarray(inputs["L0_attn_lq2"], f32)
    vecm[7, 64:96] = np.asarray(inputs["L0_attn_lk1"], f32)
    vecm[7, 96:128] = np.asarray(inputs["L0_attn_lk2"], f32)
    feed["vec"] = vecm
    return feed


def kernel(**inputs):
    x = np.asarray(inputs["L0_attn_x"], np.float32).reshape(-1, DM)
    feed = _prep_host(inputs)
    nc = build_program()
    in_maps = []
    for c in range(N_CORES):
        m = dict(feed)
        m["x"] = np.ascontiguousarray(x[c * T_CORE:(c + 1) * T_CORE])
        in_maps.append(m)
    res = run_bass_kernel_spmd(nc, in_maps, core_ids=list(range(N_CORES)))
    y = np.concatenate([np.asarray(res.results[c]["y"]) for c in range(N_CORES)], axis=0)
    return y.reshape(4, 8192, DM)


# version 13
# speedup vs baseline: 1.3666x; 1.0222x over previous
import numpy as np
import ml_dtypes
import concourse.bass as bass
import concourse.mybir as mybir
from concourse.bass_utils import run_bass_kernel_spmd

F32 = mybir.dt.float32
BF16 = mybir.dt.bfloat16
AF = mybir.ActivationFunctionType
ALU = mybir.AluOpType
AX = mybir.AxisListType

N_CORES = 8
DBG_STAGES = 5
T_CORE = 4096
BLK = 512
CH = 128
NBLK = T_CORE // BLK
NCH = BLK // CH
DM = 1024
FF = 4096
EPS = 1e-5
RM = float(np.float32(1.5 * 2.0 ** 23))
LAMBDA_INIT = 0.8 - 0.6 * float(np.exp(-0.3 * 1.0))
QSCALE = float(32 ** -0.5)


def _perms():
    pq = np.empty(1024, dtype=np.int64)
    for q in range(32):
        s, h = q // 16, q % 16
        for d in range(32):
            pq[q * 32 + d] = h * 64 + d * 2 + s
    pv = np.empty(1024, dtype=np.int64)
    for dd in range(64):
        for h in range(16):
            pv[dd * 16 + h] = h * 64 + dd
    pwo = np.empty(1024, dtype=np.int64)
    for dd in range(64):
        for a in range(16):
            pwo[dd * 16 + a] = a * 64 + dd
    return pq, pv, pwo


class Em:
    def __init__(self, nc):
        self.nc = nc
        self.ops = {k: [] for k in ("pe", "dve", "act", "pool", "sp")}
        self.sems = {}
        self.cnt = {}
        self.hwm = {k: {} for k in self.ops}

    def sem(self, name):
        self.sems[name] = self.nc.semaphore(name).__enter__()
        self.cnt[name] = 0

    def inc(self, name, by=1):
        self.cnt[name] += by
        return self.cnt[name]

    def val(self, name):
        return self.cnt[name]

    def wait(self, eng, name, v):
        if v <= 0 or self.hwm[eng].get(name, 0) >= v:
            return
        self.hwm[eng][name] = v
        s = self.sems[name]
        self.ops[eng].append(lambda e, s=s, v=v: e.wait_ge(s, v))

    def op(self, eng, fn, inc=None, by=1):
        if inc is None:
            self.ops[eng].append(fn)
            return None
        if eng in ("dve", "act", "pool"):
            self.wait(eng, inc, self.val(inc))
        v = self.inc(inc, by)
        s = self.sems[inc]
        self.ops[eng].append(lambda e, fn=fn, s=s, by=by: fn(e).then_inc(s, by))
        return v

    def pesync(self):
        self.wait("pe", "sPE", self.val("sPE"))


def build_program():
    nc = bass.Bass("TRN2", target_bir_lowering=False)

    xin = nc.dram_tensor("x", [T_CORE, DM], F32, kind="ExternalInput").ap()
    yout = nc.dram_tensor("y", [T_CORE, DM], F32, kind="ExternalOutput").ap()
    wqt = nc.dram_tensor("wqt", [DM, DM], F32, kind="ExternalInput").ap()
    wkt = nc.dram_tensor("wkt", [DM, DM], F32, kind="ExternalInput").ap()
    wvt = nc.dram_tensor("wvt", [DM, DM], F32, kind="ExternalInput").ap()
    wot = nc.dram_tensor("wot", [DM, DM], F32, kind="ExternalInput").ap()
    w1t = nc.dram_tensor("w1t", [DM, FF], F32, kind="ExternalInput").ap()
    w2t = nc.dram_tensor("w2t", [FF, DM], F32, kind="ExternalInput").ap()
    q1t = nc.dram_tensor("q1t", [DM, FF], F32, kind="ExternalInput").ap()
    q2t = nc.dram_tensor("q2t", [FF, DM], F32, kind="ExternalInput").ap()
    vec = nc.dram_tensor("vec", [16, DM], F32, kind="ExternalInput").ap()
    vb1 = nc.dram_tensor("vb1", [1, FF], F32, kind="ExternalInput").ap()
    vfb1 = nc.dram_tensor("vfb1", [1, FF], F32, kind="ExternalInput").ap()
    simg = [nc.dram_tensor(f"simg{i}", [128, 32768], BF16, kind="Internal").ap()
            for i in range(5)]

    em = Em(nc)
    for s in ("sDMA", "sPE", "sDVE", "sACT", "sPOOL", "sDMA2", "sWL0", "sWL1", "sXY0", "sXY1", "sXY2", "sXY3", "sGP", "sPF0", "sPF1", "sPF2", "sPF3"):
        em.sem(s)

    ctxs = []

    def sb(name, shape, dtype):
        t = nc.sbuf_tensor(name, shape, dtype)
        h = t.__enter__()
        ctxs.append(t)
        return h

    def psalloc(name, shape, dtype):
        t = nc.psum_tensor(name, shape, dtype)
        h = t.__enter__()
        ctxs.append(t)
        return h

    XS = sb("xs", [128, NCH * DM], F32)
    WB = [sb("wb0", [128, 32768], BF16), sb("wb1", [128, 32768], BF16)]
    UB = sb("ub", [128, DM], BF16)
    UB2 = sb("ub2", [128, DM], BF16)
    HT = sb("ht", [128, 1024], BF16)
    HTB = sb("htb", [128, 4096], BF16)
    T1Q = sb("t1q", [128, 1024], BF16)
    T1K = sb("t1k", [128, 1024], BF16)
    T1VP = sb("t1vp", [128, 2048], BF16)
    ATTN = sb("attn", [128, 1024], BF16)
    IN2 = sb("in2", [128, 1024], BF16)
    AT2T = sb("at2t", [128, 1024], BF16)
    SQ = sb("sq", [128, 2048], F32)
    XR = sb("xr", [128, 2048], BF16)
    T1N = sb("t1n", [128, 1024], BF16)
    HTOK = sb("htok", [128, 2048], BF16)
    QT = sb("qt", [128, 2048], F32)
    BROWS = sb("brows", [128, 4096], BF16)
    IDN = sb("idn", [128, 128], BF16)
    ONESB = sb("onesb", [128, 128], BF16)
    CAB = sb("cab", [128, 32], BF16)
    SM = sb("sm", [128, 16], F32)
    SM2 = sb("sm2", [128, 12], F32)
    SMX = sb("smx", [128, 32], F32)
    FACTB = sb("factb", [128, 32], F32)
    GSL = sb("gsl", [128, 24], F32)
    GSL2 = sb("gsl2", [128, 8], F32)
    GSLQ = sb("gslq", [128, 8], F32)
    GSB = sb("gsb", [128, 16], BF16)
    LAM = sb("lam", [128, 1], F32)
    SMALL = sb("small", [1, 8], F32)
    SMALLB = sb("smallb", [1, 2], BF16)
    CST = sb("cst", [128, 12], F32)

    EXPS = QT[:, 0:1024]
    TRASH = QT[:, 1024:2048]
    SSUB = SQ[:, 0:1024]
    T1O = HTOK
    VR = XS[:, 0:1024]
    VTV = HTB[:, 0:2048]
    VTQ = HTB[:, 2048:3072]
    VTK = HTB[:, 3072:4096]
    NEGMAX = SMX
    RINV = SMX
    FACT = FACTB

    ROWMAP = {
        ("CQ", 0): (0, 0), ("CQ", 1): (0, 512),
        ("CK", 0): (32, 0), ("CK", 1): (32, 512),
        ("CV", 0): (64, 0), ("CV", 1): (64, 512),
        ("CO", 0): (96, 0), ("CO", 1): (96, 512),
    }
    for c in range(8):
        ROWMAP[("Q1R", c)] = (0, 1024 + 512 * c) if c < 4 else (64, 1024 + 512 * (c - 4))
        ROWMAP[("C1", c)] = (32, 1024 + 512 * c) if c < 4 else (96, 1024 + 512 * (c - 4))
    for c in range(2):
        ROWMAP[("C2", c)] = (0, 3072 + 512 * c)
        ROWMAP[("Q2R", c)] = (32, 3072 + 512 * c)

    def brow(name, c):
        p, col = ROWMAP[(name, c)]
        return BROWS[p:p + 1, col:col + 512], p

    PA = psalloc("pa", [128, 2048], F32)
    PB = psalloc("pb", [128, 1024], F32)
    PC = psalloc("pc", [128, 1024], F32)
    PTTA = PA[:, 0:512].bitcast(BF16)
    PTTB = PB[:, 0:512].bitcast(BF16)
    PTTB2 = PB[:, 512:1024].bitcast(BF16)
    PTTD = PC[:, 512:1024].bitcast(BF16)
    TRASHB = QT[:, 1024:1536].bitcast(BF16)
    TRASHB2 = QT[:, 1536:2048].bitcast(BF16)
    PTTC = PC[:, 0:512].bitcast(BF16)


    W = em.wait

    def dma(dst, src, inc="sDMA"):
        return em.op("sp", lambda e: e.dma_start(dst, src), inc=inc, by=16)

    LN1ML = float(np.log(1.0 - LAMBDA_INIT))
    for i, cv in enumerate([RM, -RM, 16.0, -0.5, 64.0, 1.0 / 64.0, -RM / 64.0, EPS, LN1ML]):
        em.op("pool", lambda e, i=i, cv=cv: e.memset(CST[:, i:i + 1], float(cv)), inc="sPOOL")
        nc.const_aps.register(F32, float(cv), CST[:, i:i + 1])
    em.op("pool", lambda e: e.memset(ONESB[:], 1.0), inc="sPOOL")
    W("pool", "sPOOL", em.val("sPOOL"))
    em.op("pool", lambda e: e.affine_select(
        IDN[:], ONESB[:], [[-1, 128]], ALU.is_equal, 0.0,
        base=0, channel_multiplier=1), inc="sPOOL")
    em.op("pool", lambda e: e.memset(T1VP[:], 0.0), inc="sPOOL")
    em.op("pool", lambda e: e.memset(IN2[:], 0.0), inc="sPOOL")
    em.op("pool", lambda e: e.memset(XR[:], 0.0), inc="sPOOL")
    em.op("pool", lambda e: e.memset(CAB[:], 0.0), inc="sPOOL")
    em.op("pool", lambda e: e.memset(BROWS[:], 0.0), inc="sPOOL")
    W("pool", "sPOOL", em.val("sPOOL"))
    for c in range(4):
        em.op("pool", lambda e, c=c: e.affine_select(
            CAB[32 * c:32 * c + 32, 0:16], ONESB[32 * c:32 * c + 32, 0:16],
            [[-1, 16]], ALU.is_equal, 0.0, base=0, channel_multiplier=1), inc="sPOOL")
    v_consts = em.val("sPOOL")

    for j in range(8):
        dma(GSL[:, j:j + 1], vec[0:1, 128 * j:128 * j + 128].transpose([1, 0]))
    for j in range(8):
        dma(GSL[:, 8 + j:9 + j], vec[2:3, 128 * j:128 * j + 128].transpose([1, 0]))
    for j in range(8):
        dma(GSL[:, 16 + j:17 + j], vec[1:2, 128 * j:128 * j + 128].transpose([1, 0]))
    for j in range(8):
        dma(GSL2[:, j:j + 1], vec[3:4, 128 * j:128 * j + 128].transpose([1, 0]))
    v_gsl = em.val("sDMA")
    dma(VR[0:1, :], vec[4:5, :])
    dma(VR[32:33, :], vec[5:6, :])
    dma(VR[64:65, :], vec[6:7, :])
    dma(VR[96:97, 0:128], vec[7:8, 0:128])
    v_vr = em.val("sDMA")


    W("dve", "sDMA", v_vr)
    em.op("dve", lambda e: e.tensor_mul(VR[96:97, 256:288], VR[96:97, 0:32], VR[96:97, 64:96]), inc="sDVE")
    em.op("dve", lambda e: e.tensor_mul(VR[96:97, 288:320], VR[96:97, 32:64], VR[96:97, 96:128]), inc="sDVE")
    em.op("dve", lambda e: e.reduce_sum(VR[96:97, 320:321], VR[96:97, 256:288].rearrange("p (a b) -> p a b", a=1), axis=AX.X), inc="sDVE")
    em.op("dve", lambda e: e.reduce_sum(VR[96:97, 321:322], VR[96:97, 288:320].rearrange("p (a b) -> p a b", a=1), axis=AX.X), inc="sDVE")
    W("act", "sDVE", em.val("sDVE"))
    em.op("act", lambda e: e.activation(VR[96:97, 322:323], VR[96:97, 320:321], AF.Exp), inc="sACT")
    em.op("act", lambda e: e.activation(VR[96:97, 323:324], VR[96:97, 321:322], AF.Exp), inc="sACT")
    W("dve", "sACT", em.val("sACT"))
    em.op("dve", lambda e: e.tensor_sub(VR[96:97, 324:325], VR[96:97, 323:324], VR[96:97, 322:323]), inc="sDVE")
    em.op("dve", lambda e: e.tensor_scalar(SMALL[0:1, 0:1], VR[96:97, 324:325], float(LAMBDA_INIT), None, ALU.subtract), inc="sDVE")
    em.op("dve", lambda e: e.tensor_copy(SMALLB[0:1, 0:1], SMALL[0:1, 0:1]), inc="sDVE")
    W("pe", "sDVE", em.val("sDVE"))
    W("pe", "sPOOL", v_consts)
    em.op("pe", lambda e: e.matmul(PA[:, 0:1], ONESB[0:1, 0:128], SMALLB[0:1, 0:1],
                                   start=True, stop=True), inc="sPE")
    W("dve", "sPE", em.val("sPE"))
    em.op("dve", lambda e: e.tensor_copy(LAM[:, 0:1], PA[:, 0:1]), inc="sDVE")
    W("pool", "sDVE", em.val("sDVE"))
    W("pool", "sPOOL", v_consts)
    for c in range(4):
        em.op("pool", lambda e, c=c: e.affine_select(
            CAB[32 * c:32 * c + 32, 16:32],
            LAM[32 * c:32 * c + 32, 0:1].broadcast_to([32, 16]),
            [[-1, 16]], ALU.is_equal, 0.0, base=-16, channel_multiplier=1), inc="sPOOL")
    v_cab = em.val("sPOOL")

    def gdma(dst, src):
        return em.op("pool", lambda e: e.dma_start(dst, src), inc="sGP", by=16)

    for j in range(8):
        gdma(simg[0][:, 24576 + 1024 * j:24576 + 1024 * j + 1024], wot[128 * j:128 * j + 128, :])
    gp_img0 = em.val("sGP")
    gp_img = {1: 0, 2: 0}

    img_ready = {}
    rows_gate = {}
    HTBF = HTB[:].bitcast(F32)
    ring = [SQ[:, 0:2048], QT[:, 0:2048], HTBF[:, 0:1024], HTBF[:, 1024:2048]]
    ring_w = [2048, 2048, 1024, 1024]
    rs = {"i": 0, "done": {}}

    def _staged(src_ap, width, consume, row):
        while True:
            s = rs["i"] % 4
            rs["i"] += 1
            if ring_w[s] >= width:
                break
        psem = "sPF%d" % s
        buf = ring[s][0:1, 0:width] if row else ring[s][:, 0:width]
        if s in rs["done"]:
            for sem_n, v in rs["done"][s]:
                W("sp", sem_n, v)
        vd = dma(buf, src_ap, inc=psem)
        rs["done"][s] = consume(buf, vd, psem)

    def staged(src_ap, width, consume):
        _staged(src_ap, width, consume, False)

    def staged_row(src_ap, width, consume):
        _staged(src_ap, width, consume, True)

    def staged_q(src_ap, consume):
        while True:
            s = rs["i"] % 4
            rs["i"] += 1
            if ring_w[s] >= 2048:
                break
        psem = "sPF%d" % s
        slot = ring[s]
        if s in rs["done"]:
            for sem_n, v in rs["done"][s]:
                W("sp", sem_n, v)
        vd = dma(slot[:, 0:1024], src_ap, inc=psem)
        rs["done"][s] = consume(slot, vd, psem)

    def prep():
        W("dve", "sDMA", v_vr)
        W("act", "sDMA", v_vr)
        W("pe", "sDMA", v_vr)
        W("pe", "sPOOL", v_consts)
        em.op("dve", lambda e: e.tensor_scalar(GSLQ[:], GSL[:, 0:8], QSCALE, None, ALU.mult), inc="sDVE")
        v_gslq = em.val("sDVE")

        pa_busy = {"v": em.val("sDVE")}
        for mi, wt in enumerate([wqt, wkt, wvt]):
            for j in range(8):
                def consume(buf, vd, psem, mi=mi, j=j):
                    col = mi * 8192 + j * 1024
                    scap = GSLQ[:, j:j + 1] if mi == 0 else GSL[:, j:j + 1]
                    W("act", psem, vd)
                    if mi == 0:
                        W("act", "sDVE", v_gslq)
                    em.op("act", lambda e, col=col, scap=scap: e.activation(
                        WB[1][:, col:col + 1024], buf, AF.Copy, scale=scap), inc="sACT")
                    va = em.val("sACT")
                    W("pe", psem, vd)
                    if j == 0:
                        W("pe", "sDVE", pa_busy["v"])
                    for hh in range(2):
                        em.op("pe", lambda e, j=j, hh=hh: e.matmul(
                            PA[0:1, 512 * hh:512 * hh + 512], GSL[:, 16 + j:17 + j],
                            buf[:, 512 * hh:512 * hh + 512],
                            start=(j == 0), stop=(j == 7)), inc="sPE")
                    return [("sACT", va), ("sPE", em.val("sPE"))]
                staged(wt[128 * j:128 * j + 128, :], 1024, consume)
            W("dve", "sPE", em.val("sPE"))
            nm = ["CQ", "CK", "CV"][mi]
            sc = QSCALE if mi == 0 else 1.0
            for c in range(2):
                r, _ = brow(nm, c)
                em.op("dve", lambda e, r=r, c=c, sc=sc: e.tensor_scalar(
                    r, PA[0:1, 512 * c:512 * c + 512], float(sc), None, ALU.mult), inc="sDVE")
            pa_busy["v"] = em.val("sDVE")
        W("pool", "sDMA", v_vr)
        for c in range(2):
            r, _ = brow("CO", c)
            em.op("pool", lambda e, r=r, c=c: e.tensor_copy(r, VR[0:1, 512 * c:512 * c + 512]), inc="sPOOL")
            r2, _ = brow("C2", c)
            em.op("pool", lambda e, r=r2, c=c: e.tensor_copy(r, VR[32:33, 512 * c:512 * c + 512]), inc="sPOOL")
        v_vr_done = em.val("sPOOL")
        def qrow_consume(buf, vd, psem, names):
            W("pool", psem, vd)
            em.op("pool", lambda e: e.tensor_scalar(buf, buf, 64.0, RM, ALU.mult, ALU.add), inc="sPOOL")
            em.op("pool", lambda e: e.tensor_scalar(buf, buf, RM, -128.0, ALU.subtract, ALU.max), inc="sPOOL")
            for c, (nm, ch) in enumerate(names):
                r, _ = brow(nm, ch)
                em.op("pool", lambda e, r=r, c=c: e.tensor_scalar(
                    r, buf[0:1, 512 * c:512 * c + 512], 127.0, 1.0 / 64.0, ALU.min, ALU.mult), inc="sPOOL")
            return [("sPOOL", em.val("sPOOL"))]
        staged_row(vec[6:7, :], 1024, lambda buf, vd, psem: qrow_consume(buf, vd, psem, [("Q2R", 0), ("Q2R", 1)]))
        for hh in range(2):
            staged_row(vfb1[0:1, 2048 * hh:2048 * hh + 2048], 2048,
                       lambda buf, vd, psem, hh=hh: qrow_consume(buf, vd, psem, [("Q1R", hh * 4 + c) for c in range(4)]))
        rows_gate[0] = [("sDVE", pa_busy["v"]), ("sPOOL", v_vr_done)]
        rows_gate[3] = rows_gate[4] = [("sPOOL", em.val("sPOOL")), ("sPOOL", v_vr_done)]
        W("sp", "sACT", em.val("sACT"))
        W("sp", "sDMA2", em.val("sDMA2"))
        vstore = dma(simg[0][:, 0:24576], WB[1][:, 0:24576], inc="sDMA2")
        img_ready[0] = (vstore, gp_img0)

        for half in range(2):
            W("act", "sDMA2", em.val("sDMA2"))
            for j in range(8):
                def consume(buf, vd, psem, j=j, half=half):
                    col = j * 2048
                    W("act", psem, vd)
                    em.op("act", lambda e, col=col, j=j: e.activation(
                        WB[1][:, col:col + 2048], buf, AF.Copy, scale=GSL[:, 8 + j:9 + j]), inc="sACT")
                    va = em.val("sACT")
                    W("pe", psem, vd)
                    if j == 0:
                        W("pe", "sDVE", pa_busy["v"])
                    for qo in range(4):
                        em.op("pe", lambda e, j=j, qo=qo: e.matmul(
                            PA[0:1, 512 * qo:512 * qo + 512], GSL2[:, j:j + 1],
                            buf[:, 512 * qo:512 * qo + 512],
                            start=(j == 0), stop=(j == 7)), inc="sPE")
                    return [("sACT", va), ("sPE", em.val("sPE"))]
                staged(w1t[128 * j:128 * j + 128, 2048 * half:2048 * half + 2048], 2048, consume)
            for j in range(16):
                lane = ("act", "dve", "pool")[j % 3]
                def consume_w2(buf, vd, psem, j=j, lane=lane):
                    lsem = {"act": "sACT", "dve": "sDVE", "pool": "sPOOL"}[lane]
                    W(lane, psem, vd)
                    dst = WB[1][:, 16384 + 1024 * j:16384 + 1024 * j + 1024]
                    if lane == "act":
                        em.op("act", lambda e: e.copy(dst, buf), inc="sACT")
                    else:
                        em.op(lane, lambda e: e.tensor_copy(dst, buf), inc=lsem)
                    return [(lsem, em.val(lsem))]
                staged(w2t[2048 * half + 128 * j:2048 * half + 128 * j + 128, :], 1024, consume_w2)
            def consume_b(buf, vd, psem, half=half):
                W("dve", psem, vd)
                W("dve", "sPE", em.val("sPE"))
                for qo in range(4):
                    r, _ = brow("C1", half * 4 + qo)
                    em.op("dve", lambda e, r=r, qo=qo, buf=buf: e.tensor_tensor(
                        r, buf[0:1, 512 * qo:512 * qo + 512],
                        PA[0:1, 512 * qo:512 * qo + 512], ALU.add), inc="sDVE")
                return [("sDVE", em.val("sDVE"))]
            staged_row(vb1[0:1, 2048 * half:2048 * half + 2048], 2048, consume_b)
            pa_busy["v"] = em.val("sDVE")
            W("sp", "sACT", em.val("sACT"))
            W("sp", "sDVE", em.val("sDVE"))
            W("sp", "sPOOL", em.val("sPOOL"))
            W("sp", "sDMA2", em.val("sDMA2"))
            vstore = dma(simg[1 + half][:], WB[1][:], inc="sDMA2")
            img_ready[1 + half] = (vstore, gp_img[1 + half])
        rows_gate[1] = rows_gate[2] = [("sDVE", pa_busy["v"])]

        def quant_tile(src_ap, col):
            def consume(slot, vd, psem, col=col):
                a, b = slot[:, 0:1024], slot[:, 1024:2048]
                W("act", psem, vd)
                em.op("act", lambda e, a=a, b=b: e.activation(
                    b, a, AF.Identity, bias=RM, scale=64.0), inc="sACT")
                va = em.val("sACT")
                W("dve", "sACT", va)
                em.op("dve", lambda e, b=b: e.tensor_scalar(
                    b, b, RM - 128.0, RM + 127.0, ALU.max, ALU.min), inc="sDVE")
                vd2 = em.val("sDVE")
                W("act", "sDVE", vd2)
                em.op("act", lambda e, b=b, col=col: e.activation(
                    WB[1][:, col:col + 1024], b, AF.Identity, bias=-RM / 64.0, scale=1.0 / 64.0), inc="sACT")
                return [("sACT", em.val("sACT")), ("sDVE", vd2)]
            staged_q(src_ap, consume)
        for half in range(2):
            W("dve", "sDMA2", em.val("sDMA2"))
            W("pool", "sDMA2", em.val("sDMA2"))
            W("act", "sDMA2", em.val("sDMA2"))
            for j in range(8):
                for k in range(2):
                    quant_tile(q1t[128 * j:128 * j + 128, 2048 * half + 1024 * k:2048 * half + 1024 * k + 1024],
                               j * 2048 + 1024 * k)
            for j in range(16):
                quant_tile(q2t[2048 * half + 128 * j:2048 * half + 128 * j + 128, :], 16384 + j * 1024)
            W("sp", "sDVE", em.val("sDVE"))
            W("sp", "sPOOL", em.val("sPOOL"))
            W("sp", "sACT", em.val("sACT"))
            W("sp", "sDMA2", em.val("sDMA2"))
            vstore = dma(simg[3 + half][:], WB[1][:], inc="sDMA2")
            img_ready[3 + half] = (vstore, 0)

    prep()
    v_prep = (em.val("sDMA2"), em.val("sGP"))
    prep_bnd = {"pa_dve": em.val("sDVE"), "pa_act": 0, "pb_dve": 0, "pc_dve": 0, "pc_act": 0}
    v_prep_lanes = {s: em.val(s) for s in ("sDVE", "sPOOL", "sACT", "sPE")}

    for eng in ("pe", "dve", "act"):
        W(eng, "sPOOL", v_cab)

    def ln_stats(xap, compound):
        em.op("dve", lambda e: e.reduce_sum(SM[:, 0:1], xap.rearrange("p (a b) -> p a b", a=1), axis=AX.X), inc="sDVE")
        em.op("act", lambda e: e.activation(TRASH, xap, AF.Square, accum_out=SM2[:, 0:1]), inc="sACT")
        W("dve", "sACT", em.val("sACT"))
        em.op("dve", lambda e: e.tensor_scalar(SM[:, 0:1], SM[:, 0:1], 1.0 / 1024.0, None, ALU.mult), inc="sDVE")
        em.op("dve", lambda e: e.tensor_mul(SM2[:, 1:2], SM[:, 0:1], SM[:, 0:1]), inc="sDVE")
        em.op("dve", lambda e: e.tensor_scalar(SM2[:, 2:3], SM2[:, 0:1], 1.0 / 1024.0, None, ALU.mult), inc="sDVE")
        em.op("dve", lambda e: e.tensor_sub(SM2[:, 3:4], SM2[:, 2:3], SM2[:, 1:2]), inc="sDVE")
        em.op("dve", lambda e: e.tensor_scalar(SM2[:, 4:5], SM2[:, 3:4], EPS, None, ALU.add), inc="sDVE")
        W("act", "sDVE", em.val("sDVE"))
        em.op("act", lambda e: e.activation(SM2[:, 5:6], SM2[:, 4:5], AF.Ln), inc="sACT")
        em.op("act", lambda e: e.activation(SM[:, 1:2], SM2[:, 5:6], AF.Exp, scale=-0.5), inc="sACT")
        if compound:
            em.op("dve", lambda e: e.reciprocal(SM2[:, 6:7], SM2[:, 4:5]), inc="sDVE")
            em.op("dve", lambda e: e.tensor_mul(SM2[:, 7:8], SM2[:, 3:4], SM2[:, 6:7]), inc="sDVE")
            em.op("dve", lambda e: e.tensor_scalar(SM2[:, 8:9], SM2[:, 7:8], EPS, None, ALU.add), inc="sDVE")
            W("act", "sDVE", em.val("sDVE"))
            em.op("act", lambda e: e.activation(SM2[:, 9:10], SM2[:, 8:9], AF.Ln), inc="sACT")
            W("dve", "sACT", em.val("sACT"))
            em.op("dve", lambda e: e.tensor_add(SM2[:, 10:11], SM2[:, 5:6], SM2[:, 9:10]), inc="sDVE")
            W("act", "sDVE", em.val("sDVE"))
            em.op("act", lambda e: e.activation(SM[:, 2:3], SM2[:, 10:11], AF.Exp, scale=-0.5), inc="sACT")
        return em.val("sACT")

    def transposes(tiles, dst_fn, ntiles, ptt, ptt2=None):
        em.pesync()
        evac_hwm = {0: 0, 1: 0}
        gvals = []
        for j in range(ntiles):
            g = j // 4
            par = g % 2 if ptt2 is not None else 0
            buf = ptt2 if par == 1 else ptt
            if j % 4 == 0 and evac_hwm[par]:
                W("pe", "sACT", evac_hwm[par])
            em.op("pe", lambda e, j=j, buf=buf: e.transpose(
                buf[:, 128 * (j % 4):128 * (j % 4) + 128], tiles(j), IDN[:]), inc="sPE")
            if j % 4 == 3:
                W("act", "sPE", em.val("sPE"))
                em.op("act", lambda e, g=g, buf=buf: e.copy(dst_fn(g), buf[:, 0:512]), inc="sACT")
                evac_hwm[par] = em.val("sACT")
                gvals.append(em.val("sACT"))
        return em.val("sACT"), gvals

    def bias_mm(ps_ap, name, chunk, start=False, stop=True):
        r, p = brow(name, chunk)
        em.op("pe", lambda e, r=r, p=p, ps_ap=ps_ap: e.matmul(
            ps_ap, ONESB[p:p + 1, 0:128], r, start=start, stop=stop,
            tile_position=(p, 0)), inc="sPE")

    def ubp(c):
        return UB if c % 2 == 0 else UB2

    AST = {}

    def attn_front(c, xl):
        st = AST.setdefault(c, {})
        xap = XS[:, DM * c:DM * c + DM]
        sx, vx = xl[c]
        W("dve", sx, vx)
        W("act", sx, vx)
        W("pool", sx, vx)
        if c - 3 in AST and "applied_p" in AST[c - 3]:
            W("act", "sPOOL", AST[c - 3]["applied_p"])
            W("act", "sDVE", AST[c - 3]["applied_d"])
        if c - 2 in AST and "ht_pe" in AST[c - 2]:
            W("pool", "sPE", AST[c - 2]["ht_pe"])
        if c - 1 in AST and "ub" in AST[c - 1]:
            W("dve", "sPOOL", AST[c - 1]["ub"])
            W("act", "sPOOL", AST[c - 1]["ub"])
        va = ln_stats(xap, compound=True)
        W("pool", "sACT", va)
        W("pool", "sDVE", em.val("sDVE"))
        em.op("pool", lambda e: e.tensor_scalar(ubp(c)[:], xap, SM[:, 0:1], SM[:, 2:3], ALU.subtract, ALU.mult), inc="sPOOL")
        p2 = 8 + 2 * c
        em.op("pool", lambda e, p2=p2: e.tensor_copy(SM[:, p2:p2 + 2], SM[:, 0:2]), inc="sPOOL")
        st["ub"] = em.val("sPOOL")

    def attn_fq(c, wb, v_wload, prev, bnd=None):
        st = AST[c]
        if prev is not None:
            W("pe", "sDVE", AST[prev]["t1_dve"])
            W("pe", "sACT", max(AST[prev]["t1_act"], AST[prev]["exp"]))
        else:
            W("pe", "sDVE", max(bnd["pa_dve"], bnd["pb_dve"]))
            W("pe", "sACT", bnd["pa_act"])
        W("pe", "sPOOL", st["ub"])
        va, htg = transposes(lambda j: ubp(c)[:, 128 * j:128 * j + 128],
                             lambda g: HT[:, 512 * g:512 * g + 512], 8, PTTB, PTTB2)
        st["ht_pe"] = em.val("sPE")
        W("pe", v_wload[0], v_wload[1])
        em.pesync()
        for j in range(8):
            if j % 4 == 0:
                W("pe", "sACT", htg[j // 4])
            for mi in range(2):
                for half in range(2):
                    em.op("pe", lambda e, j=j, mi=mi, half=half: e.matmul(
                        PA[:, 1024 * mi + 512 * half:1024 * mi + 512 * half + 512],
                        HT[:, 128 * j:128 * j + 128],
                        WB[wb][:, 8192 * mi + 1024 * j + 512 * half:8192 * mi + 1024 * j + 512 * half + 512],
                        start=(j == 0), stop=False), inc="sPE")
        for j in range(8):
            for half in range(2):
                em.op("pe", lambda e, j=j, half=half: e.matmul(
                    PB[:, 512 * half:512 * half + 512],
                    HT[:, 128 * j:128 * j + 128],
                    WB[wb][:, 16384 + 1024 * j + 512 * half:16384 + 1024 * j + 512 * half + 512],
                    start=(j == 0), stop=False), inc="sPE")
        for mi, nm in enumerate(["CQ", "CK", "CV"]):
            dstps, dcol = (PA, 1024 * mi) if mi < 2 else (PB, 0)
            for half in range(2):
                bias_mm(dstps[:, dcol + 512 * half:dcol + 512 * half + 512], nm, half)
        st["qkv"] = em.val("sPE")

    def attn_evacs(c, prevst=None):
        st = AST[c]
        W("dve", "sPE", st["qkv"])
        W("act", "sPE", st["qkv"])
        if prevst is not None:
            W("act", "sDVE", prevst["vt"])
        em.op("dve", lambda e: e.tensor_copy(T1Q[:], PA[:, 0:1024]), inc="sDVE")
        em.op("act", lambda e: e.copy(T1K[:], PA[:, 1024:2048]), inc="sACT")
        st["t1_act"] = em.val("sACT")
        em.op("dve", lambda e: e.tensor_copy(
            T1VP[:].rearrange("p (d h) -> p d h", d=64)[:, :, 0:16], PB[:, 0:1024]), inc="sDVE")
        st["t1_dve"] = em.val("sDVE")

    def attn_vtrans_qk(c, prevst=None):
        st = AST[c]
        if prevst is not None:
            W("dve", "sPE", prevst["scores"])
        W("dve", "sACT", st["t1_act"])
        em.op("dve", lambda e: e.transpose(VTQ[:], T1Q[:]), inc="sDVE")
        em.op("dve", lambda e: e.transpose(VTK[:], T1K[:]), inc="sDVE")
        st["vt"] = em.val("sDVE")

    def attn_vtrans_v(c, prevst=None):
        st = AST[c]
        if prevst is not None:
            W("dve", "sPE", prevst["o2"])
        em.op("dve", lambda e: e.transpose(VTV[:], T1VP[:]), inc="sDVE")
        st["vtv"] = em.val("sDVE")

    def attn_scores(c):
        st = AST[c]
        W("pe", "sDVE", st["vt"])
        W("pe", "sACT", st["t1_act"])
        em.pesync()
        for i in range(128):
            r, tl = i % 4, i // 4
            em.op("pe", lambda e, r=r, tl=tl: e.matmul(
                PA[32 * r:32 * r + 32, 32 * tl:32 * tl + 32],
                VTQ[32 * r:32 * r + 32, :].rearrange("p (w cc) -> p w cc", cc=32)[:, :, tl],
                VTK[32 * r:32 * r + 32, :].rearrange("p (w cc) -> p w cc", cc=32)[:, :, tl],
                start=True, stop=True, tile_position=(32 * r, 32 * r)), inc="sPE")
        st["scores"] = em.val("sPE")

    def attn_softmax(c):
        st = AST[c]
        W("act", "sPE", st["scores"])
        em.op("act", lambda e: e.activation(EXPS, PA[:, 0:1024], AF.Exp), inc="sACT")
        st["exp"] = em.val("sACT")
        W("dve", "sACT", st["exp"])
        em.op("dve", lambda e: e.reduce_sum(
            RINV[:], EXPS.rearrange("p (t k) -> p t k", k=32), axis=AX.X), inc="sDVE")
        em.op("dve", lambda e: e.reciprocal(RINV[:], RINV[:]), inc="sDVE")
        st["rinv"] = em.val("sDVE")
        W("pool", "sDVE", st["rinv"])
        em.op("pool", lambda e: e.tensor_tensor(
            ATTN[:].rearrange("p (t k) -> p t k", k=32),
            EXPS.rearrange("p (t k) -> p t k", k=32),
            RINV[:].unsqueeze(2).broadcast_to([128, 32, 32]), ALU.mult), inc="sPOOL")
        st["attn"] = em.val("sPOOL")

    def attn_combine(c, prev, bnd=None):
        st = AST[c]
        W("pe", "sPOOL", st["attn"])
        W("pe", "sPOOL", v_cab)
        if prev is not None and "conn" in AST.get(prev, {}):
            W("pe", "sDVE", AST[prev]["conn"])
        elif bnd is not None:
            W("pe", "sDVE", bnd["pc_dve"])
            W("pe", "sACT", bnd["pc_act"])
        em.pesync()
        for r in range(4):
            for half in range(2):
                em.op("pe", lambda e, r=r, half=half: e.matmul(
                    PC[32 * r:32 * r + 16, 0:512],
                    CAB[32 * r:32 * r + 32, 16 * half:16 * half + 16],
                    ATTN[32 * r:32 * r + 32, :].rearrange("p (t k) -> p t k", k=32)[:, :, 16 * half:16 * half + 16],
                    start=(half == 0), stop=(half == 1), tile_position=(32 * r, 32 * r)), inc="sPE")
        st["comb"] = em.val("sPE")

    def attn_in2(c):
        st = AST[c]
        W("act", "sPE", st["comb"])
        for r in range(4):
            em.op("act", lambda e, r=r: e.copy(
                IN2[32 * r:32 * r + 16, :].rearrange("p (t k) -> p t k", k=32)[:, :, 0:16],
                PC[32 * r:32 * r + 16, 0:512]), inc="sACT")
        st["in2"] = em.val("sACT")
        W("dve", "sACT", st["in2"])
        em.op("dve", lambda e: e.transpose(AT2T[:], IN2[:]), inc="sDVE")
        st["at2t"] = em.val("sDVE")

    def attn_einsum2(c):
        st = AST[c]
        W("pe", "sDVE", max(st["at2t"], st["vtv"]))
        W("pe", "sACT", st["in2"])
        for h in range(2):
            em.pesync()
            for i in range(64):
                r, tj = i % 4, i // 4
                tl = 16 * h + tj
                em.op("pe", lambda e, r=r, tl=tl, tj=tj: e.matmul(
                    PC[32 * r:32 * r + 32, 64 * tj:64 * tj + 64],
                    AT2T[32 * r:32 * r + 32, 32 * tl:32 * tl + 32],
                    VTV[32 * r:32 * r + 32, :].rearrange("p (w cc) -> p w cc", cc=32)[:, :, tl],
                    start=True, stop=True, tile_position=(32 * r, 32 * r)), inc="sPE")
            W("act", "sPE", em.val("sPE"))
            dstb = TRASHB if h == 0 else TRASHB2
            em.op("act", lambda e, h=h: e.activation(
                SQ[:, 1024 * h:1024 * h + 1024], PC[:], AF.Square), inc="sACT")
            st["o2sq%d" % h] = em.val("sACT")
            em.op("act", lambda e, dstb=dstb: e.copy(dstb, PC[:]), inc="sACT")
            st["o2e%d" % h] = em.val("sACT")
            if h == 0:
                W("pe", "sACT", st["o2e0"])
        st["o2"] = em.val("sPE")

    def attn_rms(c):
        st = AST[c]
        W("dve", "sACT", st["o2sq0"])
        em.op("dve", lambda e: e.reduce_sum(
            FACT[:, 0:16], SQ[:, 0:1024].rearrange("p (t d) -> p t d", d=64), axis=AX.X), inc="sDVE")
        W("dve", "sACT", st["o2sq1"])
        em.op("dve", lambda e: e.reduce_sum(
            FACT[:, 16:32], SQ[:, 1024:2048].rearrange("p (t d) -> p t d", d=64), axis=AX.X), inc="sDVE")
        W("act", "sDVE", em.val("sDVE"))
        em.op("act", lambda e: e.activation(FACT[:], FACT[:], AF.Ln, scale=1.0 / 64.0, bias=EPS), inc="sACT")
        em.op("act", lambda e: e.activation(FACT[:], FACT[:], AF.Exp, scale=-0.5, bias=LN1ML), inc="sACT")
        vfact = em.val("sACT")
        W("dve", "sACT", vfact)
        W("pool", "sACT", vfact)
        W("pool", "sACT", st["o2e1"])
        for f in range(2):
            em.op("dve", lambda e, f=f: e.tensor_tensor(
                bass.AP(XR[:].tensor, XR[:].offset + 1024 * f, [[2048, 128], [1, 16], [32, 32]]),
                TRASHB.rearrange("p (t d) -> p t d", d=64)[:, :, 32 * f:32 * f + 32],
                FACT[:, 0:16].unsqueeze(2).broadcast_to([128, 16, 32]), ALU.mult), inc="sDVE")
            em.op("pool", lambda e, f=f: e.tensor_tensor(
                bass.AP(XR[:].tensor, XR[:].offset + 1024 * f + 16, [[2048, 128], [1, 16], [32, 32]]),
                TRASHB2.rearrange("p (t d) -> p t d", d=64)[:, :, 32 * f:32 * f + 32],
                FACT[:, 16:32].unsqueeze(2).broadcast_to([128, 16, 32]), ALU.mult), inc="sPOOL")
            if f == 0:
                st["ap0d"] = em.val("sDVE")
                st["ap0p"] = em.val("sPOOL")
        st["applied_d"] = em.val("sDVE")
        st["applied_p"] = em.val("sPOOL")
        W("dve", "sPOOL", st["ap0p"])
        em.op("dve", lambda e: e.transpose(T1O[:, 0:1024], XR[:, 0:1024]), inc="sDVE")
        em.op("dve", lambda e: e.tensor_copy(
            T1N[:, 0:512].rearrange("p (d a) -> p d a", a=16),
            T1O[:, 0:1024].rearrange("p (d a) -> p d a", a=32)[:, :, 0:16]), inc="sDVE")
        st["t1n0"] = em.val("sDVE")
        W("dve", "sPOOL", st["applied_p"])
        em.op("dve", lambda e: e.transpose(T1O[:, 1024:2048], XR[:, 1024:2048]), inc="sDVE")
        em.op("dve", lambda e: e.tensor_copy(
            T1N[:, 512:1024].rearrange("p (d a) -> p d a", a=16),
            T1O[:, 1024:2048].rearrange("p (d a) -> p d a", a=32)[:, :, 0:16]), inc="sDVE")
        st["t1n1"] = em.val("sDVE")

    def attn_fmwo(c, wb):
        st = AST[c]
        W("pe", "sDVE", st["t1n0"])
        if c - 1 in AST and "fm" in AST[c - 1]:
            W("pe", "sACT", AST[c - 1]["fm"])
        va0, _ = transposes(lambda j: T1N[:, 128 * j:128 * j + 128],
                            lambda g: HT[:, 0:512], 4, PTTC, None)
        W("pe", "sDVE", st["t1n1"])
        va, _ = transposes(lambda j: T1N[:, 512 + 128 * j:512 + 128 * j + 128],
                           lambda g: HT[:, 512:1024], 4, PTTD, None)
        st["fm"] = va
        W("pe", "sACT", va)
        em.pesync()
        for j in range(8):
            for half in range(2):
                em.op("pe", lambda e, j=j, half=half: e.matmul(
                    PC[:, 512 * half:512 * half + 512],
                    HT[:, 128 * j:128 * j + 128],
                    WB[wb][:, 24576 + 1024 * j + 512 * half:24576 + 1024 * j + 512 * half + 512],
                    start=(j == 0), stop=False), inc="sPE")
        for half in range(2):
            bias_mm(PC[:, 512 * half:512 * half + 512], "CO", half)
        st["wo"] = em.val("sPE")

    def attn_conn(c):
        st = AST[c]
        xap = XS[:, DM * c:DM * c + DM]
        p2 = 8 + 2 * c
        em.op("pool", lambda e, p2=p2: e.tensor_scalar(
            SSUB[:], xap, SM[:, p2:p2 + 1], SM[:, p2 + 1:p2 + 2], ALU.subtract, ALU.mult), inc="sPOOL")
        vss = em.val("sPOOL")
        W("dve", "sPE", st["wo"])
        W("dve", "sPOOL", vss)
        em.op("dve", lambda e: e.tensor_tensor(xap, SSUB[:], PC[:], ALU.add), inc="sDVE")
        st["conn"] = em.val("sDVE")

    def stage_attn_pipelined(wb, v_wload, xl, bnd):
        AST.clear()
        attn_front(0, xl)
        attn_fq(0, wb, v_wload, None, bnd)
        attn_front(1, xl)
        attn_evacs(0)
        attn_vtrans_qk(0)
        attn_vtrans_v(0)
        attn_scores(0)
        attn_softmax(0)
        attn_fq(1, wb, v_wload, 0)
        for k in range(NCH):
            if k + 1 < NCH:
                attn_evacs(k + 1, AST[k])
                attn_vtrans_qk(k + 1, AST[k])
            if k + 2 < NCH:
                attn_front(k + 2, xl)
            attn_combine(k, k - 1 if k > 0 else None, bnd)
            attn_in2(k)
            if k + 1 < NCH:
                attn_scores(k + 1)
            attn_einsum2(k)
            if k + 1 < NCH:
                attn_vtrans_v(k + 1, AST[k])
            attn_rms(k)
            if k + 1 < NCH:
                attn_softmax(k + 1)
            if k + 2 < NCH:
                attn_fq(k + 2, wb, v_wload, k + 1)
            attn_fmwo(k, wb)
            attn_conn(k)
        return {"pa_dve": AST[3]["t1_dve"], "pa_act": AST[3]["exp"],
                "pb_dve": AST[3]["t1_dve"], "pc_dve": AST[3]["conn"], "pc_act": 0}

    FST = {}

    def ffn_front(blk, c, half, quant):
        st = FST.setdefault(c, {})
        xap = XS[:, DM * c:DM * c + DM]
        if half == 1:
            return
        W("act", "sDVE", em.val("sDVE"))
        va = ln_stats(xap, compound=False)
        W("dve", "sACT", va)
        if not quant:
            em.op("dve", lambda e: e.tensor_scalar(
                ubp(c)[:], xap, SM[:, 0:1], SM[:, 1:2], ALU.subtract, ALU.mult), inc="sDVE")
        else:
            em.op("dve", lambda e: e.tensor_scalar(SM[:, 3:4], SM[:, 1:2], 16.0, None, ALU.mult), inc="sDVE")
            em.op("dve", lambda e: e.tensor_scalar(
                SSUB[:], xap, SM[:, 0:1], SM[:, 3:4], ALU.subtract, ALU.mult), inc="sDVE")
            em.op("dve", lambda e: e.tensor_scalar(
                SSUB[:], SSUB[:], RM, RM, ALU.add, ALU.subtract), inc="sDVE")
            em.op("dve", lambda e: e.tensor_scalar(
                SSUB[:], SSUB[:], -128.0, 127.0, ALU.max, ALU.min), inc="sDVE")
            em.op("dve", lambda e: e.tensor_scalar(
                ubp(c)[:], SSUB[:], 1.0 / 16.0, None, ALU.mult), inc="sDVE")
        st["ub"] = em.val("sDVE")

    def ffn_front_T(c, half, ptt_free=0):
        st = FST[c]
        if half == 1:
            return
        W("pe", "sDVE", st["ub"])
        if ptt_free:
            W("pe", "sACT", ptt_free)
        va, gg = transposes(lambda j: ubp(c)[:, 128 * j:128 * j + 128],
                            lambda g, c=c: HTB[:, 1024 * c + 512 * g:1024 * c + 512 * g + 512], 8, PTTC, PTTD)
        st["stash"] = va
        st["stash_g"] = gg
        st["stash_pe"] = em.val("sPE")

    def ffn_w1h(c, wb, v_wload, half, quant, hh, prev, bnd=None):
        st = FST[c]
        if hh == 0:
            W("pe", v_wload[0], v_wload[1])
        if prev is not None and ("evA%d" % hh) in FST.get(prev, {}):
            a, d = FST[prev]["evA%d" % hh]
            W("pe", "sACT", a)
            if d:
                W("pe", "sDVE", d)
        elif bnd is not None:
            W("pe", "sACT", bnd["pa_act"])
            W("pe", "sDVE", bnd["pa_dve"])
        em.pesync()
        base0 = 1024 * c
        sg = st.get("stash_g")
        for j in range(8):
            if sg is not None and j % 4 == 0:
                W("pe", "sACT", sg[j // 4])
            for qo in (2 * hh, 2 * hh + 1):
                em.op("pe", lambda e, j=j, qo=qo, base0=base0: e.matmul(
                    PA[:, 512 * qo:512 * qo + 512],
                    HTB[:, base0 + 128 * j:base0 + 128 * j + 128],
                    WB[wb][:, 2048 * j + 512 * qo:2048 * j + 512 * qo + 512],
                    start=(j == 0), stop=False), inc="sPE")
        nm1 = "Q1R" if quant else "C1"
        for qo in (2 * hh, 2 * hh + 1):
            bias_mm(PA[:, 512 * qo:512 * qo + 512], nm1, half * 4 + qo)
        st["w1h%d" % hh] = em.val("sPE")

    def ffn_evach(c, half, quant, hh):
        st = FST[c]
        W("act", "sPE", st["w1h%d" % hh])
        lo = 1024 * hh
        if not quant:
            em.op("act", lambda e, lo=lo: e.activation(
                HTOK[:, lo:lo + 1024], PA[:, lo:lo + 1024], AF.Relu), inc="sACT")
            st["evA%d" % hh] = (em.val("sACT"), 0)
            st["hid%d" % hh] = ("sACT", em.val("sACT"))
        else:
            if hh == 0:
                W("act", "sDVE", em.val("sDVE"))
            em.op("act", lambda e, lo=lo: e.activation(
                QT[:, lo:lo + 1024], PA[:, lo:lo + 1024], AF.Identity, bias=RM, scale=16.0), inc="sACT")
            em.op("act", lambda e, lo=lo: e.activation(
                SQ[:, lo:lo + 1024], QT[:, lo:lo + 1024], AF.Relu, bias=-RM, scale=1.0), inc="sACT")
            va = em.val("sACT")
            W("dve", "sACT", va)
            em.op("dve", lambda e, lo=lo: e.tensor_scalar(
                HTOK[:, lo:lo + 1024], SQ[:, lo:lo + 1024], 127.0, 1.0 / 16.0, ALU.min, ALU.mult), inc="sDVE")
            st["evA%d" % hh] = (va, em.val("sDVE"))
            st["hid%d" % hh] = ("sDVE", em.val("sDVE"))

    def ffn_hidTh(c, hh, ptt_free=0):
        st = FST[c]
        sem_n, v = st["hid%d" % hh]
        W("pe", sem_n, v)
        if ptt_free:
            W("pe", "sACT", ptt_free)
        va, gg = transposes(lambda j, hh=hh: HTOK[:, 1024 * hh + 128 * j:1024 * hh + 128 * j + 128],
                            lambda g, hh=hh: (T1Q if hh == 0 else T1K)[:, 512 * g:512 * g + 512], 8, PTTC, PTTD)
        st["hidT%d" % hh] = va
        st["hidTg%d" % hh] = gg

    def ffn_w2h(c, wb, half, quant, hh, prev, pb_free=0):
        st = FST[c]
        if hh == 0:
            if prev is not None and "res" in FST.get(prev, {}):
                W("pe", "sDVE", FST[prev]["res"])
            elif pb_free:
                W("pe", "sDVE", pb_free)
        em.pesync()
        srcb = T1Q if hh == 0 else T1K
        hg = st["hidTg%d" % hh]
        for j in range(8):
            if j % 4 == 0:
                W("pe", "sACT", hg[j // 4])
            for cc in range(2):
                em.op("pe", lambda e, j=j, cc=cc, srcb=srcb, hh=hh: e.matmul(
                    PB[:, 512 * cc:512 * cc + 512],
                    srcb[:, 128 * j:128 * j + 128],
                    WB[wb][:, 16384 + 1024 * (8 * hh + j) + 512 * cc:16384 + 1024 * (8 * hh + j) + 512 * cc + 512],
                    start=(hh == 0 and j == 0), stop=(half == 0 and hh == 1 and j == 7)), inc="sPE")
        if hh == 1:
            if half == 1:
                nm2 = "Q2R" if quant else "C2"
                for cc in range(2):
                    bias_mm(PB[:, 512 * cc:512 * cc + 512], nm2, cc)
            st["hidT_evac"] = st["hidT1"]
            st["w2"] = em.val("sPE")

    def ffn_residual(blk, c, half, quant, last_stage=False, xl_next=None):
        st = FST[c]
        xap = XS[:, DM * c:DM * c + DM]
        W("dve", "sPE", st["w2"])
        em.op("dve", lambda e: e.tensor_tensor(xap, xap, PB[:], ALU.add), inc="sDVE")
        if half == 1 and not quant:
            W("sp", "sDVE", em.val("sDVE"))
            xr = dma(SQ[:, 0:1024],
                     bass.AP(xin.tensor, (blk * BLK + c * CH) * DM, [[DM, 128], [1, DM]]))
            W("dve", "sDMA", xr)
            em.op("dve", lambda e: e.tensor_tensor(xap, xap, SQ[:, 0:1024], ALU.add), inc="sDVE")
        st["res"] = em.val("sDVE")
        if last_stage:
            sx = "sXY%d" % c
            W("sp", "sDVE", st["res"])
            dma(bass.AP(yout.tensor, (blk * BLK + c * CH) * DM, [[DM, 128], [1, DM]]),
                XS[:, DM * c:DM * c + DM], inc=sx)
            if blk + 1 < NBLK:
                W("sp", sx, em.val(sx))
                dma(XS[:, DM * c:DM * c + DM],
                    bass.AP(xin.tensor, ((blk + 1) * BLK + c * CH) * DM, [[DM, 128], [1, DM]]),
                    inc=sx)
                xl_next[c] = (sx, em.val(sx))

    def stage_ffn_pipelined(blk, wb, v_wload, half, quant, bnd, last_stage=False, xl_next=None):
        if half == 0:
            FST.clear()
        ffn_front(blk, 0, half, quant)
        W("pe", "sDVE", bnd["pc_dve"])
        ffn_front_T(0, half, ptt_free=bnd["pc_act"])
        for cm in range(NCH):
            cf = cm + 1 if cm + 1 < NCH else None
            if cf is not None:
                ffn_front(blk, cf, half, quant)
            prev = cm - 1 if cm > 0 else None
            ffn_w1h(cm, wb, v_wload, half, quant, 0, prev, bnd)
            ffn_w1h(cm, wb, v_wload, half, quant, 1, prev, bnd)
            ffn_evach(cm, half, quant, 0)
            prev_hevac = FST.get(cm - 1, {}).get("hidT_evac", 0)
            if cm == 0:
                prev_hevac = FST[0].get("stash", 0) if half == 0 else bnd["pc_act"]
            early_ft = (cf is not None) and cm > 0
            if early_ft:
                ffn_front_T(cf, half, ptt_free=prev_hevac)
            if half == 0 and early_ft:
                pfree = FST[cf].get("stash", 0)
            else:
                pfree = prev_hevac
            ffn_hidTh(cm, 0, ptt_free=pfree)
            ffn_evach(cm, half, quant, 1)
            ffn_w2h(cm, wb, half, quant, 0, prev, pb_free=bnd["pb_dve"] if cm == 0 else 0)
            if cf is not None and not early_ft:
                ffn_front_T(cf, half, ptt_free=FST[cm]["hidT0"])
                pf1 = FST[cf].get("stash", 0) if half == 0 else FST[cm]["hidT0"]
            else:
                pf1 = FST[cm]["hidT0"]
            ffn_hidTh(cm, 1, ptt_free=pf1)
            ffn_w2h(cm, wb, half, quant, 1, prev)
            ffn_residual(blk, cm, half, quant, last_stage=last_stage, xl_next=xl_next)
        return {"pa_act": max(FST[3]["evA0"][0], FST[3]["evA1"][0]),
                "pa_dve": max(FST[3]["evA0"][1], FST[3]["evA1"][1]),
                "pb_dve": FST[3]["res"], "pc_act": FST[3]["hidT_evac"], "pc_dve": 0}

    prev_pe_done = [0, 0]
    bnd = prep_bnd
    gidx = 0
    xl = {}
    for blk in range(NBLK):
        if blk == 0:
            W("sp", "sDVE", em.val("sDVE"))
            W("sp", "sPOOL", v_prep_lanes["sPOOL"])
            for c in range(NCH):
                dma(XS[:, DM * c:DM * c + DM],
                    bass.AP(xin.tensor, (c * CH) * DM, [[DM, 128], [1, DM]]),
                    inc="sXY%d" % c)
                xl[c] = ("sXY%d" % c, em.val("sXY%d" % c))
        for sidx in range(DBG_STAGES):
            par = gidx % 2
            wsem = "sWL%d" % par
            if gidx in (1, 3):
                W("sp", "sDMA2", v_prep[0])
                W("sp", "sGP", v_prep[1])
            else:
                st_v, gp_v = img_ready[sidx]
                W("sp", "sDMA2", st_v)
                if gp_v:
                    W("sp", "sGP", gp_v)
            for sem_n, v in rows_gate[sidx]:
                W("pe", sem_n, v)
            if blk == 0:
                W("pe", "sPOOL", v_prep_lanes["sPOOL"])
            if prev_pe_done[par] > 0:
                W("sp", "sPE", prev_pe_done[par])
            wl = dma(WB[par][:], simg[sidx][:], inc=wsem)
            if sidx == 0:
                bnd = stage_attn_pipelined(par, (wsem, wl), xl, bnd)
            elif sidx in (1, 2):
                bnd = stage_ffn_pipelined(blk, par, (wsem, wl), sidx - 1, False, bnd)
            else:
                last = (sidx == DBG_STAGES - 1)
                xln = {}
                bnd = stage_ffn_pipelined(blk, par, (wsem, wl), sidx - 3, True, bnd,
                                          last_stage=last, xl_next=xln)
                if last:
                    xl = xln
            prev_pe_done[par] = em.val("sPE")
            gidx += 1

    finals = [("sXY%d" % c, em.val("sXY%d" % c)) for c in range(NCH)]
    for _s, _v in finals:
        em.ops["sp"].append(lambda e, _s=_s, _v=_v: e.wait_ge(em.sems[_s], _v))

    with nc.Block() as block:
        block.sync(lambda e: ([f(e) for f in em.ops["sp"]], None)[1])
        block.tensor(lambda e: ([f(e) for f in em.ops["pe"]], None)[1])
        block.vector(lambda e: ([f(e) for f in em.ops["dve"]], None)[1])
        block.scalar(lambda e: ([f(e) for f in em.ops["act"]], None)[1])
        block.gpsimd(lambda e: ([f(e) for f in em.ops["pool"]], None)[1])

    for c in reversed(ctxs):
        c.__exit__(None, None, None)
    return nc


def _prep_host(inputs):
    pq, pv, pwo = _perms()
    f32 = np.float32
    Wq = np.asarray(inputs["L0_attn_Wq"], f32)
    Wk = np.asarray(inputs["L0_attn_Wk"], f32)
    Wv = np.asarray(inputs["L0_attn_Wv"], f32)
    Wo = np.asarray(inputs["L0_attn_Wo"], f32)
    feed = {
        "wqt": np.ascontiguousarray(Wq[pq, :].T),
        "wkt": np.ascontiguousarray(Wk[pq, :].T),
        "wvt": np.ascontiguousarray(Wv[pv, :].T),
        "wot": np.ascontiguousarray(Wo[:, pwo].T),
        "w1t": np.ascontiguousarray(np.asarray(inputs["L0_attn_W1"], f32).T),
        "w2t": np.ascontiguousarray(np.asarray(inputs["L0_attn_W2"], f32).T),
        "q1t": np.ascontiguousarray(np.asarray(inputs["L0_ffn_w1"], f32).T),
        "q2t": np.ascontiguousarray(np.asarray(inputs["L0_ffn_w2"], f32).T),
        "vb1": np.ascontiguousarray(np.asarray(inputs["L0_attn_b1"], f32).reshape(1, FF)),
        "vfb1": np.ascontiguousarray(np.asarray(inputs["L0_ffn_b1"], f32).reshape(1, FF)),
    }
    vecm = np.zeros((16, DM), f32)
    vecm[0] = np.asarray(inputs["L0_attn_ln1_g"], f32)
    vecm[1] = np.asarray(inputs["L0_attn_ln1_b"], f32)
    vecm[2] = np.asarray(inputs["L0_attn_ln2_g"], f32)
    vecm[3] = np.asarray(inputs["L0_attn_ln2_b"], f32)
    vecm[4] = np.asarray(inputs["L0_attn_bo"], f32)
    vecm[5] = np.asarray(inputs["L0_attn_b2"], f32)
    vecm[6] = np.asarray(inputs["L0_ffn_b2"], f32)
    vecm[7, 0:32] = np.asarray(inputs["L0_attn_lq1"], f32)
    vecm[7, 32:64] = np.asarray(inputs["L0_attn_lq2"], f32)
    vecm[7, 64:96] = np.asarray(inputs["L0_attn_lk1"], f32)
    vecm[7, 96:128] = np.asarray(inputs["L0_attn_lk2"], f32)
    feed["vec"] = vecm
    return feed


def kernel(**inputs):
    x = np.asarray(inputs["L0_attn_x"], np.float32).reshape(-1, DM)
    feed = _prep_host(inputs)
    nc = build_program()
    in_maps = []
    for c in range(N_CORES):
        m = dict(feed)
        m["x"] = np.ascontiguousarray(x[c * T_CORE:(c + 1) * T_CORE])
        in_maps.append(m)
    res = run_bass_kernel_spmd(nc, in_maps, core_ids=list(range(N_CORES)))
    y = np.concatenate([np.asarray(res.results[c]["y"]) for c in range(N_CORES)], axis=0)
    return y.reshape(4, 8192, DM)
